# version 01
# baseline (speedup 1.0000x reference)
import numpy as np
import ml_dtypes
from contextlib import ExitStack
import concourse.bass as bass
import concourse.mybir as mybir
from concourse.bass_utils import run_bass_kernel_spmd

F32 = mybir.dt.float32
BF16 = mybir.dt.bfloat16
AF = mybir.ActivationFunctionType
ALU = mybir.AluOpType

B, T, D = 4, 4096, 1024
PC = 128
ND = D // PC
ENGS = ("tensor", "vector", "scalar", "gpsimd", "sync")


class Prog:

    def __init__(self, nc, es):
        self.nc = nc
        self.es = es
        self.q = {e: [] for e in ENGS}
        self.sems = {e: es.enter_context(nc.semaphore(f"pm_{e}")) for e in ENGS}
        self.cnt = {e: 0 for e in ENGS}
        self.dsems = {}
        self.dcnt = {}

    def on(self, eng, fn, inc=0):
        self.cnt[eng] += inc
        sem = self.sems[eng]

        def thunk(h, fn=fn, inc=inc, sem=sem):
            inst = fn(h)
            if inc:
                inst.then_inc(sem, inc)
        self.q[eng].append(thunk)
        return self.cnt[eng]

    def dma(self, eng, out, in_, slot=None):
        if slot is None:
            t = out.tensor if hasattr(out, "tensor") else out
            slot = getattr(t, "name", None) or str(id(out))
        if slot not in self.dsems:
            self.dsems[slot] = self.es.enter_context(
                self.nc.semaphore(f"dm_{len(self.dsems)}"))
            self.dcnt[slot] = 0
        self.dcnt[slot] += 16
        sem = self.dsems[slot]

        def thunk(h, o=out, i=in_, sem=sem):
            h.dma_start(out=o, in_=i).then_inc(sem, 16)
        self.q[eng].append(thunk)
        return ("D", slot, self.dcnt[slot])

    def wait(self, eng, other, count):
        if isinstance(count, tuple) and count and count[0] == "D":
            _, slot, c = count
            sem = self.dsems[slot]
            self.q[eng].append(lambda h, sem=sem, c=c: h.wait_ge(sem, c))
            return
        if not isinstance(count, int) or count <= 0:
            return
        sem = self.sems[other]
        self.q[eng].append(lambda h, sem=sem, c=count: h.wait_ge(sem, c))

    def wm(self, eng):
        return self.cnt[eng]

    def barrier(self, engs=ENGS):
        marks = {e: self.cnt[e] for e in engs}
        dmarks = dict(self.dcnt)
        for e in engs:
            for o in engs:
                if o != e:
                    self.wait(e, o, marks[o])
            for slot, c in dmarks.items():
                if c > 0:
                    sem = self.dsems[slot]
                    self.q[e].append(lambda h, sem=sem, c=c: h.wait_ge(sem, c))

    def dma_last(self, slot):
        return ("D", slot, self.dcnt.get(slot, 0)) if self.dcnt.get(slot, 0) else 0

    def drain_dma(self, eng):
        for slot, c in self.dcnt.items():
            if c > 0:
                sem = self.dsems[slot]
                self.q[eng].append(lambda h, sem=sem, c=c: h.wait_ge(sem, c))

    def finalize(self, block):
        for e, thunks in self.q.items():
            def body(h, thunks=thunks):
                for t in thunks:
                    t(h)
            getattr(block, e)(body)


def sb(es, nc, name, shape, dt):
    return es.enter_context(nc.sbuf_tensor(name, shape, dt))


def ps(es, nc, name, shape):
    return es.enter_context(nc.psum_tensor(name, shape, F32))


def fm(es, nc, name, nchunks, n, dt):
    return es.enter_context(nc.sbuf_tensor(name, [PC, nchunks, n], dt))


def fm_dram(dram_ap, nchunks):
    return dram_ap.rearrange("(c p) n -> p c n", p=PC)


CONST_VALS = [1e-5, -0.5, 0.5, 1e-8, 1.1920929e-7, 60.0, -1.0, 1.0, 0.0,
              0.99, 0.125, 2.0, -2.0]


def load_consts(p, es, nc, io):
    C = {}
    C["invn"] = sb(es, nc, "sb_invn", [PC, 1], F32)[:, :]
    C["ones1"] = sb(es, nc, "sb_ones1", [1, PC], F32)[:, :]
    C["identbf"] = sb(es, nc, "sb_identbf", [PC, PC], BF16)[:, :]
    C["ones1b"] = sb(es, nc, "sb_ones1b", [1, PC], BF16)[:, :]
    p.dma("gpsimd", C["identbf"], io["c_ident"])
    gm = p.dma("gpsimd", C["ones1b"], io["c_ones1"])
    cv = sb(es, nc, "sb_cvals", [PC, len(CONST_VALS)], F32)
    for i, v in enumerate(CONST_VALS):
        nc.const_aps.register(F32, float(v), cv[:, i:i + 1])
    C["cval"] = {float(v): cv[:, i:i + 1] for i, v in enumerate(CONST_VALS)}
    p.dma("sync", C["invn"], io["c_invn"])
    p.dma("sync", cv[:, :], io["c_vals"])
    wm = p.dma("sync", C["ones1"], io["c_ones1"])
    for e2 in ("tensor", "vector", "scalar", "gpsimd"):
        p.wait(e2, "sync", wm)
        p.wait(e2, "sync", p.dma_last("sb_invn"))
        p.wait(e2, "sync", p.dma_last("sb_cvals"))
        p.wait(e2, "gpsimd", p.dma_last("sb_identbf"))
        p.wait(e2, "gpsimd", gm)
    return C



def emit_ln(p, C, x_sb, xn_sb, tc, ps_m, ps_r, x2_sb, rows, gcols=None,
            bcols=None, eps=1e-5, out_bf=None, outs=None):
    p.wait("scalar", "vector", p.wm("vector"))
    s_sq = p.on("scalar", lambda e: e.activation(
        x2_sb[:, :, 0:tc], x_sb[:, :, 0:tc], AF.Square), inc=1)
    p.wait("tensor", "scalar", s_sq)
    p.wait("tensor", "vector", p.wm("vector"))
    mrow = rows[0:1, 0:tc]
    vrow = rows[0:1, tc:2 * tc]
    t_mm = 0
    for src, dst in ((x_sb, ps_m), (x2_sb, ps_r)):
        for c in range(ND):
            t_mm = p.on("tensor", lambda e, src=src, dst=dst, c=c: e.matmul(
                dst[0:1, 0:tc], C["invn"], src[:, c, 0:tc],
                start=(c == 0), stop=(c == ND - 1)), inc=(1 if c == ND - 1 else 0))
    p.wait("vector", "tensor", t_mm)
    p.on("vector", lambda e: e.tensor_copy(mrow, ps_m[0:1, 0:tc]))
    p.on("vector", lambda e: e.tensor_mul(vrow, mrow, mrow))
    v_var = p.on("vector", lambda e: e.tensor_sub(vrow, ps_r[0:1, 0:tc], vrow), inc=1)
    p.wait("scalar", "vector", v_var)
    p.on("scalar", lambda e: e.activation(vrow, vrow, AF.Ln, bias=eps))
    s_rs = p.on("scalar", lambda e: e.activation(vrow, vrow, AF.Exp, scale=-0.5), inc=1)
    p.wait("tensor", "scalar", s_rs)
    p.on("tensor", lambda e: e.matmul(ps_m[:, 0:tc], C["ones1"], mrow,
                                      start=True, stop=True))
    t_bc = p.on("tensor", lambda e: e.matmul(ps_r[:, 0:tc], C["ones1"], vrow,
                                             start=True, stop=True), inc=1)
    p.wait("vector", "tensor", t_bc)
    if outs is None:
        outs = lambda c: xn_sb[:, c, 0:tc]
    v_last = 0
    for c in range(ND):
        p.on("vector", lambda e, c=c: e.tensor_sub(
            outs(c), x_sb[:, c, 0:tc], ps_m[:, 0:tc]))
        if gcols is None:
            v_last = p.on("vector", lambda e, c=c: e.tensor_mul(
                outs(c), outs(c), ps_r[:, 0:tc]), inc=1)
        else:
            p.on("vector", lambda e, c=c: e.tensor_mul(
                outs(c), outs(c), ps_r[:, 0:tc]))
            v_last = p.on("vector", lambda e, c=c: e.tensor_scalar(
                outs(c), outs(c), gcols[:, c:c + 1],
                None if bcols is None else bcols[:, c:c + 1],
                ALU.mult, ALU.mult if bcols is None else ALU.add), inc=1)
        if out_bf is not None:
            v_last = p.on("vector", lambda e, c=c: e.tensor_copy(
                out_bf[:, c, 0:tc], outs(c)), inc=1)
    return v_last



def emit_l1ffn(p, es, io, C, tag):
    nc = p.nc
    TC = 512
    wb = fm(es, nc, tag + "wb", ND, D, BF16)
    wA = fm(es, nc, tag + "wA", ND, 8, BF16)
    wR = fm(es, nc, tag + "wR", ND, 8, BF16)
    wB = sb(es, nc, tag + "wB", [128, D], BF16)
    eG = sb(es, nc, tag + "eG", [8, 64], BF16)
    eM = sb(es, nc, tag + "eM", [8, 64], BF16)
    bbc = sb(es, nc, tag + "bbc", [PC, ND], F32)
    rb = sb(es, nc, tag + "rb", [8, 1], F32)
    wtoks = [p.dma("gpsimd", wb[:, :, :], fm_dram(io["ffn_base_w"], ND)),
             p.dma("gpsimd", wA[:, :, :], fm_dram(io["ffn_lora_A"], ND)),
             p.dma("gpsimd", wR[:, :, :], fm_dram(io["ffn_router_w"], ND)),
             p.dma("gpsimd", wB[0:64, :], io["ffn_lora_B"]),
             p.dma("gpsimd", eG[:, :], io["ffn_eG"]),
             p.dma("gpsimd", eM[:, :], io["ffn_eM"]),
             p.dma("sync", bbc[:, :], io["ffn_base_b"]),
             p.dma("sync", rb[:, :], io["ffn_router_b"])]
    v_z = p.on("vector", lambda e: e.memset(wB[64:128, :], 0.0), inc=1)
    for tok in wtoks:
        p.wait("tensor", "gpsimd", tok)
        p.wait("vector", "gpsimd", tok)
    p.wait("tensor", "vector", v_z)

    x_sb = fm(es, nc, tag + "x", ND, TC, F32)
    xn = fm(es, nc, tag + "xn", ND, TC, F32)
    xnb = fm(es, nc, tag + "xnb", ND, TC, BF16)
    x2 = fm(es, nc, tag + "x2", ND, TC, F32)
    rows = sb(es, nc, tag + "rows", [1, 2 * TC], F32)
    prows = sb(es, nc, tag + "prows", [8, TC], F32)
    gat = sb(es, nc, tag + "gat", [8, TC], BF16)
    midb = sb(es, nc, tag + "midb", [8, TC], BF16)
    mge = sb(es, nc, tag + "mge", [64, TC], F32)
    mg = sb(es, nc, tag + "mg", [128, TC], BF16)
    zrow = sb(es, nc, tag + "zrow", [1, TC], F32)
    xo = fm(es, nc, tag + "xo", ND, TC, F32)
    psA = ps(es, nc, tag + "psA", [PC, 512])
    psB = ps(es, nc, tag + "psB", [PC, 512])
    psC = ps(es, nc, tag + "psC", [PC, 512])
    psO = [ps(es, nc, tag + f"psO{i}", [PC, 512]) for i in range(4)]
    v_z2 = p.on("vector", lambda e: e.memset(mg[64:128, :], 0.0), inc=1)

    v_out_prev = v_z2
    g_st_prev = 0
    for ci in range(T // TC):
        c0 = ci * TC
        p.wait("sync", "vector", v_out_prev)
        x_wm = p.dma("sync", x_sb[:, :, :], fm_dram(io["x_in"], ND)[:, :, c0:c0 + TC])
        for e2 in ("scalar", "tensor", "vector"):
            p.wait(e2, "sync", x_wm)
        emit_ln(p, C, x_sb, xn, TC, psA, psB, x2, rows, out_bf=xnb)
        p.wait("tensor", "vector", p.wm("vector"))
        t_g = 0
        for c in range(ND):
            t_g = p.on("tensor", lambda e, c=c: e.matmul(
                psC[0:8, :], wR[:, c, :], xnb[:, c, :], start=(c == 0), stop=(c == ND - 1)),
                inc=(1 if c == ND - 1 else 0))
        p.wait("vector", "tensor", t_g)
        v_lg = p.on("vector", lambda e: e.tensor_scalar(
            prows[:, :], psC[0:8, :], rb[:, :], None, ALU.add), inc=1)
        p.wait("scalar", "vector", v_lg)
        s_pe = p.on("scalar", lambda e: e.activation(prows[:, :], prows[:, :], AF.Exp), inc=1)
        p.wait("tensor", "scalar", s_pe)
        t_z = p.on("tensor", lambda e: e.matmul(
            psC[0:1, :], C["ones8col"], prows[:, :], start=True, stop=True), inc=1)
        p.wait("vector", "tensor", t_z)
        v_zc = p.on("vector", lambda e: e.tensor_copy(zrow[:, :], psC[0:1, :]), inc=1)
        p.wait("scalar", "vector", v_zc)
        p.on("scalar", lambda e: e.activation(zrow[:, :], zrow[:, :], AF.Ln))
        s_rz = p.on("scalar", lambda e: e.activation(
            zrow[:, :], zrow[:, :], AF.Exp, scale=-1.0), inc=1)
        p.wait("tensor", "scalar", s_rz)
        t_rz = p.on("tensor", lambda e: e.matmul(
            psC[0:8, :], C["ones1"][0:1, 0:8], zrow[:, :], start=True, stop=True), inc=1)
        p.wait("vector", "tensor", t_rz)
        v_gat = p.on("vector", lambda e: e.tensor_mul(gat[:, :], prows[:, :], psC[0:8, :]), inc=1)
        t_mid = 0
        for c in range(ND):
            t_mid = p.on("tensor", lambda e, c=c: e.matmul(
                psB[0:8, :], wA[:, c, :], xnb[:, c, :], start=(c == 0), stop=(c == ND - 1)),
                inc=(1 if c == ND - 1 else 0))
        p.wait("vector", "tensor", t_mid)
        v_mid = p.on("vector", lambda e: e.tensor_copy(midb[:, :], psB[0:8, :]), inc=1)
        p.wait("tensor", "vector", max(v_gat, v_mid))
        t_ge = p.on("tensor", lambda e: e.matmul(
            psA[0:64, :], eG[:, :], gat[:, :], start=True, stop=True), inc=1)
        t_me = p.on("tensor", lambda e: e.matmul(
            psB[0:64, :], eM[:, :], midb[:, :], start=True, stop=True), inc=1)
        p.wait("vector", "tensor", t_me)
        p.on("vector", lambda e: e.tensor_copy(mge[:, :], psB[0:64, :]))
        v_mg = p.on("vector", lambda e: e.tensor_mul(mg[0:64, :], mge[:, :], psA[0:64, :]), inc=1)
        p.wait("tensor", "vector", v_mg)
        v_out_prev = 0
        for f in range(2):
            fs = f * 256
            t_o = 0
            for oc in range(ND):
                pt = psO[oc // 2][:, (oc % 2) * 256:(oc % 2) * 256 + 256]
                for c in range(ND):
                    p.on("tensor", lambda e, pt=pt, oc=oc, c=c, fs=fs: e.matmul(
                        pt, wb[:, c, oc * PC:(oc + 1) * PC],
                        xnb[:, c, fs:fs + 256],
                        start=(c == 0), stop=False))
                t_o = p.on("tensor", lambda e, pt=pt, oc=oc, fs=fs: e.matmul(
                    pt, wB[:, oc * PC:(oc + 1) * PC], mg[:, fs:fs + 256],
                    start=False, stop=True), inc=1)
            p.wait("vector", "tensor", t_o)
            if f == 0:
                p.wait("vector", "gpsimd", g_st_prev)
            for oc in range(ND):
                pslice = psO[oc // 2][:, (oc % 2) * 256:(oc % 2) * 256 + 256]
                p.on("vector", lambda e, pslice=pslice, fs=fs, oc=oc: e.tensor_scalar(
                    xo[:, oc, fs:fs + 256], pslice, bbc[:, oc:oc + 1], None, ALU.add))
                v_out_prev = p.on("vector", lambda e, fs=fs, oc=oc: e.tensor_add(
                    xo[:, oc, fs:fs + 256], xo[:, oc, fs:fs + 256],
                    x_sb[:, oc, fs:fs + 256]), inc=1)
            if f == 0:
                p.wait("tensor", "vector", v_out_prev)
        p.wait("gpsimd", "vector", v_out_prev)
        g_st_prev = p.dma("gpsimd", fm_dram(io["x_out"], ND)[:, :, c0:c0 + TC], xo[:, :, :])
    p.drain_dma("gpsimd")


def declare_l1ffn(nc, io):
    io["ffn_base_w"] = nc.dram_tensor("ffn_base_w", [D, D], F32, kind="ExternalInput").ap()
    io["ffn_base_b"] = nc.dram_tensor("ffn_base_b", [PC, ND], F32, kind="ExternalInput").ap()
    io["ffn_lora_A"] = nc.dram_tensor("ffn_lora_A", [D, 8], F32, kind="ExternalInput").ap()
    io["ffn_lora_B"] = nc.dram_tensor("ffn_lora_B", [64, D], F32, kind="ExternalInput").ap()
    io["ffn_router_w"] = nc.dram_tensor("ffn_router_w", [D, 8], F32, kind="ExternalInput").ap()
    io["ffn_router_b"] = nc.dram_tensor("ffn_router_b", [8, 1], F32, kind="ExternalInput").ap()
    io["ffn_eG"] = nc.dram_tensor("ffn_eG", [8, 64], F32, kind="ExternalInput").ap()
    io["ffn_eM"] = nc.dram_tensor("ffn_eM", [8, 64], F32, kind="ExternalInput").ap()


def prep_l1ffn(inputs):
    eG = np.zeros((8, 64), np.float32)
    eM = np.zeros((8, 64), np.float32)
    for e in range(8):
        for r in range(8):
            eG[e, e * 8 + r] = 2.0
            eM[r, e * 8 + r] = 1.0
    return {
        "ffn_base_w": np.ascontiguousarray(inputs["L1_ffn_base_w"].T),
        "ffn_base_b": np.ascontiguousarray(inputs["L1_ffn_base_b"].reshape(ND, PC).T),
        "ffn_lora_A": np.ascontiguousarray(inputs["L1_ffn_lora_A"]),
        "ffn_lora_B": np.ascontiguousarray(inputs["L1_ffn_lora_B"].reshape(64, D)),
        "ffn_router_w": np.ascontiguousarray(inputs["L1_ffn_router_w"].T),
        "ffn_router_b": np.ascontiguousarray(inputs["L1_ffn_router_b"].reshape(8, 1)),
        "ffn_eG": eG,
        "ffn_eM": eM,
    }



def emit_l0ffn(p, es, io, C, tag, dbg=None):
    nc = p.nc
    TC = 512
    NH, NZC = 4, 128
    bd1 = [fm(es, nc, tag + f"bd1_{h}", ND, PC, BF16) for h in range(NH)]
    for h in range(NH):
        g_w = p.dma("gpsimd", bd1[h][:, :, :], io["ffn0_bd1"][h].rearrange("c p k -> p c k"))
        p.wait("tensor", "gpsimd", g_w)
    bd2buf = fm(es, nc, tag + "bd2", ND, 512, BF16)

    x_sb = fm(es, nc, tag + "x", ND, TC, F32)
    xnb = fm(es, nc, tag + "xnb", ND, TC, BF16)
    x2 = fm(es, nc, tag + "x2", ND, TC, F32)
    rows = sb(es, nc, tag + "rows", [1, 2 * TC], F32)
    y_h = fm(es, nc, tag + "yh", ND, TC, BF16)
    h_all = fm(es, nc, tag + "hall", NZC, TC, BF16)
    wcp = [sb(es, nc, tag + f"wcp{i}", [PC, D], BF16) for i in range(2)]
    pool = [ps(es, nc, tag + f"pp{i}", [PC, 512]) for i in range(8)]

    v_prev = 0
    g_st_prev = 0
    s_gelu = 0
    for ci in range(T // TC):
        c0 = ci * TC
        p.wait("sync", "vector", v_prev)
        p.wait("sync", "gpsimd", g_st_prev)
        x_wm = p.dma("sync", x_sb[:, :, :], fm_dram(io["x_in"], ND)[:, :, c0:c0 + TC])
        for e2 in ("scalar", "tensor", "vector"):
            p.wait(e2, "sync", x_wm)
        emit_ln(p, C, x_sb, x2, TC, pool[3], pool[4], x2, rows, out_bf=xnb)
        p.wait("tensor", "vector", p.wm("vector"))
        for h in range(NH):
            p.wait("gpsimd", "tensor", p.wm("tensor"))
            g_b2 = p.dma("gpsimd", bd2buf[:, :, :],
                         io["ffn0_bd2"][h].rearrange("c p k -> p c k"))
            for c in range(ND):
                t_y = p.on("tensor", lambda e, h=h, c=c: e.matmul(
                    pool[0][:, :], bd1[h][:, c, :], xnb[:, c, :],
                    start=True, stop=True), inc=1)
                p.wait("vector", "tensor", t_y)
                v_y = p.on("vector", lambda e, c=c: e.tensor_copy(
                    y_h[:, c, :], pool[0][:, :]), inc=1)
                p.wait("tensor", "vector", v_y)
            p.wait("tensor", "gpsimd", g_b2)
            if dbg is not None and ci == 0 and h == 0:
                vv = p.wm("vector")
                p.wait("gpsimd", "vector", vv)
                p.dma("gpsimd", dbg["yh"], y_h[:, :, :])
            for c in range(ND):
                for s4 in range(4):
                    zc = h * 32 + c * 4 + s4
                    pz = pool[1 + (zc % 2)]
                    p.wait("tensor", "scalar", max(0, s_gelu - 1))
                    t_zz = p.on("tensor", lambda e, c=c, s4=s4, pz=pz: e.matmul(
                        pz[:, :], bd2buf[:, c, s4 * PC:(s4 + 1) * PC], y_h[:, c, :],
                        start=True, stop=True), inc=1)
                    p.wait("scalar", "tensor", t_zz)
                    s_gelu = p.on("scalar", lambda e, zc=zc, pz=pz: e.activation(
                        h_all[:, zc, :], pz[:, :], AF.Gelu), inc=1)
        p.wait("tensor", "scalar", s_gelu)
        if dbg is not None and ci == 0:
            p.wait("gpsimd", "scalar", s_gelu)
            p.dma("gpsimd", dbg["hall"], h_all[:, :, :])
            vv2 = p.wm("vector")
            p.wait("gpsimd", "vector", vv2)
            p.dma("gpsimd", dbg["xnb"], xnb[:, :, :])
        p.wait("gpsimd", "tensor", p.wm("tensor"))
        t_o = 0
        g_wm = {}
        g_wm[0] = p.dma("gpsimd", wcp[0][:, :], io["ffn0_wcpT"][0:PC, :])
        g_wm[1] = p.dma("gpsimd", wcp[1][:, :], io["ffn0_wcpT"][PC:2 * PC, :])
        for m in range(NZC):
            p.wait("tensor", "gpsimd", g_wm[m % 2])
            for oc in range(ND):
                t_o = p.on("tensor", lambda e, m=m, oc=oc: e.matmul(
                    pool[oc][:, :], wcp[m % 2][:, oc * PC:(oc + 1) * PC],
                    h_all[:, m, :], start=(m == 0), stop=(m == NZC - 1)),
                    inc=(1 if oc == ND - 1 else 0))
            if m + 2 < NZC:
                p.wait("gpsimd", "tensor", t_o)
                g_wm[m % 2] = p.dma("gpsimd", wcp[m % 2][:, :],
                                    io["ffn0_wcpT"][(m + 2) * PC:(m + 3) * PC, :])
        p.wait("vector", "tensor", t_o)
        p.wait("vector", "gpsimd", g_st_prev)
        v_prev = 0
        for oc in range(ND):
            v_prev = p.on("vector", lambda e, oc=oc: e.tensor_add(
                x_sb[:, oc, :], pool[oc][:, :], x_sb[:, oc, :]), inc=1)
        p.wait("gpsimd", "vector", v_prev)
        g_st_prev = p.dma("gpsimd", fm_dram(io["x_out"], ND)[:, :, c0:c0 + TC], x_sb[:, :, :])
    p.drain_dma("gpsimd")


def declare_l0ffn(nc, io):
    io["ffn0_bd1"] = [nc.dram_tensor(f"ffn0_bd1_{h}", [ND, PC, PC], F32,
                                     kind="ExternalInput").ap() for h in range(4)]
    io["ffn0_bd2"] = [nc.dram_tensor(f"ffn0_bd2_{h}", [ND, PC, 512], F32,
                                     kind="ExternalInput").ap() for h in range(4)]
    io["ffn0_wcpT"] = nc.dram_tensor("ffn0_wcpT", [16384, D], F32,
                                     kind="ExternalInput").ap()


def prep_l0ffn(inputs):
    maker = inputs["L0_ffn_rail_maker"]
    up = inputs["L0_ffn_rail_up_proj"]
    d = {}
    bd1 = np.zeros((4, 8, PC, PC), np.float32)
    bd2 = np.zeros((4, 8, PC, 512), np.float32)
    for h in range(4):
        for c in range(8):
            for rl in range(8):
                r = 8 * c + rl
                bd1[h, c, 16 * rl:16 * rl + 16, 16 * rl:16 * rl + 16] = maker[h, r]
                bd2[h, c, 16 * rl:16 * rl + 16, 64 * rl:64 * rl + 64] = up[h, r].T
    for h in range(4):
        d[f"ffn0_bd1_{h}"] = np.ascontiguousarray(bd1[h])
        d[f"ffn0_bd2_{h}"] = np.ascontiguousarray(bd2[h])
    d["ffn0_wcpT"] = np.ascontiguousarray(inputs["L0_ffn_w_cproj"].T)
    return d



def emit_l1attnb(p, es, io, C, tag, dbg=None):
    nc = p.nc
    TC = 512
    NT = T // TC
    wsl = [fm(es, nc, tag + f"wsl{i}", ND, PC, BF16) for i in range(2)]
    ehs = sb(es, nc, tag + "sehs", [PC, 8, 16], BF16)
    xhs = sb(es, nc, tag + "sxhs", [16, 8, PC], BF16)
    g_e = p.dma("gpsimd", ehs[:, :, :], io["b7_ehs"].rearrange("c p j -> p c j"))
    g_w = p.dma("gpsimd", xhs[:, :, :], io["b7_xhs"].rearrange("c j p -> j c p"))
    p.wait("tensor", "gpsimd", g_e)
    p.wait("tensor", "gpsimd", g_w)

    x_sb = fm(es, nc, tag + "x", ND, TC, F32)
    x2 = fm(es, nc, tag + "x2", ND, TC, F32)
    rows = sb(es, nc, tag + "rows", [1, 2 * TC], F32)
    xnb = fm(es, nc, tag + "xnb", ND, T, BF16)
    o1c = fm(es, nc, tag + "o1c", ND, TC, BF16)
    w_oc = sb(es, nc, tag + "woc", [PC, T], F32)
    u_oc = sb(es, nc, tag + "uoc", [PC, T], F32)
    S_oc = sb(es, nc, tag + "soc", [PC, T], F32)
    k_oc = sb(es, nc, tag + "koc", [PC, T], BF16)
    ksum = sb(es, nc, tag + "ksum", [2, T], BF16)
    o1_dram = nc.dram_tensor(tag + "o1d", [PC, ND, T], BF16, kind="Internal").ap()
    psA = ps(es, nc, tag + "psA", [PC, 512])
    psB = ps(es, nc, tag + "psB", [PC, 512])
    psK = ps(es, nc, tag + "psK", [16, 512])
    psO = [ps(es, nc, tag + f"psO{i}", [PC, 512]) for i in range(2)]

    g_st_prev = 0
    v_ln = 0
    for ci in range(NT):
        c0 = ci * TC
        p.wait("sync", "vector", v_ln)
        x_wm = p.dma("sync", x_sb[:, :, :], fm_dram(io["x_in"], ND)[:, :, c0:c0 + TC])
        for e2 in ("scalar", "tensor", "vector"):
            p.wait(e2, "sync", x_wm)
        v_ln = emit_ln(p, C, x_sb, x2, TC, psA, psB, x2, rows,
                       out_bf=None)
        for c in range(ND):
            v_ln = p.on("vector", lambda e, c=c, c0=c0: e.tensor_copy(
                xnb[:, c, c0:c0 + TC], x2[:, c, 0:TC]), inc=1)

    p.wait("tensor", "vector", p.wm("vector"))
    p.wait("scalar", "vector", p.wm("vector"))
    p.wait("gpsimd", "vector", p.wm("vector"))

    def proj_oc(wname, oc, consumer):
        buf = proj_oc.flip
        proj_oc.flip ^= 1
        p.wait("gpsimd", "tensor", p.wm("tensor"))
        g_s = p.dma("gpsimd", wsl[buf][:, :, :],
                    fm_dram(io["b7_" + wname], ND)[:, :, oc * PC:(oc + 1) * PC])
        p.wait("tensor", "gpsimd", g_s)
        vres = 0
        for f in range(NT):
            fs = f * TC
            pt = psO[f % 2]
            t_p = 0
            for c in range(ND):
                t_p = p.on("tensor", lambda e, pt=pt, c=c, fs=fs, buf=buf:
                           e.matmul(pt[:, :], wsl[buf][:, c, :],
                                    xnb[:, c, fs:fs + TC],
                                    start=(c == 0), stop=(c == ND - 1)),
                           inc=(1 if c == ND - 1 else 0))
            p.wait("vector", "tensor", t_p)
            p.wait("scalar", "tensor", t_p)
            vres = consumer(fs, pt)
            p.wait("tensor", "vector", p.wm("vector"))
            p.wait("tensor", "scalar", p.wm("scalar"))
        return vres
    proj_oc.flip = 0

    for oc in range(ND):
        proj_oc("wk", oc, lambda fs, pt: p.on(
            "vector", lambda e, fs=fs, pt=pt: e.tensor_copy(
                k_oc[:, fs:fs + TC], pt[:, :]), inc=1))
        v_kb = p.wm("vector")
        p.wait("tensor", "vector", v_kb)
        for f in range(NT):
            fs = f * TC
            t_ks = p.on("tensor", lambda e, fs=fs: e.matmul(
                psK[0:2, :], ehs[:, 0, 0:2], k_oc[:, fs:fs + TC],
                start=True, stop=True), inc=1)
            p.wait("vector", "tensor", t_ks)
            v_ks = p.on("vector", lambda e, fs=fs: e.tensor_copy(
                ksum[:, fs:fs + TC], psK[0:2, :]), inc=1)
            p.wait("tensor", "vector", v_ks)
        if dbg is not None and oc == 0:
            vks_ = p.wm("vector")
            p.wait("gpsimd", "vector", vks_)
            p.dma("gpsimd", dbg["koc"], k_oc[:, :])
            g_dk = p.dma("gpsimd", dbg["ksum"], ksum[:, :])
            p.wait("vector", "gpsimd", g_dk)
            p.wait("tensor", "gpsimd", g_dk)
        def cons_v(fs, pt):
            t_e = p.on("tensor", lambda e, fs=fs: e.matmul(
                psB[:, :], xhs[0:2, 0, :], ksum[:, fs:fs + TC],
                start=True, stop=True), inc=1)
            p.on("vector", lambda e, fs=fs, pt=pt: e.tensor_copy(
                u_oc[:, fs:fs + TC], pt[:, :]))
            p.wait("vector", "tensor", t_e)
            return p.on("vector", lambda e, fs=fs: e.tensor_mul(
                u_oc[:, fs:fs + TC], u_oc[:, fs:fs + TC], psB[:, :]), inc=1)
        proj_oc("wv", oc, cons_v)
        def cons_w(fs, pt):
            s_s = p.on("scalar", lambda e, fs=fs, pt=pt: e.activation(
                w_oc[:, fs:fs + TC], pt[:, :], AF.Sigmoid), inc=1)
            p.wait("vector", "scalar", s_s)
            return p.on("vector", lambda e, fs=fs: e.tensor_scalar(
                w_oc[:, fs:fs + TC], w_oc[:, fs:fs + TC], 0.99, None, ALU.mult), inc=1)
        proj_oc("ww", oc, cons_w)
        v_sc = p.on("vector", lambda e: e.tensor_tensor_scan(
            S_oc[:, :], w_oc[:, :], u_oc[:, :], 0.0, ALU.mult, ALU.add), inc=1)
        def cons_r(fs, pt):
            s_s = p.on("scalar", lambda e, fs=fs, pt=pt: e.activation(
                u_oc[:, fs:fs + TC], pt[:, :], AF.Sigmoid), inc=1)
            p.wait("vector", "scalar", s_s)
            return p.on("vector", lambda e, fs=fs: e.tensor_mul(
                k_oc[:, fs:fs + TC], u_oc[:, fs:fs + TC], S_oc[:, fs:fs + TC]), inc=1)
        p.wait("scalar", "vector", v_sc)
        if dbg is not None and oc == 0:
            p.wait("gpsimd", "vector", v_sc)
            p.dma("gpsimd", dbg["Soc"], S_oc[:, :])
            p.dma("gpsimd", dbg["woc"], w_oc[:, :])
            g_dd = p.dma("gpsimd", dbg["uoc"], u_oc[:, :])
            p.wait("vector", "gpsimd", g_dd)
            p.wait("scalar", "gpsimd", g_dd)
        v_o1 = proj_oc("wr", oc, cons_r)
        p.wait("gpsimd", "vector", v_o1)
        g_o1 = p.dma("gpsimd", o1_dram[:, oc, :], k_oc[:, :])
        p.wait("vector", "gpsimd", g_o1)

    if dbg is not None:
        p.wait("gpsimd", "vector", p.wm("vector"))
        p.dma("gpsimd", dbg["o1"], o1_dram)
    p.wait("sync", "vector", p.wm("vector"))
    p.drain_dma("sync")
    wo = fm(es, nc, tag + "swo", ND, D, BF16)
    g_wo = p.dma("gpsimd", wo[:, :, :], fm_dram(io["b7_wo"], ND))
    p.wait("tensor", "gpsimd", g_wo)
    rsb = fm(es, nc, tag + "rsb", ND, TC, BF16)
    v_prev = 0
    for ci in range(NT):
        c0 = ci * TC
        p.wait("sync", "vector", v_prev)
        p.wait("sync", "gpsimd", g_st_prev)
        x_wm = p.dma("sync", x_sb[:, :, :], fm_dram(io["x_in"], ND)[:, :, c0:c0 + TC])
        for e2 in ("scalar", "tensor", "vector"):
            p.wait(e2, "sync", x_wm)
        p.wait("sync", "scalar", p.wm("scalar"))
        o_wm = p.dma("sync", o1c[:, :, :], o1_dram[:, :, c0:c0 + TC])
        for e2 in ("scalar", "vector", "tensor"):
            p.wait(e2, "sync", o_wm)
        s_q = p.on("scalar", lambda e: e.activation(
            x2[:, :, 0:TC], o1c[:, :, 0:TC], AF.Square), inc=1)
        p.wait("tensor", "scalar", s_q)
        t_ms = 0
        for c in range(ND):
            t_ms = p.on("tensor", lambda e, c=c: e.matmul(
                psA[0:1, 0:TC], C["invn"], x2[:, c, 0:TC],
                start=(c == 0), stop=(c == ND - 1)), inc=(1 if c == ND - 1 else 0))
        p.wait("vector", "tensor", t_ms)
        v_ms = p.on("vector", lambda e: e.tensor_copy(rows[0:1, 0:TC], psA[0:1, 0:TC]), inc=1)
        p.wait("scalar", "vector", v_ms)
        p.on("scalar", lambda e: e.activation(rows[0:1, 0:TC], rows[0:1, 0:TC],
                                              AF.Ln, bias=1.1920929e-7))
        s_rr = p.on("scalar", lambda e: e.activation(rows[0:1, 0:TC], rows[0:1, 0:TC],
                                                     AF.Exp, scale=-0.5), inc=1)
        p.wait("tensor", "scalar", s_rr)
        t_bc = p.on("tensor", lambda e: e.matmul(
            psA[:, 0:TC], C["ones1"], rows[0:1, 0:TC], start=True, stop=True), inc=1)
        p.wait("vector", "tensor", t_bc)
        v_rs = 0
        for c in range(ND):
            v_rs = p.on("vector", lambda e, c=c: e.tensor_mul(
                rsb[:, c, :], o1c[:, c, 0:TC], psA[:, 0:TC]), inc=1)
        p.wait("tensor", "vector", v_rs)
        v_prev = 0
        for f in range(2):
            fs = f * 256
            t_p = 0
            for oc2 in range(ND):
                pb = (psO[0], psO[1], psA, psB)[oc2 // 2]
                pt = pb[:, (oc2 % 2) * 256:(oc2 % 2) * 256 + 256]
                for c in range(ND):
                    t_p = p.on("tensor", lambda e, pt=pt, oc2=oc2, c=c, fs=fs: e.matmul(
                        pt, wo[:, c, oc2 * PC:(oc2 + 1) * PC],
                        rsb[:, c, fs:fs + 256],
                        start=(c == 0), stop=(c == ND - 1)),
                        inc=(1 if c == ND - 1 else 0))
            p.wait("vector", "tensor", t_p)
            if f == 0:
                p.wait("vector", "gpsimd", g_st_prev)
            for oc2 in range(ND):
                pb = (psO[0], psO[1], psA, psB)[oc2 // 2]
                pt = pb[:, (oc2 % 2) * 256:(oc2 % 2) * 256 + 256]
                v_prev = p.on("vector", lambda e, oc2=oc2, fs=fs, pt=pt: e.tensor_add(
                    x_sb[:, oc2, fs:fs + 256], pt, x_sb[:, oc2, fs:fs + 256]), inc=1)
            if f == 0:
                p.wait("tensor", "vector", v_prev)
        p.wait("gpsimd", "vector", v_prev)
        g_st_prev = p.dma("gpsimd", fm_dram(io["x_out"], ND)[:, :, c0:c0 + TC], x_sb[:, :, :])
    p.drain_dma("gpsimd")


def declare_l1attnb(nc, io):
    for nm in ("wr", "wk", "wv", "ww", "wo"):
        io["b7_" + nm] = nc.dram_tensor("b7_" + nm, [D, D], F32, kind="ExternalInput").ap()
    io["b7_ehs"] = nc.dram_tensor("b7_ehs", [8, PC, 16], F32, kind="ExternalInput").ap()
    io["b7_xhs"] = nc.dram_tensor("b7_xhs", [8, 16, PC], F32, kind="ExternalInput").ap()


def prep_l1attnb(inputs):
    d = {}
    for nm, key in (("wr", "L1_attn_b_Wr"), ("wk", "L1_attn_b_Wk"), ("wv", "L1_attn_b_Wv"),
                    ("ww", "L1_attn_b_Ww"), ("wo", "L1_attn_b_Wo")):
        d["b7_" + nm] = np.ascontiguousarray(inputs[key].T)
    ehs = np.zeros((8, PC, 16), np.float32)
    xhs = np.zeros((8, 16, PC), np.float32)
    for c in range(8):
        for pp in range(PC):
            hl = pp // 64
            ehs[c, pp, hl] = 1.0
            xhs[c, hl, pp] = 1.0
    d["b7_ehs"] = ehs
    d["b7_xhs"] = xhs
    return d



def emit_l0attna(p, es, io, C, tag, phases="ABCD"):
    nc = p.nc
    TC = 512
    NT = T // TC
    NG = 32
    x0i_dram = nc.dram_tensor(tag + "x0i", [PC, ND, T], F32, kind="Internal").ap()
    xr_dram = nc.dram_tensor(tag + "xr", [PC, ND, T], BF16, kind="Internal").ap()
    rw_dram = nc.dram_tensor(tag + "rw", [PC, NG, T], BF16, kind="Internal").ap()
    dc_dram = nc.dram_tensor(tag + "dcd", [PC, 2 * NG], F32, kind="Internal").ap()

    dcol = sb(es, nc, tag + "s_dcol", [PC, NG], F32)
    fcol = sb(es, nc, tag + "s_fcol", [PC, NG], F32)
    mixk = sb(es, nc, tag + "s_mixk", [PC, ND], F32)
    mixv = sb(es, nc, tag + "s_mixv", [PC, ND], F32)
    mixr = sb(es, nc, tag + "s_mixr", [PC, ND], F32)
    cmixk = sb(es, nc, tag + "s_cmixk", [PC, ND], F32)
    cmixr = sb(es, nc, tag + "s_cmixr", [PC, ND], F32)
    lncols = sb(es, nc, tag + "s_lnc", [PC, 6 * ND], F32)
    ctoks = [p.dma("sync", mixk[:, :], io["a1_mixk"]),
             p.dma("sync", mixv[:, :], io["a1_mixv"]),
             p.dma("sync", mixr[:, :], io["a1_mixr"]),
             p.dma("sync", cmixk[:, :], io["a1_cmixk"]),
             p.dma("sync", cmixr[:, :], io["a1_cmixr"]),
             p.dma("sync", lncols[:, :], io["a1_lncols"]),
             p.dma("sync", dcol[:, :], io["a1_decay"]),
             p.dma("sync", fcol[:, :], io["a1_first"])]
    for tok in ctoks[-2:]:
        p.wait("scalar", "sync", tok)
    p.on("scalar", lambda e: e.activation(dcol[:, :], dcol[:, :], AF.Exp))
    p.on("scalar", lambda e: e.activation(dcol[:, :], dcol[:, :], AF.Exp, scale=-1.0))
    s_d = p.on("scalar", lambda e: e.activation(fcol[:, :], fcol[:, :], AF.Exp), inc=1)
    p.wait("gpsimd", "scalar", s_d)
    p.dma("gpsimd", dc_dram[:, 0:NG], dcol[:, :])
    g_rt = p.dma("gpsimd", dc_dram[:, NG:2 * NG], fcol[:, :])
    p.wait("sync", "gpsimd", g_rt)
    rt1 = p.dma("sync", dcol[:, :], dc_dram[:, 0:NG])
    rt2 = p.dma("sync", fcol[:, :], dc_dram[:, NG:2 * NG])
    for e2 in ("vector", "scalar", "tensor", "gpsimd"):
        for tok in ctoks[:6] + [rt1, rt2]:
            p.wait(e2, "sync", tok)

    esAB = ExitStack()
    xk = fm(esAB, nc, tag + "s_xk", ND, T, BF16)
    xv = fm(esAB, nc, tag + "s_xv", ND, T, BF16)

    esP = ExitStack()
    x_sb = fm(esP, nc, tag + "xA", ND, TC, F32)
    x2 = fm(esP, nc, tag + "x2A", ND, TC, F32)
    x0c = fm(esP, nc, tag + "x0c", ND, TC, F32)
    rows = sb(esP, nc, tag + "rowsA", [1, 2 * TC], F32)
    xn_buf = esP.enter_context(nc.sbuf_tensor(tag + "xnbuf", [PC, ND, TC + 1], F32))
    xrc = fm(esP, nc, tag + "xrc", ND, TC, BF16)
    tmix = x2
    psA_ = ps(esP, nc, tag + "psA", [PC, 512])
    psB_ = ps(esP, nc, tag + "psB", [PC, 512])
    v_pr = p.on("vector", lambda e: e.memset(xn_buf[:, :, 0:1], 0.0), inc=1)

    g_pr = 0
    for ci in range(NT):
        c0 = ci * TC
        p.wait("sync", "vector", v_pr)
        p.wait("sync", "gpsimd", g_pr)
        x_wm = p.dma("sync", x_sb[:, :, :], fm_dram(io["x_in"], ND)[:, :, c0:c0 + TC])
        for e2 in ("scalar", "tensor", "vector"):
            p.wait(e2, "sync", x_wm)
        emit_ln(p, C, x_sb, x2, TC, psA_, psB_, x2, rows)
        emit_ln(p, C, x2, x0c, TC, psA_, psB_, x_sb, rows,
                gcols=lncols[:, 0:ND], bcols=lncols[:, ND:2 * ND])
        emit_ln(p, C, x0c, None, TC, psA_, psB_, x_sb, rows,
                gcols=lncols[:, 2 * ND:3 * ND], bcols=lncols[:, 3 * ND:4 * ND],
                outs=lambda c: xn_buf[:, c, 1:TC + 1])
        v_sp = p.wm("vector")
        p.wait("gpsimd", "vector", v_sp)
        g_pr = p.dma("gpsimd", x0i_dram[:, :, c0:c0 + TC], x0c[:, :, :])
        for c in range(ND):
            p.on("vector", lambda e, c=c: e.tensor_sub(
                tmix[:, c, :], xn_buf[:, c, 1:TC + 1], xn_buf[:, c, 0:TC]))
        for mq, kind in ((mixk, "k"), (mixv, "v"), (mixr, "r")):
            for c in range(ND):
                if kind == "k":
                    dv = xk[:, c, c0:c0 + TC]
                elif kind == "v":
                    dv = xv[:, c, c0:c0 + TC]
                else:
                    dv = xrc[:, c, :]
                v_pr = p.on("vector", lambda e, c=c, mq=mq, dv=dv: e.scalar_tensor_tensor(
                    dv, tmix[:, c, :], mq[:, c:c + 1], xn_buf[:, c, 0:TC],
                    ALU.mult, ALU.add), inc=1)
        p.wait("gpsimd", "vector", v_pr)
        g_pr = p.dma("gpsimd", xr_dram[:, :, c0:c0 + TC], xrc[:, :, :])
        v_pr = p.on("vector", lambda e: e.tensor_copy(
            xn_buf[:, :, 0:1], xn_buf[:, :, TC:TC + 1]), inc=1)
    p.wait("vector", "gpsimd", g_pr)
    p.barrier()
    if "B" not in phases:
        for ci2 in range(NT):
            p.dma("gpsimd", fm_dram(io["x_out"], ND)[:, :, ci2 * TC:(ci2 + 1) * TC],
                  x_sb[:, :, :] if False else None) if False else None
        g_f = 0
        xf = x_sb
        for ci2 in range(NT):
            p.wait("sync", "gpsimd", g_f)
            w1 = p.dma("sync", xf[:, :, :], x0i_dram[:, :, ci2 * TC:(ci2 + 1) * TC])
            p.wait("gpsimd", "sync", w1)
            g_f = p.dma("gpsimd", fm_dram(io["x_out"], ND)[:, :, ci2 * TC:(ci2 + 1) * TC],
                        xf[:, :, :])
        p.drain_dma("gpsimd")
        esP.close()
        esAB.close()
        return
    esP.close()

    esB = ExitStack()
    wsl = [fm(esB, nc, tag + f"wslB{i}", ND, PC, BF16) for i in range(2)]
    k_buf = esB.enter_context(nc.sbuf_tensor(tag + "kbuf", [PC, T + 1], F32))
    kv_buf = esB.enter_context(nc.sbuf_tensor(tag + "kvbuf", [PC, T + 1], F32))
    sk_b = sb(esB, nc, tag + "skb", [PC, T], F32)
    skv_b = sb(esB, nc, tag + "skvb", [PC, T], F32)
    rwb = sb(esB, nc, tag + "rwbB", [PC, T], BF16)
    psP = [ps(esB, nc, tag + f"psP{i}", [PC, 512]) for i in range(2)]
    v_zb = p.on("vector", lambda e: e.memset(k_buf[:, 0:1], 0.0))
    v_zb = p.on("vector", lambda e: e.memset(kv_buf[:, 0:1], 0.0), inc=1)
    p.wait("vector", "vector", v_zb)
    g_rwsp = 0

    for g in range(NG):
        p.wait("gpsimd", "tensor", p.wm("tensor"))
        g_k = p.dma("gpsimd", wsl[0][:, :, :],
                    fm_dram(io["a1_wk"], ND)[:, :, g * PC:(g + 1) * PC])
        p.wait("tensor", "gpsimd", g_k)
        for f in range(NT):
            fs = f * TC
            pt = psP[f % 2]
            t_k = 0
            for c in range(ND):
                t_k = p.on("tensor", lambda e, pt=pt, c=c, fs=fs: e.matmul(
                    pt[:, :], wsl[0][:, c, :], xk[:, c, fs:fs + TC],
                    start=(c == 0), stop=(c == ND - 1)), inc=(1 if c == ND - 1 else 0))
            p.wait("scalar", "tensor", t_k)
            s_k = p.on("scalar", lambda e, pt=pt, fs=fs: e.activation(
                k_buf[:, 1 + fs:1 + fs + TC], pt[:, :], AF.Exp), inc=1)
            p.wait("vector", "scalar", s_k)
            v_k = p.on("vector", lambda e, fs=fs: e.tensor_scalar(
                k_buf[:, 1 + fs:1 + fs + TC], k_buf[:, 1 + fs:1 + fs + TC],
                1.1420073898156842e26, None, ALU.min), inc=1)
            p.wait("tensor", "vector", v_k)
        p.wait("gpsimd", "tensor", p.wm("tensor"))
        g_v = p.dma("gpsimd", wsl[1][:, :, :],
                    fm_dram(io["a1_wv"], ND)[:, :, g * PC:(g + 1) * PC])
        p.wait("tensor", "gpsimd", g_v)
        for f in range(NT):
            fs = f * TC
            pt = psP[f % 2]
            t_v = 0
            for c in range(ND):
                t_v = p.on("tensor", lambda e, pt=pt, c=c, fs=fs: e.matmul(
                    pt[:, :], wsl[1][:, c, :], xv[:, c, fs:fs + TC],
                    start=(c == 0), stop=(c == ND - 1)), inc=(1 if c == ND - 1 else 0))
            p.wait("vector", "tensor", t_v)
            v_kv = p.on("vector", lambda e, pt=pt, fs=fs: e.tensor_mul(
                kv_buf[:, 1 + fs:1 + fs + TC], k_buf[:, 1 + fs:1 + fs + TC],
                pt[:, :]), inc=1)
            p.wait("tensor", "vector", v_kv)
        dbc = dcol[:, g:g + 1].broadcast_to([PC, T])
        p.on("vector", lambda e, dbc=dbc: e.tensor_tensor_scan(
            sk_b[:, :], dbc, k_buf[:, 0:T], 0.0, ALU.mult, ALU.add))
        p.on("vector", lambda e, dbc=dbc: e.tensor_tensor_scan(
            skv_b[:, :], dbc, kv_buf[:, 0:T], 0.0, ALU.mult, ALU.add))
        p.on("vector", lambda e, g=g: e.scalar_tensor_tensor(
            sk_b[:, :], k_buf[:, 1:T + 1], fcol[:, g:g + 1], sk_b[:, :],
            ALU.mult, ALU.add))
        v_wk = p.on("vector", lambda e: e.tensor_scalar(
            sk_b[:, :], sk_b[:, :], 1e-8, None, ALU.add), inc=1)
        p.on("vector", lambda e, g=g: e.scalar_tensor_tensor(
            skv_b[:, :], kv_buf[:, 1:T + 1], fcol[:, g:g + 1], skv_b[:, :],
            ALU.mult, ALU.add))
        p.wait("scalar", "vector", v_wk)
        p.on("scalar", lambda e: e.activation(sk_b[:, :], sk_b[:, :], AF.Ln))
        s_inv = p.on("scalar", lambda e: e.activation(
            sk_b[:, :], sk_b[:, :], AF.Exp, scale=-1.0), inc=1)
        p.wait("vector", "scalar", s_inv)
        p.wait("vector", "gpsimd", g_rwsp)
        v_rw = p.on("vector", lambda e: e.tensor_mul(
            rwb[:, :], skv_b[:, :], sk_b[:, :]), inc=1)
        p.wait("gpsimd", "vector", v_rw)
        g_rwsp = p.dma("gpsimd", rw_dram[:, g, :], rwb[:, :])
    p.barrier()
    esB.close()
    esAB.close()
    if "C" not in phases:
        esF = ExitStack()
        xf = fm(esF, nc, tag + "xf", ND, TC, F32)
        g_f = 0
        for ci2 in range(NT):
            p.wait("sync", "gpsimd", g_f)
            w1 = p.dma("sync", xf[:, :, :], x0i_dram[:, :, ci2 * TC:(ci2 + 1) * TC])
            p.wait("gpsimd", "sync", w1)
            g_f = p.dma("gpsimd", fm_dram(io["x_out"], ND)[:, :, ci2 * TC:(ci2 + 1) * TC],
                        xf[:, :, :])
        p.drain_dma("gpsimd")
        esF.close()
        return

    esC = ExitStack()
    wslr = fm(esC, nc, tag + "wslr", ND, PC, BF16)
    wslo = sb(esC, nc, tag + "wslo", [PC, D], BF16)
    xrb = fm(esC, nc, tag + "xrb", ND, TC, BF16)
    rwall = esC.enter_context(nc.sbuf_tensor(tag + "rwall", [PC, NG, TC], BF16))
    rwg = sb(esC, nc, tag + "rwgC", [PC, TC], BF16)
    sgt = sb(esC, nc, tag + "sgtC", [PC, TC], F32)
    x0b = fm(esC, nc, tag + "x0b", ND, TC, F32)
    psO = [ps(esC, nc, tag + f"psOC{i}", [PC, 512]) for i in range(8)]
    g_st = 0
    for f in range(NT):
        fs = f * TC
        p.wait("sync", "vector", p.wm("vector"))
        xr_wm = p.dma("sync", xrb[:, :, :], xr_dram[:, :, fs:fs + TC])
        p.wait("tensor", "sync", xr_wm)
        for g in range(NG):
            p.wait("gpsimd", "tensor", p.wm("tensor"))
            g_r = p.dma("gpsimd", wslr[:, :, :],
                        fm_dram(io["a1_wr"], ND)[:, :, g * PC:(g + 1) * PC])
            p.wait("sync", "vector", p.wm("vector"))
            rg_wm = p.dma("sync", rwg[:, :], rw_dram[:, g, fs:fs + TC])
            p.wait("tensor", "gpsimd", g_r)
            pt = psO[g % 2]
            t_r = 0
            for c in range(ND):
                t_r = p.on("tensor", lambda e, pt=pt, c=c: e.matmul(
                    pt[:, :], wslr[:, c, :], xrb[:, c, :],
                    start=(c == 0), stop=(c == ND - 1)), inc=(1 if c == ND - 1 else 0))
            p.wait("scalar", "tensor", t_r)
            s_s = p.on("scalar", lambda e, pt=pt: e.activation(
                sgt[:, :], pt[:, :], AF.Sigmoid), inc=1)
            p.wait("vector", "scalar", s_s)
            p.wait("vector", "sync", rg_wm)
            v_rg = p.on("vector", lambda e, g=g: e.tensor_mul(
                rwall[:, g, :], rwg[:, :], sgt[:, :]), inc=1)
            p.wait("tensor", "vector", v_rg)
        t_o = 0
        for g in range(NG):
            p.wait("gpsimd", "tensor", p.wm("tensor"))
            g_o = p.dma("gpsimd", wslo[:, :], io["a1_wo"][g * PC:(g + 1) * PC, :])
            p.wait("tensor", "gpsimd", g_o)
            for oc in range(ND):
                t_o = p.on("tensor", lambda e, oc=oc, g=g: e.matmul(
                    psO[oc][:, :], wslo[:, oc * PC:(oc + 1) * PC], rwall[:, g, :],
                    start=(g == 0), stop=(g == NG - 1)), inc=(1 if oc == ND - 1 else 0))
        p.wait("sync", "vector", p.wm("vector"))
        p.wait("sync", "gpsimd", g_st)
        x0_wm = p.dma("sync", x0b[:, :, :], x0i_dram[:, :, fs:fs + TC])
        p.wait("vector", "sync", x0_wm)
        p.wait("vector", "tensor", t_o)
        v_ev = 0
        for oc in range(ND):
            v_ev = p.on("vector", lambda e, oc=oc: e.tensor_add(
                x0b[:, oc, :], psO[oc][:, :], x0b[:, oc, :]), inc=1)
        p.wait("gpsimd", "vector", v_ev)
        g_st = p.dma("gpsimd", x0i_dram[:, :, fs:fs + TC], x0b[:, :, :])
        p.wait("tensor", "vector", v_ev)
    p.wait("vector", "gpsimd", g_st)
    p.barrier()
    esC.close()
    if "D" not in phases:
        esF = ExitStack()
        xf = fm(esF, nc, tag + "xf2", ND, TC, F32)
        g_f = 0
        for ci2 in range(NT):
            p.wait("sync", "gpsimd", g_f)
            w1 = p.dma("sync", xf[:, :, :], x0i_dram[:, :, ci2 * TC:(ci2 + 1) * TC])
            p.wait("gpsimd", "sync", w1)
            g_f = p.dma("gpsimd", fm_dram(io["x_out"], ND)[:, :, ci2 * TC:(ci2 + 1) * TC],
                        xf[:, :, :])
        p.drain_dma("gpsimd")
        esF.close()
        return

    esD = ExitStack()
    xD = fm(esD, nc, tag + "xD", ND, TC, F32)
    x2D = fm(esD, nc, tag + "x2D", ND, TC, F32)
    rowsD = sb(esD, nc, tag + "rowsD", [1, 2 * TC], F32)
    xnD = esD.enter_context(nc.sbuf_tensor(tag + "xnD", [PC, ND, TC + 1], F32))
    tmixD = fm(esD, nc, tag + "tmixD", ND, TC, F32)
    xk2 = fm(esD, nc, tag + "xk2", ND, TC, BF16)
    xr2 = fm(esD, nc, tag + "xr2", ND, TC, BF16)
    silb = esD.enter_context(nc.sbuf_tensor(tag + "silb", [PC, NG, TC], BF16))
    sgD = fm(esD, nc, tag + "sgD", ND, TC, F32)
    xinD = fm(esD, nc, tag + "xinD", ND, TC, F32)
    wslD = [fm(esD, nc, tag + f"wslD{i}", ND, PC, BF16) for i in range(2)]
    cmwrb = fm(esD, nc, tag + "cmwrb", ND, D, BF16)
    g_wr0 = p.dma("gpsimd", cmwrb[:, :, :], fm_dram(io["a1_cmwr"], ND))
    p.wait("tensor", "gpsimd", g_wr0)
    psD = [ps(esD, nc, tag + f"psD{i}", [PC, 512]) for i in range(8)]
    v_zD = p.on("vector", lambda e: e.memset(xnD[:, :, 0:1], 0.0), inc=1)

    v_pr = v_zD
    g_st = 0
    for ci in range(NT):
        c0 = ci * TC
        p.wait("sync", "vector", v_pr)
        p.wait("sync", "gpsimd", g_st)
        x_wm = p.dma("sync", xD[:, :, :], x0i_dram[:, :, c0:c0 + TC])
        for e2 in ("scalar", "tensor", "vector"):
            p.wait(e2, "sync", x_wm)
        emit_ln(p, C, xD, None, TC, psD[0], psD[1], x2D, rowsD,
                gcols=lncols[:, 4 * ND:5 * ND], bcols=lncols[:, 5 * ND:6 * ND],
                outs=lambda c: xnD[:, c, 1:TC + 1])
        for c in range(ND):
            p.on("vector", lambda e, c=c: e.tensor_sub(
                tmixD[:, c, :], xnD[:, c, 1:TC + 1], xnD[:, c, 0:TC]))
        v_mx = 0
        for mq, dst in ((cmixk, xk2), (cmixr, xr2)):
            for c in range(ND):
                v_mx = p.on("vector", lambda e, c=c, mq=mq, dst=dst:
                            e.scalar_tensor_tensor(
                                dst[:, c, :], tmixD[:, c, :], mq[:, c:c + 1],
                                xnD[:, c, 0:TC], ALU.mult, ALU.add), inc=1)
        v_pr = p.on("vector", lambda e: e.tensor_copy(
            xnD[:, :, 0:1], xnD[:, :, TC:TC + 1]), inc=1)
        p.wait("tensor", "vector", v_pr)
        for g in range(NG):
            p.wait("gpsimd", "tensor", p.wm("tensor"))
            g_w = p.dma("gpsimd", wslD[0][:, :, :],
                        fm_dram(io["a1_cmwk"], ND)[:, :, g * PC:(g + 1) * PC])
            p.wait("tensor", "gpsimd", g_w)
            pt = psD[2 + (g % 2)]
            t_k2 = 0
            for c in range(ND):
                t_k2 = p.on("tensor", lambda e, pt=pt, c=c: e.matmul(
                    pt[:, :], wslD[0][:, c, :], xk2[:, c, :],
                    start=(c == 0), stop=(c == ND - 1)), inc=(1 if c == ND - 1 else 0))
            p.wait("scalar", "tensor", t_k2)
            s_sl = p.on("scalar", lambda e, pt=pt, g=g: e.activation(
                silb[:, g, :], pt[:, :], AF.Silu), inc=1)
            p.wait("tensor", "scalar", s_sl)
        for oc in range(ND):
            t_r2 = 0
            pt = psD[2 + (oc % 2)]
            for c in range(ND):
                t_r2 = p.on("tensor", lambda e, pt=pt, oc=oc, c=c: e.matmul(
                    pt[:, :], cmwrb[:, c, oc * PC:(oc + 1) * PC], xr2[:, c, :],
                    start=(c == 0), stop=(c == ND - 1)), inc=(1 if c == ND - 1 else 0))
            p.wait("scalar", "tensor", t_r2)
            s_g2 = p.on("scalar", lambda e, pt=pt, oc=oc: e.activation(
                sgD[:, oc, :], pt[:, :], AF.Sigmoid), inc=1)
            p.wait("tensor", "scalar", s_g2)
        for g in range(NG):
            p.wait("gpsimd", "tensor", p.wm("tensor"))
            g_wv = p.dma("gpsimd", wslD[0][:, :, :],
                         io["a1_cmwv"][g * PC:(g + 1) * PC, :].rearrange(
                             "pp (cc o) -> pp cc o", cc=ND))
            p.wait("tensor", "gpsimd", g_wv)
            t_v2 = 0
            for oc in range(ND):
                t_v2 = p.on("tensor", lambda e, g=g, oc=oc: e.matmul(
                    psD[oc][:, :], wslD[0][:, oc, :], silb[:, g, :],
                    start=(g == 0), stop=(g == NG - 1)), inc=(1 if oc == ND - 1 else 0))
        p.wait("sync", "vector", p.wm("vector"))
        xin_wm = p.dma("sync", xinD[:, :, :], fm_dram(io["x_in"], ND)[:, :, c0:c0 + TC])
        p.wait("vector", "sync", xin_wm)
        p.wait("vector", "tensor", t_v2)
        v_pr = 0
        for oc in range(ND):
            p.on("vector", lambda e, oc=oc: e.tensor_add(
                xD[:, oc, :], xD[:, oc, :], sgD[:, oc, :]))
            p.on("vector", lambda e, oc=oc: e.tensor_add(
                xD[:, oc, :], xD[:, oc, :], psD[oc][:, :]))
            v_pr = p.on("vector", lambda e, oc=oc: e.tensor_add(
                xD[:, oc, :], xD[:, oc, :], xinD[:, oc, :]), inc=1)
        p.wait("gpsimd", "vector", v_pr)
        g_st = p.dma("gpsimd", fm_dram(io["x_out"], ND)[:, :, c0:c0 + TC], xD[:, :, :])
        p.wait("tensor", "vector", v_pr)
    p.drain_dma("gpsimd")
    p.barrier()
    esD.close()


def declare_l0attna(nc, io):
    io["a1_wk"] = nc.dram_tensor("a1_wk", [D, 4096], F32, kind="ExternalInput").ap()
    io["a1_wv"] = nc.dram_tensor("a1_wv", [D, 4096], F32, kind="ExternalInput").ap()
    io["a1_wr"] = nc.dram_tensor("a1_wr", [D, 4096], F32, kind="ExternalInput").ap()
    io["a1_wo"] = nc.dram_tensor("a1_wo", [4096, D], F32, kind="ExternalInput").ap()
    io["a1_cmwk"] = nc.dram_tensor("a1_cmwk", [D, 4096], F32, kind="ExternalInput").ap()
    io["a1_cmwv"] = nc.dram_tensor("a1_cmwv", [4096, D], F32, kind="ExternalInput").ap()
    io["a1_cmwr"] = nc.dram_tensor("a1_cmwr", [D, D], F32, kind="ExternalInput").ap()
    for nm in ("mixk", "mixv", "mixr", "cmixk", "cmixr", "decay", "first"):
        n = {"decay": 32, "first": 32}.get(nm, ND)
        io["a1_" + nm] = nc.dram_tensor("a1_" + nm, [PC, n], F32, kind="ExternalInput").ap()
    io["a1_lncols"] = nc.dram_tensor("a1_lncols", [PC, 6 * ND], F32, kind="ExternalInput").ap()


def prep_l0attna(inputs):
    d = {}
    d["a1_wk"] = np.ascontiguousarray(inputs["L0_attn_a_tm_Wk"])
    d["a1_wv"] = np.ascontiguousarray(inputs["L0_attn_a_tm_Wv"])
    d["a1_wr"] = np.ascontiguousarray(inputs["L0_attn_a_tm_Wr"])
    d["a1_wo"] = np.ascontiguousarray(inputs["L0_attn_a_tm_Wo"])
    d["a1_cmwk"] = np.ascontiguousarray(inputs["L0_attn_a_cm_Wk"])
    d["a1_cmwv"] = np.ascontiguousarray(inputs["L0_attn_a_cm_Wv"])
    d["a1_cmwr"] = np.ascontiguousarray(inputs["L0_attn_a_cm_Wr"])
    col = lambda a, n: np.ascontiguousarray(np.asarray(a, np.float32).reshape(n, PC).T)
    d["a1_mixk"] = col(inputs["L0_attn_a_tm_mix_k"], ND)
    d["a1_mixv"] = col(inputs["L0_attn_a_tm_mix_v"], ND)
    d["a1_mixr"] = col(inputs["L0_attn_a_tm_mix_r"], ND)
    d["a1_cmixk"] = col(inputs["L0_attn_a_cm_mix_k"], ND)
    d["a1_cmixr"] = col(inputs["L0_attn_a_cm_mix_r"], ND)
    d["a1_decay"] = col(inputs["L0_attn_a_tm_decay"], 32)
    d["a1_first"] = col(inputs["L0_attn_a_tm_first"], 32)
    d["a1_lncols"] = np.ascontiguousarray(np.concatenate(
        [col(inputs["L0_attn_a_" + nm], ND) for nm in
         ("ln0_g", "ln0_b", "pre_g", "pre_b", "post_g", "post_b")], axis=1))
    return d



def emit_l0attnb(p, es, io, C, tag):
    nc = p.nc
    TC = 512
    NT = T // TC
    CH = 128
    NC_ = T // CH
    NXC = 16
    z_dram = nc.dram_tensor(tag + "zd", [PC, NXC, T], BF16, kind="Internal").ap()
    xs_dram = nc.dram_tensor(tag + "xsd", [PC, NXC, T], BF16, kind="Internal").ap()
    rt_dram = nc.dram_tensor(tag + "rtd", [PC, 1], F32, kind="Internal").ap()
    dd_dram = nc.dram_tensor(tag + "ddd", [1, 64 * CH], F32, kind="Internal").ap()

    convw = sb(es, nc, tag + "sconvw", [PC, 18, 4], F32)
    convb = sb(es, nc, tag + "sconvb", [PC, 18], F32)
    acol = sb(es, nc, tag + "acol", [PC, 1], F32)
    dtb = sb(es, nc, tag + "sdtb", [PC, 1], F32)
    dcolD = sb(es, nc, tag + "sdcolD", [PC, 16], F32)
    nwcol = sb(es, nc, tag + "snwcol", [PC, 16], F32)
    mask64 = sb(es, nc, tag + "smask64", [64, CH], F32)
    p2c = sb(es, nc, tag + "sp2c", [64, 16, PC], BF16)
    dmask = sb(es, nc, tag + "sdmask", [PC, 64 * CH], BF16)
    ctoks = [p.dma("sync", convw[:, :, :], io["b3_convw"].rearrange("c p j -> p c j")),
             p.dma("sync", convb[:, :], io["b3_convb"]),
             p.dma("sync", acol[:, :], io["b3_alog"]),
             p.dma("sync", dtb[:, :], io["b3_dtb"]),
             p.dma("sync", dcolD[:, :], io["b3_dcol"]),
             p.dma("sync", nwcol[:, :], io["b3_nw"]),
             p.dma("sync", mask64[:, :], io["b3_mask64"]),
             p.dma("gpsimd", p2c[:, :, :], io["b3_p2"].rearrange("j h p -> h j p")),
             p.dma("gpsimd", dmask[:, :], io["b3_dmask"])]
    for tok in ctoks:
        for e2 in ("scalar", "vector", "tensor", "gpsimd"):
            p.wait(e2, "sync", tok)
    p.on("scalar", lambda e: e.activation(acol[:, :], acol[:, :], AF.Exp))
    s_a = p.on("scalar", lambda e: e.activation(
        acol[:, :], acol[:, :], AF.Identity, scale=-1.0), inc=1)
    p.wait("gpsimd", "scalar", s_a)
    g_a = p.dma("gpsimd", rt_dram, acol[:, :])
    p.wait("sync", "gpsimd", g_a)
    s_a2 = p.dma("sync", acol[:, :], rt_dram)
    for e2 in ("scalar", "vector", "tensor"):
        p.wait(e2, "sync", s_a2)

    Bf = sb(es, nc, tag + "Bf", [16, T], BF16)
    Cf = sb(es, nc, tag + "Cf", [16, T], BF16)
    dtf = sb(es, nc, tag + "dtf", [64, T], BF16)
    abf = sb(es, nc, tag + "abf", [64, T], F32)

    esW = ExitStack()
    x_sb = fm(esW, nc, tag + "xW", ND, TC, F32)
    x2 = fm(esW, nc, tag + "x2W", ND, TC, F32)
    rows = sb(esW, nc, tag + "rowsW", [1, 2 * TC], F32)
    xnb = fm(esW, nc, tag + "xnbW", ND, TC, BF16)
    wsl = [fm(esW, nc, tag + f"wslW{i}", ND, PC, BF16) for i in range(2)]
    xbc = esW.enter_context(nc.sbuf_tensor(tag + "xbc", [PC, 18, TC + 3], F32))
    zb = fm(esW, nc, tag + "zbW", NXC, TC, BF16)
    xsb = fm(esW, nc, tag + "xsbW", NXC, TC, BF16)
    dtr = sb(esW, nc, tag + "dtr", [64, TC], F32)
    dtt = sb(esW, nc, tag + "dtt", [64, TC], F32)
    psW = [ps(esW, nc, tag + f"psW{i}", [PC, 512]) for i in range(3)]
    v_zw = p.on("vector", lambda e: e.memset(xbc[:, :, 0:3], 0.0), inc=1)

    g_z = 0
    g_xs = 0
    for ci in range(NT):
        c0 = ci * TC
        p.wait("sync", "vector", p.wm("vector"))
        x_wm = p.dma("sync", x_sb[:, :, :], fm_dram(io["x_in"], ND)[:, :, c0:c0 + TC])
        for e2 in ("scalar", "tensor", "vector"):
            p.wait(e2, "sync", x_wm)
        emit_ln(p, C, x_sb, x2, TC, psW[0], psW[1], x2, rows, out_bf=xnb)
        p.wait("tensor", "vector", p.wm("vector"))
        for oc in range(35):
            pt = psW[oc % 3]
            p.wait("gpsimd", "tensor", p.wm("tensor"))
            g_w = p.dma("gpsimd", wsl[oc % 2][:, :, :],
                        fm_dram(io["b3_winT"], ND)[:, :, oc * PC:(oc + 1) * PC])
            p.wait("tensor", "gpsimd", g_w)
            t_p = 0
            for c in range(ND):
                t_p = p.on("tensor", lambda e, pt=pt, oc=oc, c=c: e.matmul(
                    pt[:, :], wsl[oc % 2][:, c, :], xnb[:, c, :],
                    start=(c == 0), stop=(c == ND - 1)), inc=(1 if c == ND - 1 else 0))
            p.wait("vector", "tensor", t_p)
            if oc < 16:
                if oc == 0:
                    p.wait("vector", "gpsimd", g_z)
                v_e = p.on("vector", lambda e, pt=pt, oc=oc: e.tensor_copy(
                    zb[:, oc, :], pt[:, :]), inc=1)
            elif oc < 34:
                v_e = p.on("vector", lambda e, pt=pt, oc=oc: e.tensor_copy(
                    xbc[:, oc - 16, 3:TC + 3], pt[:, :]), inc=1)
            else:
                v_e = p.on("vector", lambda e, pt=pt: e.tensor_copy(
                    dtr[:, :], pt[0:64, :]), inc=1)
            p.wait("tensor", "vector", v_e)
        v_wm = p.wm("vector")
        p.wait("gpsimd", "vector", v_wm)
        g_z = p.dma("gpsimd", z_dram[:, :, c0:c0 + TC], zb[:, :, :])
        s_cv = 0
        for cc in range(18):
            p.on("vector", lambda e, cc=cc: e.tensor_scalar(
                x2[:, 0, 0:TC], xbc[:, cc, 0:TC], convw[:, cc, 0:1], None,
                ALU.mult))
            for j in range(1, 4):
                p.on("vector", lambda e, cc=cc, j=j: e.scalar_tensor_tensor(
                    x2[:, 0, 0:TC], xbc[:, cc, j:j + TC], convw[:, cc, j:j + 1],
                    x2[:, 0, 0:TC], ALU.mult, ALU.add))
            p.on("vector", lambda e, cc=cc: e.tensor_copy(
                xbc[:, cc, 0:3], xbc[:, cc, TC:TC + 3]))
            v_cv = p.on("vector", lambda e: e.tensor_copy(
                x2[:, 1, 0:TC], x2[:, 0, 0:TC]), inc=1)
            p.wait("scalar", "vector", v_cv)
            s_cv = p.on("scalar", lambda e, cc=cc: e.activation(
                x2[:, 2, 0:TC], x2[:, 1, 0:TC], AF.Silu,
                bias=convb[:, cc:cc + 1]), inc=1)
            p.wait("vector", "scalar", s_cv)
            if cc < 16:
                v_rt = p.on("vector", lambda e, cc=cc: e.tensor_copy(
                    xsb[:, cc, :], x2[:, 2, 0:TC]), inc=1)
            elif cc == 16:
                v_rt = p.on("vector", lambda e, c0=c0: e.tensor_copy(
                    Bf[:, c0:c0 + TC], x2[0:16, 2, 0:TC]), inc=1)
            else:
                v_rt = p.on("vector", lambda e, c0=c0: e.tensor_copy(
                    Cf[:, c0:c0 + TC], x2[0:16, 2, 0:TC]), inc=1)
            p.wait("scalar", "vector", v_rt)
        v_wm2 = p.wm("vector")
        p.wait("gpsimd", "vector", v_wm2)
        g_xs = p.dma("gpsimd", xs_dram[:, :, c0:c0 + TC], xsb[:, :, :])
        p.wait("vector", "gpsimd", g_xs)
        p.on("vector", lambda e: e.tensor_scalar(
            dtr[:, :], dtr[:, :], dtb[0:64, :], None, ALU.add))
        p.on("vector", lambda e: e.scalar_tensor_tensor(
            dtt[:, :], dtr[:, :], -1.0, dtr[:, :], ALU.mult, ALU.max))
        v_dt = p.on("vector", lambda e: e.tensor_scalar(
            dtr[:, :], dtr[:, :], 0.0, None, ALU.max), inc=1)
        p.wait("scalar", "vector", v_dt)
        p.on("scalar", lambda e: e.activation(dtt[:, :], dtt[:, :], AF.Exp, scale=-1.0))
        s_sp = p.on("scalar", lambda e: e.activation(
            dtt[:, :], dtt[:, :], AF.Ln, bias=1.0), inc=1)
        p.wait("vector", "scalar", s_sp)
        v_dt2 = p.on("vector", lambda e: e.tensor_add(
            dtr[:, :], dtr[:, :], dtt[:, :]), inc=1)
        v_dt2 = p.on("vector", lambda e, c0=c0: e.tensor_copy(
            dtf[:, c0:c0 + TC], dtr[:, :]), inc=1)
        v_dt3 = p.on("vector", lambda e, c0=c0: e.tensor_scalar(
            abf[:, c0:c0 + TC], dtr[:, :], acol[0:64, :], None,
            ALU.mult), inc=1)
    p.barrier()
    esW.close()

    esS = ExitStack()
    acum = sb(esS, nc, tag + "acum", [64, CH], F32)
    d0row = sb(esS, nc, tag + "d0row", [64, CH], F32)
    d0rm = sb(esS, nc, tag + "d0rm", [64, CH], F32)
    d0b = sb(esS, nc, tag + "d0b", [PC, 64 * CH], F32)
    Gb = sb(esS, nc, tag + "Gb", [PC, 64 * CH], BF16)
    CcE = esS.enter_context(nc.sbuf_tensor(tag + "CcE", [PC, 64 * CH], BF16))
    corow = sb(esS, nc, tag + "corow", [1, 64], F32)
    corow16 = sb(esS, nc, tag + "corow16", [16, 64], F32)
    dscol = sb(esS, nc, tag + "dscol", [PC, 64], F32)
    CBb = sb(esS, nc, tag + "CBb", [PC, CH], BF16)
    xsc = esS.enter_context(nc.sbuf_tensor(tag + "xsc", [PC, NXC, CH], BF16))
    xdt = esS.enter_context(nc.sbuf_tensor(tag + "xdt", [PC, NXC, CH], BF16))
    xdtT = sb(esS, nc, tag + "xdtT", [PC, 2048], BF16)
    xdtd = sb(esS, nc, tag + "xdtd", [PC, 2048], BF16)
    BcT = sb(esS, nc, tag + "BcT", [PC, 16], BF16)
    stateT = sb(esS, nc, tag + "stateT", [16, 2048], F32)
    prevT = sb(esS, nc, tag + "prevT", [PC, 2048], BF16)
    cd16 = sb(esS, nc, tag + "cd16", [16, 64], F32)
    Ysb = esS.enter_context(nc.sbuf_tensor(tag + "Ysb", [PC, NXC, CH], F32))
    ygr = sb(esS, nc, tag + "ygr", [1, CH], F32)
    xo_sb = fm(esS, nc, tag + "xoS", ND, CH, F32)
    woutb = fm(esS, nc, tag + "woutb", NXC, D, BF16)
    psS = [ps(esS, nc, tag + f"psS{i}", [PC, 512]) for i in range(4)]
    psQ = [ps(esS, nc, tag + f"psQ{i}", [PC, 512]) for i in range(4)]
    psT = psQ[3][:, 0:64].bitcast(BF16)
    g_wo = p.dma("gpsimd", woutb[:, :, :], fm_dram(io["b3_woutT"], NXC))
    p.wait("tensor", "gpsimd", g_wo)
    v_zs = p.on("vector", lambda e: e.memset(stateT[:, :], 0.0))
    v_zs = p.on("vector", lambda e: e.memset(prevT[:, :], 0.0))
    v_zs = p.on("vector", lambda e: e.memset(CcE[:, :], 0.0), inc=1)
    p.wait("vector", "vector", v_zs)

    g_st = 0
    for cn in range(NC_):
        l0 = cn * CH
        p.wait("vector", "tensor", p.wm("tensor"))
        v_ac = p.on("vector", lambda e, l0=l0: e.tensor_tensor_scan(
            acum[:, :], C["cval"][1.0][0:64, :].broadcast_to([64, CH]),
            abf[:, l0:l0 + CH], 0.0, ALU.mult, ALU.add), inc=1)
        p.wait("scalar", "vector", v_ac)
        s_d0 = p.on("scalar", lambda e, l0=l0: e.activation(
            d0row[:, :], abf[:, l0:l0 + CH], AF.Exp), inc=1)
        p.wait("vector", "scalar", s_d0)
        v_d0 = p.on("vector", lambda e: e.tensor_mul(
            d0rm[:, :], d0row[:, :], mask64[:, :]), inc=1)
        p.wait("tensor", "vector", v_d0)
        p.wait("gpsimd", "vector", v_d0)
        g_dd = p.dma("gpsimd", dd_dram[0:1, :].rearrange("x (h l) -> (x h) l", h=64),
                     d0rm[:, :])
        p.wait("sync", "gpsimd", g_dd)
        s_dd = p.dma("sync", d0b[0:1, :], dd_dram[0:1, :])
        p.wait("tensor", "sync", s_dd)
        s_bc = 0
        for hh in range(0, 64, 4):
            pt = psS[(hh // 4) % 2]
            t_b = p.on("tensor", lambda e, pt=pt, hh=hh: e.matmul(
                pt[:, :], C["ones1"], d0b[0:1, hh * CH:(hh + 4) * CH],
                start=True, stop=True), inc=1)
            p.wait("scalar", "tensor", t_b)
            s_bc = p.on("scalar", lambda e, pt=pt, hh=hh: e.activation(
                d0b[:, hh * CH:(hh + 4) * CH], pt[:, :], AF.Identity), inc=1)
            p.wait("tensor", "scalar", s_bc)
        p.wait("vector", "scalar", s_bc)
        v_g = p.on("vector", lambda e: e.tensor_tensor_scan(
            Gb[:, :], d0b[:, :], dmask[:, :], 0.0, ALU.mult, ALU.add), inc=1)
        v_ds = p.on("vector", lambda e: e.tensor_copy(
            dscol[:, :].rearrange("s (h x) -> s h x", x=1),
            Gb[:, :].rearrange("s (h l) -> s h l", h=64)[:, :, CH - 1:CH]), inc=1)
        p.wait("gpsimd", "vector", v_d0)
        g_cr = p.dma("gpsimd", rt_dram[0:64, 0:1], d0row[:, 0:1])
        p.wait("sync", "gpsimd", g_cr)
        s_cr = p.dma("sync", corow16[:, :],
                     rt_dram[0:64, 0:1].rearrange("h x -> x h").broadcast_to([16, 64]))
        p.wait("vector", "sync", s_cr)
        p.wait("tensor", "vector", v_g)
        v_eab = 0
        for pc_ in range(16):
            pt = psS[3]
            t_e = p.on("tensor", lambda e, pc_=pc_: e.matmul(
                pt[0:16, :], C["ones1b"][0:1, 0:16],
                Gb[0:1, pc_ * 512:(pc_ + 1) * 512], start=True, stop=True), inc=1)
            p.wait("vector", "tensor", t_e)
            v_eab = p.on("vector", lambda e, pc_=pc_: e.tensor_copy(
                CcE[0:16, pc_ * 512:(pc_ + 1) * 512], pt[0:16, :]), inc=1)
            p.wait("tensor", "vector", v_eab)
        v_cce = p.on("vector", lambda e: e.tensor_mul(
            CcE[0:16, :].rearrange("n (h l) -> n h l", h=64),
            CcE[0:16, :].rearrange("n (h l) -> n h l", h=64),
            corow16[:, :].rearrange("n (h x) -> n h x", x=1)
            .broadcast_to([16, 64, CH])), inc=1)
        v_cce = p.on("vector", lambda e, l0=l0: e.tensor_mul(
            CcE[0:16, :].rearrange("n (h l) -> n h l", h=64),
            CcE[0:16, :].rearrange("n (h l) -> n h l", h=64),
            Cf[:, l0:l0 + CH].rearrange("n (x l) -> n x l", x=1)
            .broadcast_to([16, 64, CH])), inc=1)
        t_cb = p.on("tensor", lambda e, l0=l0: e.matmul(
            psS[2][:, 0:CH], Bf[:, l0:l0 + CH], Cf[:, l0:l0 + CH],
            start=True, stop=True), inc=1)
        p.wait("vector", "tensor", t_cb)
        v_cb = p.on("vector", lambda e: e.tensor_copy(CBb[:, :], psS[2][:, 0:CH]), inc=1)
        v_m = p.on("vector", lambda e: e.tensor_mul(
            Gb[:, :].rearrange("p (h l) -> p h l", h=64),
            Gb[:, :].rearrange("p (h l) -> p h l", h=64),
            CBb[:, :].rearrange("p (a l) -> p a l", a=1).broadcast_to([PC, 64, CH])),
            inc=1)
        p.wait("sync", "vector", p.wm("vector"))
        xs_wm = p.dma("sync", xsc[:, :, :], xs_dram[:, :, l0:l0 + CH])
        for e2 in ("tensor", "vector", "scalar"):
            p.wait(e2, "sync", xs_wm)
        for jj in range(0, 16, 4):
            pt = psS[2]
            t_d = 0
            for j in range(4):
                t_d = p.on("tensor", lambda e, pt=pt, jj=jj, j=j, l0=l0: e.matmul(
                    pt[:, j * CH:(j + 1) * CH], p2c[:, jj + j, :],
                    dtf[:, l0:l0 + CH], start=True, stop=True),
                    inc=(1 if j == 3 else 0))
            p.wait("vector", "tensor", t_d)
            v_x = 0
            for j in range(4):
                v_x = p.on("vector", lambda e, pt=pt, jj=jj, j=j: e.tensor_mul(
                    xdt[:, jj + j, :], xsc[:, jj + j, :], pt[:, j * CH:(j + 1) * CH]),
                    inc=1)
            p.wait("tensor", "vector", v_x)
        for j in range(NXC):
            t_t = p.on("tensor", lambda e, j=j: e.transpose(
                psT[:, :], xdt[:, j, :], C["identbf"]), inc=1)
            p.wait("vector", "tensor", t_t)
            v_t = p.on("vector", lambda e, j=j: e.tensor_copy(
                xdtT[:, j * CH:(j + 1) * CH], psT[:, :]), inc=1)
            p.wait("tensor", "vector", v_t)
        t_bt = p.on("tensor", lambda e, l0=l0: e.transpose(
            psT[:, 0:16], Bf[:, l0:l0 + CH], C["identbf"][0:16, 0:16]), inc=1)
        p.wait("vector", "tensor", t_bt)
        v_bt = p.on("vector", lambda e: e.tensor_copy(BcT[:, :], psT[:, 0:16]), inc=1)
        v_xd = p.on("vector", lambda e: e.tensor_mul(
            xdtd[:, :].rearrange("s (h pp) -> s h pp", h=64),
            xdtT[:, :].rearrange("s (h pp) -> s h pp", h=64),
            dscol[:, :].rearrange("s (h x) -> s h x", x=1)
            .broadcast_to([PC, 64, 32])), inc=1)
        p.wait("tensor", "vector", max(v_bt, v_xd))
        t_st = 0
        for ff in range(4):
            t_st = p.on("tensor", lambda e, ff=ff: e.matmul(
                psS[ff][0:16, :], BcT[:, :], xdtd[:, ff * 512:(ff + 1) * 512],
                start=True, stop=True), inc=1)
        p.wait("scalar", "vector", v_ac)
        s_cd = p.on("scalar", lambda e: e.activation(
            d0rm[:, 0:1], acum[:, CH - 1:CH], AF.Exp), inc=1)
        p.wait("gpsimd", "scalar", s_cd)
        g_cd = p.dma("gpsimd", rt_dram[0:64, 0:1], d0rm[:, 0:1])
        p.wait("sync", "gpsimd", g_cd)
        s_cd2 = p.dma("sync", corow[:, :], rt_dram[0:64, 0:1].rearrange("h 1 -> 1 h"))
        p.wait("tensor", "sync", s_cd2)
        t_cd = p.on("tensor", lambda e: e.matmul(
            psT[0:16, 0:64].bitcast(BF16) if False else psS[0][0:16, 0:64],
            C["ones1"][0:1, 0:16], corow[:, :], start=True, stop=True), inc=1) \
            if False else p.on("tensor", lambda e: e.matmul(
                psQ[0][0:16, 0:64], C["ones1"][0:1, 0:16], corow[:, :],
                start=True, stop=True), inc=1)
        p.wait("vector", "tensor", max(t_st, t_cd))
        v_cd = p.on("vector", lambda e: e.tensor_copy(cd16[:, :], psQ[0][0:16, 0:64]), inc=1)
        v_pv = p.on("vector", lambda e: e.tensor_copy(
            prevT[0:16, :], stateT[:, :]), inc=1)
        v_st = p.on("vector", lambda e: e.tensor_mul(
            stateT[:, :].rearrange("n (h pp) -> n h pp", h=64),
            stateT[:, :].rearrange("n (h pp) -> n h pp", h=64),
            cd16[:, :].rearrange("n (h x) -> n h x", x=1)
            .broadcast_to([16, 64, 32])))
        for ff in range(4):
            v_st = p.on("vector", lambda e, ff=ff: e.tensor_add(
                stateT[:, ff * 512:(ff + 1) * 512],
                stateT[:, ff * 512:(ff + 1) * 512], psS[ff][0:16, :]), inc=1)
        p.wait("tensor", "vector", max(v_m, v_cce, v_pv, v_st))
        ybank = [psQ[0], psQ[1], psQ[2], psQ[3], psS[0], psS[1]]

        def yslot(h):
            bk, sl = ybank[h // 12], h % 12
            return bk, (sl % 3) * 32, (sl // 3) * 128
        t_y = 0
        for h in range(64):
            pq, prow, qoff = yslot(h)
            p.on("tensor", lambda e, h=h, pq=pq, qoff=qoff, prow=prow: e.matmul(
                pq[prow:prow + 32, qoff:qoff + CH],
                xdtT[:, h * 32:(h + 1) * 32], Gb[:, h * CH:(h + 1) * CH],
                start=True, stop=False))
            t_y = p.on("tensor", lambda e, h=h, pq=pq, qoff=qoff, prow=prow: e.matmul(
                pq[prow:prow + 32, qoff:qoff + CH],
                prevT[:, h * 32:(h + 1) * 32], CcE[:, h * CH:(h + 1) * CH],
                start=False, stop=True), inc=1)
        p.wait("vector", "tensor", t_y)
        v_y = 0
        for h in range(64):
            pq, prow, qoff = yslot(h)
            r = (h % 4) * 32
            j = h // 4
            v_y = p.on("vector",
                       lambda e, pq=pq, prow=prow, qoff=qoff, r=r, j=j:
                       e.scalar_tensor_tensor(
                           Ysb[r:r + 32, j, :], xsc[r:r + 32, j, :],
                           dcolD[r:r + 32, j:j + 1],
                           pq[prow:prow + 32, qoff:qoff + CH],
                           ALU.mult, ALU.add), inc=1)
        p.wait("tensor", "vector", v_y)
        p.wait("sync", "tensor", t_y)
        z_wm = p.dma("sync", xdtT[:, :].rearrange("p (a b) -> p a b", a=NXC),
                     z_dram[:, :, l0:l0 + CH])
        p.wait("scalar", "sync", z_wm)
        p.wait("scalar", "vector", v_y)
        p.wait("scalar", "tensor", p.wm("tensor"))
        s_z = p.on("scalar", lambda e: e.activation(
            d0b[:, 0:NXC * CH].rearrange("p (a b) -> p a b", a=NXC),
            xdtT[:, :].rearrange("p (a b) -> p a b", a=NXC), AF.Silu), inc=1)
        p.wait("vector", "scalar", s_z)
        v_yg = 0
        for j in range(NXC):
            v_yg = p.on("vector", lambda e, j=j: e.tensor_mul(
                Ysb[:, j, :], Ysb[:, j, :],
                d0b[:, j * CH:(j + 1) * CH]), inc=1)
        p.wait("scalar", "vector", v_yg)
        s_q = p.on("scalar", lambda e: e.activation(
            d0b[:, 0:NXC * CH].rearrange("p (a b) -> p a b", a=NXC),
            Ysb[:, :, :], AF.Square), inc=1)
        p.wait("tensor", "scalar", s_q)
        t_ms = 0
        for j in range(NXC):
            t_ms = p.on("tensor", lambda e, j=j: e.matmul(
                psS[0][0:1, 0:CH], C["invn"],
                d0b[:, j * CH:(j + 1) * CH],
                start=(j == 0), stop=(j == NXC - 1)), inc=(1 if j == NXC - 1 else 0))
        p.wait("vector", "tensor", t_ms)
        v_ms = p.on("vector", lambda e: e.tensor_scalar(
            ygr[:, :], psS[0][0:1, 0:CH], 0.5, None, ALU.mult), inc=1)
        p.wait("scalar", "vector", v_ms)
        p.on("scalar", lambda e: e.activation(ygr[:, :], ygr[:, :], AF.Ln, bias=1e-5))
        s_rs = p.on("scalar", lambda e: e.activation(
            ygr[:, :], ygr[:, :], AF.Exp, scale=-0.5), inc=1)
        p.wait("tensor", "scalar", s_rs)
        t_bcr = p.on("tensor", lambda e: e.matmul(
            psS[1][:, 0:CH], C["ones1"], ygr[:, :], start=True, stop=True), inc=1)
        p.wait("vector", "tensor", t_bcr)
        v_yn = 0
        for j in range(NXC):
            p.on("vector", lambda e, j=j: e.tensor_mul(
                Ysb[:, j, :], Ysb[:, j, :], psS[1][:, 0:CH]))
            v_yn = p.on("vector", lambda e, j=j: e.tensor_scalar(
                xdt[:, j, :], Ysb[:, j, :], nwcol[:, j:j + 1], None, ALU.mult), inc=1)
        p.wait("tensor", "vector", v_yn)
        t_wo = 0
        for oc in range(ND):
            pw = psS[2 + oc % 2][:, (oc // 2 % 2) * CH:(oc // 2 % 2) * CH + CH] \
                if False else psS[2][:, (oc % 4) * CH:(oc % 4) * CH + CH]
            pw = (psS[2], psS[3])[oc // 4][:, (oc % 4) * CH:(oc % 4) * CH + CH]
            for j in range(NXC):
                t_wo = p.on("tensor", lambda e, pw=pw, oc=oc, j=j: e.matmul(
                    pw, woutb[:, j, oc * PC:(oc + 1) * PC], xdt[:, j, :],
                    start=(j == 0), stop=(j == NXC - 1)), inc=(1 if j == NXC - 1 else 0))
        p.wait("sync", "vector", p.wm("vector"))
        p.wait("sync", "gpsimd", g_st)
        xi_wm = p.dma("sync", xo_sb[:, :, :], fm_dram(io["x_in"], ND)[:, :, l0:l0 + CH])
        p.wait("vector", "sync", xi_wm)
        p.wait("vector", "tensor", t_wo)
        v_f = 0
        for oc in range(ND):
            pw = (psS[2], psS[3])[oc // 4][:, (oc % 4) * CH:(oc % 4) * CH + CH]
            v_f = p.on("vector", lambda e, pw=pw, oc=oc: e.tensor_add(
                xo_sb[:, oc, :], pw, xo_sb[:, oc, :]), inc=1)
        p.wait("gpsimd", "vector", v_f)
        g_st = p.dma("gpsimd", fm_dram(io["x_out"], ND)[:, :, l0:l0 + CH], xo_sb[:, :, :])
        p.wait("tensor", "vector", v_f)
    p.drain_dma("gpsimd")
    p.barrier()
    esS.close()


def declare_l0attnb(nc, io):
    io["b3_winT"] = nc.dram_tensor("b3_winT", [D, 4480], F32, kind="ExternalInput").ap()
    io["b3_woutT"] = nc.dram_tensor("b3_woutT", [2048, D], F32, kind="ExternalInput").ap()
    io["b3_convw"] = nc.dram_tensor("b3_convw", [18, PC, 4], F32, kind="ExternalInput").ap()
    io["b3_convb"] = nc.dram_tensor("b3_convb", [PC, 18], F32, kind="ExternalInput").ap()
    io["b3_alog"] = nc.dram_tensor("b3_alog", [PC, 1], F32, kind="ExternalInput").ap()
    io["b3_dtb"] = nc.dram_tensor("b3_dtb", [PC, 1], F32, kind="ExternalInput").ap()
    io["b3_dcol"] = nc.dram_tensor("b3_dcol", [PC, 16], F32, kind="ExternalInput").ap()
    io["b3_nw"] = nc.dram_tensor("b3_nw", [PC, 16], F32, kind="ExternalInput").ap()
    io["b3_mask64"] = nc.dram_tensor("b3_mask64", [64, 128], F32, kind="ExternalInput").ap()
    io["b3_p2"] = nc.dram_tensor("b3_p2", [16, 64, PC], F32, kind="ExternalInput").ap()
    io["b3_dmask"] = nc.dram_tensor("b3_dmask", [PC, 64 * 128], F32, kind="ExternalInput").ap()


def prep_l0attnb(inputs):
    d = {}
    W_in = inputs["L0_attn_b_W_in"]
    winp = np.zeros((4480, D), np.float32)
    winp[0:2048] = W_in[0:2048]
    winp[2048:4096] = W_in[2048:4096]
    winp[4096:4112] = W_in[4096:4112]
    winp[4224:4240] = W_in[4112:4128]
    winp[4352:4416] = W_in[4128:4192]
    d["b3_winT"] = np.ascontiguousarray(winp.T)
    d["b3_woutT"] = np.ascontiguousarray(inputs["L0_attn_b_W_out"].T)
    cw = inputs["L0_attn_b_conv_w"][:, 0, :]
    cwp = np.zeros((18 * PC, 4), np.float32)
    cwp[0:2048] = cw[0:2048]
    cwp[2048:2064] = cw[2048:2064]
    cwp[2176:2192] = cw[2064:2080]
    d["b3_convw"] = np.ascontiguousarray(cwp.reshape(18, PC, 4))
    cb = inputs["L0_attn_b_conv_b"]
    cbp = np.zeros((18 * PC,), np.float32)
    cbp[0:2048] = cb[0:2048]
    cbp[2048:2064] = cb[2048:2064]
    cbp[2176:2192] = cb[2064:2080]
    d["b3_convb"] = np.ascontiguousarray(cbp.reshape(18, PC).T)
    al = np.zeros((PC, 1), np.float32)
    al[0:64, 0] = inputs["L0_attn_b_A_log"]
    d["b3_alog"] = al
    db = np.zeros((PC, 1), np.float32)
    db[0:64, 0] = inputs["L0_attn_b_dt_bias"]
    d["b3_dtb"] = db
    d["b3_dcol"] = np.ascontiguousarray(
        np.repeat(inputs["L0_attn_b_D"], 32).reshape(16, PC).T)
    d["b3_nw"] = np.ascontiguousarray(inputs["L0_attn_b_norm_w"].reshape(16, PC).T)
    m64 = np.ones((64, 128), np.float32)
    m64[:, 0] = 0.0
    d["b3_mask64"] = m64
    p2 = np.zeros((16, 64, PC), np.float32)
    for j in range(16):
        for pp in range(PC):
            p2[j, (j * PC + pp) // 32, pp] = 1.0
    d["b3_p2"] = p2
    dm = np.zeros((PC, 64 * 128), np.float32)
    for h in range(64):
        for s in range(128):
            dm[s, h * 128 + s] = 1.0
    d["b3_dmask"] = dm
    return d



def emit_l1attna(p, es, io, C, tag):
    nc = p.nc
    TC = 512
    NT = T // TC
    NH = 16
    q_dram = nc.dram_tensor(tag + "qd", [PC, 24, T], BF16, kind="Internal").ap()
    ao_dram = nc.dram_tensor(tag + "aod", [PC, 8, T], BF16, kind="Internal").ap()

    esA = ExitStack()
    wqkv = fm(esA, nc, tag + "s_wqkv", ND, 3072, BF16)
    bq = sb(esA, nc, tag + "s_bq", [PC, 24], F32)
    cosq = sb(esA, nc, tag + "s_cosq", [PC, T], F32)
    sinq = sb(esA, nc, tag + "s_sinq", [PC, T], F32)
    cosk = sb(esA, nc, tag + "s_cosk", [PC, T], F32)
    sink = sb(esA, nc, tag + "s_sink", [PC, T], F32)
    g_w = p.dma("gpsimd", wqkv[:, :, :], fm_dram(io["a5_wqkvT"], ND))
    toks = [p.dma("sync", bq[:, :], io["a5_bqkv"]),
            p.dma("sync", cosq[:, :], io["a5_cosq"]),
            p.dma("sync", sinq[:, :], io["a5_sinq"]),
            p.dma("sync", cosk[:, :], io["a5_cosk"]),
            p.dma("sync", sink[:, :], io["a5_sink"])]
    p.wait("tensor", "gpsimd", g_w)
    for tok in toks:
        p.wait("tensor", "sync", tok)
        p.wait("vector", "sync", tok)

    x_sb = fm(esA, nc, tag + "x", ND, TC, F32)
    x2 = fm(esA, nc, tag + "x2", ND, TC, F32)
    rows = sb(esA, nc, tag + "rows", [1, 2 * TC], F32)
    xnb = fm(esA, nc, tag + "xnb", ND, TC, BF16)
    ev = sb(esA, nc, tag + "ev", [PC, TC], F32)
    qs = sb(esA, nc, tag + "qs", [PC, TC], F32)
    qb = sb(esA, nc, tag + "qb", [PC, TC], BF16)
    psA_ = ps(esA, nc, tag + "psA", [PC, 512])
    psB_ = ps(esA, nc, tag + "psB", [PC, 512])
    pool = [ps(esA, nc, tag + f"pp{i}", [PC, 512]) for i in range(3)]

    v_ev = 0
    for ci in range(NT):
        c0 = ci * TC
        p.wait("sync", "vector", v_ev)
        x_wm = p.dma("sync", x_sb[:, :, :], fm_dram(io["x_in"], ND)[:, :, c0:c0 + TC])
        for e2 in ("scalar", "tensor", "vector"):
            p.wait(e2, "sync", x_wm)
        emit_ln(p, C, x_sb, x2, TC, psA_, psB_, x2, rows, out_bf=xnb)
        p.wait("tensor", "vector", p.wm("vector"))
        for oc in range(24):
            pt = pool[oc % 3]
            t_p = 0
            for c in range(ND):
                t_p = p.on("tensor", lambda e, pt=pt, oc=oc, c=c: e.matmul(
                    pt[:, :], wqkv[:, c, oc * PC:(oc + 1) * PC], xnb[:, c, :],
                    start=(c == 0), stop=(c == ND - 1)), inc=(1 if c == ND - 1 else 0))
            p.wait("vector", "tensor", t_p)
            p.on("vector", lambda e, pt=pt, oc=oc: e.tensor_scalar(
                ev[:, :], pt[:, :], bq[:, oc:oc + 1], None, ALU.add))
            if oc < 16:
                cos_t, sin_t = (cosq, sinq) if oc < 8 else (cosk, sink)
                p.on("vector", lambda e: e.tensor_scalar(
                    qs[0:32, :], ev[32:64, :], -1.0, None, ALU.mult))
                p.on("vector", lambda e: e.tensor_copy(qs[32:64, :], ev[0:32, :]))
                p.on("vector", lambda e: e.tensor_scalar(
                    qs[64:96, :], ev[96:128, :], -1.0, None, ALU.mult))
                p.on("vector", lambda e: e.tensor_copy(qs[96:128, :], ev[64:96, :]))
                p.on("vector", lambda e, cos_t=cos_t, c0=c0: e.tensor_mul(
                    ev[:, :], ev[:, :], cos_t[:, c0:c0 + TC]))
                p.on("vector", lambda e, sin_t=sin_t, c0=c0: e.tensor_mul(
                    qs[:, :], qs[:, :], sin_t[:, c0:c0 + TC]))
                p.on("vector", lambda e: e.tensor_add(ev[:, :], ev[:, :], qs[:, :]))
            v_ev = p.on("vector", lambda e: e.tensor_copy(qb[:, :], ev[:, :]), inc=1)
            p.wait("tensor", "vector", v_ev)
            p.wait("gpsimd", "vector", v_ev)
            g_sp = p.dma("gpsimd", q_dram[:, oc, c0:c0 + TC], qb[:, :])
            p.wait("vector", "gpsimd", g_sp)
    p.barrier()
    esA.close()

    esB = ExitStack()
    kh = sb(esB, nc, tag + "kh", [PC, T], BF16)
    vh = sb(esB, nc, tag + "vh", [PC, T], BF16)
    vt = sb(esB, nc, tag + "vt", [PC, 32, 65], BF16)
    qt_b = sb(esB, nc, tag + "qtb", [PC, TC], BF16)
    P_sb = [sb(esB, nc, tag + f"Psb{i}", [PC, 1024], BF16) for i in range(2)]
    zrow = sb(esB, nc, tag + "zr", [1, TC], F32)
    zb = sb(esB, nc, tag + "zb", [64, TC], F32)
    ao = sb(esB, nc, tag + "ao", [64, TC], BF16)
    ps_s = [ps(esB, nc, tag + f"pss{i}", [PC, 1024]) for i in range(2)]
    ps_o = ps(esB, nc, tag + "pso", [PC, 512])
    ps_tr = esB.enter_context(nc.psum_tensor(tag + "pstr", [PC, 64], BF16))
    ps_z = ps(esB, nc, tag + "psz", [64, 512])
    v_z1 = p.on("vector", lambda e: e.memset(vt[:, :, 64:65], 1.0), inc=1)
    p.wait("vector", "vector", v_z1)

    for h in range(NH):
        hrow = (h % 2) * 64
        hc = h // 2
        p.wait("sync", "tensor", p.wm("tensor"))
        p.wait("sync", "vector", p.wm("vector"))
        p.dma("sync", kh[hrow:hrow + 64, :], q_dram[hrow:hrow + 64, 8 + hc, :])
        v_wm = p.dma("sync", vh[hrow:hrow + 64, :], q_dram[hrow:hrow + 64, 16 + hc, :])
        p.wait("tensor", "sync", v_wm)
        for kt in range(32):
            t_tr = p.on("tensor", lambda e, kt=kt, hrow=hrow: e.transpose(
                ps_tr[:, :], vh[hrow:hrow + 64, kt * 128:(kt + 1) * 128],
                C["identbf"][hrow:hrow + 64, hrow:hrow + 64]), inc=1)
            p.wait("vector", "tensor", t_tr)
            v_vt = p.on("vector", lambda e, kt=kt: e.tensor_copy(
                vt[:, kt, 0:64], ps_tr[:, :]), inc=1)
            p.wait("tensor", "vector", v_vt)
        for qt in range(NT):
            q0 = qt * TC
            p.wait("sync", "tensor", p.wm("tensor"))
            q_wm = p.dma("sync", qt_b[hrow:hrow + 64, :],
                         q_dram[hrow:hrow + 64, hc, q0:q0 + TC])
            p.wait("tensor", "sync", q_wm)
            av_wm = {0: 0, 1: 0}
            for kg in range(16):
                pss = ps_s[kg % 2]
                t_s = 0
                for j in range(2):
                    kt = kg * 2 + j
                    t_s = p.on("tensor", lambda e, pss=pss, j=j, kt=kt, hrow=hrow:
                               e.matmul(pss[:, j * 512:(j + 1) * 512],
                                        kh[hrow:hrow + 64, kt * 128:(kt + 1) * 128],
                                        qt_b[hrow:hrow + 64, :],
                                        start=True, stop=True), inc=(1 if j == 1 else 0))
                p.wait("scalar", "tensor", t_s)
                p.wait("scalar", "tensor", av_wm[kg % 2])
                s_exp = p.on("scalar", lambda e, pss=pss, kg=kg: e.activation(
                    P_sb[kg % 2][:, :], pss[:, :], AF.Exp), inc=1)
                p.wait("tensor", "scalar", s_exp)
                for j in range(2):
                    kt = kg * 2 + j
                    av_wm[kg % 2] = p.on("tensor", lambda e, j=j, kt=kt, kg=kg: e.matmul(
                        ps_o[0:65, :], vt[:, kt, :], P_sb[kg % 2][:, j * 512:(j + 1) * 512],
                        start=(kt == 0), stop=(kt == 31)), inc=1)
            t_av = p.wm("tensor")
            p.wait("scalar", "tensor", t_av)
            p.on("scalar", lambda e: e.activation(zrow[:, :], ps_o[64:65, :], AF.Ln))
            s_z = p.on("scalar", lambda e: e.activation(
                zrow[:, :], zrow[:, :], AF.Exp, scale=-1.0), inc=1)
            p.wait("tensor", "scalar", s_z)
            t_zb = p.on("tensor", lambda e: e.matmul(
                ps_z[:, :], C["ones1"][0:1, 0:64], zrow[:, :], start=True, stop=True), inc=1)
            p.wait("vector", "tensor", t_zb)
            p.on("vector", lambda e: e.tensor_copy(zb[:, :], ps_z[:, :]))
            v_ao = p.on("vector", lambda e: e.tensor_mul(
                ao[:, :], zb[:, :], ps_o[0:64, :]), inc=1)
            p.wait("gpsimd", "vector", v_ao)
            g_ao = p.dma("gpsimd", ao_dram[hrow:hrow + 64, hc, q0:q0 + TC], ao[:, :])
            p.wait("vector", "gpsimd", g_ao)
            p.wait("tensor", "vector", p.wm("vector"))
    p.barrier()
    esB.close()

    esC = ExitStack()
    wout = fm(esC, nc, tag + "s_wout", ND, D, BF16)
    bo = sb(esC, nc, tag + "s_bo", [PC, 8], F32)
    aob = fm(esC, nc, tag + "aob", ND, TC, BF16)
    x_sb2 = fm(esC, nc, tag + "x2b", ND, TC, F32)
    psO = [ps(esC, nc, tag + f"psO{i}", [PC, 512]) for i in range(4)]
    g_wo = p.dma("gpsimd", wout[:, :, :], fm_dram(io["a5_woutT"], ND))
    g_w2 = p.dma("sync", bo[:, :], io["a5_bout"])
    p.wait("tensor", "gpsimd", g_wo)
    p.wait("vector", "sync", g_w2)
    v_prev = 0
    g_st = 0
    for ci in range(NT):
        c0 = ci * TC
        p.wait("sync", "vector", v_prev)
        p.wait("sync", "gpsimd", g_st)
        x_wm = p.dma("sync", x_sb2[:, :, :], fm_dram(io["x_in"], ND)[:, :, c0:c0 + TC])
        a_wm = p.dma("sync", aob[:, :, :], ao_dram[:, :, c0:c0 + TC])
        for e2 in ("tensor", "vector"):
            p.wait(e2, "sync", a_wm)
        v_prev = 0
        for f in range(2):
            fs = f * 256
            t_p = 0
            for oc in range(ND):
                pt = psO[oc // 2][:, (oc % 2) * 256:(oc % 2) * 256 + 256]
                for c in range(ND):
                    t_p = p.on("tensor", lambda e, pt=pt, oc=oc, c=c, fs=fs: e.matmul(
                        pt, wout[:, c, oc * PC:(oc + 1) * PC], aob[:, c, fs:fs + 256],
                        start=(c == 0), stop=(c == ND - 1)),
                        inc=(1 if c == ND - 1 else 0))
            p.wait("vector", "tensor", t_p)
            for oc in range(ND):
                pt = psO[oc // 2][:, (oc % 2) * 256:(oc % 2) * 256 + 256]
                v_prev = p.on("vector", lambda e, pt=pt, oc=oc, fs=fs:
                              e.scalar_tensor_tensor(
                                  x_sb2[:, oc, fs:fs + 256], pt, bo[:, oc:oc + 1],
                                  x_sb2[:, oc, fs:fs + 256], ALU.add, ALU.add), inc=1)
            if f == 0:
                p.wait("tensor", "vector", v_prev)
        p.wait("gpsimd", "vector", v_prev)
        g_st = p.dma("gpsimd", fm_dram(io["x_out"], ND)[:, :, c0:c0 + TC], x_sb2[:, :, :])
    p.drain_dma("gpsimd")
    p.barrier()
    esC.close()


def declare_l1attna(nc, io):
    io["a5_wqkvT"] = nc.dram_tensor("a5_wqkvT", [D, 3072], F32, kind="ExternalInput").ap()
    io["a5_bqkv"] = nc.dram_tensor("a5_bqkv", [PC, 24], F32, kind="ExternalInput").ap()
    io["a5_woutT"] = nc.dram_tensor("a5_woutT", [D, D], F32, kind="ExternalInput").ap()
    io["a5_bout"] = nc.dram_tensor("a5_bout", [PC, 8], F32, kind="ExternalInput").ap()
    for nm in ("cosq", "sinq", "cosk", "sink"):
        io["a5_" + nm] = nc.dram_tensor("a5_" + nm, [PC, T], F32, kind="ExternalInput").ap()


def prep_l1attna(inputs):
    d = {}
    d["a5_wqkvT"] = np.ascontiguousarray(inputs["L1_attn_a_W_qkv"].T)
    d["a5_bqkv"] = np.ascontiguousarray(inputs["L1_attn_a_b_qkv"].reshape(24, PC).T)
    d["a5_woutT"] = np.ascontiguousarray(inputs["L1_attn_a_W_out"].T)
    d["a5_bout"] = np.ascontiguousarray(inputs["L1_attn_a_b_out"].reshape(8, PC).T)
    hd = 64
    inv_freq = 1.0 / 10000.0 ** (np.arange(0, hd, 2, dtype=np.float64) / hd)
    t = np.arange(T, dtype=np.float64)
    freqs = np.einsum("i,j->ij", t, inv_freq).astype(np.float32)
    emb = np.concatenate([freqs, freqs], axis=-1)
    cos = np.cos(emb).T.astype(np.float32)
    sin = np.sin(emb).T.astype(np.float32)
    cos2 = np.concatenate([cos, cos], 0)
    sin2 = np.concatenate([sin, sin], 0)
    d["a5_cosq"] = np.ascontiguousarray(cos2 * 0.125)
    d["a5_sinq"] = np.ascontiguousarray(sin2 * 0.125)
    d["a5_cosk"] = np.ascontiguousarray(cos2)
    d["a5_sink"] = np.ascontiguousarray(sin2)
    return d



def declare_common(nc, io):
    io["x_in"] = nc.dram_tensor("x_in", [D, T], F32, kind="ExternalInput").ap()
    io["x_out"] = nc.dram_tensor("x_out", [D, T], F32, kind="ExternalOutput").ap()
    io["c_invn"] = nc.dram_tensor("c_invn", [PC, 1], F32, kind="ExternalInput").ap()
    io["c_ones1"] = nc.dram_tensor("c_ones1", [1, PC], F32, kind="ExternalInput").ap()
    io["c_ones8col"] = nc.dram_tensor("c_ones8col", [8, 1], F32, kind="ExternalInput").ap()
    io["c_vals"] = nc.dram_tensor("c_vals", [PC, len(CONST_VALS)], F32,
                                  kind="ExternalInput").ap()
    io["c_ident"] = nc.dram_tensor("c_ident", [PC, PC], F32, kind="ExternalInput").ap()


def host_consts():
    return {
        "c_invn": np.full((PC, 1), 1.0 / D, np.float32),
        "c_ones1": np.ones((1, PC), np.float32),
        "c_ones8col": np.ones((8, 1), np.float32),
        "c_vals": np.tile(np.asarray(CONST_VALS, np.float32)[None, :], (PC, 1)),
        "c_ident": np.eye(PC, dtype=np.float32),
    }


def _build(declares, emits):
    nc = bass.Bass("TRN2", target_bir_lowering=False)
    io = {}
    declare_common(nc, io)
    for d in declares:
        d(nc, io)
    es = ExitStack()
    p = Prog(nc, es)
    C = load_consts(p, es, nc, io)
    C["ones8col"] = sb(es, nc, "sb_o8", [8, 1], F32)[:, :]
    wm = p.dma("sync", C["ones8col"], io["c_ones8col"])
    p.wait("tensor", "sync", wm)
    for em in emits:
        em(p, es, io, C)
    block = es.enter_context(nc.Block())
    p.finalize(block)
    es.close()
    return nc


def _run(nc, x_full, weight_dicts):
    cc = host_consts()
    in_maps = []
    for core in range(8):
        b = core // 2
        m = {"x_in": np.ascontiguousarray(x_full[b].T)}
        for wd in weight_dicts:
            m.update(wd)
        m.update(cc)
        in_maps.append(m)
    res = run_bass_kernel_spmd(nc, in_maps, core_ids=list(range(8)))
    return np.stack([np.asarray(res.results[2 * b]["x_out"]).T for b in range(B)])


def run_l1ffn(x_full, inputs):
    nc = _build([declare_l1ffn], [lambda p, es, io, C: emit_l1ffn(p, es, io, C, "f1_")])
    return _run(nc, x_full, [prep_l1ffn(inputs)])


def run_l0attnb(x_full, inputs):
    nc = _build([declare_l0attnb], [lambda p, es, io, C: emit_l0attnb(p, es, io, C, "b3_")])
    return _run(nc, x_full, [prep_l0attnb(inputs)])


def run_l0attna(x_full, inputs, phases="ABCD"):
    nc = _build([declare_l0attna],
                [lambda p, es, io, C: emit_l0attna(p, es, io, C, "a1_", phases=phases)])
    return _run(nc, x_full, [prep_l0attna(inputs)])


def run_l1attna(x_full, inputs):
    nc = _build([declare_l1attna], [lambda p, es, io, C: emit_l1attna(p, es, io, C, "a5_")])
    return _run(nc, x_full, [prep_l1attna(inputs)])


def run_l1attnb(x_full, inputs, debug=False):
    def decl(nc, io):
        declare_l1attnb(nc, io)
        if debug:
            io["dbg"] = {"o1": nc.dram_tensor("dbg_o1", [PC, ND, T], BF16,
                                              kind="ExternalOutput").ap()}
            for nm in ("Soc", "woc", "uoc"):
                io["dbg"][nm] = nc.dram_tensor("dbg_" + nm, [PC, T], F32,
                                               kind="ExternalOutput").ap()
            io["dbg"]["koc"] = nc.dram_tensor("dbg_koc", [PC, T], BF16,
                                              kind="ExternalOutput").ap()
            io["dbg"]["ksum"] = nc.dram_tensor("dbg_ksum", [2, T], BF16,
                                               kind="ExternalOutput").ap()
    nc = _build([decl], [lambda p, es, io, C: emit_l1attnb(p, es, io, C, "b7_",
                                                          dbg=io.get("dbg"))])
    cc = host_consts()
    wd = prep_l1attnb(inputs)
    in_maps = []
    for core in range(8):
        bb = core // 2
        m = {"x_in": np.ascontiguousarray(x_full[bb].T)}
        m.update(wd)
        m.update(cc)
        in_maps.append(m)
    res = run_bass_kernel_spmd(nc, in_maps, core_ids=list(range(8)))
    out = np.stack([np.asarray(res.results[2 * bb]["x_out"]).T for bb in range(B)])
    if debug:
        return out, {k2: np.asarray(res.results[0]["dbg_" + k2])
                     for k2 in ("o1", "Soc", "woc", "uoc", "koc", "ksum")}
    return out


def run_l0ffn(x_full, inputs, debug=False):
    def decl_dbg(nc, io):
        declare_l0ffn(nc, io)
        if debug:
            io["dbg"] = {
                "yh": nc.dram_tensor("dbg_yh", [PC, ND, 512], BF16, kind="ExternalOutput").ap(),
                "hall": nc.dram_tensor("dbg_hall", [PC, 128, 512], BF16, kind="ExternalOutput").ap(),
                "xnb": nc.dram_tensor("dbg_xnb", [PC, ND, 512], BF16, kind="ExternalOutput").ap(),
            }
    nc = _build([decl_dbg], [lambda p, es, io, C: emit_l0ffn(p, es, io, C, "f0_",
                                                            dbg=io.get("dbg"))])
    cc = host_consts()
    wd = prep_l0ffn(inputs)
    in_maps = []
    for core in range(8):
        bb = core // 2
        m = {"x_in": np.ascontiguousarray(x_full[bb].T)}
        m.update(wd)
        m.update(cc)
        in_maps.append(m)
    res = run_bass_kernel_spmd(nc, in_maps, core_ids=list(range(8)))
    out = np.stack([np.asarray(res.results[2 * bb]["x_out"]).T for bb in range(B)])
    if debug:
        return out, {k2: np.asarray(res.results[0][f"dbg_{k2}"]) for k2 in ("yh", "hall", "xnb")}
    return out



def kernel(**inputs):
    x = np.ascontiguousarray(np.asarray(inputs["L0_attn_a_x"], np.float32))
    seq = [
        ("l0a", [declare_l0attna],
         [lambda p, es, io, C: emit_l0attna(p, es, io, C, "a1_")], prep_l0attna),
        ("l0f", [declare_l0ffn],
         [lambda p, es, io, C: emit_l0ffn(p, es, io, C, "f0_")], prep_l0ffn),
        ("l0b", [declare_l0attnb],
         [lambda p, es, io, C: emit_l0attnb(p, es, io, C, "b3_")], prep_l0attnb),
        ("l0f", None, None, prep_l0ffn),
        ("l1a", [declare_l1attna],
         [lambda p, es, io, C: emit_l1attna(p, es, io, C, "a5_")], prep_l1attna),
        ("l1f", [declare_l1ffn],
         [lambda p, es, io, C: emit_l1ffn(p, es, io, C, "f1_")], prep_l1ffn),
        ("l1b", [declare_l1attnb],
         [lambda p, es, io, C: emit_l1attnb(p, es, io, C, "b7_")], prep_l1attnb),
        ("l1f", None, None, prep_l1ffn),
    ]
    progs = {}
    preps = {}
    for name, decl, emits, prep in seq:
        if name not in progs:
            progs[name] = _build(decl, emits)
        if name not in preps:
            preps[name] = prep(inputs)
        x = _run(progs[name], x, [preps[name]])
    return x


# version 02
# speedup vs baseline: 1.0055x; 1.0055x over previous
import numpy as np
import ml_dtypes
from contextlib import ExitStack
import concourse.bass as bass
import concourse.mybir as mybir
from concourse.bass_utils import run_bass_kernel_spmd

F32 = mybir.dt.float32
BF16 = mybir.dt.bfloat16
AF = mybir.ActivationFunctionType
ALU = mybir.AluOpType

B, T, D = 4, 4096, 1024
PC = 128
ND = D // PC
ENGS = ("tensor", "vector", "scalar", "gpsimd", "sync")


class Prog:

    def __init__(self, nc, es):
        self.nc = nc
        self.es = es
        self.q = {e: [] for e in ENGS}
        self.sems = {e: es.enter_context(nc.semaphore(f"pm_{e}")) for e in ENGS}
        self.cnt = {e: 0 for e in ENGS}
        self.dsems = {}
        self.dcnt = {}
        self.free_dsems = []
        self.sem_serial = 0

    def nsem(self):
        self.sem_serial += 1
        return self.sem_serial

    def recycle_dma_sems(self):
        for slot, sem in self.dsems.items():
            self.free_dsems.append((sem, self.dcnt[slot]))
        self.dsems = {}
        self.dcnt = {}

    def on(self, eng, fn, inc=0):
        self.cnt[eng] += inc
        sem = self.sems[eng]

        def thunk(h, fn=fn, inc=inc, sem=sem):
            inst = fn(h)
            if inc:
                inst.then_inc(sem, inc)
        self.q[eng].append(thunk)
        return self.cnt[eng]

    def dma(self, eng, out, in_, slot=None):
        if slot is None:
            t = out.tensor if hasattr(out, "tensor") else out
            slot = getattr(t, "name", None) or str(id(out))
        if slot not in self.dsems:
            if self.free_dsems:
                sem, base = self.free_dsems.pop()
                self.dsems[slot] = sem
                self.dcnt[slot] = base
            else:
                self.dsems[slot] = self.es.enter_context(
                    self.nc.semaphore(f"dm_{self.nsem()}"))
                self.dcnt[slot] = 0
        self.dcnt[slot] += 16
        sem = self.dsems[slot]

        def thunk(h, o=out, i=in_, sem=sem):
            h.dma_start(out=o, in_=i).then_inc(sem, 16)
        self.q[eng].append(thunk)
        return ("D", slot, self.dcnt[slot])

    def wait(self, eng, other, count):
        if isinstance(count, tuple) and count and count[0] == "D":
            _, slot, c = count
            sem = self.dsems[slot]
            self.q[eng].append(lambda h, sem=sem, c=c: h.wait_ge(sem, c))
            return
        if not isinstance(count, int) or count <= 0:
            return
        sem = self.sems[other]
        self.q[eng].append(lambda h, sem=sem, c=count: h.wait_ge(sem, c))

    def wm(self, eng):
        return self.cnt[eng]

    def barrier(self, engs=ENGS):
        marks = {e: self.cnt[e] for e in engs}
        dmarks = dict(self.dcnt)
        for e in engs:
            for o in engs:
                if o != e:
                    self.wait(e, o, marks[o])
            for slot, c in dmarks.items():
                if c > 0:
                    sem = self.dsems[slot]
                    self.q[e].append(lambda h, sem=sem, c=c: h.wait_ge(sem, c))

    def dma_last(self, slot):
        return ("D", slot, self.dcnt.get(slot, 0)) if self.dcnt.get(slot, 0) else 0

    def drain_dma(self, eng):
        for slot, c in self.dcnt.items():
            if c > 0:
                sem = self.dsems[slot]
                self.q[eng].append(lambda h, sem=sem, c=c: h.wait_ge(sem, c))

    def finalize(self, block):
        for e, thunks in self.q.items():
            def body(h, thunks=thunks):
                for t in thunks:
                    t(h)
            getattr(block, e)(body)


def sb(es, nc, name, shape, dt):
    return es.enter_context(nc.sbuf_tensor(name, shape, dt))


def ps(es, nc, name, shape):
    return es.enter_context(nc.psum_tensor(name, shape, F32))


def fm(es, nc, name, nchunks, n, dt):
    return es.enter_context(nc.sbuf_tensor(name, [PC, nchunks, n], dt))


def fm_dram(dram_ap, nchunks):
    return dram_ap.rearrange("(c p) n -> p c n", p=PC)


CONST_VALS = [1e-5, -0.5, 0.5, 1e-8, 1.1920929e-7, 60.0, -1.0, 1.0, 0.0,
              0.99, 0.125, 2.0, -2.0]


def load_consts(p, es, nc, io):
    C = {}
    C["invn"] = sb(es, nc, "sb_invn", [PC, 1], F32)[:, :]
    C["ones1"] = sb(es, nc, "sb_ones1", [1, PC], F32)[:, :]
    C["identbf"] = sb(es, nc, "sb_identbf", [PC, PC], BF16)[:, :]
    C["ones1b"] = sb(es, nc, "sb_ones1b", [1, PC], BF16)[:, :]
    p.dma("gpsimd", C["identbf"], io["c_ident"])
    gm = p.dma("gpsimd", C["ones1b"], io["c_ones1"])
    cv = sb(es, nc, "sb_cvals", [PC, len(CONST_VALS)], F32)
    for i, v in enumerate(CONST_VALS):
        nc.const_aps.register(F32, float(v), cv[:, i:i + 1])
    C["cval"] = {float(v): cv[:, i:i + 1] for i, v in enumerate(CONST_VALS)}
    p.dma("sync", C["invn"], io["c_invn"])
    p.dma("sync", cv[:, :], io["c_vals"])
    wm = p.dma("sync", C["ones1"], io["c_ones1"])
    for e2 in ("tensor", "vector", "scalar", "gpsimd"):
        p.wait(e2, "sync", wm)
        p.wait(e2, "sync", p.dma_last("sb_invn"))
        p.wait(e2, "sync", p.dma_last("sb_cvals"))
        p.wait(e2, "gpsimd", p.dma_last("sb_identbf"))
        p.wait(e2, "gpsimd", gm)
    return C



def emit_ln(p, C, x_sb, xn_sb, tc, ps_m, ps_r, x2_sb, rows, gcols=None,
            bcols=None, eps=1e-5, out_bf=None, outs=None):
    p.wait("scalar", "vector", p.wm("vector"))
    s_sq = p.on("scalar", lambda e: e.activation(
        x2_sb[:, :, 0:tc], x_sb[:, :, 0:tc], AF.Square), inc=1)
    p.wait("tensor", "scalar", s_sq)
    p.wait("tensor", "vector", p.wm("vector"))
    mrow = rows[0:1, 0:tc]
    vrow = rows[0:1, tc:2 * tc]
    t_mm = 0
    for src, dst in ((x_sb, ps_m), (x2_sb, ps_r)):
        for c in range(ND):
            t_mm = p.on("tensor", lambda e, src=src, dst=dst, c=c: e.matmul(
                dst[0:1, 0:tc], C["invn"], src[:, c, 0:tc],
                start=(c == 0), stop=(c == ND - 1)), inc=(1 if c == ND - 1 else 0))
    p.wait("vector", "tensor", t_mm)
    p.on("vector", lambda e: e.tensor_copy(mrow, ps_m[0:1, 0:tc]))
    p.on("vector", lambda e: e.tensor_mul(vrow, mrow, mrow))
    v_var = p.on("vector", lambda e: e.tensor_sub(vrow, ps_r[0:1, 0:tc], vrow), inc=1)
    p.wait("scalar", "vector", v_var)
    p.on("scalar", lambda e: e.activation(vrow, vrow, AF.Ln, bias=eps))
    s_rs = p.on("scalar", lambda e: e.activation(vrow, vrow, AF.Exp, scale=-0.5), inc=1)
    p.wait("tensor", "scalar", s_rs)
    p.on("tensor", lambda e: e.matmul(ps_m[:, 0:tc], C["ones1"], mrow,
                                      start=True, stop=True))
    t_bc = p.on("tensor", lambda e: e.matmul(ps_r[:, 0:tc], C["ones1"], vrow,
                                             start=True, stop=True), inc=1)
    p.wait("vector", "tensor", t_bc)
    if outs is None:
        outs = lambda c: xn_sb[:, c, 0:tc]
    v_last = 0
    for c in range(ND):
        p.on("vector", lambda e, c=c: e.tensor_sub(
            outs(c), x_sb[:, c, 0:tc], ps_m[:, 0:tc]))
        if gcols is None:
            v_last = p.on("vector", lambda e, c=c: e.tensor_mul(
                outs(c), outs(c), ps_r[:, 0:tc]), inc=1)
        else:
            p.on("vector", lambda e, c=c: e.tensor_mul(
                outs(c), outs(c), ps_r[:, 0:tc]))
            v_last = p.on("vector", lambda e, c=c: e.tensor_scalar(
                outs(c), outs(c), gcols[:, c:c + 1],
                None if bcols is None else bcols[:, c:c + 1],
                ALU.mult, ALU.mult if bcols is None else ALU.add), inc=1)
        if out_bf is not None:
            v_last = p.on("vector", lambda e, c=c: e.tensor_copy(
                out_bf[:, c, 0:tc], outs(c)), inc=1)
    return v_last



def emit_l1ffn(p, es, io, C, tag):
    nc = p.nc
    TC = 512
    wb = fm(es, nc, tag + "wb", ND, D, BF16)
    wA = fm(es, nc, tag + "wA", ND, 8, BF16)
    wR = fm(es, nc, tag + "wR", ND, 8, BF16)
    wB = sb(es, nc, tag + "wB", [128, D], BF16)
    eG = sb(es, nc, tag + "eG", [8, 64], BF16)
    eM = sb(es, nc, tag + "eM", [8, 64], BF16)
    bbc = sb(es, nc, tag + "bbc", [PC, ND], F32)
    rb = sb(es, nc, tag + "rb", [8, 1], F32)
    wtoks = [p.dma("gpsimd", wb[:, :, :], fm_dram(io["ffn_base_w"], ND)),
             p.dma("gpsimd", wA[:, :, :], fm_dram(io["ffn_lora_A"], ND)),
             p.dma("gpsimd", wR[:, :, :], fm_dram(io["ffn_router_w"], ND)),
             p.dma("gpsimd", wB[0:64, :], io["ffn_lora_B"]),
             p.dma("gpsimd", eG[:, :], io["ffn_eG"]),
             p.dma("gpsimd", eM[:, :], io["ffn_eM"]),
             p.dma("sync", bbc[:, :], io["ffn_base_b"]),
             p.dma("sync", rb[:, :], io["ffn_router_b"])]
    v_z = p.on("vector", lambda e: e.memset(wB[64:128, :], 0.0), inc=1)
    for tok in wtoks:
        p.wait("tensor", "gpsimd", tok)
        p.wait("vector", "gpsimd", tok)
    p.wait("tensor", "vector", v_z)

    x_sb = fm(es, nc, tag + "x", ND, TC, F32)
    xn = fm(es, nc, tag + "xn", ND, TC, F32)
    xnb = fm(es, nc, tag + "xnb", ND, TC, BF16)
    x2 = fm(es, nc, tag + "x2", ND, TC, F32)
    rows = sb(es, nc, tag + "rows", [1, 2 * TC], F32)
    prows = sb(es, nc, tag + "prows", [8, TC], F32)
    gat = sb(es, nc, tag + "gat", [8, TC], BF16)
    midb = sb(es, nc, tag + "midb", [8, TC], BF16)
    mge = sb(es, nc, tag + "mge", [64, TC], F32)
    mg = sb(es, nc, tag + "mg", [128, TC], BF16)
    zrow = sb(es, nc, tag + "zrow", [1, TC], F32)
    xo = fm(es, nc, tag + "xo", ND, TC, F32)
    psA = ps(es, nc, tag + "psA", [PC, 512])
    psB = ps(es, nc, tag + "psB", [PC, 512])
    psC = ps(es, nc, tag + "psC", [PC, 512])
    psO = [ps(es, nc, tag + f"psO{i}", [PC, 512]) for i in range(4)]
    v_z2 = p.on("vector", lambda e: e.memset(mg[64:128, :], 0.0), inc=1)

    v_out_prev = v_z2
    g_st_prev = 0
    for ci in range(T // TC):
        c0 = ci * TC
        p.wait("sync", "vector", v_out_prev)
        x_wm = p.dma("sync", x_sb[:, :, :], fm_dram(io["x_in"], ND)[:, :, c0:c0 + TC])
        for e2 in ("scalar", "tensor", "vector"):
            p.wait(e2, "sync", x_wm)
        emit_ln(p, C, x_sb, xn, TC, psA, psB, x2, rows, out_bf=xnb)
        p.wait("tensor", "vector", p.wm("vector"))
        t_g = 0
        for c in range(ND):
            t_g = p.on("tensor", lambda e, c=c: e.matmul(
                psC[0:8, :], wR[:, c, :], xnb[:, c, :], start=(c == 0), stop=(c == ND - 1)),
                inc=(1 if c == ND - 1 else 0))
        p.wait("vector", "tensor", t_g)
        v_lg = p.on("vector", lambda e: e.tensor_scalar(
            prows[:, :], psC[0:8, :], rb[:, :], None, ALU.add), inc=1)
        p.wait("scalar", "vector", v_lg)
        s_pe = p.on("scalar", lambda e: e.activation(prows[:, :], prows[:, :], AF.Exp), inc=1)
        p.wait("tensor", "scalar", s_pe)
        t_z = p.on("tensor", lambda e: e.matmul(
            psC[0:1, :], C["ones8col"], prows[:, :], start=True, stop=True), inc=1)
        p.wait("vector", "tensor", t_z)
        v_zc = p.on("vector", lambda e: e.tensor_copy(zrow[:, :], psC[0:1, :]), inc=1)
        p.wait("scalar", "vector", v_zc)
        p.on("scalar", lambda e: e.activation(zrow[:, :], zrow[:, :], AF.Ln))
        s_rz = p.on("scalar", lambda e: e.activation(
            zrow[:, :], zrow[:, :], AF.Exp, scale=-1.0), inc=1)
        p.wait("tensor", "scalar", s_rz)
        t_rz = p.on("tensor", lambda e: e.matmul(
            psC[0:8, :], C["ones1"][0:1, 0:8], zrow[:, :], start=True, stop=True), inc=1)
        p.wait("vector", "tensor", t_rz)
        v_gat = p.on("vector", lambda e: e.tensor_mul(gat[:, :], prows[:, :], psC[0:8, :]), inc=1)
        t_mid = 0
        for c in range(ND):
            t_mid = p.on("tensor", lambda e, c=c: e.matmul(
                psB[0:8, :], wA[:, c, :], xnb[:, c, :], start=(c == 0), stop=(c == ND - 1)),
                inc=(1 if c == ND - 1 else 0))
        p.wait("vector", "tensor", t_mid)
        v_mid = p.on("vector", lambda e: e.tensor_copy(midb[:, :], psB[0:8, :]), inc=1)
        p.wait("tensor", "vector", max(v_gat, v_mid))
        t_ge = p.on("tensor", lambda e: e.matmul(
            psA[0:64, :], eG[:, :], gat[:, :], start=True, stop=True), inc=1)
        t_me = p.on("tensor", lambda e: e.matmul(
            psB[0:64, :], eM[:, :], midb[:, :], start=True, stop=True), inc=1)
        p.wait("vector", "tensor", t_me)
        p.on("vector", lambda e: e.tensor_copy(mge[:, :], psB[0:64, :]))
        v_mg = p.on("vector", lambda e: e.tensor_mul(mg[0:64, :], mge[:, :], psA[0:64, :]), inc=1)
        p.wait("tensor", "vector", v_mg)
        v_out_prev = 0
        for f in range(2):
            fs = f * 256
            t_o = 0
            for oc in range(ND):
                pt = psO[oc // 2][:, (oc % 2) * 256:(oc % 2) * 256 + 256]
                for c in range(ND):
                    p.on("tensor", lambda e, pt=pt, oc=oc, c=c, fs=fs: e.matmul(
                        pt, wb[:, c, oc * PC:(oc + 1) * PC],
                        xnb[:, c, fs:fs + 256],
                        start=(c == 0), stop=False))
                t_o = p.on("tensor", lambda e, pt=pt, oc=oc, fs=fs: e.matmul(
                    pt, wB[:, oc * PC:(oc + 1) * PC], mg[:, fs:fs + 256],
                    start=False, stop=True), inc=1)
            p.wait("vector", "tensor", t_o)
            if f == 0:
                p.wait("vector", "gpsimd", g_st_prev)
            for oc in range(ND):
                pslice = psO[oc // 2][:, (oc % 2) * 256:(oc % 2) * 256 + 256]
                p.on("vector", lambda e, pslice=pslice, fs=fs, oc=oc: e.tensor_scalar(
                    xo[:, oc, fs:fs + 256], pslice, bbc[:, oc:oc + 1], None, ALU.add))
                v_out_prev = p.on("vector", lambda e, fs=fs, oc=oc: e.tensor_add(
                    xo[:, oc, fs:fs + 256], xo[:, oc, fs:fs + 256],
                    x_sb[:, oc, fs:fs + 256]), inc=1)
            if f == 0:
                p.wait("tensor", "vector", v_out_prev)
        p.wait("gpsimd", "vector", v_out_prev)
        g_st_prev = p.dma("gpsimd", fm_dram(io["x_out"], ND)[:, :, c0:c0 + TC], xo[:, :, :])
    p.drain_dma("gpsimd")


def declare_l1ffn(nc, io):
    io["ffn_base_w"] = nc.dram_tensor("ffn_base_w", [D, D], F32, kind="ExternalInput").ap()
    io["ffn_base_b"] = nc.dram_tensor("ffn_base_b", [PC, ND], F32, kind="ExternalInput").ap()
    io["ffn_lora_A"] = nc.dram_tensor("ffn_lora_A", [D, 8], F32, kind="ExternalInput").ap()
    io["ffn_lora_B"] = nc.dram_tensor("ffn_lora_B", [64, D], F32, kind="ExternalInput").ap()
    io["ffn_router_w"] = nc.dram_tensor("ffn_router_w", [D, 8], F32, kind="ExternalInput").ap()
    io["ffn_router_b"] = nc.dram_tensor("ffn_router_b", [8, 1], F32, kind="ExternalInput").ap()
    io["ffn_eG"] = nc.dram_tensor("ffn_eG", [8, 64], F32, kind="ExternalInput").ap()
    io["ffn_eM"] = nc.dram_tensor("ffn_eM", [8, 64], F32, kind="ExternalInput").ap()


def prep_l1ffn(inputs):
    eG = np.zeros((8, 64), np.float32)
    eM = np.zeros((8, 64), np.float32)
    for e in range(8):
        for r in range(8):
            eG[e, e * 8 + r] = 2.0
            eM[r, e * 8 + r] = 1.0
    return {
        "ffn_base_w": np.ascontiguousarray(inputs["L1_ffn_base_w"].T),
        "ffn_base_b": np.ascontiguousarray(inputs["L1_ffn_base_b"].reshape(ND, PC).T),
        "ffn_lora_A": np.ascontiguousarray(inputs["L1_ffn_lora_A"]),
        "ffn_lora_B": np.ascontiguousarray(inputs["L1_ffn_lora_B"].reshape(64, D)),
        "ffn_router_w": np.ascontiguousarray(inputs["L1_ffn_router_w"].T),
        "ffn_router_b": np.ascontiguousarray(inputs["L1_ffn_router_b"].reshape(8, 1)),
        "ffn_eG": eG,
        "ffn_eM": eM,
    }



def emit_l0ffn(p, es, io, C, tag, dbg=None):
    nc = p.nc
    TC = 512
    NH, NZC = 4, 128
    bd1 = [fm(es, nc, tag + f"bd1_{h}", ND, PC, BF16) for h in range(NH)]
    for h in range(NH):
        g_w = p.dma("gpsimd", bd1[h][:, :, :], io["ffn0_bd1"][h].rearrange("c p k -> p c k"))
        p.wait("tensor", "gpsimd", g_w)
    bd2buf = fm(es, nc, tag + "bd2", ND, 512, BF16)

    x_sb = fm(es, nc, tag + "x", ND, TC, F32)
    xnb = fm(es, nc, tag + "xnb", ND, TC, BF16)
    x2 = fm(es, nc, tag + "x2", ND, TC, F32)
    rows = sb(es, nc, tag + "rows", [1, 2 * TC], F32)
    y_h = fm(es, nc, tag + "yh", ND, TC, BF16)
    h_all = fm(es, nc, tag + "hall", NZC, TC, BF16)
    wcp = [sb(es, nc, tag + f"wcp{i}", [PC, D], BF16) for i in range(2)]
    pool = [ps(es, nc, tag + f"pp{i}", [PC, 512]) for i in range(8)]

    v_prev = 0
    g_st_prev = 0
    s_gelu = 0
    for ci in range(T // TC):
        c0 = ci * TC
        p.wait("sync", "vector", v_prev)
        p.wait("sync", "gpsimd", g_st_prev)
        x_wm = p.dma("sync", x_sb[:, :, :], fm_dram(io["x_in"], ND)[:, :, c0:c0 + TC])
        for e2 in ("scalar", "tensor", "vector"):
            p.wait(e2, "sync", x_wm)
        emit_ln(p, C, x_sb, x2, TC, pool[3], pool[4], x2, rows, out_bf=xnb)
        p.wait("tensor", "vector", p.wm("vector"))
        for h in range(NH):
            p.wait("gpsimd", "tensor", p.wm("tensor"))
            g_b2 = p.dma("gpsimd", bd2buf[:, :, :],
                         io["ffn0_bd2"][h].rearrange("c p k -> p c k"))
            for c in range(ND):
                t_y = p.on("tensor", lambda e, h=h, c=c: e.matmul(
                    pool[0][:, :], bd1[h][:, c, :], xnb[:, c, :],
                    start=True, stop=True), inc=1)
                p.wait("vector", "tensor", t_y)
                v_y = p.on("vector", lambda e, c=c: e.tensor_copy(
                    y_h[:, c, :], pool[0][:, :]), inc=1)
                p.wait("tensor", "vector", v_y)
            p.wait("tensor", "gpsimd", g_b2)
            if dbg is not None and ci == 0 and h == 0:
                vv = p.wm("vector")
                p.wait("gpsimd", "vector", vv)
                p.dma("gpsimd", dbg["yh"], y_h[:, :, :])
            for c in range(ND):
                for s4 in range(4):
                    zc = h * 32 + c * 4 + s4
                    pz = pool[1 + (zc % 2)]
                    p.wait("tensor", "scalar", max(0, s_gelu - 1))
                    t_zz = p.on("tensor", lambda e, c=c, s4=s4, pz=pz: e.matmul(
                        pz[:, :], bd2buf[:, c, s4 * PC:(s4 + 1) * PC], y_h[:, c, :],
                        start=True, stop=True), inc=1)
                    p.wait("scalar", "tensor", t_zz)
                    s_gelu = p.on("scalar", lambda e, zc=zc, pz=pz: e.activation(
                        h_all[:, zc, :], pz[:, :], AF.Gelu), inc=1)
        p.wait("tensor", "scalar", s_gelu)
        if dbg is not None and ci == 0:
            p.wait("gpsimd", "scalar", s_gelu)
            p.dma("gpsimd", dbg["hall"], h_all[:, :, :])
            vv2 = p.wm("vector")
            p.wait("gpsimd", "vector", vv2)
            p.dma("gpsimd", dbg["xnb"], xnb[:, :, :])
        p.wait("gpsimd", "tensor", p.wm("tensor"))
        t_o = 0
        g_wm = {}
        g_wm[0] = p.dma("gpsimd", wcp[0][:, :], io["ffn0_wcpT"][0:PC, :])
        g_wm[1] = p.dma("gpsimd", wcp[1][:, :], io["ffn0_wcpT"][PC:2 * PC, :])
        for m in range(NZC):
            p.wait("tensor", "gpsimd", g_wm[m % 2])
            for oc in range(ND):
                t_o = p.on("tensor", lambda e, m=m, oc=oc: e.matmul(
                    pool[oc][:, :], wcp[m % 2][:, oc * PC:(oc + 1) * PC],
                    h_all[:, m, :], start=(m == 0), stop=(m == NZC - 1)),
                    inc=(1 if oc == ND - 1 else 0))
            if m + 2 < NZC:
                p.wait("gpsimd", "tensor", t_o)
                g_wm[m % 2] = p.dma("gpsimd", wcp[m % 2][:, :],
                                    io["ffn0_wcpT"][(m + 2) * PC:(m + 3) * PC, :])
        p.wait("vector", "tensor", t_o)
        p.wait("vector", "gpsimd", g_st_prev)
        v_prev = 0
        for oc in range(ND):
            v_prev = p.on("vector", lambda e, oc=oc: e.tensor_add(
                x_sb[:, oc, :], pool[oc][:, :], x_sb[:, oc, :]), inc=1)
        p.wait("gpsimd", "vector", v_prev)
        g_st_prev = p.dma("gpsimd", fm_dram(io["x_out"], ND)[:, :, c0:c0 + TC], x_sb[:, :, :])
    p.drain_dma("gpsimd")


def declare_l0ffn(nc, io):
    io["ffn0_bd1"] = [nc.dram_tensor(f"ffn0_bd1_{h}", [ND, PC, PC], F32,
                                     kind="ExternalInput").ap() for h in range(4)]
    io["ffn0_bd2"] = [nc.dram_tensor(f"ffn0_bd2_{h}", [ND, PC, 512], F32,
                                     kind="ExternalInput").ap() for h in range(4)]
    io["ffn0_wcpT"] = nc.dram_tensor("ffn0_wcpT", [16384, D], F32,
                                     kind="ExternalInput").ap()


def prep_l0ffn(inputs):
    maker = inputs["L0_ffn_rail_maker"]
    up = inputs["L0_ffn_rail_up_proj"]
    d = {}
    bd1 = np.zeros((4, 8, PC, PC), np.float32)
    bd2 = np.zeros((4, 8, PC, 512), np.float32)
    for h in range(4):
        for c in range(8):
            for rl in range(8):
                r = 8 * c + rl
                bd1[h, c, 16 * rl:16 * rl + 16, 16 * rl:16 * rl + 16] = maker[h, r]
                bd2[h, c, 16 * rl:16 * rl + 16, 64 * rl:64 * rl + 64] = up[h, r].T
    for h in range(4):
        d[f"ffn0_bd1_{h}"] = np.ascontiguousarray(bd1[h])
        d[f"ffn0_bd2_{h}"] = np.ascontiguousarray(bd2[h])
    d["ffn0_wcpT"] = np.ascontiguousarray(inputs["L0_ffn_w_cproj"].T)
    return d



def emit_l1attnb(p, es, io, C, tag, dbg=None):
    nc = p.nc
    TC = 512
    NT = T // TC
    wsl = [fm(es, nc, tag + f"wsl{i}", ND, PC, BF16) for i in range(2)]
    ehs = sb(es, nc, tag + "sehs", [PC, 8, 16], BF16)
    xhs = sb(es, nc, tag + "sxhs", [16, 8, PC], BF16)
    g_e = p.dma("gpsimd", ehs[:, :, :], io["b7_ehs"].rearrange("c p j -> p c j"))
    g_w = p.dma("gpsimd", xhs[:, :, :], io["b7_xhs"].rearrange("c j p -> j c p"))
    p.wait("tensor", "gpsimd", g_e)
    p.wait("tensor", "gpsimd", g_w)

    x_sb = fm(es, nc, tag + "x", ND, TC, F32)
    x2 = fm(es, nc, tag + "x2", ND, TC, F32)
    rows = sb(es, nc, tag + "rows", [1, 2 * TC], F32)
    xnb = fm(es, nc, tag + "xnb", ND, T, BF16)
    o1c = fm(es, nc, tag + "o1c", ND, TC, BF16)
    w_oc = sb(es, nc, tag + "woc", [PC, T], F32)
    u_oc = sb(es, nc, tag + "uoc", [PC, T], F32)
    S_oc = sb(es, nc, tag + "soc", [PC, T], F32)
    k_oc = sb(es, nc, tag + "koc", [PC, T], BF16)
    ksum = sb(es, nc, tag + "ksum", [2, T], BF16)
    o1_dram = nc.dram_tensor(tag + "o1d", [PC, ND, T], BF16, kind="Internal").ap()
    psA = ps(es, nc, tag + "psA", [PC, 512])
    psB = ps(es, nc, tag + "psB", [PC, 512])
    psK = ps(es, nc, tag + "psK", [16, 512])
    psO = [ps(es, nc, tag + f"psO{i}", [PC, 512]) for i in range(2)]

    g_st_prev = 0
    v_ln = 0
    for ci in range(NT):
        c0 = ci * TC
        p.wait("sync", "vector", v_ln)
        x_wm = p.dma("sync", x_sb[:, :, :], fm_dram(io["x_in"], ND)[:, :, c0:c0 + TC])
        for e2 in ("scalar", "tensor", "vector"):
            p.wait(e2, "sync", x_wm)
        v_ln = emit_ln(p, C, x_sb, x2, TC, psA, psB, x2, rows,
                       out_bf=None)
        for c in range(ND):
            v_ln = p.on("vector", lambda e, c=c, c0=c0: e.tensor_copy(
                xnb[:, c, c0:c0 + TC], x2[:, c, 0:TC]), inc=1)

    p.wait("tensor", "vector", p.wm("vector"))
    p.wait("scalar", "vector", p.wm("vector"))
    p.wait("gpsimd", "vector", p.wm("vector"))

    def proj_oc(wname, oc, consumer):
        buf = proj_oc.flip
        proj_oc.flip ^= 1
        p.wait("gpsimd", "tensor", p.wm("tensor"))
        g_s = p.dma("gpsimd", wsl[buf][:, :, :],
                    fm_dram(io["b7_" + wname], ND)[:, :, oc * PC:(oc + 1) * PC])
        p.wait("tensor", "gpsimd", g_s)
        vres = 0
        for f in range(NT):
            fs = f * TC
            pt = psO[f % 2]
            t_p = 0
            for c in range(ND):
                t_p = p.on("tensor", lambda e, pt=pt, c=c, fs=fs, buf=buf:
                           e.matmul(pt[:, :], wsl[buf][:, c, :],
                                    xnb[:, c, fs:fs + TC],
                                    start=(c == 0), stop=(c == ND - 1)),
                           inc=(1 if c == ND - 1 else 0))
            p.wait("vector", "tensor", t_p)
            p.wait("scalar", "tensor", t_p)
            vres = consumer(fs, pt)
            p.wait("tensor", "vector", p.wm("vector"))
            p.wait("tensor", "scalar", p.wm("scalar"))
        return vres
    proj_oc.flip = 0

    for oc in range(ND):
        proj_oc("wk", oc, lambda fs, pt: p.on(
            "vector", lambda e, fs=fs, pt=pt: e.tensor_copy(
                k_oc[:, fs:fs + TC], pt[:, :]), inc=1))
        v_kb = p.wm("vector")
        p.wait("tensor", "vector", v_kb)
        for f in range(NT):
            fs = f * TC
            t_ks = p.on("tensor", lambda e, fs=fs: e.matmul(
                psK[0:2, :], ehs[:, 0, 0:2], k_oc[:, fs:fs + TC],
                start=True, stop=True), inc=1)
            p.wait("vector", "tensor", t_ks)
            v_ks = p.on("vector", lambda e, fs=fs: e.tensor_copy(
                ksum[:, fs:fs + TC], psK[0:2, :]), inc=1)
            p.wait("tensor", "vector", v_ks)
        if dbg is not None and oc == 0:
            vks_ = p.wm("vector")
            p.wait("gpsimd", "vector", vks_)
            p.dma("gpsimd", dbg["koc"], k_oc[:, :])
            g_dk = p.dma("gpsimd", dbg["ksum"], ksum[:, :])
            p.wait("vector", "gpsimd", g_dk)
            p.wait("tensor", "gpsimd", g_dk)
        def cons_v(fs, pt):
            t_e = p.on("tensor", lambda e, fs=fs: e.matmul(
                psB[:, :], xhs[0:2, 0, :], ksum[:, fs:fs + TC],
                start=True, stop=True), inc=1)
            p.on("vector", lambda e, fs=fs, pt=pt: e.tensor_copy(
                u_oc[:, fs:fs + TC], pt[:, :]))
            p.wait("vector", "tensor", t_e)
            return p.on("vector", lambda e, fs=fs: e.tensor_mul(
                u_oc[:, fs:fs + TC], u_oc[:, fs:fs + TC], psB[:, :]), inc=1)
        proj_oc("wv", oc, cons_v)
        def cons_w(fs, pt):
            s_s = p.on("scalar", lambda e, fs=fs, pt=pt: e.activation(
                w_oc[:, fs:fs + TC], pt[:, :], AF.Sigmoid), inc=1)
            p.wait("vector", "scalar", s_s)
            return p.on("vector", lambda e, fs=fs: e.tensor_scalar(
                w_oc[:, fs:fs + TC], w_oc[:, fs:fs + TC], 0.99, None, ALU.mult), inc=1)
        proj_oc("ww", oc, cons_w)
        v_sc = p.on("vector", lambda e: e.tensor_tensor_scan(
            S_oc[:, :], w_oc[:, :], u_oc[:, :], 0.0, ALU.mult, ALU.add), inc=1)
        def cons_r(fs, pt):
            s_s = p.on("scalar", lambda e, fs=fs, pt=pt: e.activation(
                u_oc[:, fs:fs + TC], pt[:, :], AF.Sigmoid), inc=1)
            p.wait("vector", "scalar", s_s)
            return p.on("vector", lambda e, fs=fs: e.tensor_mul(
                k_oc[:, fs:fs + TC], u_oc[:, fs:fs + TC], S_oc[:, fs:fs + TC]), inc=1)
        p.wait("scalar", "vector", v_sc)
        if dbg is not None and oc == 0:
            p.wait("gpsimd", "vector", v_sc)
            p.dma("gpsimd", dbg["Soc"], S_oc[:, :])
            p.dma("gpsimd", dbg["woc"], w_oc[:, :])
            g_dd = p.dma("gpsimd", dbg["uoc"], u_oc[:, :])
            p.wait("vector", "gpsimd", g_dd)
            p.wait("scalar", "gpsimd", g_dd)
        v_o1 = proj_oc("wr", oc, cons_r)
        p.wait("gpsimd", "vector", v_o1)
        g_o1 = p.dma("gpsimd", o1_dram[:, oc, :], k_oc[:, :])
        p.wait("vector", "gpsimd", g_o1)

    if dbg is not None:
        p.wait("gpsimd", "vector", p.wm("vector"))
        p.dma("gpsimd", dbg["o1"], o1_dram)
    p.wait("sync", "vector", p.wm("vector"))
    p.drain_dma("sync")
    wo = fm(es, nc, tag + "swo", ND, D, BF16)
    g_wo = p.dma("gpsimd", wo[:, :, :], fm_dram(io["b7_wo"], ND))
    p.wait("tensor", "gpsimd", g_wo)
    rsb = fm(es, nc, tag + "rsb", ND, TC, BF16)
    v_prev = 0
    for ci in range(NT):
        c0 = ci * TC
        p.wait("sync", "vector", v_prev)
        p.wait("sync", "gpsimd", g_st_prev)
        x_wm = p.dma("sync", x_sb[:, :, :], fm_dram(io["x_in"], ND)[:, :, c0:c0 + TC])
        for e2 in ("scalar", "tensor", "vector"):
            p.wait(e2, "sync", x_wm)
        p.wait("sync", "scalar", p.wm("scalar"))
        o_wm = p.dma("sync", o1c[:, :, :], o1_dram[:, :, c0:c0 + TC])
        for e2 in ("scalar", "vector", "tensor"):
            p.wait(e2, "sync", o_wm)
        s_q = p.on("scalar", lambda e: e.activation(
            x2[:, :, 0:TC], o1c[:, :, 0:TC], AF.Square), inc=1)
        p.wait("tensor", "scalar", s_q)
        t_ms = 0
        for c in range(ND):
            t_ms = p.on("tensor", lambda e, c=c: e.matmul(
                psA[0:1, 0:TC], C["invn"], x2[:, c, 0:TC],
                start=(c == 0), stop=(c == ND - 1)), inc=(1 if c == ND - 1 else 0))
        p.wait("vector", "tensor", t_ms)
        v_ms = p.on("vector", lambda e: e.tensor_copy(rows[0:1, 0:TC], psA[0:1, 0:TC]), inc=1)
        p.wait("scalar", "vector", v_ms)
        p.on("scalar", lambda e: e.activation(rows[0:1, 0:TC], rows[0:1, 0:TC],
                                              AF.Ln, bias=1.1920929e-7))
        s_rr = p.on("scalar", lambda e: e.activation(rows[0:1, 0:TC], rows[0:1, 0:TC],
                                                     AF.Exp, scale=-0.5), inc=1)
        p.wait("tensor", "scalar", s_rr)
        t_bc = p.on("tensor", lambda e: e.matmul(
            psA[:, 0:TC], C["ones1"], rows[0:1, 0:TC], start=True, stop=True), inc=1)
        p.wait("vector", "tensor", t_bc)
        v_rs = 0
        for c in range(ND):
            v_rs = p.on("vector", lambda e, c=c: e.tensor_mul(
                rsb[:, c, :], o1c[:, c, 0:TC], psA[:, 0:TC]), inc=1)
        p.wait("tensor", "vector", v_rs)
        v_prev = 0
        for f in range(2):
            fs = f * 256
            t_p = 0
            for oc2 in range(ND):
                pb = (psO[0], psO[1], psA, psB)[oc2 // 2]
                pt = pb[:, (oc2 % 2) * 256:(oc2 % 2) * 256 + 256]
                for c in range(ND):
                    t_p = p.on("tensor", lambda e, pt=pt, oc2=oc2, c=c, fs=fs: e.matmul(
                        pt, wo[:, c, oc2 * PC:(oc2 + 1) * PC],
                        rsb[:, c, fs:fs + 256],
                        start=(c == 0), stop=(c == ND - 1)),
                        inc=(1 if c == ND - 1 else 0))
            p.wait("vector", "tensor", t_p)
            if f == 0:
                p.wait("vector", "gpsimd", g_st_prev)
            for oc2 in range(ND):
                pb = (psO[0], psO[1], psA, psB)[oc2 // 2]
                pt = pb[:, (oc2 % 2) * 256:(oc2 % 2) * 256 + 256]
                v_prev = p.on("vector", lambda e, oc2=oc2, fs=fs, pt=pt: e.tensor_add(
                    x_sb[:, oc2, fs:fs + 256], pt, x_sb[:, oc2, fs:fs + 256]), inc=1)
            if f == 0:
                p.wait("tensor", "vector", v_prev)
        p.wait("gpsimd", "vector", v_prev)
        g_st_prev = p.dma("gpsimd", fm_dram(io["x_out"], ND)[:, :, c0:c0 + TC], x_sb[:, :, :])
    p.drain_dma("gpsimd")


def declare_l1attnb(nc, io):
    for nm in ("wr", "wk", "wv", "ww", "wo"):
        io["b7_" + nm] = nc.dram_tensor("b7_" + nm, [D, D], F32, kind="ExternalInput").ap()
    io["b7_ehs"] = nc.dram_tensor("b7_ehs", [8, PC, 16], F32, kind="ExternalInput").ap()
    io["b7_xhs"] = nc.dram_tensor("b7_xhs", [8, 16, PC], F32, kind="ExternalInput").ap()


def prep_l1attnb(inputs):
    d = {}
    for nm, key in (("wr", "L1_attn_b_Wr"), ("wk", "L1_attn_b_Wk"), ("wv", "L1_attn_b_Wv"),
                    ("ww", "L1_attn_b_Ww"), ("wo", "L1_attn_b_Wo")):
        d["b7_" + nm] = np.ascontiguousarray(inputs[key].T)
    ehs = np.zeros((8, PC, 16), np.float32)
    xhs = np.zeros((8, 16, PC), np.float32)
    for c in range(8):
        for pp in range(PC):
            hl = pp // 64
            ehs[c, pp, hl] = 1.0
            xhs[c, hl, pp] = 1.0
    d["b7_ehs"] = ehs
    d["b7_xhs"] = xhs
    return d



def emit_l0attna(p, es, io, C, tag, phases="ABCD"):
    nc = p.nc
    TC = 512
    NT = T // TC
    NG = 32
    x0i_dram = nc.dram_tensor(tag + "x0i", [PC, ND, T], F32, kind="Internal").ap()
    xr_dram = nc.dram_tensor(tag + "xr", [PC, ND, T], BF16, kind="Internal").ap()
    rw_dram = nc.dram_tensor(tag + "rw", [PC, NG, T], BF16, kind="Internal").ap()
    dc_dram = nc.dram_tensor(tag + "dcd", [PC, 2 * NG], F32, kind="Internal").ap()

    dcol = sb(es, nc, tag + "s_dcol", [PC, NG], F32)
    fcol = sb(es, nc, tag + "s_fcol", [PC, NG], F32)
    mixk = sb(es, nc, tag + "s_mixk", [PC, ND], F32)
    mixv = sb(es, nc, tag + "s_mixv", [PC, ND], F32)
    mixr = sb(es, nc, tag + "s_mixr", [PC, ND], F32)
    cmixk = sb(es, nc, tag + "s_cmixk", [PC, ND], F32)
    cmixr = sb(es, nc, tag + "s_cmixr", [PC, ND], F32)
    lncols = sb(es, nc, tag + "s_lnc", [PC, 6 * ND], F32)
    ctoks = [p.dma("sync", mixk[:, :], io["a1_mixk"]),
             p.dma("sync", mixv[:, :], io["a1_mixv"]),
             p.dma("sync", mixr[:, :], io["a1_mixr"]),
             p.dma("sync", cmixk[:, :], io["a1_cmixk"]),
             p.dma("sync", cmixr[:, :], io["a1_cmixr"]),
             p.dma("sync", lncols[:, :], io["a1_lncols"]),
             p.dma("sync", dcol[:, :], io["a1_decay"]),
             p.dma("sync", fcol[:, :], io["a1_first"])]
    for tok in ctoks[-2:]:
        p.wait("scalar", "sync", tok)
    p.on("scalar", lambda e: e.activation(dcol[:, :], dcol[:, :], AF.Exp))
    p.on("scalar", lambda e: e.activation(dcol[:, :], dcol[:, :], AF.Exp, scale=-1.0))
    s_d = p.on("scalar", lambda e: e.activation(fcol[:, :], fcol[:, :], AF.Exp), inc=1)
    p.wait("gpsimd", "scalar", s_d)
    p.dma("gpsimd", dc_dram[:, 0:NG], dcol[:, :])
    g_rt = p.dma("gpsimd", dc_dram[:, NG:2 * NG], fcol[:, :])
    p.wait("sync", "gpsimd", g_rt)
    rt1 = p.dma("sync", dcol[:, :], dc_dram[:, 0:NG])
    rt2 = p.dma("sync", fcol[:, :], dc_dram[:, NG:2 * NG])
    for e2 in ("vector", "scalar", "tensor", "gpsimd"):
        for tok in ctoks[:6] + [rt1, rt2]:
            p.wait(e2, "sync", tok)

    esAB = ExitStack()
    xk = fm(esAB, nc, tag + "s_xk", ND, T, BF16)
    xv = fm(esAB, nc, tag + "s_xv", ND, T, BF16)

    esP = ExitStack()
    x_sb = fm(esP, nc, tag + "xA", ND, TC, F32)
    x2 = fm(esP, nc, tag + "x2A", ND, TC, F32)
    x0c = fm(esP, nc, tag + "x0c", ND, TC, F32)
    rows = sb(esP, nc, tag + "rowsA", [1, 2 * TC], F32)
    xn_buf = esP.enter_context(nc.sbuf_tensor(tag + "xnbuf", [PC, ND, TC + 1], F32))
    xrc = fm(esP, nc, tag + "xrc", ND, TC, BF16)
    tmix = x2
    psA_ = ps(esP, nc, tag + "psA", [PC, 512])
    psB_ = ps(esP, nc, tag + "psB", [PC, 512])
    v_pr = p.on("vector", lambda e: e.memset(xn_buf[:, :, 0:1], 0.0), inc=1)

    g_pr = 0
    for ci in range(NT):
        c0 = ci * TC
        p.wait("sync", "vector", v_pr)
        p.wait("sync", "gpsimd", g_pr)
        x_wm = p.dma("sync", x_sb[:, :, :], fm_dram(io["x_in"], ND)[:, :, c0:c0 + TC])
        for e2 in ("scalar", "tensor", "vector"):
            p.wait(e2, "sync", x_wm)
        emit_ln(p, C, x_sb, x2, TC, psA_, psB_, x2, rows)
        emit_ln(p, C, x2, x0c, TC, psA_, psB_, x_sb, rows,
                gcols=lncols[:, 0:ND], bcols=lncols[:, ND:2 * ND])
        emit_ln(p, C, x0c, None, TC, psA_, psB_, x_sb, rows,
                gcols=lncols[:, 2 * ND:3 * ND], bcols=lncols[:, 3 * ND:4 * ND],
                outs=lambda c: xn_buf[:, c, 1:TC + 1])
        v_sp = p.wm("vector")
        p.wait("gpsimd", "vector", v_sp)
        g_pr = p.dma("gpsimd", x0i_dram[:, :, c0:c0 + TC], x0c[:, :, :])
        for c in range(ND):
            p.on("vector", lambda e, c=c: e.tensor_sub(
                tmix[:, c, :], xn_buf[:, c, 1:TC + 1], xn_buf[:, c, 0:TC]))
        for mq, kind in ((mixk, "k"), (mixv, "v"), (mixr, "r")):
            for c in range(ND):
                if kind == "k":
                    dv = xk[:, c, c0:c0 + TC]
                elif kind == "v":
                    dv = xv[:, c, c0:c0 + TC]
                else:
                    dv = xrc[:, c, :]
                v_pr = p.on("vector", lambda e, c=c, mq=mq, dv=dv: e.scalar_tensor_tensor(
                    dv, tmix[:, c, :], mq[:, c:c + 1], xn_buf[:, c, 0:TC],
                    ALU.mult, ALU.add), inc=1)
        p.wait("gpsimd", "vector", v_pr)
        g_pr = p.dma("gpsimd", xr_dram[:, :, c0:c0 + TC], xrc[:, :, :])
        v_pr = p.on("vector", lambda e: e.tensor_copy(
            xn_buf[:, :, 0:1], xn_buf[:, :, TC:TC + 1]), inc=1)
    p.wait("vector", "gpsimd", g_pr)
    p.barrier()
    if "B" not in phases:
        for ci2 in range(NT):
            p.dma("gpsimd", fm_dram(io["x_out"], ND)[:, :, ci2 * TC:(ci2 + 1) * TC],
                  x_sb[:, :, :] if False else None) if False else None
        g_f = 0
        xf = x_sb
        for ci2 in range(NT):
            p.wait("sync", "gpsimd", g_f)
            w1 = p.dma("sync", xf[:, :, :], x0i_dram[:, :, ci2 * TC:(ci2 + 1) * TC])
            p.wait("gpsimd", "sync", w1)
            g_f = p.dma("gpsimd", fm_dram(io["x_out"], ND)[:, :, ci2 * TC:(ci2 + 1) * TC],
                        xf[:, :, :])
        p.drain_dma("gpsimd")
        esP.close()
        esAB.close()
        return
    esP.close()

    esB = ExitStack()
    wsl = [fm(esB, nc, tag + f"wslB{i}", ND, PC, BF16) for i in range(2)]
    k_buf = esB.enter_context(nc.sbuf_tensor(tag + "kbuf", [PC, T + 1], F32))
    kv_buf = esB.enter_context(nc.sbuf_tensor(tag + "kvbuf", [PC, T + 1], F32))
    sk_b = sb(esB, nc, tag + "skb", [PC, T], F32)
    skv_b = sb(esB, nc, tag + "skvb", [PC, T], F32)
    rwb = sb(esB, nc, tag + "rwbB", [PC, T], BF16)
    psP = [ps(esB, nc, tag + f"psP{i}", [PC, 512]) for i in range(2)]
    v_zb = p.on("vector", lambda e: e.memset(k_buf[:, 0:1], 0.0))
    v_zb = p.on("vector", lambda e: e.memset(kv_buf[:, 0:1], 0.0), inc=1)
    p.wait("vector", "vector", v_zb)
    g_rwsp = 0

    for g in range(NG):
        p.wait("gpsimd", "tensor", p.wm("tensor"))
        g_k = p.dma("gpsimd", wsl[0][:, :, :],
                    fm_dram(io["a1_wk"], ND)[:, :, g * PC:(g + 1) * PC])
        p.wait("tensor", "gpsimd", g_k)
        for f in range(NT):
            fs = f * TC
            pt = psP[f % 2]
            t_k = 0
            for c in range(ND):
                t_k = p.on("tensor", lambda e, pt=pt, c=c, fs=fs: e.matmul(
                    pt[:, :], wsl[0][:, c, :], xk[:, c, fs:fs + TC],
                    start=(c == 0), stop=(c == ND - 1)), inc=(1 if c == ND - 1 else 0))
            p.wait("scalar", "tensor", t_k)
            s_k = p.on("scalar", lambda e, pt=pt, fs=fs: e.activation(
                k_buf[:, 1 + fs:1 + fs + TC], pt[:, :], AF.Exp), inc=1)
            p.wait("vector", "scalar", s_k)
            v_k = p.on("vector", lambda e, fs=fs: e.tensor_scalar(
                k_buf[:, 1 + fs:1 + fs + TC], k_buf[:, 1 + fs:1 + fs + TC],
                1.1420073898156842e26, None, ALU.min), inc=1)
            p.wait("tensor", "vector", v_k)
        p.wait("gpsimd", "tensor", p.wm("tensor"))
        g_v = p.dma("gpsimd", wsl[1][:, :, :],
                    fm_dram(io["a1_wv"], ND)[:, :, g * PC:(g + 1) * PC])
        p.wait("tensor", "gpsimd", g_v)
        for f in range(NT):
            fs = f * TC
            pt = psP[f % 2]
            t_v = 0
            for c in range(ND):
                t_v = p.on("tensor", lambda e, pt=pt, c=c, fs=fs: e.matmul(
                    pt[:, :], wsl[1][:, c, :], xv[:, c, fs:fs + TC],
                    start=(c == 0), stop=(c == ND - 1)), inc=(1 if c == ND - 1 else 0))
            p.wait("vector", "tensor", t_v)
            v_kv = p.on("vector", lambda e, pt=pt, fs=fs: e.tensor_mul(
                kv_buf[:, 1 + fs:1 + fs + TC], k_buf[:, 1 + fs:1 + fs + TC],
                pt[:, :]), inc=1)
            p.wait("tensor", "vector", v_kv)
        dbc = dcol[:, g:g + 1].broadcast_to([PC, T])
        p.on("vector", lambda e, dbc=dbc: e.tensor_tensor_scan(
            sk_b[:, :], dbc, k_buf[:, 0:T], 0.0, ALU.mult, ALU.add))
        p.on("vector", lambda e, dbc=dbc: e.tensor_tensor_scan(
            skv_b[:, :], dbc, kv_buf[:, 0:T], 0.0, ALU.mult, ALU.add))
        p.on("vector", lambda e, g=g: e.scalar_tensor_tensor(
            sk_b[:, :], k_buf[:, 1:T + 1], fcol[:, g:g + 1], sk_b[:, :],
            ALU.mult, ALU.add))
        v_wk = p.on("vector", lambda e: e.tensor_scalar(
            sk_b[:, :], sk_b[:, :], 1e-8, None, ALU.add), inc=1)
        p.on("vector", lambda e, g=g: e.scalar_tensor_tensor(
            skv_b[:, :], kv_buf[:, 1:T + 1], fcol[:, g:g + 1], skv_b[:, :],
            ALU.mult, ALU.add))
        p.wait("scalar", "vector", v_wk)
        p.on("scalar", lambda e: e.activation(sk_b[:, :], sk_b[:, :], AF.Ln))
        s_inv = p.on("scalar", lambda e: e.activation(
            sk_b[:, :], sk_b[:, :], AF.Exp, scale=-1.0), inc=1)
        p.wait("vector", "scalar", s_inv)
        p.wait("vector", "gpsimd", g_rwsp)
        v_rw = p.on("vector", lambda e: e.tensor_mul(
            rwb[:, :], skv_b[:, :], sk_b[:, :]), inc=1)
        p.wait("gpsimd", "vector", v_rw)
        g_rwsp = p.dma("gpsimd", rw_dram[:, g, :], rwb[:, :])
    p.barrier()
    esB.close()
    esAB.close()
    if "C" not in phases:
        esF = ExitStack()
        xf = fm(esF, nc, tag + "xf", ND, TC, F32)
        g_f = 0
        for ci2 in range(NT):
            p.wait("sync", "gpsimd", g_f)
            w1 = p.dma("sync", xf[:, :, :], x0i_dram[:, :, ci2 * TC:(ci2 + 1) * TC])
            p.wait("gpsimd", "sync", w1)
            g_f = p.dma("gpsimd", fm_dram(io["x_out"], ND)[:, :, ci2 * TC:(ci2 + 1) * TC],
                        xf[:, :, :])
        p.drain_dma("gpsimd")
        esF.close()
        return

    esC = ExitStack()
    wslr = fm(esC, nc, tag + "wslr", ND, PC, BF16)
    wslo = sb(esC, nc, tag + "wslo", [PC, D], BF16)
    xrb = fm(esC, nc, tag + "xrb", ND, TC, BF16)
    rwall = esC.enter_context(nc.sbuf_tensor(tag + "rwall", [PC, NG, TC], BF16))
    rwg = sb(esC, nc, tag + "rwgC", [PC, TC], BF16)
    sgt = sb(esC, nc, tag + "sgtC", [PC, TC], F32)
    x0b = fm(esC, nc, tag + "x0b", ND, TC, F32)
    psO = [ps(esC, nc, tag + f"psOC{i}", [PC, 512]) for i in range(8)]
    g_st = 0
    for f in range(NT):
        fs = f * TC
        p.wait("sync", "vector", p.wm("vector"))
        xr_wm = p.dma("sync", xrb[:, :, :], xr_dram[:, :, fs:fs + TC])
        p.wait("tensor", "sync", xr_wm)
        for g in range(NG):
            p.wait("gpsimd", "tensor", p.wm("tensor"))
            g_r = p.dma("gpsimd", wslr[:, :, :],
                        fm_dram(io["a1_wr"], ND)[:, :, g * PC:(g + 1) * PC])
            p.wait("sync", "vector", p.wm("vector"))
            rg_wm = p.dma("sync", rwg[:, :], rw_dram[:, g, fs:fs + TC])
            p.wait("tensor", "gpsimd", g_r)
            pt = psO[g % 2]
            t_r = 0
            for c in range(ND):
                t_r = p.on("tensor", lambda e, pt=pt, c=c: e.matmul(
                    pt[:, :], wslr[:, c, :], xrb[:, c, :],
                    start=(c == 0), stop=(c == ND - 1)), inc=(1 if c == ND - 1 else 0))
            p.wait("scalar", "tensor", t_r)
            s_s = p.on("scalar", lambda e, pt=pt: e.activation(
                sgt[:, :], pt[:, :], AF.Sigmoid), inc=1)
            p.wait("vector", "scalar", s_s)
            p.wait("vector", "sync", rg_wm)
            v_rg = p.on("vector", lambda e, g=g: e.tensor_mul(
                rwall[:, g, :], rwg[:, :], sgt[:, :]), inc=1)
            p.wait("tensor", "vector", v_rg)
        t_o = 0
        for g in range(NG):
            p.wait("gpsimd", "tensor", p.wm("tensor"))
            g_o = p.dma("gpsimd", wslo[:, :], io["a1_wo"][g * PC:(g + 1) * PC, :])
            p.wait("tensor", "gpsimd", g_o)
            for oc in range(ND):
                t_o = p.on("tensor", lambda e, oc=oc, g=g: e.matmul(
                    psO[oc][:, :], wslo[:, oc * PC:(oc + 1) * PC], rwall[:, g, :],
                    start=(g == 0), stop=(g == NG - 1)), inc=(1 if oc == ND - 1 else 0))
        p.wait("sync", "vector", p.wm("vector"))
        p.wait("sync", "gpsimd", g_st)
        x0_wm = p.dma("sync", x0b[:, :, :], x0i_dram[:, :, fs:fs + TC])
        p.wait("vector", "sync", x0_wm)
        p.wait("vector", "tensor", t_o)
        v_ev = 0
        for oc in range(ND):
            v_ev = p.on("vector", lambda e, oc=oc: e.tensor_add(
                x0b[:, oc, :], psO[oc][:, :], x0b[:, oc, :]), inc=1)
        p.wait("gpsimd", "vector", v_ev)
        g_st = p.dma("gpsimd", x0i_dram[:, :, fs:fs + TC], x0b[:, :, :])
        p.wait("tensor", "vector", v_ev)
    p.wait("vector", "gpsimd", g_st)
    p.barrier()
    esC.close()
    if "D" not in phases:
        esF = ExitStack()
        xf = fm(esF, nc, tag + "xf2", ND, TC, F32)
        g_f = 0
        for ci2 in range(NT):
            p.wait("sync", "gpsimd", g_f)
            w1 = p.dma("sync", xf[:, :, :], x0i_dram[:, :, ci2 * TC:(ci2 + 1) * TC])
            p.wait("gpsimd", "sync", w1)
            g_f = p.dma("gpsimd", fm_dram(io["x_out"], ND)[:, :, ci2 * TC:(ci2 + 1) * TC],
                        xf[:, :, :])
        p.drain_dma("gpsimd")
        esF.close()
        return

    esD = ExitStack()
    xD = fm(esD, nc, tag + "xD", ND, TC, F32)
    x2D = fm(esD, nc, tag + "x2D", ND, TC, F32)
    rowsD = sb(esD, nc, tag + "rowsD", [1, 2 * TC], F32)
    xnD = esD.enter_context(nc.sbuf_tensor(tag + "xnD", [PC, ND, TC + 1], F32))
    tmixD = fm(esD, nc, tag + "tmixD", ND, TC, F32)
    xk2 = fm(esD, nc, tag + "xk2", ND, TC, BF16)
    xr2 = fm(esD, nc, tag + "xr2", ND, TC, BF16)
    silb = esD.enter_context(nc.sbuf_tensor(tag + "silb", [PC, NG, TC], BF16))
    sgD = fm(esD, nc, tag + "sgD", ND, TC, F32)
    xinD = fm(esD, nc, tag + "xinD", ND, TC, F32)
    wslD = [fm(esD, nc, tag + f"wslD{i}", ND, PC, BF16) for i in range(2)]
    cmwrb = fm(esD, nc, tag + "cmwrb", ND, D, BF16)
    g_wr0 = p.dma("gpsimd", cmwrb[:, :, :], fm_dram(io["a1_cmwr"], ND))
    p.wait("tensor", "gpsimd", g_wr0)
    psD = [ps(esD, nc, tag + f"psD{i}", [PC, 512]) for i in range(8)]
    v_zD = p.on("vector", lambda e: e.memset(xnD[:, :, 0:1], 0.0), inc=1)

    v_pr = v_zD
    g_st = 0
    for ci in range(NT):
        c0 = ci * TC
        p.wait("sync", "vector", v_pr)
        p.wait("sync", "gpsimd", g_st)
        x_wm = p.dma("sync", xD[:, :, :], x0i_dram[:, :, c0:c0 + TC])
        for e2 in ("scalar", "tensor", "vector"):
            p.wait(e2, "sync", x_wm)
        emit_ln(p, C, xD, None, TC, psD[0], psD[1], x2D, rowsD,
                gcols=lncols[:, 4 * ND:5 * ND], bcols=lncols[:, 5 * ND:6 * ND],
                outs=lambda c: xnD[:, c, 1:TC + 1])
        for c in range(ND):
            p.on("vector", lambda e, c=c: e.tensor_sub(
                tmixD[:, c, :], xnD[:, c, 1:TC + 1], xnD[:, c, 0:TC]))
        v_mx = 0
        for mq, dst in ((cmixk, xk2), (cmixr, xr2)):
            for c in range(ND):
                v_mx = p.on("vector", lambda e, c=c, mq=mq, dst=dst:
                            e.scalar_tensor_tensor(
                                dst[:, c, :], tmixD[:, c, :], mq[:, c:c + 1],
                                xnD[:, c, 0:TC], ALU.mult, ALU.add), inc=1)
        v_pr = p.on("vector", lambda e: e.tensor_copy(
            xnD[:, :, 0:1], xnD[:, :, TC:TC + 1]), inc=1)
        p.wait("tensor", "vector", v_pr)
        for g in range(NG):
            p.wait("gpsimd", "tensor", p.wm("tensor"))
            g_w = p.dma("gpsimd", wslD[0][:, :, :],
                        fm_dram(io["a1_cmwk"], ND)[:, :, g * PC:(g + 1) * PC])
            p.wait("tensor", "gpsimd", g_w)
            pt = psD[2 + (g % 2)]
            t_k2 = 0
            for c in range(ND):
                t_k2 = p.on("tensor", lambda e, pt=pt, c=c: e.matmul(
                    pt[:, :], wslD[0][:, c, :], xk2[:, c, :],
                    start=(c == 0), stop=(c == ND - 1)), inc=(1 if c == ND - 1 else 0))
            p.wait("scalar", "tensor", t_k2)
            s_sl = p.on("scalar", lambda e, pt=pt, g=g: e.activation(
                silb[:, g, :], pt[:, :], AF.Silu), inc=1)
            p.wait("tensor", "scalar", s_sl)
        for oc in range(ND):
            t_r2 = 0
            pt = psD[2 + (oc % 2)]
            for c in range(ND):
                t_r2 = p.on("tensor", lambda e, pt=pt, oc=oc, c=c: e.matmul(
                    pt[:, :], cmwrb[:, c, oc * PC:(oc + 1) * PC], xr2[:, c, :],
                    start=(c == 0), stop=(c == ND - 1)), inc=(1 if c == ND - 1 else 0))
            p.wait("scalar", "tensor", t_r2)
            s_g2 = p.on("scalar", lambda e, pt=pt, oc=oc: e.activation(
                sgD[:, oc, :], pt[:, :], AF.Sigmoid), inc=1)
            p.wait("tensor", "scalar", s_g2)
        for g in range(NG):
            p.wait("gpsimd", "tensor", p.wm("tensor"))
            g_wv = p.dma("gpsimd", wslD[0][:, :, :],
                         io["a1_cmwv"][g * PC:(g + 1) * PC, :].rearrange(
                             "pp (cc o) -> pp cc o", cc=ND))
            p.wait("tensor", "gpsimd", g_wv)
            t_v2 = 0
            for oc in range(ND):
                t_v2 = p.on("tensor", lambda e, g=g, oc=oc: e.matmul(
                    psD[oc][:, :], wslD[0][:, oc, :], silb[:, g, :],
                    start=(g == 0), stop=(g == NG - 1)), inc=(1 if oc == ND - 1 else 0))
        p.wait("sync", "vector", p.wm("vector"))
        xin_wm = p.dma("sync", xinD[:, :, :], fm_dram(io["x_in"], ND)[:, :, c0:c0 + TC])
        p.wait("vector", "sync", xin_wm)
        p.wait("vector", "tensor", t_v2)
        v_pr = 0
        for oc in range(ND):
            p.on("vector", lambda e, oc=oc: e.tensor_add(
                xD[:, oc, :], xD[:, oc, :], sgD[:, oc, :]))
            p.on("vector", lambda e, oc=oc: e.tensor_add(
                xD[:, oc, :], xD[:, oc, :], psD[oc][:, :]))
            v_pr = p.on("vector", lambda e, oc=oc: e.tensor_add(
                xD[:, oc, :], xD[:, oc, :], xinD[:, oc, :]), inc=1)
        p.wait("gpsimd", "vector", v_pr)
        g_st = p.dma("gpsimd", fm_dram(io["x_out"], ND)[:, :, c0:c0 + TC], xD[:, :, :])
        p.wait("tensor", "vector", v_pr)
    p.drain_dma("gpsimd")
    p.barrier()
    esD.close()


def declare_l0attna(nc, io):
    io["a1_wk"] = nc.dram_tensor("a1_wk", [D, 4096], F32, kind="ExternalInput").ap()
    io["a1_wv"] = nc.dram_tensor("a1_wv", [D, 4096], F32, kind="ExternalInput").ap()
    io["a1_wr"] = nc.dram_tensor("a1_wr", [D, 4096], F32, kind="ExternalInput").ap()
    io["a1_wo"] = nc.dram_tensor("a1_wo", [4096, D], F32, kind="ExternalInput").ap()
    io["a1_cmwk"] = nc.dram_tensor("a1_cmwk", [D, 4096], F32, kind="ExternalInput").ap()
    io["a1_cmwv"] = nc.dram_tensor("a1_cmwv", [4096, D], F32, kind="ExternalInput").ap()
    io["a1_cmwr"] = nc.dram_tensor("a1_cmwr", [D, D], F32, kind="ExternalInput").ap()
    for nm in ("mixk", "mixv", "mixr", "cmixk", "cmixr", "decay", "first"):
        n = {"decay": 32, "first": 32}.get(nm, ND)
        io["a1_" + nm] = nc.dram_tensor("a1_" + nm, [PC, n], F32, kind="ExternalInput").ap()
    io["a1_lncols"] = nc.dram_tensor("a1_lncols", [PC, 6 * ND], F32, kind="ExternalInput").ap()


def prep_l0attna(inputs):
    d = {}
    d["a1_wk"] = np.ascontiguousarray(inputs["L0_attn_a_tm_Wk"])
    d["a1_wv"] = np.ascontiguousarray(inputs["L0_attn_a_tm_Wv"])
    d["a1_wr"] = np.ascontiguousarray(inputs["L0_attn_a_tm_Wr"])
    d["a1_wo"] = np.ascontiguousarray(inputs["L0_attn_a_tm_Wo"])
    d["a1_cmwk"] = np.ascontiguousarray(inputs["L0_attn_a_cm_Wk"])
    d["a1_cmwv"] = np.ascontiguousarray(inputs["L0_attn_a_cm_Wv"])
    d["a1_cmwr"] = np.ascontiguousarray(inputs["L0_attn_a_cm_Wr"])
    col = lambda a, n: np.ascontiguousarray(np.asarray(a, np.float32).reshape(n, PC).T)
    d["a1_mixk"] = col(inputs["L0_attn_a_tm_mix_k"], ND)
    d["a1_mixv"] = col(inputs["L0_attn_a_tm_mix_v"], ND)
    d["a1_mixr"] = col(inputs["L0_attn_a_tm_mix_r"], ND)
    d["a1_cmixk"] = col(inputs["L0_attn_a_cm_mix_k"], ND)
    d["a1_cmixr"] = col(inputs["L0_attn_a_cm_mix_r"], ND)
    d["a1_decay"] = col(inputs["L0_attn_a_tm_decay"], 32)
    d["a1_first"] = col(inputs["L0_attn_a_tm_first"], 32)
    d["a1_lncols"] = np.ascontiguousarray(np.concatenate(
        [col(inputs["L0_attn_a_" + nm], ND) for nm in
         ("ln0_g", "ln0_b", "pre_g", "pre_b", "post_g", "post_b")], axis=1))
    return d



def emit_l0attnb(p, es, io, C, tag):
    nc = p.nc
    TC = 512
    NT = T // TC
    CH = 128
    NC_ = T // CH
    NXC = 16
    z_dram = nc.dram_tensor(tag + "zd", [PC, NXC, T], BF16, kind="Internal").ap()
    xs_dram = nc.dram_tensor(tag + "xsd", [PC, NXC, T], BF16, kind="Internal").ap()
    rt_dram = nc.dram_tensor(tag + "rtd", [PC, 1], F32, kind="Internal").ap()
    dd_dram = nc.dram_tensor(tag + "ddd", [1, 64 * CH], F32, kind="Internal").ap()

    convw = sb(es, nc, tag + "sconvw", [PC, 18, 4], F32)
    convb = sb(es, nc, tag + "sconvb", [PC, 18], F32)
    acol = sb(es, nc, tag + "acol", [PC, 1], F32)
    dtb = sb(es, nc, tag + "sdtb", [PC, 1], F32)
    dcolD = sb(es, nc, tag + "sdcolD", [PC, 16], F32)
    nwcol = sb(es, nc, tag + "snwcol", [PC, 16], F32)
    mask64 = sb(es, nc, tag + "smask64", [64, CH], F32)
    p2c = sb(es, nc, tag + "sp2c", [64, 16, PC], BF16)
    dmask = sb(es, nc, tag + "sdmask", [PC, 64 * CH], BF16)
    ctoks = [p.dma("sync", convw[:, :, :], io["b3_convw"].rearrange("c p j -> p c j")),
             p.dma("sync", convb[:, :], io["b3_convb"]),
             p.dma("sync", acol[:, :], io["b3_alog"]),
             p.dma("sync", dtb[:, :], io["b3_dtb"]),
             p.dma("sync", dcolD[:, :], io["b3_dcol"]),
             p.dma("sync", nwcol[:, :], io["b3_nw"]),
             p.dma("sync", mask64[:, :], io["b3_mask64"]),
             p.dma("gpsimd", p2c[:, :, :], io["b3_p2"].rearrange("j h p -> h j p")),
             p.dma("gpsimd", dmask[:, :], io["b3_dmask"])]
    for tok in ctoks:
        for e2 in ("scalar", "vector", "tensor", "gpsimd"):
            p.wait(e2, "sync", tok)
    p.on("scalar", lambda e: e.activation(acol[:, :], acol[:, :], AF.Exp))
    s_a = p.on("scalar", lambda e: e.activation(
        acol[:, :], acol[:, :], AF.Identity, scale=-1.0), inc=1)
    p.wait("gpsimd", "scalar", s_a)
    g_a = p.dma("gpsimd", rt_dram, acol[:, :])
    p.wait("sync", "gpsimd", g_a)
    s_a2 = p.dma("sync", acol[:, :], rt_dram)
    for e2 in ("scalar", "vector", "tensor"):
        p.wait(e2, "sync", s_a2)

    Bf = sb(es, nc, tag + "Bf", [16, T], BF16)
    Cf = sb(es, nc, tag + "Cf", [16, T], BF16)
    dtf = sb(es, nc, tag + "dtf", [64, T], BF16)
    abf = sb(es, nc, tag + "abf", [64, T], F32)

    esW = ExitStack()
    x_sb = fm(esW, nc, tag + "xW", ND, TC, F32)
    x2 = fm(esW, nc, tag + "x2W", ND, TC, F32)
    rows = sb(esW, nc, tag + "rowsW", [1, 2 * TC], F32)
    xnb = fm(esW, nc, tag + "xnbW", ND, TC, BF16)
    wsl = [fm(esW, nc, tag + f"wslW{i}", ND, PC, BF16) for i in range(2)]
    xbc = esW.enter_context(nc.sbuf_tensor(tag + "xbc", [PC, 18, TC + 3], F32))
    zb = fm(esW, nc, tag + "zbW", NXC, TC, BF16)
    xsb = fm(esW, nc, tag + "xsbW", NXC, TC, BF16)
    dtr = sb(esW, nc, tag + "dtr", [64, TC], F32)
    dtt = sb(esW, nc, tag + "dtt", [64, TC], F32)
    psW = [ps(esW, nc, tag + f"psW{i}", [PC, 512]) for i in range(3)]
    v_zw = p.on("vector", lambda e: e.memset(xbc[:, :, 0:3], 0.0), inc=1)

    g_z = 0
    g_xs = 0
    for ci in range(NT):
        c0 = ci * TC
        p.wait("sync", "vector", p.wm("vector"))
        x_wm = p.dma("sync", x_sb[:, :, :], fm_dram(io["x_in"], ND)[:, :, c0:c0 + TC])
        for e2 in ("scalar", "tensor", "vector"):
            p.wait(e2, "sync", x_wm)
        emit_ln(p, C, x_sb, x2, TC, psW[0], psW[1], x2, rows, out_bf=xnb)
        p.wait("tensor", "vector", p.wm("vector"))
        for oc in range(35):
            pt = psW[oc % 3]
            p.wait("gpsimd", "tensor", p.wm("tensor"))
            g_w = p.dma("gpsimd", wsl[oc % 2][:, :, :],
                        fm_dram(io["b3_winT"], ND)[:, :, oc * PC:(oc + 1) * PC])
            p.wait("tensor", "gpsimd", g_w)
            t_p = 0
            for c in range(ND):
                t_p = p.on("tensor", lambda e, pt=pt, oc=oc, c=c: e.matmul(
                    pt[:, :], wsl[oc % 2][:, c, :], xnb[:, c, :],
                    start=(c == 0), stop=(c == ND - 1)), inc=(1 if c == ND - 1 else 0))
            p.wait("vector", "tensor", t_p)
            if oc < 16:
                if oc == 0:
                    p.wait("vector", "gpsimd", g_z)
                v_e = p.on("vector", lambda e, pt=pt, oc=oc: e.tensor_copy(
                    zb[:, oc, :], pt[:, :]), inc=1)
            elif oc < 34:
                v_e = p.on("vector", lambda e, pt=pt, oc=oc: e.tensor_copy(
                    xbc[:, oc - 16, 3:TC + 3], pt[:, :]), inc=1)
            else:
                v_e = p.on("vector", lambda e, pt=pt: e.tensor_copy(
                    dtr[:, :], pt[0:64, :]), inc=1)
            p.wait("tensor", "vector", v_e)
        v_wm = p.wm("vector")
        p.wait("gpsimd", "vector", v_wm)
        g_z = p.dma("gpsimd", z_dram[:, :, c0:c0 + TC], zb[:, :, :])
        s_cv = 0
        for cc in range(18):
            p.on("vector", lambda e, cc=cc: e.tensor_scalar(
                x2[:, 0, 0:TC], xbc[:, cc, 0:TC], convw[:, cc, 0:1], None,
                ALU.mult))
            for j in range(1, 4):
                p.on("vector", lambda e, cc=cc, j=j: e.scalar_tensor_tensor(
                    x2[:, 0, 0:TC], xbc[:, cc, j:j + TC], convw[:, cc, j:j + 1],
                    x2[:, 0, 0:TC], ALU.mult, ALU.add))
            p.on("vector", lambda e, cc=cc: e.tensor_copy(
                xbc[:, cc, 0:3], xbc[:, cc, TC:TC + 3]))
            v_cv = p.on("vector", lambda e: e.tensor_copy(
                x2[:, 1, 0:TC], x2[:, 0, 0:TC]), inc=1)
            p.wait("scalar", "vector", v_cv)
            s_cv = p.on("scalar", lambda e, cc=cc: e.activation(
                x2[:, 2, 0:TC], x2[:, 1, 0:TC], AF.Silu,
                bias=convb[:, cc:cc + 1]), inc=1)
            p.wait("vector", "scalar", s_cv)
            if cc < 16:
                v_rt = p.on("vector", lambda e, cc=cc: e.tensor_copy(
                    xsb[:, cc, :], x2[:, 2, 0:TC]), inc=1)
            elif cc == 16:
                v_rt = p.on("vector", lambda e, c0=c0: e.tensor_copy(
                    Bf[:, c0:c0 + TC], x2[0:16, 2, 0:TC]), inc=1)
            else:
                v_rt = p.on("vector", lambda e, c0=c0: e.tensor_copy(
                    Cf[:, c0:c0 + TC], x2[0:16, 2, 0:TC]), inc=1)
            p.wait("scalar", "vector", v_rt)
        v_wm2 = p.wm("vector")
        p.wait("gpsimd", "vector", v_wm2)
        g_xs = p.dma("gpsimd", xs_dram[:, :, c0:c0 + TC], xsb[:, :, :])
        p.wait("vector", "gpsimd", g_xs)
        p.on("vector", lambda e: e.tensor_scalar(
            dtr[:, :], dtr[:, :], dtb[0:64, :], None, ALU.add))
        p.on("vector", lambda e: e.scalar_tensor_tensor(
            dtt[:, :], dtr[:, :], -1.0, dtr[:, :], ALU.mult, ALU.max))
        v_dt = p.on("vector", lambda e: e.tensor_scalar(
            dtr[:, :], dtr[:, :], 0.0, None, ALU.max), inc=1)
        p.wait("scalar", "vector", v_dt)
        p.on("scalar", lambda e: e.activation(dtt[:, :], dtt[:, :], AF.Exp, scale=-1.0))
        s_sp = p.on("scalar", lambda e: e.activation(
            dtt[:, :], dtt[:, :], AF.Ln, bias=1.0), inc=1)
        p.wait("vector", "scalar", s_sp)
        v_dt2 = p.on("vector", lambda e: e.tensor_add(
            dtr[:, :], dtr[:, :], dtt[:, :]), inc=1)
        v_dt2 = p.on("vector", lambda e, c0=c0: e.tensor_copy(
            dtf[:, c0:c0 + TC], dtr[:, :]), inc=1)
        v_dt3 = p.on("vector", lambda e, c0=c0: e.tensor_scalar(
            abf[:, c0:c0 + TC], dtr[:, :], acol[0:64, :], None,
            ALU.mult), inc=1)
    p.barrier()
    esW.close()

    esS = ExitStack()
    acum = sb(esS, nc, tag + "acum", [64, CH], F32)
    d0row = sb(esS, nc, tag + "d0row", [64, CH], F32)
    d0rm = sb(esS, nc, tag + "d0rm", [64, CH], F32)
    d0b = sb(esS, nc, tag + "d0b", [PC, 64 * CH], F32)
    Gb = sb(esS, nc, tag + "Gb", [PC, 64 * CH], BF16)
    CcE = esS.enter_context(nc.sbuf_tensor(tag + "CcE", [PC, 64 * CH], BF16))
    corow = sb(esS, nc, tag + "corow", [1, 64], F32)
    corow16 = sb(esS, nc, tag + "corow16", [16, 64], F32)
    dscol = sb(esS, nc, tag + "dscol", [PC, 64], F32)
    CBb = sb(esS, nc, tag + "CBb", [PC, CH], BF16)
    xsc = esS.enter_context(nc.sbuf_tensor(tag + "xsc", [PC, NXC, CH], BF16))
    xdt = esS.enter_context(nc.sbuf_tensor(tag + "xdt", [PC, NXC, CH], BF16))
    xdtT = sb(esS, nc, tag + "xdtT", [PC, 2048], BF16)
    xdtd = sb(esS, nc, tag + "xdtd", [PC, 2048], BF16)
    BcT = sb(esS, nc, tag + "BcT", [PC, 16], BF16)
    stateT = sb(esS, nc, tag + "stateT", [16, 2048], F32)
    prevT = sb(esS, nc, tag + "prevT", [PC, 2048], BF16)
    cd16 = sb(esS, nc, tag + "cd16", [16, 64], F32)
    Ysb = esS.enter_context(nc.sbuf_tensor(tag + "Ysb", [PC, NXC, CH], F32))
    ygr = sb(esS, nc, tag + "ygr", [1, CH], F32)
    xo_sb = fm(esS, nc, tag + "xoS", ND, CH, F32)
    woutb = fm(esS, nc, tag + "woutb", NXC, D, BF16)
    psS = [ps(esS, nc, tag + f"psS{i}", [PC, 512]) for i in range(4)]
    psQ = [ps(esS, nc, tag + f"psQ{i}", [PC, 512]) for i in range(4)]
    psT = psQ[3][:, 0:64].bitcast(BF16)
    g_wo = p.dma("gpsimd", woutb[:, :, :], fm_dram(io["b3_woutT"], NXC))
    p.wait("tensor", "gpsimd", g_wo)
    v_zs = p.on("vector", lambda e: e.memset(stateT[:, :], 0.0))
    v_zs = p.on("vector", lambda e: e.memset(prevT[:, :], 0.0))
    v_zs = p.on("vector", lambda e: e.memset(CcE[:, :], 0.0), inc=1)
    p.wait("vector", "vector", v_zs)

    g_st = 0
    for cn in range(NC_):
        l0 = cn * CH
        p.wait("vector", "tensor", p.wm("tensor"))
        v_ac = p.on("vector", lambda e, l0=l0: e.tensor_tensor_scan(
            acum[:, :], C["cval"][1.0][0:64, :].broadcast_to([64, CH]),
            abf[:, l0:l0 + CH], 0.0, ALU.mult, ALU.add), inc=1)
        p.wait("scalar", "vector", v_ac)
        s_d0 = p.on("scalar", lambda e, l0=l0: e.activation(
            d0row[:, :], abf[:, l0:l0 + CH], AF.Exp), inc=1)
        p.wait("vector", "scalar", s_d0)
        v_d0 = p.on("vector", lambda e: e.tensor_mul(
            d0rm[:, :], d0row[:, :], mask64[:, :]), inc=1)
        p.wait("tensor", "vector", v_d0)
        p.wait("gpsimd", "vector", v_d0)
        g_dd = p.dma("gpsimd", dd_dram[0:1, :].rearrange("x (h l) -> (x h) l", h=64),
                     d0rm[:, :])
        p.wait("sync", "gpsimd", g_dd)
        s_dd = p.dma("sync", d0b[0:1, :], dd_dram[0:1, :])
        p.wait("tensor", "sync", s_dd)
        s_bc = 0
        for hh in range(0, 64, 4):
            pt = psS[(hh // 4) % 2]
            t_b = p.on("tensor", lambda e, pt=pt, hh=hh: e.matmul(
                pt[:, :], C["ones1"], d0b[0:1, hh * CH:(hh + 4) * CH],
                start=True, stop=True), inc=1)
            p.wait("scalar", "tensor", t_b)
            s_bc = p.on("scalar", lambda e, pt=pt, hh=hh: e.activation(
                d0b[:, hh * CH:(hh + 4) * CH], pt[:, :], AF.Identity), inc=1)
            p.wait("tensor", "scalar", s_bc)
        p.wait("vector", "scalar", s_bc)
        v_g = p.on("vector", lambda e: e.tensor_tensor_scan(
            Gb[:, :], d0b[:, :], dmask[:, :], 0.0, ALU.mult, ALU.add), inc=1)
        v_ds = p.on("vector", lambda e: e.tensor_copy(
            dscol[:, :].rearrange("s (h x) -> s h x", x=1),
            Gb[:, :].rearrange("s (h l) -> s h l", h=64)[:, :, CH - 1:CH]), inc=1)
        p.wait("gpsimd", "vector", v_d0)
        g_cr = p.dma("gpsimd", rt_dram[0:64, 0:1], d0row[:, 0:1])
        p.wait("sync", "gpsimd", g_cr)
        s_cr = p.dma("sync", corow16[:, :],
                     rt_dram[0:64, 0:1].rearrange("h x -> x h").broadcast_to([16, 64]))
        p.wait("vector", "sync", s_cr)
        p.wait("tensor", "vector", v_g)
        v_eab = 0
        for pc_ in range(16):
            pt = psS[3]
            t_e = p.on("tensor", lambda e, pc_=pc_: e.matmul(
                pt[0:16, :], C["ones1b"][0:1, 0:16],
                Gb[0:1, pc_ * 512:(pc_ + 1) * 512], start=True, stop=True), inc=1)
            p.wait("vector", "tensor", t_e)
            v_eab = p.on("vector", lambda e, pc_=pc_: e.tensor_copy(
                CcE[0:16, pc_ * 512:(pc_ + 1) * 512], pt[0:16, :]), inc=1)
            p.wait("tensor", "vector", v_eab)
        v_cce = p.on("vector", lambda e: e.tensor_mul(
            CcE[0:16, :].rearrange("n (h l) -> n h l", h=64),
            CcE[0:16, :].rearrange("n (h l) -> n h l", h=64),
            corow16[:, :].rearrange("n (h x) -> n h x", x=1)
            .broadcast_to([16, 64, CH])), inc=1)
        v_cce = p.on("vector", lambda e, l0=l0: e.tensor_mul(
            CcE[0:16, :].rearrange("n (h l) -> n h l", h=64),
            CcE[0:16, :].rearrange("n (h l) -> n h l", h=64),
            Cf[:, l0:l0 + CH].rearrange("n (x l) -> n x l", x=1)
            .broadcast_to([16, 64, CH])), inc=1)
        t_cb = p.on("tensor", lambda e, l0=l0: e.matmul(
            psS[2][:, 0:CH], Bf[:, l0:l0 + CH], Cf[:, l0:l0 + CH],
            start=True, stop=True), inc=1)
        p.wait("vector", "tensor", t_cb)
        v_cb = p.on("vector", lambda e: e.tensor_copy(CBb[:, :], psS[2][:, 0:CH]), inc=1)
        v_m = p.on("vector", lambda e: e.tensor_mul(
            Gb[:, :].rearrange("p (h l) -> p h l", h=64),
            Gb[:, :].rearrange("p (h l) -> p h l", h=64),
            CBb[:, :].rearrange("p (a l) -> p a l", a=1).broadcast_to([PC, 64, CH])),
            inc=1)
        p.wait("sync", "vector", p.wm("vector"))
        xs_wm = p.dma("sync", xsc[:, :, :], xs_dram[:, :, l0:l0 + CH])
        for e2 in ("tensor", "vector", "scalar"):
            p.wait(e2, "sync", xs_wm)
        for jj in range(0, 16, 4):
            pt = psS[2]
            t_d = 0
            for j in range(4):
                t_d = p.on("tensor", lambda e, pt=pt, jj=jj, j=j, l0=l0: e.matmul(
                    pt[:, j * CH:(j + 1) * CH], p2c[:, jj + j, :],
                    dtf[:, l0:l0 + CH], start=True, stop=True),
                    inc=(1 if j == 3 else 0))
            p.wait("vector", "tensor", t_d)
            v_x = 0
            for j in range(4):
                v_x = p.on("vector", lambda e, pt=pt, jj=jj, j=j: e.tensor_mul(
                    xdt[:, jj + j, :], xsc[:, jj + j, :], pt[:, j * CH:(j + 1) * CH]),
                    inc=1)
            p.wait("tensor", "vector", v_x)
        for j in range(NXC):
            t_t = p.on("tensor", lambda e, j=j: e.transpose(
                psT[:, :], xdt[:, j, :], C["identbf"]), inc=1)
            p.wait("vector", "tensor", t_t)
            v_t = p.on("vector", lambda e, j=j: e.tensor_copy(
                xdtT[:, j * CH:(j + 1) * CH], psT[:, :]), inc=1)
            p.wait("tensor", "vector", v_t)
        t_bt = p.on("tensor", lambda e, l0=l0: e.transpose(
            psT[:, 0:16], Bf[:, l0:l0 + CH], C["identbf"][0:16, 0:16]), inc=1)
        p.wait("vector", "tensor", t_bt)
        v_bt = p.on("vector", lambda e: e.tensor_copy(BcT[:, :], psT[:, 0:16]), inc=1)
        v_xd = p.on("vector", lambda e: e.tensor_mul(
            xdtd[:, :].rearrange("s (h pp) -> s h pp", h=64),
            xdtT[:, :].rearrange("s (h pp) -> s h pp", h=64),
            dscol[:, :].rearrange("s (h x) -> s h x", x=1)
            .broadcast_to([PC, 64, 32])), inc=1)
        p.wait("tensor", "vector", max(v_bt, v_xd))
        t_st = 0
        for ff in range(4):
            t_st = p.on("tensor", lambda e, ff=ff: e.matmul(
                psS[ff][0:16, :], BcT[:, :], xdtd[:, ff * 512:(ff + 1) * 512],
                start=True, stop=True), inc=1)
        p.wait("scalar", "vector", v_ac)
        s_cd = p.on("scalar", lambda e: e.activation(
            d0rm[:, 0:1], acum[:, CH - 1:CH], AF.Exp), inc=1)
        p.wait("gpsimd", "scalar", s_cd)
        g_cd = p.dma("gpsimd", rt_dram[0:64, 0:1], d0rm[:, 0:1])
        p.wait("sync", "gpsimd", g_cd)
        s_cd2 = p.dma("sync", corow[:, :], rt_dram[0:64, 0:1].rearrange("h 1 -> 1 h"))
        p.wait("tensor", "sync", s_cd2)
        t_cd = p.on("tensor", lambda e: e.matmul(
            psT[0:16, 0:64].bitcast(BF16) if False else psS[0][0:16, 0:64],
            C["ones1"][0:1, 0:16], corow[:, :], start=True, stop=True), inc=1) \
            if False else p.on("tensor", lambda e: e.matmul(
                psQ[0][0:16, 0:64], C["ones1"][0:1, 0:16], corow[:, :],
                start=True, stop=True), inc=1)
        p.wait("vector", "tensor", max(t_st, t_cd))
        v_cd = p.on("vector", lambda e: e.tensor_copy(cd16[:, :], psQ[0][0:16, 0:64]), inc=1)
        v_pv = p.on("vector", lambda e: e.tensor_copy(
            prevT[0:16, :], stateT[:, :]), inc=1)
        v_st = p.on("vector", lambda e: e.tensor_mul(
            stateT[:, :].rearrange("n (h pp) -> n h pp", h=64),
            stateT[:, :].rearrange("n (h pp) -> n h pp", h=64),
            cd16[:, :].rearrange("n (h x) -> n h x", x=1)
            .broadcast_to([16, 64, 32])))
        for ff in range(4):
            v_st = p.on("vector", lambda e, ff=ff: e.tensor_add(
                stateT[:, ff * 512:(ff + 1) * 512],
                stateT[:, ff * 512:(ff + 1) * 512], psS[ff][0:16, :]), inc=1)
        p.wait("tensor", "vector", max(v_m, v_cce, v_pv, v_st))
        ybank = [psQ[0], psQ[1], psQ[2], psQ[3], psS[0], psS[1]]

        def yslot(h):
            bk, sl = ybank[h // 12], h % 12
            return bk, (sl % 3) * 32, (sl // 3) * 128
        t_y = 0
        for h in range(64):
            pq, prow, qoff = yslot(h)
            p.on("tensor", lambda e, h=h, pq=pq, qoff=qoff, prow=prow: e.matmul(
                pq[prow:prow + 32, qoff:qoff + CH],
                xdtT[:, h * 32:(h + 1) * 32], Gb[:, h * CH:(h + 1) * CH],
                start=True, stop=False))
            t_y = p.on("tensor", lambda e, h=h, pq=pq, qoff=qoff, prow=prow: e.matmul(
                pq[prow:prow + 32, qoff:qoff + CH],
                prevT[:, h * 32:(h + 1) * 32], CcE[:, h * CH:(h + 1) * CH],
                start=False, stop=True), inc=1)
        p.wait("vector", "tensor", t_y)
        v_y = 0
        for h in range(64):
            pq, prow, qoff = yslot(h)
            r = (h % 4) * 32
            j = h // 4
            v_y = p.on("vector",
                       lambda e, pq=pq, prow=prow, qoff=qoff, r=r, j=j:
                       e.scalar_tensor_tensor(
                           Ysb[r:r + 32, j, :], xsc[r:r + 32, j, :],
                           dcolD[r:r + 32, j:j + 1],
                           pq[prow:prow + 32, qoff:qoff + CH],
                           ALU.mult, ALU.add), inc=1)
        p.wait("tensor", "vector", v_y)
        p.wait("sync", "tensor", t_y)
        z_wm = p.dma("sync", xdtT[:, :].rearrange("p (a b) -> p a b", a=NXC),
                     z_dram[:, :, l0:l0 + CH])
        p.wait("scalar", "sync", z_wm)
        p.wait("scalar", "vector", v_y)
        p.wait("scalar", "tensor", p.wm("tensor"))
        s_z = p.on("scalar", lambda e: e.activation(
            d0b[:, 0:NXC * CH].rearrange("p (a b) -> p a b", a=NXC),
            xdtT[:, :].rearrange("p (a b) -> p a b", a=NXC), AF.Silu), inc=1)
        p.wait("vector", "scalar", s_z)
        v_yg = 0
        for j in range(NXC):
            v_yg = p.on("vector", lambda e, j=j: e.tensor_mul(
                Ysb[:, j, :], Ysb[:, j, :],
                d0b[:, j * CH:(j + 1) * CH]), inc=1)
        p.wait("scalar", "vector", v_yg)
        s_q = p.on("scalar", lambda e: e.activation(
            d0b[:, 0:NXC * CH].rearrange("p (a b) -> p a b", a=NXC),
            Ysb[:, :, :], AF.Square), inc=1)
        p.wait("tensor", "scalar", s_q)
        t_ms = 0
        for j in range(NXC):
            t_ms = p.on("tensor", lambda e, j=j: e.matmul(
                psS[0][0:1, 0:CH], C["invn"],
                d0b[:, j * CH:(j + 1) * CH],
                start=(j == 0), stop=(j == NXC - 1)), inc=(1 if j == NXC - 1 else 0))
        p.wait("vector", "tensor", t_ms)
        v_ms = p.on("vector", lambda e: e.tensor_scalar(
            ygr[:, :], psS[0][0:1, 0:CH], 0.5, None, ALU.mult), inc=1)
        p.wait("scalar", "vector", v_ms)
        p.on("scalar", lambda e: e.activation(ygr[:, :], ygr[:, :], AF.Ln, bias=1e-5))
        s_rs = p.on("scalar", lambda e: e.activation(
            ygr[:, :], ygr[:, :], AF.Exp, scale=-0.5), inc=1)
        p.wait("tensor", "scalar", s_rs)
        t_bcr = p.on("tensor", lambda e: e.matmul(
            psS[1][:, 0:CH], C["ones1"], ygr[:, :], start=True, stop=True), inc=1)
        p.wait("vector", "tensor", t_bcr)
        v_yn = 0
        for j in range(NXC):
            p.on("vector", lambda e, j=j: e.tensor_mul(
                Ysb[:, j, :], Ysb[:, j, :], psS[1][:, 0:CH]))
            v_yn = p.on("vector", lambda e, j=j: e.tensor_scalar(
                xdt[:, j, :], Ysb[:, j, :], nwcol[:, j:j + 1], None, ALU.mult), inc=1)
        p.wait("tensor", "vector", v_yn)
        t_wo = 0
        for oc in range(ND):
            pw = psS[2 + oc % 2][:, (oc // 2 % 2) * CH:(oc // 2 % 2) * CH + CH] \
                if False else psS[2][:, (oc % 4) * CH:(oc % 4) * CH + CH]
            pw = (psS[2], psS[3])[oc // 4][:, (oc % 4) * CH:(oc % 4) * CH + CH]
            for j in range(NXC):
                t_wo = p.on("tensor", lambda e, pw=pw, oc=oc, j=j: e.matmul(
                    pw, woutb[:, j, oc * PC:(oc + 1) * PC], xdt[:, j, :],
                    start=(j == 0), stop=(j == NXC - 1)), inc=(1 if j == NXC - 1 else 0))
        p.wait("sync", "vector", p.wm("vector"))
        p.wait("sync", "gpsimd", g_st)
        xi_wm = p.dma("sync", xo_sb[:, :, :], fm_dram(io["x_in"], ND)[:, :, l0:l0 + CH])
        p.wait("vector", "sync", xi_wm)
        p.wait("vector", "tensor", t_wo)
        v_f = 0
        for oc in range(ND):
            pw = (psS[2], psS[3])[oc // 4][:, (oc % 4) * CH:(oc % 4) * CH + CH]
            v_f = p.on("vector", lambda e, pw=pw, oc=oc: e.tensor_add(
                xo_sb[:, oc, :], pw, xo_sb[:, oc, :]), inc=1)
        p.wait("gpsimd", "vector", v_f)
        g_st = p.dma("gpsimd", fm_dram(io["x_out"], ND)[:, :, l0:l0 + CH], xo_sb[:, :, :])
        p.wait("tensor", "vector", v_f)
    p.drain_dma("gpsimd")
    p.barrier()
    esS.close()


def declare_l0attnb(nc, io):
    io["b3_winT"] = nc.dram_tensor("b3_winT", [D, 4480], F32, kind="ExternalInput").ap()
    io["b3_woutT"] = nc.dram_tensor("b3_woutT", [2048, D], F32, kind="ExternalInput").ap()
    io["b3_convw"] = nc.dram_tensor("b3_convw", [18, PC, 4], F32, kind="ExternalInput").ap()
    io["b3_convb"] = nc.dram_tensor("b3_convb", [PC, 18], F32, kind="ExternalInput").ap()
    io["b3_alog"] = nc.dram_tensor("b3_alog", [PC, 1], F32, kind="ExternalInput").ap()
    io["b3_dtb"] = nc.dram_tensor("b3_dtb", [PC, 1], F32, kind="ExternalInput").ap()
    io["b3_dcol"] = nc.dram_tensor("b3_dcol", [PC, 16], F32, kind="ExternalInput").ap()
    io["b3_nw"] = nc.dram_tensor("b3_nw", [PC, 16], F32, kind="ExternalInput").ap()
    io["b3_mask64"] = nc.dram_tensor("b3_mask64", [64, 128], F32, kind="ExternalInput").ap()
    io["b3_p2"] = nc.dram_tensor("b3_p2", [16, 64, PC], F32, kind="ExternalInput").ap()
    io["b3_dmask"] = nc.dram_tensor("b3_dmask", [PC, 64 * 128], F32, kind="ExternalInput").ap()


def prep_l0attnb(inputs):
    d = {}
    W_in = inputs["L0_attn_b_W_in"]
    winp = np.zeros((4480, D), np.float32)
    winp[0:2048] = W_in[0:2048]
    winp[2048:4096] = W_in[2048:4096]
    winp[4096:4112] = W_in[4096:4112]
    winp[4224:4240] = W_in[4112:4128]
    winp[4352:4416] = W_in[4128:4192]
    d["b3_winT"] = np.ascontiguousarray(winp.T)
    d["b3_woutT"] = np.ascontiguousarray(inputs["L0_attn_b_W_out"].T)
    cw = inputs["L0_attn_b_conv_w"][:, 0, :]
    cwp = np.zeros((18 * PC, 4), np.float32)
    cwp[0:2048] = cw[0:2048]
    cwp[2048:2064] = cw[2048:2064]
    cwp[2176:2192] = cw[2064:2080]
    d["b3_convw"] = np.ascontiguousarray(cwp.reshape(18, PC, 4))
    cb = inputs["L0_attn_b_conv_b"]
    cbp = np.zeros((18 * PC,), np.float32)
    cbp[0:2048] = cb[0:2048]
    cbp[2048:2064] = cb[2048:2064]
    cbp[2176:2192] = cb[2064:2080]
    d["b3_convb"] = np.ascontiguousarray(cbp.reshape(18, PC).T)
    al = np.zeros((PC, 1), np.float32)
    al[0:64, 0] = inputs["L0_attn_b_A_log"]
    d["b3_alog"] = al
    db = np.zeros((PC, 1), np.float32)
    db[0:64, 0] = inputs["L0_attn_b_dt_bias"]
    d["b3_dtb"] = db
    d["b3_dcol"] = np.ascontiguousarray(
        np.repeat(inputs["L0_attn_b_D"], 32).reshape(16, PC).T)
    d["b3_nw"] = np.ascontiguousarray(inputs["L0_attn_b_norm_w"].reshape(16, PC).T)
    m64 = np.ones((64, 128), np.float32)
    m64[:, 0] = 0.0
    d["b3_mask64"] = m64
    p2 = np.zeros((16, 64, PC), np.float32)
    for j in range(16):
        for pp in range(PC):
            p2[j, (j * PC + pp) // 32, pp] = 1.0
    d["b3_p2"] = p2
    dm = np.zeros((PC, 64 * 128), np.float32)
    for h in range(64):
        for s in range(128):
            dm[s, h * 128 + s] = 1.0
    d["b3_dmask"] = dm
    return d



def emit_l1attna(p, es, io, C, tag):
    nc = p.nc
    TC = 512
    NT = T // TC
    NH = 16
    q_dram = nc.dram_tensor(tag + "qd", [PC, 24, T], BF16, kind="Internal").ap()
    ao_dram = nc.dram_tensor(tag + "aod", [PC, 8, T], BF16, kind="Internal").ap()

    esA = ExitStack()
    wqkv = fm(esA, nc, tag + "s_wqkv", ND, 3072, BF16)
    bq = sb(esA, nc, tag + "s_bq", [PC, 24], F32)
    cosq = sb(esA, nc, tag + "s_cosq", [PC, T], F32)
    sinq = sb(esA, nc, tag + "s_sinq", [PC, T], F32)
    cosk = sb(esA, nc, tag + "s_cosk", [PC, T], F32)
    sink = sb(esA, nc, tag + "s_sink", [PC, T], F32)
    g_w = p.dma("gpsimd", wqkv[:, :, :], fm_dram(io["a5_wqkvT"], ND))
    toks = [p.dma("sync", bq[:, :], io["a5_bqkv"]),
            p.dma("sync", cosq[:, :], io["a5_cosq"]),
            p.dma("sync", sinq[:, :], io["a5_sinq"]),
            p.dma("sync", cosk[:, :], io["a5_cosk"]),
            p.dma("sync", sink[:, :], io["a5_sink"])]
    p.wait("tensor", "gpsimd", g_w)
    for tok in toks:
        p.wait("tensor", "sync", tok)
        p.wait("vector", "sync", tok)

    x_sb = fm(esA, nc, tag + "x", ND, TC, F32)
    x2 = fm(esA, nc, tag + "x2", ND, TC, F32)
    rows = sb(esA, nc, tag + "rows", [1, 2 * TC], F32)
    xnb = fm(esA, nc, tag + "xnb", ND, TC, BF16)
    ev = sb(esA, nc, tag + "ev", [PC, TC], F32)
    qs = sb(esA, nc, tag + "qs", [PC, TC], F32)
    qb = sb(esA, nc, tag + "qb", [PC, TC], BF16)
    psA_ = ps(esA, nc, tag + "psA", [PC, 512])
    psB_ = ps(esA, nc, tag + "psB", [PC, 512])
    pool = [ps(esA, nc, tag + f"pp{i}", [PC, 512]) for i in range(3)]

    v_ev = 0
    for ci in range(NT):
        c0 = ci * TC
        p.wait("sync", "vector", v_ev)
        x_wm = p.dma("sync", x_sb[:, :, :], fm_dram(io["x_in"], ND)[:, :, c0:c0 + TC])
        for e2 in ("scalar", "tensor", "vector"):
            p.wait(e2, "sync", x_wm)
        emit_ln(p, C, x_sb, x2, TC, psA_, psB_, x2, rows, out_bf=xnb)
        p.wait("tensor", "vector", p.wm("vector"))
        for oc in range(24):
            pt = pool[oc % 3]
            t_p = 0
            for c in range(ND):
                t_p = p.on("tensor", lambda e, pt=pt, oc=oc, c=c: e.matmul(
                    pt[:, :], wqkv[:, c, oc * PC:(oc + 1) * PC], xnb[:, c, :],
                    start=(c == 0), stop=(c == ND - 1)), inc=(1 if c == ND - 1 else 0))
            p.wait("vector", "tensor", t_p)
            p.on("vector", lambda e, pt=pt, oc=oc: e.tensor_scalar(
                ev[:, :], pt[:, :], bq[:, oc:oc + 1], None, ALU.add))
            if oc < 16:
                cos_t, sin_t = (cosq, sinq) if oc < 8 else (cosk, sink)
                p.on("vector", lambda e: e.tensor_scalar(
                    qs[0:32, :], ev[32:64, :], -1.0, None, ALU.mult))
                p.on("vector", lambda e: e.tensor_copy(qs[32:64, :], ev[0:32, :]))
                p.on("vector", lambda e: e.tensor_scalar(
                    qs[64:96, :], ev[96:128, :], -1.0, None, ALU.mult))
                p.on("vector", lambda e: e.tensor_copy(qs[96:128, :], ev[64:96, :]))
                p.on("vector", lambda e, cos_t=cos_t, c0=c0: e.tensor_mul(
                    ev[:, :], ev[:, :], cos_t[:, c0:c0 + TC]))
                p.on("vector", lambda e, sin_t=sin_t, c0=c0: e.tensor_mul(
                    qs[:, :], qs[:, :], sin_t[:, c0:c0 + TC]))
                p.on("vector", lambda e: e.tensor_add(ev[:, :], ev[:, :], qs[:, :]))
            v_ev = p.on("vector", lambda e: e.tensor_copy(qb[:, :], ev[:, :]), inc=1)
            p.wait("tensor", "vector", v_ev)
            p.wait("gpsimd", "vector", v_ev)
            g_sp = p.dma("gpsimd", q_dram[:, oc, c0:c0 + TC], qb[:, :])
            p.wait("vector", "gpsimd", g_sp)
    p.barrier()
    esA.close()

    esB = ExitStack()
    kh = sb(esB, nc, tag + "kh", [PC, T], BF16)
    vh = sb(esB, nc, tag + "vh", [PC, T], BF16)
    vt = sb(esB, nc, tag + "vt", [PC, 32, 65], BF16)
    qt_b = sb(esB, nc, tag + "qtb", [PC, TC], BF16)
    P_sb = [sb(esB, nc, tag + f"Psb{i}", [PC, 1024], BF16) for i in range(2)]
    zrow = sb(esB, nc, tag + "zr", [1, TC], F32)
    zb = sb(esB, nc, tag + "zb", [64, TC], F32)
    ao = sb(esB, nc, tag + "ao", [64, TC], BF16)
    ps_s = [ps(esB, nc, tag + f"pss{i}", [PC, 1024]) for i in range(2)]
    ps_o = ps(esB, nc, tag + "pso", [PC, 512])
    ps_tr = esB.enter_context(nc.psum_tensor(tag + "pstr", [PC, 64], BF16))
    ps_z = ps(esB, nc, tag + "psz", [64, 512])
    v_z1 = p.on("vector", lambda e: e.memset(vt[:, :, 64:65], 1.0), inc=1)
    p.wait("vector", "vector", v_z1)

    for h in range(NH):
        hrow = (h % 2) * 64
        hc = h // 2
        p.wait("sync", "tensor", p.wm("tensor"))
        p.wait("sync", "vector", p.wm("vector"))
        p.dma("sync", kh[hrow:hrow + 64, :], q_dram[hrow:hrow + 64, 8 + hc, :])
        v_wm = p.dma("sync", vh[hrow:hrow + 64, :], q_dram[hrow:hrow + 64, 16 + hc, :])
        p.wait("tensor", "sync", v_wm)
        for kt in range(32):
            t_tr = p.on("tensor", lambda e, kt=kt, hrow=hrow: e.transpose(
                ps_tr[:, :], vh[hrow:hrow + 64, kt * 128:(kt + 1) * 128],
                C["identbf"][hrow:hrow + 64, hrow:hrow + 64]), inc=1)
            p.wait("vector", "tensor", t_tr)
            v_vt = p.on("vector", lambda e, kt=kt: e.tensor_copy(
                vt[:, kt, 0:64], ps_tr[:, :]), inc=1)
            p.wait("tensor", "vector", v_vt)
        for qt in range(NT):
            q0 = qt * TC
            p.wait("sync", "tensor", p.wm("tensor"))
            q_wm = p.dma("sync", qt_b[hrow:hrow + 64, :],
                         q_dram[hrow:hrow + 64, hc, q0:q0 + TC])
            p.wait("tensor", "sync", q_wm)
            av_wm = {0: 0, 1: 0}
            for kg in range(16):
                pss = ps_s[kg % 2]
                t_s = 0
                for j in range(2):
                    kt = kg * 2 + j
                    t_s = p.on("tensor", lambda e, pss=pss, j=j, kt=kt, hrow=hrow:
                               e.matmul(pss[:, j * 512:(j + 1) * 512],
                                        kh[hrow:hrow + 64, kt * 128:(kt + 1) * 128],
                                        qt_b[hrow:hrow + 64, :],
                                        start=True, stop=True), inc=(1 if j == 1 else 0))
                p.wait("scalar", "tensor", t_s)
                p.wait("scalar", "tensor", av_wm[kg % 2])
                s_exp = p.on("scalar", lambda e, pss=pss, kg=kg: e.activation(
                    P_sb[kg % 2][:, :], pss[:, :], AF.Exp), inc=1)
                p.wait("tensor", "scalar", s_exp)
                for j in range(2):
                    kt = kg * 2 + j
                    av_wm[kg % 2] = p.on("tensor", lambda e, j=j, kt=kt, kg=kg: e.matmul(
                        ps_o[0:65, :], vt[:, kt, :], P_sb[kg % 2][:, j * 512:(j + 1) * 512],
                        start=(kt == 0), stop=(kt == 31)), inc=1)
            t_av = p.wm("tensor")
            p.wait("scalar", "tensor", t_av)
            p.on("scalar", lambda e: e.activation(zrow[:, :], ps_o[64:65, :], AF.Ln))
            s_z = p.on("scalar", lambda e: e.activation(
                zrow[:, :], zrow[:, :], AF.Exp, scale=-1.0), inc=1)
            p.wait("tensor", "scalar", s_z)
            t_zb = p.on("tensor", lambda e: e.matmul(
                ps_z[:, :], C["ones1"][0:1, 0:64], zrow[:, :], start=True, stop=True), inc=1)
            p.wait("vector", "tensor", t_zb)
            p.on("vector", lambda e: e.tensor_copy(zb[:, :], ps_z[:, :]))
            v_ao = p.on("vector", lambda e: e.tensor_mul(
                ao[:, :], zb[:, :], ps_o[0:64, :]), inc=1)
            p.wait("gpsimd", "vector", v_ao)
            g_ao = p.dma("gpsimd", ao_dram[hrow:hrow + 64, hc, q0:q0 + TC], ao[:, :])
            p.wait("vector", "gpsimd", g_ao)
            p.wait("tensor", "vector", p.wm("vector"))
    p.barrier()
    esB.close()

    esC = ExitStack()
    wout = fm(esC, nc, tag + "s_wout", ND, D, BF16)
    bo = sb(esC, nc, tag + "s_bo", [PC, 8], F32)
    aob = fm(esC, nc, tag + "aob", ND, TC, BF16)
    x_sb2 = fm(esC, nc, tag + "x2b", ND, TC, F32)
    psO = [ps(esC, nc, tag + f"psO{i}", [PC, 512]) for i in range(4)]
    g_wo = p.dma("gpsimd", wout[:, :, :], fm_dram(io["a5_woutT"], ND))
    g_w2 = p.dma("sync", bo[:, :], io["a5_bout"])
    p.wait("tensor", "gpsimd", g_wo)
    p.wait("vector", "sync", g_w2)
    v_prev = 0
    g_st = 0
    for ci in range(NT):
        c0 = ci * TC
        p.wait("sync", "vector", v_prev)
        p.wait("sync", "gpsimd", g_st)
        x_wm = p.dma("sync", x_sb2[:, :, :], fm_dram(io["x_in"], ND)[:, :, c0:c0 + TC])
        a_wm = p.dma("sync", aob[:, :, :], ao_dram[:, :, c0:c0 + TC])
        for e2 in ("tensor", "vector"):
            p.wait(e2, "sync", a_wm)
        v_prev = 0
        for f in range(2):
            fs = f * 256
            t_p = 0
            for oc in range(ND):
                pt = psO[oc // 2][:, (oc % 2) * 256:(oc % 2) * 256 + 256]
                for c in range(ND):
                    t_p = p.on("tensor", lambda e, pt=pt, oc=oc, c=c, fs=fs: e.matmul(
                        pt, wout[:, c, oc * PC:(oc + 1) * PC], aob[:, c, fs:fs + 256],
                        start=(c == 0), stop=(c == ND - 1)),
                        inc=(1 if c == ND - 1 else 0))
            p.wait("vector", "tensor", t_p)
            for oc in range(ND):
                pt = psO[oc // 2][:, (oc % 2) * 256:(oc % 2) * 256 + 256]
                v_prev = p.on("vector", lambda e, pt=pt, oc=oc, fs=fs:
                              e.scalar_tensor_tensor(
                                  x_sb2[:, oc, fs:fs + 256], pt, bo[:, oc:oc + 1],
                                  x_sb2[:, oc, fs:fs + 256], ALU.add, ALU.add), inc=1)
            if f == 0:
                p.wait("tensor", "vector", v_prev)
        p.wait("gpsimd", "vector", v_prev)
        g_st = p.dma("gpsimd", fm_dram(io["x_out"], ND)[:, :, c0:c0 + TC], x_sb2[:, :, :])
    p.drain_dma("gpsimd")
    p.barrier()
    esC.close()


def declare_l1attna(nc, io):
    io["a5_wqkvT"] = nc.dram_tensor("a5_wqkvT", [D, 3072], F32, kind="ExternalInput").ap()
    io["a5_bqkv"] = nc.dram_tensor("a5_bqkv", [PC, 24], F32, kind="ExternalInput").ap()
    io["a5_woutT"] = nc.dram_tensor("a5_woutT", [D, D], F32, kind="ExternalInput").ap()
    io["a5_bout"] = nc.dram_tensor("a5_bout", [PC, 8], F32, kind="ExternalInput").ap()
    for nm in ("cosq", "sinq", "cosk", "sink"):
        io["a5_" + nm] = nc.dram_tensor("a5_" + nm, [PC, T], F32, kind="ExternalInput").ap()


def prep_l1attna(inputs):
    d = {}
    d["a5_wqkvT"] = np.ascontiguousarray(inputs["L1_attn_a_W_qkv"].T)
    d["a5_bqkv"] = np.ascontiguousarray(inputs["L1_attn_a_b_qkv"].reshape(24, PC).T)
    d["a5_woutT"] = np.ascontiguousarray(inputs["L1_attn_a_W_out"].T)
    d["a5_bout"] = np.ascontiguousarray(inputs["L1_attn_a_b_out"].reshape(8, PC).T)
    hd = 64
    inv_freq = 1.0 / 10000.0 ** (np.arange(0, hd, 2, dtype=np.float64) / hd)
    t = np.arange(T, dtype=np.float64)
    freqs = np.einsum("i,j->ij", t, inv_freq).astype(np.float32)
    emb = np.concatenate([freqs, freqs], axis=-1)
    cos = np.cos(emb).T.astype(np.float32)
    sin = np.sin(emb).T.astype(np.float32)
    cos2 = np.concatenate([cos, cos], 0)
    sin2 = np.concatenate([sin, sin], 0)
    d["a5_cosq"] = np.ascontiguousarray(cos2 * 0.125)
    d["a5_sinq"] = np.ascontiguousarray(sin2 * 0.125)
    d["a5_cosk"] = np.ascontiguousarray(cos2)
    d["a5_sink"] = np.ascontiguousarray(sin2)
    return d



def declare_common(nc, io):
    io["x_in"] = nc.dram_tensor("x_in", [D, T], F32, kind="ExternalInput").ap()
    io["x_out"] = nc.dram_tensor("x_out", [D, T], F32, kind="ExternalOutput").ap()
    io["c_invn"] = nc.dram_tensor("c_invn", [PC, 1], F32, kind="ExternalInput").ap()
    io["c_ones1"] = nc.dram_tensor("c_ones1", [1, PC], F32, kind="ExternalInput").ap()
    io["c_ones8col"] = nc.dram_tensor("c_ones8col", [8, 1], F32, kind="ExternalInput").ap()
    io["c_vals"] = nc.dram_tensor("c_vals", [PC, len(CONST_VALS)], F32,
                                  kind="ExternalInput").ap()
    io["c_ident"] = nc.dram_tensor("c_ident", [PC, PC], F32, kind="ExternalInput").ap()


def host_consts():
    return {
        "c_invn": np.full((PC, 1), 1.0 / D, np.float32),
        "c_ones1": np.ones((1, PC), np.float32),
        "c_ones8col": np.ones((8, 1), np.float32),
        "c_vals": np.tile(np.asarray(CONST_VALS, np.float32)[None, :], (PC, 1)),
        "c_ident": np.eye(PC, dtype=np.float32),
    }


def _build(declares, emits):
    nc = bass.Bass("TRN2", target_bir_lowering=False)
    io = {}
    declare_common(nc, io)
    for d in declares:
        d(nc, io)
    es = ExitStack()
    p = Prog(nc, es)
    C = load_consts(p, es, nc, io)
    C["ones8col"] = sb(es, nc, "sb_o8", [8, 1], F32)[:, :]
    wm = p.dma("sync", C["ones8col"], io["c_ones8col"])
    p.wait("tensor", "sync", wm)
    for em in emits:
        em(p, es, io, C)
    block = es.enter_context(nc.Block())
    p.finalize(block)
    es.close()
    return nc


def _run(nc, x_full, weight_dicts):
    cc = host_consts()
    in_maps = []
    for core in range(8):
        b = core // 2
        m = {"x_in": np.ascontiguousarray(x_full[b].T)}
        for wd in weight_dicts:
            m.update(wd)
        m.update(cc)
        in_maps.append(m)
    res = run_bass_kernel_spmd(nc, in_maps, core_ids=list(range(8)))
    return np.stack([np.asarray(res.results[2 * b]["x_out"]).T for b in range(B)])


FUSED_EMITS = [
    ("a1_", lambda p, es, io, C, tag: emit_l0attna(p, es, io, C, tag)),
    ("f0_", lambda p, es, io, C, tag: emit_l0ffn(p, es, io, C, tag)),
    ("b3_", lambda p, es, io, C, tag: emit_l0attnb(p, es, io, C, tag)),
    ("f2_", lambda p, es, io, C, tag: emit_l0ffn(p, es, io, C, tag)),
    ("a5_", lambda p, es, io, C, tag: emit_l1attna(p, es, io, C, tag)),
    ("f1_", lambda p, es, io, C, tag: emit_l1ffn(p, es, io, C, tag)),
    ("b7_", lambda p, es, io, C, tag: emit_l1attnb(p, es, io, C, tag)),
    ("f3_", lambda p, es, io, C, tag: emit_l1ffn(p, es, io, C, tag)),
]


def build_fused():
    nc = bass.Bass("TRN2", target_bir_lowering=False)
    io = {}
    declare_common(nc, io)
    for d in (declare_l0attna, declare_l0ffn, declare_l0attnb,
              declare_l1attna, declare_l1ffn, declare_l1attnb):
        d(nc, io)
    xs = [io["x_in"]]
    for i in range(len(FUSED_EMITS) - 1):
        xs.append(nc.dram_tensor(f"xh{i}", [D, T], F32, kind="Internal").ap())
    xs.append(io["x_out"])
    es = ExitStack()
    p = Prog(nc, es)
    C = load_consts(p, es, nc, io)
    C["ones8col"] = sb(es, nc, "sb_o8", [8, 1], F32)[:, :]
    wm = p.dma("sync", C["ones8col"], io["c_ones8col"])
    p.wait("tensor", "sync", wm)
    for k_, (tag, em) in enumerate(FUSED_EMITS):
        io_k = dict(io)
        io_k["x_in"] = xs[k_]
        io_k["x_out"] = xs[k_ + 1]
        with ExitStack() as es_k:
            em(p, es_k, io_k, C, tag)
        p.barrier()
        p.recycle_dma_sems()
    block = es.enter_context(nc.Block())
    p.finalize(block)
    es.close()
    return nc


def fused_weight_dict(inputs):
    wd = {}
    for prep in (prep_l0attna, prep_l0ffn, prep_l0attnb,
                 prep_l1attna, prep_l1ffn, prep_l1attnb):
        d = prep(inputs)
        clash = set(d) & set(wd)
        assert not clash, clash
        wd.update(d)
    return wd


def run_fused(x_full, inputs):
    nc = build_fused()
    return _run(nc, x_full, [fused_weight_dict(inputs)])


def run_l1ffn(x_full, inputs):
    nc = _build([declare_l1ffn], [lambda p, es, io, C: emit_l1ffn(p, es, io, C, "f1_")])
    return _run(nc, x_full, [prep_l1ffn(inputs)])


def run_l0attnb(x_full, inputs):
    nc = _build([declare_l0attnb], [lambda p, es, io, C: emit_l0attnb(p, es, io, C, "b3_")])
    return _run(nc, x_full, [prep_l0attnb(inputs)])


def run_l0attna(x_full, inputs, phases="ABCD"):
    nc = _build([declare_l0attna],
                [lambda p, es, io, C: emit_l0attna(p, es, io, C, "a1_", phases=phases)])
    return _run(nc, x_full, [prep_l0attna(inputs)])


def run_l1attna(x_full, inputs):
    nc = _build([declare_l1attna], [lambda p, es, io, C: emit_l1attna(p, es, io, C, "a5_")])
    return _run(nc, x_full, [prep_l1attna(inputs)])


def run_l1attnb(x_full, inputs, debug=False):
    def decl(nc, io):
        declare_l1attnb(nc, io)
        if debug:
            io["dbg"] = {"o1": nc.dram_tensor("dbg_o1", [PC, ND, T], BF16,
                                              kind="ExternalOutput").ap()}
            for nm in ("Soc", "woc", "uoc"):
                io["dbg"][nm] = nc.dram_tensor("dbg_" + nm, [PC, T], F32,
                                               kind="ExternalOutput").ap()
            io["dbg"]["koc"] = nc.dram_tensor("dbg_koc", [PC, T], BF16,
                                              kind="ExternalOutput").ap()
            io["dbg"]["ksum"] = nc.dram_tensor("dbg_ksum", [2, T], BF16,
                                               kind="ExternalOutput").ap()
    nc = _build([decl], [lambda p, es, io, C: emit_l1attnb(p, es, io, C, "b7_",
                                                          dbg=io.get("dbg"))])
    cc = host_consts()
    wd = prep_l1attnb(inputs)
    in_maps = []
    for core in range(8):
        bb = core // 2
        m = {"x_in": np.ascontiguousarray(x_full[bb].T)}
        m.update(wd)
        m.update(cc)
        in_maps.append(m)
    res = run_bass_kernel_spmd(nc, in_maps, core_ids=list(range(8)))
    out = np.stack([np.asarray(res.results[2 * bb]["x_out"]).T for bb in range(B)])
    if debug:
        return out, {k2: np.asarray(res.results[0]["dbg_" + k2])
                     for k2 in ("o1", "Soc", "woc", "uoc", "koc", "ksum")}
    return out


def run_l0ffn(x_full, inputs, debug=False):
    def decl_dbg(nc, io):
        declare_l0ffn(nc, io)
        if debug:
            io["dbg"] = {
                "yh": nc.dram_tensor("dbg_yh", [PC, ND, 512], BF16, kind="ExternalOutput").ap(),
                "hall": nc.dram_tensor("dbg_hall", [PC, 128, 512], BF16, kind="ExternalOutput").ap(),
                "xnb": nc.dram_tensor("dbg_xnb", [PC, ND, 512], BF16, kind="ExternalOutput").ap(),
            }
    nc = _build([decl_dbg], [lambda p, es, io, C: emit_l0ffn(p, es, io, C, "f0_",
                                                            dbg=io.get("dbg"))])
    cc = host_consts()
    wd = prep_l0ffn(inputs)
    in_maps = []
    for core in range(8):
        bb = core // 2
        m = {"x_in": np.ascontiguousarray(x_full[bb].T)}
        m.update(wd)
        m.update(cc)
        in_maps.append(m)
    res = run_bass_kernel_spmd(nc, in_maps, core_ids=list(range(8)))
    out = np.stack([np.asarray(res.results[2 * bb]["x_out"]).T for bb in range(B)])
    if debug:
        return out, {k2: np.asarray(res.results[0][f"dbg_{k2}"]) for k2 in ("yh", "hall", "xnb")}
    return out



def kernel(**inputs):
    x = np.ascontiguousarray(np.asarray(inputs["L0_attn_a_x"], np.float32))
    nc = build_fused()
    return _run(nc, x, [fused_weight_dict(inputs)])


# version 03
# speedup vs baseline: 1.2544x; 1.2476x over previous
import numpy as np
import ml_dtypes
from contextlib import ExitStack
import concourse.bass as bass
import concourse.mybir as mybir
from concourse.bass_utils import run_bass_kernel_spmd

F32 = mybir.dt.float32
BF16 = mybir.dt.bfloat16
AF = mybir.ActivationFunctionType
ALU = mybir.AluOpType

B, T, D = 4, 4096, 1024
PC = 128
ND = D // PC
ENGS = ("tensor", "vector", "scalar", "gpsimd", "sync")


class Prog:

    def __init__(self, nc, es):
        self.nc = nc
        self.es = es
        self.q = {e: [] for e in ENGS}
        self.sems = {e: es.enter_context(nc.semaphore(f"pm_{e}")) for e in ENGS}
        self.cnt = {e: 0 for e in ENGS}
        self.dsems = {}
        self.dcnt = {}
        self.free_dsems = []
        self.sem_serial = 0

    def nsem(self):
        self.sem_serial += 1
        return self.sem_serial

    def recycle_dma_sems(self):
        for slot, sem in self.dsems.items():
            self.free_dsems.append((sem, self.dcnt[slot]))
        self.dsems = {}
        self.dcnt = {}

    def on(self, eng, fn, inc=0):
        self.cnt[eng] += inc
        sem = self.sems[eng]

        def thunk(h, fn=fn, inc=inc, sem=sem):
            inst = fn(h)
            if inc:
                inst.then_inc(sem, inc)
        self.q[eng].append(thunk)
        return self.cnt[eng]

    def dma(self, eng, out, in_, slot=None):
        if slot is None:
            t = out.tensor if hasattr(out, "tensor") else out
            slot = getattr(t, "name", None) or str(id(out))
        if slot not in self.dsems:
            if self.free_dsems:
                sem, base = self.free_dsems.pop()
                self.dsems[slot] = sem
                self.dcnt[slot] = base
            else:
                self.dsems[slot] = self.es.enter_context(
                    self.nc.semaphore(f"dm_{self.nsem()}"))
                self.dcnt[slot] = 0
        self.dcnt[slot] += 16
        sem = self.dsems[slot]

        def thunk(h, o=out, i=in_, sem=sem):
            h.dma_start(out=o, in_=i).then_inc(sem, 16)
        self.q[eng].append(thunk)
        return ("D", slot, self.dcnt[slot])

    def wait(self, eng, other, count):
        if isinstance(count, tuple) and count and count[0] == "D":
            _, slot, c = count
            sem = self.dsems[slot]
            self.q[eng].append(lambda h, sem=sem, c=c: h.wait_ge(sem, c))
            return
        if not isinstance(count, int) or count <= 0:
            return
        sem = self.sems[other]
        self.q[eng].append(lambda h, sem=sem, c=count: h.wait_ge(sem, c))

    def wm(self, eng):
        return self.cnt[eng]

    def barrier(self, engs=ENGS):
        marks = {e: self.cnt[e] for e in engs}
        dmarks = dict(self.dcnt)
        for e in engs:
            for o in engs:
                if o != e:
                    self.wait(e, o, marks[o])
            for slot, c in dmarks.items():
                if c > 0:
                    sem = self.dsems[slot]
                    self.q[e].append(lambda h, sem=sem, c=c: h.wait_ge(sem, c))

    def dma_last(self, slot):
        return ("D", slot, self.dcnt.get(slot, 0)) if self.dcnt.get(slot, 0) else 0

    def drain_dma(self, eng):
        for slot, c in self.dcnt.items():
            if c > 0:
                sem = self.dsems[slot]
                self.q[eng].append(lambda h, sem=sem, c=c: h.wait_ge(sem, c))

    def finalize(self, block):
        for e, thunks in self.q.items():
            def body(h, thunks=thunks):
                for t in thunks:
                    t(h)
            getattr(block, e)(body)


def sb(es, nc, name, shape, dt):
    return es.enter_context(nc.sbuf_tensor(name, shape, dt))


def ps(es, nc, name, shape):
    return es.enter_context(nc.psum_tensor(name, shape, F32))


def fm(es, nc, name, nchunks, n, dt):
    return es.enter_context(nc.sbuf_tensor(name, [PC, nchunks, n], dt))


def fm_dram(dram_ap, nchunks):
    return dram_ap.rearrange("(c p) n -> p c n", p=PC)


CONST_VALS = [1e-5, -0.5, 0.5, 1e-8, 1.1920929e-7, 60.0, -1.0, 1.0, 0.0,
              0.99, 0.125, 2.0, -2.0]


def load_consts(p, es, nc, io):
    C = {}
    C["invn"] = sb(es, nc, "sb_invn", [PC, 1], F32)[:, :]
    C["ones1"] = sb(es, nc, "sb_ones1", [1, PC], F32)[:, :]
    C["identbf"] = sb(es, nc, "sb_identbf", [PC, PC], BF16)[:, :]
    C["ones1b"] = sb(es, nc, "sb_ones1b", [1, PC], BF16)[:, :]
    p.dma("gpsimd", C["identbf"], io["c_ident"])
    gm = p.dma("gpsimd", C["ones1b"], io["c_ones1"])
    cv = sb(es, nc, "sb_cvals", [PC, len(CONST_VALS)], F32)
    for i, v in enumerate(CONST_VALS):
        nc.const_aps.register(F32, float(v), cv[:, i:i + 1])
    C["cval"] = {float(v): cv[:, i:i + 1] for i, v in enumerate(CONST_VALS)}
    p.dma("sync", C["invn"], io["c_invn"])
    p.dma("sync", cv[:, :], io["c_vals"])
    wm = p.dma("sync", C["ones1"], io["c_ones1"])
    for e2 in ("tensor", "vector", "scalar", "gpsimd"):
        p.wait(e2, "sync", wm)
        p.wait(e2, "sync", p.dma_last("sb_invn"))
        p.wait(e2, "sync", p.dma_last("sb_cvals"))
        p.wait(e2, "gpsimd", p.dma_last("sb_identbf"))
        p.wait(e2, "gpsimd", gm)
    return C



def emit_ln(p, C, x_sb, xn_sb, tc, ps_m, ps_r, x2_sb, rows, gcols=None,
            bcols=None, eps=1e-5, out_bf=None, outs=None):
    p.wait("scalar", "vector", p.wm("vector"))
    s_sq = p.on("scalar", lambda e: e.activation(
        x2_sb[:, :, 0:tc], x_sb[:, :, 0:tc], AF.Square), inc=1)
    p.wait("tensor", "scalar", s_sq)
    p.wait("tensor", "vector", p.wm("vector"))
    mrow = rows[0:1, 0:tc]
    vrow = rows[0:1, tc:2 * tc]
    t_mm = 0
    for src, dst in ((x_sb, ps_m), (x2_sb, ps_r)):
        for c in range(ND):
            t_mm = p.on("tensor", lambda e, src=src, dst=dst, c=c: e.matmul(
                dst[0:1, 0:tc], C["invn"], src[:, c, 0:tc],
                start=(c == 0), stop=(c == ND - 1)), inc=(1 if c == ND - 1 else 0))
    p.wait("vector", "tensor", t_mm)
    p.on("vector", lambda e: e.tensor_copy(mrow, ps_m[0:1, 0:tc]))
    p.on("vector", lambda e: e.tensor_mul(vrow, mrow, mrow))
    v_var = p.on("vector", lambda e: e.tensor_sub(vrow, ps_r[0:1, 0:tc], vrow), inc=1)
    p.wait("scalar", "vector", v_var)
    p.on("scalar", lambda e: e.activation(vrow, vrow, AF.Ln, bias=eps))
    s_rs = p.on("scalar", lambda e: e.activation(vrow, vrow, AF.Exp, scale=-0.5), inc=1)
    p.wait("tensor", "scalar", s_rs)
    p.on("tensor", lambda e: e.matmul(ps_m[:, 0:tc], C["ones1"], mrow,
                                      start=True, stop=True))
    t_bc = p.on("tensor", lambda e: e.matmul(ps_r[:, 0:tc], C["ones1"], vrow,
                                             start=True, stop=True), inc=1)
    p.wait("vector", "tensor", t_bc)
    if outs is None:
        outs = lambda c: xn_sb[:, c, 0:tc]
    v_last = 0
    for c in range(ND):
        p.on("vector", lambda e, c=c: e.tensor_sub(
            outs(c), x_sb[:, c, 0:tc], ps_m[:, 0:tc]))
        if gcols is None:
            v_last = p.on("vector", lambda e, c=c: e.tensor_mul(
                outs(c), outs(c), ps_r[:, 0:tc]), inc=1)
        else:
            p.on("vector", lambda e, c=c: e.tensor_mul(
                outs(c), outs(c), ps_r[:, 0:tc]))
            v_last = p.on("vector", lambda e, c=c: e.tensor_scalar(
                outs(c), outs(c), gcols[:, c:c + 1],
                None if bcols is None else bcols[:, c:c + 1],
                ALU.mult, ALU.mult if bcols is None else ALU.add), inc=1)
        if out_bf is not None:
            v_last = p.on("vector", lambda e, c=c: e.tensor_copy(
                out_bf[:, c, 0:tc], outs(c)), inc=1)
    return v_last



def emit_l1ffn(p, es, io, C, tag):
    nc = p.nc
    TC = 512
    wb = fm(es, nc, tag + "wb", ND, D, BF16)
    wA = fm(es, nc, tag + "wA", ND, 8, BF16)
    wR = fm(es, nc, tag + "wR", ND, 8, BF16)
    wB = sb(es, nc, tag + "wB", [128, D], BF16)
    eG = sb(es, nc, tag + "eG", [8, 64], BF16)
    eM = sb(es, nc, tag + "eM", [8, 64], BF16)
    bbc = sb(es, nc, tag + "bbc", [PC, ND], F32)
    rb = sb(es, nc, tag + "rb", [8, 1], F32)
    wtoks = [p.dma("gpsimd", wb[:, :, :], fm_dram(io["ffn_base_w"], ND)),
             p.dma("gpsimd", wA[:, :, :], fm_dram(io["ffn_lora_A"], ND)),
             p.dma("gpsimd", wR[:, :, :], fm_dram(io["ffn_router_w"], ND)),
             p.dma("gpsimd", wB[0:64, :], io["ffn_lora_B"]),
             p.dma("gpsimd", eG[:, :], io["ffn_eG"]),
             p.dma("gpsimd", eM[:, :], io["ffn_eM"]),
             p.dma("sync", bbc[:, :], io["ffn_base_b"]),
             p.dma("sync", rb[:, :], io["ffn_router_b"])]
    v_z = p.on("vector", lambda e: e.memset(wB[64:128, :], 0.0), inc=1)
    for tok in wtoks:
        p.wait("tensor", "gpsimd", tok)
        p.wait("vector", "gpsimd", tok)
    p.wait("tensor", "vector", v_z)

    x_sb = fm(es, nc, tag + "x", ND, TC, F32)
    xn = fm(es, nc, tag + "xn", ND, TC, F32)
    xnb = fm(es, nc, tag + "xnb", ND, TC, BF16)
    x2 = fm(es, nc, tag + "x2", ND, TC, F32)
    rows = sb(es, nc, tag + "rows", [1, 2 * TC], F32)
    prows = sb(es, nc, tag + "prows", [8, TC], F32)
    gat = sb(es, nc, tag + "gat", [8, TC], BF16)
    midb = sb(es, nc, tag + "midb", [8, TC], BF16)
    mge = sb(es, nc, tag + "mge", [64, TC], F32)
    mg = sb(es, nc, tag + "mg", [128, TC], BF16)
    zrow = sb(es, nc, tag + "zrow", [1, TC], F32)
    xo = fm(es, nc, tag + "xo", ND, TC, F32)
    psA = ps(es, nc, tag + "psA", [PC, 512])
    psB = ps(es, nc, tag + "psB", [PC, 512])
    psC = ps(es, nc, tag + "psC", [PC, 512])
    psO = [ps(es, nc, tag + f"psO{i}", [PC, 512]) for i in range(4)]
    v_z2 = p.on("vector", lambda e: e.memset(mg[64:128, :], 0.0), inc=1)

    v_out_prev = v_z2
    g_st_prev = 0
    for ci in range(T // TC):
        c0 = ci * TC
        p.wait("sync", "vector", v_out_prev)
        x_wm = p.dma("sync", x_sb[:, :, :], fm_dram(io["x_in"], ND)[:, :, c0:c0 + TC])
        for e2 in ("scalar", "tensor", "vector"):
            p.wait(e2, "sync", x_wm)
        emit_ln(p, C, x_sb, xn, TC, psA, psB, x2, rows, out_bf=xnb)
        p.wait("tensor", "vector", p.wm("vector"))
        t_g = 0
        for c in range(ND):
            t_g = p.on("tensor", lambda e, c=c: e.matmul(
                psC[0:8, :], wR[:, c, :], xnb[:, c, :], start=(c == 0), stop=(c == ND - 1)),
                inc=(1 if c == ND - 1 else 0))
        p.wait("vector", "tensor", t_g)
        v_lg = p.on("vector", lambda e: e.tensor_scalar(
            prows[:, :], psC[0:8, :], rb[:, :], None, ALU.add), inc=1)
        p.wait("scalar", "vector", v_lg)
        s_pe = p.on("scalar", lambda e: e.activation(prows[:, :], prows[:, :], AF.Exp), inc=1)
        p.wait("tensor", "scalar", s_pe)
        t_z = p.on("tensor", lambda e: e.matmul(
            psC[0:1, :], C["ones8col"], prows[:, :], start=True, stop=True), inc=1)
        p.wait("vector", "tensor", t_z)
        v_zc = p.on("vector", lambda e: e.tensor_copy(zrow[:, :], psC[0:1, :]), inc=1)
        p.wait("scalar", "vector", v_zc)
        p.on("scalar", lambda e: e.activation(zrow[:, :], zrow[:, :], AF.Ln))
        s_rz = p.on("scalar", lambda e: e.activation(
            zrow[:, :], zrow[:, :], AF.Exp, scale=-1.0), inc=1)
        p.wait("tensor", "scalar", s_rz)
        t_rz = p.on("tensor", lambda e: e.matmul(
            psC[0:8, :], C["ones1"][0:1, 0:8], zrow[:, :], start=True, stop=True), inc=1)
        p.wait("vector", "tensor", t_rz)
        v_gat = p.on("vector", lambda e: e.tensor_mul(gat[:, :], prows[:, :], psC[0:8, :]), inc=1)
        t_mid = 0
        for c in range(ND):
            t_mid = p.on("tensor", lambda e, c=c: e.matmul(
                psB[0:8, :], wA[:, c, :], xnb[:, c, :], start=(c == 0), stop=(c == ND - 1)),
                inc=(1 if c == ND - 1 else 0))
        p.wait("vector", "tensor", t_mid)
        v_mid = p.on("vector", lambda e: e.tensor_copy(midb[:, :], psB[0:8, :]), inc=1)
        p.wait("tensor", "vector", max(v_gat, v_mid))
        t_ge = p.on("tensor", lambda e: e.matmul(
            psA[0:64, :], eG[:, :], gat[:, :], start=True, stop=True), inc=1)
        t_me = p.on("tensor", lambda e: e.matmul(
            psB[0:64, :], eM[:, :], midb[:, :], start=True, stop=True), inc=1)
        p.wait("vector", "tensor", t_me)
        p.on("vector", lambda e: e.tensor_copy(mge[:, :], psB[0:64, :]))
        v_mg = p.on("vector", lambda e: e.tensor_mul(mg[0:64, :], mge[:, :], psA[0:64, :]), inc=1)
        p.wait("tensor", "vector", v_mg)
        v_out_prev = 0
        for f in range(2):
            fs = f * 256
            t_o = 0
            for oc in range(ND):
                pt = psO[oc // 2][:, (oc % 2) * 256:(oc % 2) * 256 + 256]
                for c in range(ND):
                    p.on("tensor", lambda e, pt=pt, oc=oc, c=c, fs=fs: e.matmul(
                        pt, wb[:, c, oc * PC:(oc + 1) * PC],
                        xnb[:, c, fs:fs + 256],
                        start=(c == 0), stop=False))
                t_o = p.on("tensor", lambda e, pt=pt, oc=oc, fs=fs: e.matmul(
                    pt, wB[:, oc * PC:(oc + 1) * PC], mg[:, fs:fs + 256],
                    start=False, stop=True), inc=1)
            p.wait("vector", "tensor", t_o)
            if f == 0:
                p.wait("vector", "gpsimd", g_st_prev)
            for oc in range(ND):
                pslice = psO[oc // 2][:, (oc % 2) * 256:(oc % 2) * 256 + 256]
                p.on("vector", lambda e, pslice=pslice, fs=fs, oc=oc: e.tensor_scalar(
                    xo[:, oc, fs:fs + 256], pslice, bbc[:, oc:oc + 1], None, ALU.add))
                v_out_prev = p.on("vector", lambda e, fs=fs, oc=oc: e.tensor_add(
                    xo[:, oc, fs:fs + 256], xo[:, oc, fs:fs + 256],
                    x_sb[:, oc, fs:fs + 256]), inc=1)
            if f == 0:
                p.wait("tensor", "vector", v_out_prev)
        p.wait("gpsimd", "vector", v_out_prev)
        g_st_prev = p.dma("gpsimd", fm_dram(io["x_out"], ND)[:, :, c0:c0 + TC], xo[:, :, :])
    p.drain_dma("gpsimd")


def declare_l1ffn(nc, io):
    io["ffn_base_w"] = nc.dram_tensor("ffn_base_w", [D, D], F32, kind="ExternalInput").ap()
    io["ffn_base_b"] = nc.dram_tensor("ffn_base_b", [PC, ND], F32, kind="ExternalInput").ap()
    io["ffn_lora_A"] = nc.dram_tensor("ffn_lora_A", [D, 8], F32, kind="ExternalInput").ap()
    io["ffn_lora_B"] = nc.dram_tensor("ffn_lora_B", [64, D], F32, kind="ExternalInput").ap()
    io["ffn_router_w"] = nc.dram_tensor("ffn_router_w", [D, 8], F32, kind="ExternalInput").ap()
    io["ffn_router_b"] = nc.dram_tensor("ffn_router_b", [8, 1], F32, kind="ExternalInput").ap()
    io["ffn_eG"] = nc.dram_tensor("ffn_eG", [8, 64], F32, kind="ExternalInput").ap()
    io["ffn_eM"] = nc.dram_tensor("ffn_eM", [8, 64], F32, kind="ExternalInput").ap()


def prep_l1ffn(inputs):
    eG = np.zeros((8, 64), np.float32)
    eM = np.zeros((8, 64), np.float32)
    for e in range(8):
        for r in range(8):
            eG[e, e * 8 + r] = 2.0
            eM[r, e * 8 + r] = 1.0
    return {
        "ffn_base_w": np.ascontiguousarray(inputs["L1_ffn_base_w"].T),
        "ffn_base_b": np.ascontiguousarray(inputs["L1_ffn_base_b"].reshape(ND, PC).T),
        "ffn_lora_A": np.ascontiguousarray(inputs["L1_ffn_lora_A"]),
        "ffn_lora_B": np.ascontiguousarray(inputs["L1_ffn_lora_B"].reshape(64, D)),
        "ffn_router_w": np.ascontiguousarray(inputs["L1_ffn_router_w"].T),
        "ffn_router_b": np.ascontiguousarray(inputs["L1_ffn_router_b"].reshape(8, 1)),
        "ffn_eG": eG,
        "ffn_eM": eM,
    }



def emit_l0ffn(p, es, io, C, tag, dbg=None):
    nc = p.nc
    TC = 512
    NH, NZC = 4, 128
    bd1 = [fm(es, nc, tag + f"bd1_{h}", ND, PC, BF16) for h in range(NH)]
    for h in range(NH):
        g_w = p.dma("gpsimd", bd1[h][:, :, :], io["ffn0_bd1"][h].rearrange("c p k -> p c k"))
        p.wait("tensor", "gpsimd", g_w)
    bd2buf = [fm(es, nc, tag + f"bd2_{i}", ND, 512, BF16) for i in range(2)]

    x_sb = fm(es, nc, tag + "x", ND, TC, F32)
    xnb = fm(es, nc, tag + "xnb", ND, TC, BF16)
    x2 = fm(es, nc, tag + "x2", ND, TC, F32)
    y_h = fm(es, nc, tag + "yh", ND, TC, BF16)
    rows = y_h[0:1, 0:4, :].bitcast(F32).rearrange("p a b -> p (a b)")
    h_all = fm(es, nc, tag + "hall", NZC, TC, BF16)
    wcp = [sb(es, nc, tag + f"wcp{i}", [PC, D], BF16) for i in range(2)]
    wcp.append(x2[:, 0, :].bitcast(BF16))
    wcp.append(x2[:, 1, :].bitcast(BF16))
    pool = [ps(es, nc, tag + f"pp{i}", [PC, 512]) for i in range(8)]

    v_prev = 0
    g_st_prev = 0
    s_gelu = 0
    vy_last = [0, 0]
    mb2_last = [0, 0]
    for ci in range(T // TC):
        c0 = ci * TC
        p.wait("sync", "vector", v_prev)
        p.wait("sync", "gpsimd", g_st_prev)
        x_wm = p.dma("sync", x_sb[:, :, :], fm_dram(io["x_in"], ND)[:, :, c0:c0 + TC])
        for e2 in ("scalar", "tensor", "vector"):
            p.wait(e2, "sync", x_wm)
        emit_ln(p, C, x_sb, x2, TC, pool[3], pool[4], x2, rows, out_bf=xnb)
        p.wait("tensor", "vector", p.wm("vector"))
        for h in range(NH):
            b2 = h % 2
            p.wait("gpsimd", "tensor", mb2_last[b2])
            g_b2 = p.dma("gpsimd", bd2buf[b2][:, :, :],
                         io["ffn0_bd2"][h].rearrange("c p k -> p c k"))
            v_y = 0
            for c in range(ND):
                py = pool[(3, 4)[c % 2]]
                if vy_last[c % 2]:
                    p.wait("tensor", "vector", vy_last[c % 2])
                t_y = p.on("tensor", lambda e, h=h, c=c, py=py: e.matmul(
                    py[:, :], bd1[h][:, c, :], xnb[:, c, :],
                    start=True, stop=True), inc=1)
                p.wait("vector", "tensor", t_y)
                v_y = p.on("vector", lambda e, c=c, py=py: e.tensor_copy(
                    y_h[:, c, :], py[:, :]), inc=1)
                vy_last[c % 2] = v_y
            p.wait("tensor", "gpsimd", g_b2)
            mb2_last[b2] = 0
            if dbg is not None and ci == 0 and h == 0:
                vv = p.wm("vector")
                p.wait("gpsimd", "vector", vv)
                p.dma("gpsimd", dbg["yh"], y_h[:, :, :])
            p.wait("tensor", "vector", v_y)
            for c in range(ND):
                for s4 in range(4):
                    zc = h * 32 + c * 4 + s4
                    pz = pool[1 + (zc % 2)]
                    p.wait("tensor", "scalar", max(0, s_gelu - 1))
                    t_zz = p.on("tensor", lambda e, c=c, s4=s4, pz=pz, b2=b2:
                                e.matmul(
                        pz[:, :], bd2buf[b2][:, c, s4 * PC:(s4 + 1) * PC],
                        y_h[:, c, :], start=True, stop=True), inc=1)
                    p.wait("scalar", "tensor", t_zz)
                    s_gelu = p.on("scalar", lambda e, zc=zc, pz=pz: e.activation(
                        h_all[:, zc, :], pz[:, :], AF.Gelu), inc=1)
                    mb2_last[b2] = t_zz
        p.wait("tensor", "scalar", s_gelu)
        if dbg is not None and ci == 0:
            p.wait("gpsimd", "scalar", s_gelu)
            p.dma("gpsimd", dbg["hall"], h_all[:, :, :])
            vv2 = p.wm("vector")
            p.wait("gpsimd", "vector", vv2)
            p.dma("gpsimd", dbg["xnb"], xnb[:, :, :])
        p.wait("gpsimd", "tensor", p.wm("tensor"))
        p.wait("gpsimd", "vector", p.wm("vector"))
        t_o = 0
        g_wm = {}
        for j in range(4):
            g_wm[j] = p.dma("gpsimd", wcp[j][:, :],
                            io["ffn0_wcpT"][j * PC:(j + 1) * PC, :])
        for m in range(NZC):
            p.wait("tensor", "gpsimd", g_wm[m % 4])
            for oc in range(ND):
                t_o = p.on("tensor", lambda e, m=m, oc=oc: e.matmul(
                    pool[oc][:, :], wcp[m % 4][:, oc * PC:(oc + 1) * PC],
                    h_all[:, m, :], start=(m == 0), stop=(m == NZC - 1)),
                    inc=(1 if oc == ND - 1 else 0))
            if m + 4 < NZC:
                p.wait("gpsimd", "tensor", t_o)
                g_wm[m % 4] = p.dma("gpsimd", wcp[m % 4][:, :],
                                    io["ffn0_wcpT"][(m + 4) * PC:(m + 5) * PC, :])
        p.wait("vector", "tensor", t_o)
        p.wait("vector", "gpsimd", g_st_prev)
        v_prev = 0
        for oc in range(ND):
            v_prev = p.on("vector", lambda e, oc=oc: e.tensor_add(
                x_sb[:, oc, :], pool[oc][:, :], x_sb[:, oc, :]), inc=1)
        p.wait("gpsimd", "vector", v_prev)
        g_st_prev = p.dma("gpsimd", fm_dram(io["x_out"], ND)[:, :, c0:c0 + TC], x_sb[:, :, :])
    p.drain_dma("gpsimd")


def declare_l0ffn(nc, io):
    io["ffn0_bd1"] = [nc.dram_tensor(f"ffn0_bd1_{h}", [ND, PC, PC], F32,
                                     kind="ExternalInput").ap() for h in range(4)]
    io["ffn0_bd2"] = [nc.dram_tensor(f"ffn0_bd2_{h}", [ND, PC, 512], F32,
                                     kind="ExternalInput").ap() for h in range(4)]
    io["ffn0_wcpT"] = nc.dram_tensor("ffn0_wcpT", [16384, D], F32,
                                     kind="ExternalInput").ap()


def prep_l0ffn(inputs):
    maker = inputs["L0_ffn_rail_maker"]
    up = inputs["L0_ffn_rail_up_proj"]
    d = {}
    bd1 = np.zeros((4, 8, PC, PC), np.float32)
    bd2 = np.zeros((4, 8, PC, 512), np.float32)
    for h in range(4):
        for c in range(8):
            for rl in range(8):
                r = 8 * c + rl
                bd1[h, c, 16 * rl:16 * rl + 16, 16 * rl:16 * rl + 16] = maker[h, r]
                bd2[h, c, 16 * rl:16 * rl + 16, 64 * rl:64 * rl + 64] = up[h, r].T
    for h in range(4):
        d[f"ffn0_bd1_{h}"] = np.ascontiguousarray(bd1[h])
        d[f"ffn0_bd2_{h}"] = np.ascontiguousarray(bd2[h])
    d["ffn0_wcpT"] = np.ascontiguousarray(inputs["L0_ffn_w_cproj"].T)
    return d



def emit_l1attnb(p, es, io, C, tag, dbg=None):
    nc = p.nc
    TC = 512
    NT = T // TC
    wsl = [fm(es, nc, tag + f"wsl{i}", ND, PC, BF16) for i in range(2)]
    ehs = sb(es, nc, tag + "sehs", [PC, 8, 16], BF16)
    xhs = sb(es, nc, tag + "sxhs", [16, 8, PC], BF16)
    g_e = p.dma("gpsimd", ehs[:, :, :], io["b7_ehs"].rearrange("c p j -> p c j"))
    g_w = p.dma("gpsimd", xhs[:, :, :], io["b7_xhs"].rearrange("c j p -> j c p"))
    p.wait("tensor", "gpsimd", g_e)
    p.wait("tensor", "gpsimd", g_w)

    x_sb = fm(es, nc, tag + "x", ND, TC, F32)
    x2 = fm(es, nc, tag + "x2", ND, TC, F32)
    rows = sb(es, nc, tag + "rows", [1, 2 * TC], F32)
    xnb = fm(es, nc, tag + "xnb", ND, T, BF16)
    o1c = fm(es, nc, tag + "o1c", ND, TC, BF16)
    w_oc = sb(es, nc, tag + "woc", [PC, T], F32)
    u_oc = sb(es, nc, tag + "uoc", [PC, T], F32)
    S_oc = sb(es, nc, tag + "soc", [PC, T], F32)
    k_oc = sb(es, nc, tag + "koc", [PC, T], BF16)
    ksum = sb(es, nc, tag + "ksum", [2, T], BF16)
    o1_dram = nc.dram_tensor(tag + "o1d", [PC, ND, T], BF16, kind="Internal").ap()
    psA = ps(es, nc, tag + "psA", [PC, 512])
    psB = ps(es, nc, tag + "psB", [PC, 512])
    psK = ps(es, nc, tag + "psK", [16, 512])
    psO = [ps(es, nc, tag + f"psO{i}", [PC, 512]) for i in range(2)]

    g_st_prev = 0
    v_ln = 0
    for ci in range(NT):
        c0 = ci * TC
        p.wait("sync", "vector", v_ln)
        x_wm = p.dma("sync", x_sb[:, :, :], fm_dram(io["x_in"], ND)[:, :, c0:c0 + TC])
        for e2 in ("scalar", "tensor", "vector"):
            p.wait(e2, "sync", x_wm)
        v_ln = emit_ln(p, C, x_sb, x2, TC, psA, psB, x2, rows,
                       out_bf=None)
        for c in range(ND):
            v_ln = p.on("vector", lambda e, c=c, c0=c0: e.tensor_copy(
                xnb[:, c, c0:c0 + TC], x2[:, c, 0:TC]), inc=1)

    p.wait("tensor", "vector", p.wm("vector"))
    p.wait("scalar", "vector", p.wm("vector"))
    p.wait("gpsimd", "vector", p.wm("vector"))

    def proj_oc(wname, oc, consumer):
        buf = proj_oc.flip
        proj_oc.flip ^= 1
        p.wait("gpsimd", "tensor", p.wm("tensor"))
        g_s = p.dma("gpsimd", wsl[buf][:, :, :],
                    fm_dram(io["b7_" + wname], ND)[:, :, oc * PC:(oc + 1) * PC])
        p.wait("tensor", "gpsimd", g_s)
        vres = 0
        for f in range(NT):
            fs = f * TC
            pt = psO[f % 2]
            t_p = 0
            for c in range(ND):
                t_p = p.on("tensor", lambda e, pt=pt, c=c, fs=fs, buf=buf:
                           e.matmul(pt[:, :], wsl[buf][:, c, :],
                                    xnb[:, c, fs:fs + TC],
                                    start=(c == 0), stop=(c == ND - 1)),
                           inc=(1 if c == ND - 1 else 0))
            p.wait("vector", "tensor", t_p)
            p.wait("scalar", "tensor", t_p)
            vres = consumer(fs, pt)
            p.wait("tensor", "vector", p.wm("vector"))
            p.wait("tensor", "scalar", p.wm("scalar"))
        return vres
    proj_oc.flip = 0

    for oc in range(ND):
        proj_oc("wk", oc, lambda fs, pt: p.on(
            "vector", lambda e, fs=fs, pt=pt: e.tensor_copy(
                k_oc[:, fs:fs + TC], pt[:, :]), inc=1))
        v_kb = p.wm("vector")
        p.wait("tensor", "vector", v_kb)
        for f in range(NT):
            fs = f * TC
            t_ks = p.on("tensor", lambda e, fs=fs: e.matmul(
                psK[0:2, :], ehs[:, 0, 0:2], k_oc[:, fs:fs + TC],
                start=True, stop=True), inc=1)
            p.wait("vector", "tensor", t_ks)
            v_ks = p.on("vector", lambda e, fs=fs: e.tensor_copy(
                ksum[:, fs:fs + TC], psK[0:2, :]), inc=1)
            p.wait("tensor", "vector", v_ks)
        if dbg is not None and oc == 0:
            vks_ = p.wm("vector")
            p.wait("gpsimd", "vector", vks_)
            p.dma("gpsimd", dbg["koc"], k_oc[:, :])
            g_dk = p.dma("gpsimd", dbg["ksum"], ksum[:, :])
            p.wait("vector", "gpsimd", g_dk)
            p.wait("tensor", "gpsimd", g_dk)
        def cons_v(fs, pt):
            t_e = p.on("tensor", lambda e, fs=fs: e.matmul(
                psB[:, :], xhs[0:2, 0, :], ksum[:, fs:fs + TC],
                start=True, stop=True), inc=1)
            p.on("vector", lambda e, fs=fs, pt=pt: e.tensor_copy(
                u_oc[:, fs:fs + TC], pt[:, :]))
            p.wait("vector", "tensor", t_e)
            return p.on("vector", lambda e, fs=fs: e.tensor_mul(
                u_oc[:, fs:fs + TC], u_oc[:, fs:fs + TC], psB[:, :]), inc=1)
        proj_oc("wv", oc, cons_v)
        def cons_w(fs, pt):
            s_s = p.on("scalar", lambda e, fs=fs, pt=pt: e.activation(
                w_oc[:, fs:fs + TC], pt[:, :], AF.Sigmoid), inc=1)
            p.wait("vector", "scalar", s_s)
            return p.on("vector", lambda e, fs=fs: e.tensor_scalar(
                w_oc[:, fs:fs + TC], w_oc[:, fs:fs + TC], 0.99, None, ALU.mult), inc=1)
        proj_oc("ww", oc, cons_w)
        v_sc = p.on("vector", lambda e: e.tensor_tensor_scan(
            S_oc[:, :], w_oc[:, :], u_oc[:, :], 0.0, ALU.mult, ALU.add), inc=1)
        def cons_r(fs, pt):
            s_s = p.on("scalar", lambda e, fs=fs, pt=pt: e.activation(
                u_oc[:, fs:fs + TC], pt[:, :], AF.Sigmoid), inc=1)
            p.wait("vector", "scalar", s_s)
            return p.on("vector", lambda e, fs=fs: e.tensor_mul(
                k_oc[:, fs:fs + TC], u_oc[:, fs:fs + TC], S_oc[:, fs:fs + TC]), inc=1)
        p.wait("scalar", "vector", v_sc)
        if dbg is not None and oc == 0:
            p.wait("gpsimd", "vector", v_sc)
            p.dma("gpsimd", dbg["Soc"], S_oc[:, :])
            p.dma("gpsimd", dbg["woc"], w_oc[:, :])
            g_dd = p.dma("gpsimd", dbg["uoc"], u_oc[:, :])
            p.wait("vector", "gpsimd", g_dd)
            p.wait("scalar", "gpsimd", g_dd)
        v_o1 = proj_oc("wr", oc, cons_r)
        p.wait("gpsimd", "vector", v_o1)
        g_o1 = p.dma("gpsimd", o1_dram[:, oc, :], k_oc[:, :])
        p.wait("vector", "gpsimd", g_o1)

    if dbg is not None:
        p.wait("gpsimd", "vector", p.wm("vector"))
        p.dma("gpsimd", dbg["o1"], o1_dram)
    p.wait("sync", "vector", p.wm("vector"))
    p.drain_dma("sync")
    wo = fm(es, nc, tag + "swo", ND, D, BF16)
    g_wo = p.dma("gpsimd", wo[:, :, :], fm_dram(io["b7_wo"], ND))
    p.wait("tensor", "gpsimd", g_wo)
    rsb = fm(es, nc, tag + "rsb", ND, TC, BF16)
    v_prev = 0
    for ci in range(NT):
        c0 = ci * TC
        p.wait("sync", "vector", v_prev)
        p.wait("sync", "gpsimd", g_st_prev)
        x_wm = p.dma("sync", x_sb[:, :, :], fm_dram(io["x_in"], ND)[:, :, c0:c0 + TC])
        for e2 in ("scalar", "tensor", "vector"):
            p.wait(e2, "sync", x_wm)
        p.wait("sync", "scalar", p.wm("scalar"))
        o_wm = p.dma("sync", o1c[:, :, :], o1_dram[:, :, c0:c0 + TC])
        for e2 in ("scalar", "vector", "tensor"):
            p.wait(e2, "sync", o_wm)
        s_q = p.on("scalar", lambda e: e.activation(
            x2[:, :, 0:TC], o1c[:, :, 0:TC], AF.Square), inc=1)
        p.wait("tensor", "scalar", s_q)
        t_ms = 0
        for c in range(ND):
            t_ms = p.on("tensor", lambda e, c=c: e.matmul(
                psA[0:1, 0:TC], C["invn"], x2[:, c, 0:TC],
                start=(c == 0), stop=(c == ND - 1)), inc=(1 if c == ND - 1 else 0))
        p.wait("vector", "tensor", t_ms)
        v_ms = p.on("vector", lambda e: e.tensor_copy(rows[0:1, 0:TC], psA[0:1, 0:TC]), inc=1)
        p.wait("scalar", "vector", v_ms)
        p.on("scalar", lambda e: e.activation(rows[0:1, 0:TC], rows[0:1, 0:TC],
                                              AF.Ln, bias=1.1920929e-7))
        s_rr = p.on("scalar", lambda e: e.activation(rows[0:1, 0:TC], rows[0:1, 0:TC],
                                                     AF.Exp, scale=-0.5), inc=1)
        p.wait("tensor", "scalar", s_rr)
        t_bc = p.on("tensor", lambda e: e.matmul(
            psA[:, 0:TC], C["ones1"], rows[0:1, 0:TC], start=True, stop=True), inc=1)
        p.wait("vector", "tensor", t_bc)
        v_rs = 0
        for c in range(ND):
            v_rs = p.on("vector", lambda e, c=c: e.tensor_mul(
                rsb[:, c, :], o1c[:, c, 0:TC], psA[:, 0:TC]), inc=1)
        p.wait("tensor", "vector", v_rs)
        v_prev = 0
        for f in range(2):
            fs = f * 256
            t_p = 0
            for oc2 in range(ND):
                pb = (psO[0], psO[1], psA, psB)[oc2 // 2]
                pt = pb[:, (oc2 % 2) * 256:(oc2 % 2) * 256 + 256]
                for c in range(ND):
                    t_p = p.on("tensor", lambda e, pt=pt, oc2=oc2, c=c, fs=fs: e.matmul(
                        pt, wo[:, c, oc2 * PC:(oc2 + 1) * PC],
                        rsb[:, c, fs:fs + 256],
                        start=(c == 0), stop=(c == ND - 1)),
                        inc=(1 if c == ND - 1 else 0))
            p.wait("vector", "tensor", t_p)
            if f == 0:
                p.wait("vector", "gpsimd", g_st_prev)
            for oc2 in range(ND):
                pb = (psO[0], psO[1], psA, psB)[oc2 // 2]
                pt = pb[:, (oc2 % 2) * 256:(oc2 % 2) * 256 + 256]
                v_prev = p.on("vector", lambda e, oc2=oc2, fs=fs, pt=pt: e.tensor_add(
                    x_sb[:, oc2, fs:fs + 256], pt, x_sb[:, oc2, fs:fs + 256]), inc=1)
            if f == 0:
                p.wait("tensor", "vector", v_prev)
        p.wait("gpsimd", "vector", v_prev)
        g_st_prev = p.dma("gpsimd", fm_dram(io["x_out"], ND)[:, :, c0:c0 + TC], x_sb[:, :, :])
    p.drain_dma("gpsimd")


def declare_l1attnb(nc, io):
    for nm in ("wr", "wk", "wv", "ww", "wo"):
        io["b7_" + nm] = nc.dram_tensor("b7_" + nm, [D, D], F32, kind="ExternalInput").ap()
    io["b7_ehs"] = nc.dram_tensor("b7_ehs", [8, PC, 16], F32, kind="ExternalInput").ap()
    io["b7_xhs"] = nc.dram_tensor("b7_xhs", [8, 16, PC], F32, kind="ExternalInput").ap()


def prep_l1attnb(inputs):
    d = {}
    for nm, key in (("wr", "L1_attn_b_Wr"), ("wk", "L1_attn_b_Wk"), ("wv", "L1_attn_b_Wv"),
                    ("ww", "L1_attn_b_Ww"), ("wo", "L1_attn_b_Wo")):
        d["b7_" + nm] = np.ascontiguousarray(inputs[key].T)
    ehs = np.zeros((8, PC, 16), np.float32)
    xhs = np.zeros((8, 16, PC), np.float32)
    for c in range(8):
        for pp in range(PC):
            hl = pp // 64
            ehs[c, pp, hl] = 1.0
            xhs[c, hl, pp] = 1.0
    d["b7_ehs"] = ehs
    d["b7_xhs"] = xhs
    return d



def emit_l0attna(p, es, io, C, tag, phases="ABCD"):
    nc = p.nc
    TC = 512
    NT = T // TC
    NG = 32
    x0i_dram = nc.dram_tensor(tag + "x0i", [PC, ND, T], F32, kind="Internal").ap()
    xr_dram = nc.dram_tensor(tag + "xr", [PC, ND, T], BF16, kind="Internal").ap()
    rw_dram = nc.dram_tensor(tag + "rw", [PC, NG, T], BF16, kind="Internal").ap()
    dc_dram = nc.dram_tensor(tag + "dcd", [PC, 2 * NG], F32, kind="Internal").ap()

    dcol = sb(es, nc, tag + "s_dcol", [PC, NG], F32)
    fcol = sb(es, nc, tag + "s_fcol", [PC, NG], F32)
    mixk = sb(es, nc, tag + "s_mixk", [PC, ND], F32)
    mixv = sb(es, nc, tag + "s_mixv", [PC, ND], F32)
    mixr = sb(es, nc, tag + "s_mixr", [PC, ND], F32)
    cmixk = sb(es, nc, tag + "s_cmixk", [PC, ND], F32)
    cmixr = sb(es, nc, tag + "s_cmixr", [PC, ND], F32)
    lncols = sb(es, nc, tag + "s_lnc", [PC, 6 * ND], F32)
    ctoks = [p.dma("sync", mixk[:, :], io["a1_mixk"]),
             p.dma("sync", mixv[:, :], io["a1_mixv"]),
             p.dma("sync", mixr[:, :], io["a1_mixr"]),
             p.dma("sync", cmixk[:, :], io["a1_cmixk"]),
             p.dma("sync", cmixr[:, :], io["a1_cmixr"]),
             p.dma("sync", lncols[:, :], io["a1_lncols"]),
             p.dma("sync", dcol[:, :], io["a1_decay"]),
             p.dma("sync", fcol[:, :], io["a1_first"])]
    for tok in ctoks[-2:]:
        p.wait("scalar", "sync", tok)
    p.on("scalar", lambda e: e.activation(dcol[:, :], dcol[:, :], AF.Exp))
    p.on("scalar", lambda e: e.activation(dcol[:, :], dcol[:, :], AF.Exp, scale=-1.0))
    s_d = p.on("scalar", lambda e: e.activation(fcol[:, :], fcol[:, :], AF.Exp), inc=1)
    p.wait("gpsimd", "scalar", s_d)
    p.dma("gpsimd", dc_dram[:, 0:NG], dcol[:, :])
    g_rt = p.dma("gpsimd", dc_dram[:, NG:2 * NG], fcol[:, :])
    p.wait("sync", "gpsimd", g_rt)
    rt1 = p.dma("sync", dcol[:, :], dc_dram[:, 0:NG])
    rt2 = p.dma("sync", fcol[:, :], dc_dram[:, NG:2 * NG])
    for e2 in ("vector", "scalar", "tensor", "gpsimd"):
        for tok in ctoks[:6] + [rt1, rt2]:
            p.wait(e2, "sync", tok)

    esAB = ExitStack()
    xk = fm(esAB, nc, tag + "s_xk", ND, T, BF16)
    xv = fm(esAB, nc, tag + "s_xv", ND, T, BF16)

    esP = ExitStack()
    x_sb = fm(esP, nc, tag + "xA", ND, TC, F32)
    x2 = fm(esP, nc, tag + "x2A", ND, TC, F32)
    x0c = fm(esP, nc, tag + "x0c", ND, TC, F32)
    rows = sb(esP, nc, tag + "rowsA", [1, 2 * TC], F32)
    xn_buf = esP.enter_context(nc.sbuf_tensor(tag + "xnbuf", [PC, ND, TC + 1], F32))
    xrc = fm(esP, nc, tag + "xrc", ND, TC, BF16)
    tmix = x2
    psA_ = ps(esP, nc, tag + "psA", [PC, 512])
    psB_ = ps(esP, nc, tag + "psB", [PC, 512])
    v_pr = p.on("vector", lambda e: e.memset(xn_buf[:, :, 0:1], 0.0), inc=1)

    g_pr = 0
    for ci in range(NT):
        c0 = ci * TC
        p.wait("sync", "vector", v_pr)
        p.wait("sync", "gpsimd", g_pr)
        x_wm = p.dma("sync", x_sb[:, :, :], fm_dram(io["x_in"], ND)[:, :, c0:c0 + TC])
        for e2 in ("scalar", "tensor", "vector"):
            p.wait(e2, "sync", x_wm)
        emit_ln(p, C, x_sb, x2, TC, psA_, psB_, x2, rows)
        emit_ln(p, C, x2, x0c, TC, psA_, psB_, x_sb, rows,
                gcols=lncols[:, 0:ND], bcols=lncols[:, ND:2 * ND])
        emit_ln(p, C, x0c, None, TC, psA_, psB_, x_sb, rows,
                gcols=lncols[:, 2 * ND:3 * ND], bcols=lncols[:, 3 * ND:4 * ND],
                outs=lambda c: xn_buf[:, c, 1:TC + 1])
        v_sp = p.wm("vector")
        p.wait("gpsimd", "vector", v_sp)
        g_pr = p.dma("gpsimd", x0i_dram[:, :, c0:c0 + TC], x0c[:, :, :])
        for c in range(ND):
            p.on("vector", lambda e, c=c: e.tensor_sub(
                tmix[:, c, :], xn_buf[:, c, 1:TC + 1], xn_buf[:, c, 0:TC]))
        for mq, kind in ((mixk, "k"), (mixv, "v"), (mixr, "r")):
            for c in range(ND):
                if kind == "k":
                    dv = xk[:, c, c0:c0 + TC]
                elif kind == "v":
                    dv = xv[:, c, c0:c0 + TC]
                else:
                    dv = xrc[:, c, :]
                v_pr = p.on("vector", lambda e, c=c, mq=mq, dv=dv: e.scalar_tensor_tensor(
                    dv, tmix[:, c, :], mq[:, c:c + 1], xn_buf[:, c, 0:TC],
                    ALU.mult, ALU.add), inc=1)
        p.wait("gpsimd", "vector", v_pr)
        g_pr = p.dma("gpsimd", xr_dram[:, :, c0:c0 + TC], xrc[:, :, :])
        v_pr = p.on("vector", lambda e: e.tensor_copy(
            xn_buf[:, :, 0:1], xn_buf[:, :, TC:TC + 1]), inc=1)
    p.wait("vector", "gpsimd", g_pr)
    p.barrier()
    if "B" not in phases:
        for ci2 in range(NT):
            p.dma("gpsimd", fm_dram(io["x_out"], ND)[:, :, ci2 * TC:(ci2 + 1) * TC],
                  x_sb[:, :, :] if False else None) if False else None
        g_f = 0
        xf = x_sb
        for ci2 in range(NT):
            p.wait("sync", "gpsimd", g_f)
            w1 = p.dma("sync", xf[:, :, :], x0i_dram[:, :, ci2 * TC:(ci2 + 1) * TC])
            p.wait("gpsimd", "sync", w1)
            g_f = p.dma("gpsimd", fm_dram(io["x_out"], ND)[:, :, ci2 * TC:(ci2 + 1) * TC],
                        xf[:, :, :])
        p.drain_dma("gpsimd")
        esP.close()
        esAB.close()
        return
    esP.close()

    esB = ExitStack()
    wsl = [fm(esB, nc, tag + f"wslB{i}", ND, PC, BF16) for i in range(2)]
    k_buf = esB.enter_context(nc.sbuf_tensor(tag + "kbuf", [PC, T + 1], F32))
    kv_buf = esB.enter_context(nc.sbuf_tensor(tag + "kvbuf", [PC, T + 1], F32))
    sk_b = sb(esB, nc, tag + "skb", [PC, T], F32)
    skv_b = sb(esB, nc, tag + "skvb", [PC, T], F32)
    rwb = sb(esB, nc, tag + "rwbB", [PC, T], BF16)
    psP = [ps(esB, nc, tag + f"psP{i}", [PC, 512]) for i in range(2)]
    v_zb = p.on("vector", lambda e: e.memset(k_buf[:, 0:1], 0.0))
    v_zb = p.on("vector", lambda e: e.memset(kv_buf[:, 0:1], 0.0), inc=1)
    p.wait("vector", "vector", v_zb)
    g_rwsp = 0

    mk_last = 0
    mv_last = 0
    ev_last = [("scalar", 0), ("scalar", 0)]
    v_scans = 0
    for g in range(NG):
        p.wait("gpsimd", "tensor", mk_last)
        g_k = p.dma("gpsimd", wsl[0][:, :, :],
                    fm_dram(io["a1_wk"], ND)[:, :, g * PC:(g + 1) * PC])
        p.wait("tensor", "gpsimd", g_k)
        s_k = 0
        for f in range(NT):
            fs = f * TC
            pt = psP[f % 2]
            eng, cc2 = ev_last[f % 2]
            if cc2:
                p.wait("tensor", eng, cc2)
            t_k = 0
            for c in range(ND):
                t_k = p.on("tensor", lambda e, pt=pt, c=c, fs=fs: e.matmul(
                    pt[:, :], wsl[0][:, c, :], xk[:, c, fs:fs + TC],
                    start=(c == 0), stop=(c == ND - 1)), inc=(1 if c == ND - 1 else 0))
            if f == NT - 1:
                mk_last = t_k
            p.wait("scalar", "tensor", t_k)
            if f == 0:
                p.wait("scalar", "vector", v_scans)
            s_k = p.on("scalar", lambda e, pt=pt, fs=fs: e.activation(
                k_buf[:, 1 + fs:1 + fs + TC], pt[:, :], AF.Exp), inc=1)
            ev_last[f % 2] = ("scalar", s_k)
        p.wait("vector", "scalar", s_k)
        v_k = p.on("vector", lambda e: e.tensor_scalar(
            k_buf[:, 1:T + 1], k_buf[:, 1:T + 1],
            1.1420073898156842e26, None, ALU.min), inc=1)
        p.wait("gpsimd", "tensor", mv_last)
        g_v = p.dma("gpsimd", wsl[1][:, :, :],
                    fm_dram(io["a1_wv"], ND)[:, :, g * PC:(g + 1) * PC])
        p.wait("tensor", "gpsimd", g_v)
        v_kv = 0
        for f in range(NT):
            fs = f * TC
            pt = psP[f % 2]
            eng, cc2 = ev_last[f % 2]
            if cc2:
                p.wait("tensor", eng, cc2)
            t_v = 0
            for c in range(ND):
                t_v = p.on("tensor", lambda e, pt=pt, c=c, fs=fs: e.matmul(
                    pt[:, :], wsl[1][:, c, :], xv[:, c, fs:fs + TC],
                    start=(c == 0), stop=(c == ND - 1)), inc=(1 if c == ND - 1 else 0))
            if f == NT - 1:
                mv_last = t_v
            p.wait("vector", "tensor", t_v)
            if f == 0:
                p.wait("vector", "vector", v_k)
            v_kv = p.on("vector", lambda e, pt=pt, fs=fs: e.tensor_mul(
                kv_buf[:, 1 + fs:1 + fs + TC], k_buf[:, 1 + fs:1 + fs + TC],
                pt[:, :]), inc=1)
            ev_last[f % 2] = ("vector", v_kv)
        dbc = dcol[:, g:g + 1].broadcast_to([PC, T])
        p.on("vector", lambda e, dbc=dbc: e.tensor_tensor_scan(
            sk_b[:, :], dbc, k_buf[:, 0:T], 0.0, ALU.mult, ALU.add))
        p.on("vector", lambda e, dbc=dbc: e.tensor_tensor_scan(
            skv_b[:, :], dbc, kv_buf[:, 0:T], 0.0, ALU.mult, ALU.add))
        p.on("vector", lambda e, g=g: e.scalar_tensor_tensor(
            sk_b[:, :], k_buf[:, 1:T + 1], fcol[:, g:g + 1], sk_b[:, :],
            ALU.mult, ALU.add))
        v_wk = p.on("vector", lambda e: e.tensor_scalar(
            sk_b[:, :], sk_b[:, :], 1e-8, None, ALU.add), inc=1)
        v_scans = p.on("vector", lambda e, g=g: e.scalar_tensor_tensor(
            skv_b[:, :], kv_buf[:, 1:T + 1], fcol[:, g:g + 1], skv_b[:, :],
            ALU.mult, ALU.add), inc=1)
        p.wait("scalar", "vector", v_wk)
        p.on("scalar", lambda e: e.activation(sk_b[:, :], sk_b[:, :], AF.Ln))
        s_inv = p.on("scalar", lambda e: e.activation(
            sk_b[:, :], sk_b[:, :], AF.Exp, scale=-1.0), inc=1)
        p.wait("vector", "scalar", s_inv)
        p.wait("vector", "gpsimd", g_rwsp)
        v_rw = p.on("vector", lambda e: e.tensor_mul(
            rwb[:, :], skv_b[:, :], sk_b[:, :]), inc=1)
        p.wait("gpsimd", "vector", v_rw)
        g_rwsp = p.dma("gpsimd", rw_dram[:, g, :], rwb[:, :])
    p.barrier()
    esB.close()
    esAB.close()
    if "C" not in phases:
        esF = ExitStack()
        xf = fm(esF, nc, tag + "xf", ND, TC, F32)
        g_f = 0
        for ci2 in range(NT):
            p.wait("sync", "gpsimd", g_f)
            w1 = p.dma("sync", xf[:, :, :], x0i_dram[:, :, ci2 * TC:(ci2 + 1) * TC])
            p.wait("gpsimd", "sync", w1)
            g_f = p.dma("gpsimd", fm_dram(io["x_out"], ND)[:, :, ci2 * TC:(ci2 + 1) * TC],
                        xf[:, :, :])
        p.drain_dma("gpsimd")
        esF.close()
        return

    esC = ExitStack()
    wslr = [fm(esC, nc, tag + f"wslr{i}", ND, PC, BF16) for i in range(2)]
    wslo = [sb(esC, nc, tag + f"wslo{i}", [PC, D], BF16) for i in range(2)]
    xrb = fm(esC, nc, tag + "xrb", ND, TC, BF16)
    rwall = esC.enter_context(nc.sbuf_tensor(tag + "rwall", [PC, NG, TC], BF16))
    rwg = [sb(esC, nc, tag + f"rwgC{i}", [PC, TC], BF16) for i in range(2)]
    sgt = [sb(esC, nc, tag + f"sgtC{i}", [PC, TC], F32) for i in range(2)]
    x0b = fm(esC, nc, tag + "x0b", ND, TC, F32)
    psO = [ps(esC, nc, tag + f"psOC{i}", [PC, 512]) for i in range(8)]
    g_st = 0
    mw_last = [0, 0]
    mo_last = [0, 0]
    ev_last = [0, 0]
    vg_last = [0, 0]
    for f in range(NT):
        fs = f * TC
        p.wait("sync", "vector", p.wm("vector"))
        xr_wm = p.dma("sync", xrb[:, :, :], xr_dram[:, :, fs:fs + TC])
        p.wait("tensor", "sync", xr_wm)
        v_rg = 0
        for g in range(NG):
            b = g % 2
            p.wait("gpsimd", "tensor", mw_last[b])
            g_r = p.dma("gpsimd", wslr[b][:, :, :],
                        fm_dram(io["a1_wr"], ND)[:, :, g * PC:(g + 1) * PC])
            p.wait("sync", "vector", vg_last[b])
            rg_wm = p.dma("sync", rwg[b][:, :], rw_dram[:, g, fs:fs + TC])
            p.wait("tensor", "gpsimd", g_r)
            if ev_last[b]:
                p.wait("tensor", "scalar", ev_last[b])
            pt = psO[b]
            t_r = 0
            for c in range(ND):
                t_r = p.on("tensor", lambda e, pt=pt, c=c, b=b: e.matmul(
                    pt[:, :], wslr[b][:, c, :], xrb[:, c, :],
                    start=(c == 0), stop=(c == ND - 1)), inc=(1 if c == ND - 1 else 0))
            mw_last[b] = t_r
            p.wait("scalar", "tensor", t_r)
            if vg_last[b]:
                p.wait("scalar", "vector", vg_last[b])
            s_s = p.on("scalar", lambda e, pt=pt, b=b: e.activation(
                sgt[b][:, :], pt[:, :], AF.Sigmoid), inc=1)
            ev_last[b] = s_s
            p.wait("vector", "scalar", s_s)
            p.wait("vector", "sync", rg_wm)
            v_rg = p.on("vector", lambda e, g=g, b=b: e.tensor_mul(
                rwall[:, g, :], rwg[b][:, :], sgt[b][:, :]), inc=1)
            vg_last[b] = v_rg
        p.wait("tensor", "vector", v_rg)
        t_o = 0
        for g in range(NG):
            b = g % 2
            p.wait("gpsimd", "tensor", mo_last[b])
            g_o = p.dma("gpsimd", wslo[b][:, :], io["a1_wo"][g * PC:(g + 1) * PC, :])
            p.wait("tensor", "gpsimd", g_o)
            for oc in range(ND):
                t_o = p.on("tensor", lambda e, oc=oc, g=g, b=b: e.matmul(
                    psO[oc][:, :], wslo[b][:, oc * PC:(oc + 1) * PC], rwall[:, g, :],
                    start=(g == 0), stop=(g == NG - 1)), inc=(1 if oc == ND - 1 else 0))
            mo_last[b] = t_o
        p.wait("sync", "vector", p.wm("vector"))
        p.wait("sync", "gpsimd", g_st)
        x0_wm = p.dma("sync", x0b[:, :, :], x0i_dram[:, :, fs:fs + TC])
        p.wait("vector", "sync", x0_wm)
        p.wait("vector", "tensor", t_o)
        v_ev = 0
        for oc in range(ND):
            v_ev = p.on("vector", lambda e, oc=oc: e.tensor_add(
                x0b[:, oc, :], psO[oc][:, :], x0b[:, oc, :]), inc=1)
        p.wait("gpsimd", "vector", v_ev)
        g_st = p.dma("gpsimd", x0i_dram[:, :, fs:fs + TC], x0b[:, :, :])
        p.wait("tensor", "vector", v_ev)
    p.wait("vector", "gpsimd", g_st)
    p.barrier()
    esC.close()
    if "D" not in phases:
        esF = ExitStack()
        xf = fm(esF, nc, tag + "xf2", ND, TC, F32)
        g_f = 0
        for ci2 in range(NT):
            p.wait("sync", "gpsimd", g_f)
            w1 = p.dma("sync", xf[:, :, :], x0i_dram[:, :, ci2 * TC:(ci2 + 1) * TC])
            p.wait("gpsimd", "sync", w1)
            g_f = p.dma("gpsimd", fm_dram(io["x_out"], ND)[:, :, ci2 * TC:(ci2 + 1) * TC],
                        xf[:, :, :])
        p.drain_dma("gpsimd")
        esF.close()
        return

    esD = ExitStack()
    xD = fm(esD, nc, tag + "xD", ND, TC, F32)
    x2D = fm(esD, nc, tag + "x2D", ND, TC, F32)
    rowsD = sb(esD, nc, tag + "rowsD", [1, 2 * TC], F32)
    xnD = esD.enter_context(nc.sbuf_tensor(tag + "xnD", [PC, ND, TC + 1], F32))
    tmixD = fm(esD, nc, tag + "tmixD", ND, TC, F32)
    xk2 = fm(esD, nc, tag + "xk2", ND, TC, BF16)
    xr2 = fm(esD, nc, tag + "xr2", ND, TC, BF16)
    silb = esD.enter_context(nc.sbuf_tensor(tag + "silb", [PC, NG, TC], BF16))
    sgD = fm(esD, nc, tag + "sgD", ND, TC, F32)
    xinD = fm(esD, nc, tag + "xinD", ND, TC, F32)
    wslD = [fm(esD, nc, tag + f"wslD{i}", ND, PC, BF16) for i in range(2)]
    cmwrb = fm(esD, nc, tag + "cmwrb", ND, D, BF16)
    g_wr0 = p.dma("gpsimd", cmwrb[:, :, :], fm_dram(io["a1_cmwr"], ND))
    p.wait("tensor", "gpsimd", g_wr0)
    psD = [ps(esD, nc, tag + f"psD{i}", [PC, 512]) for i in range(8)]
    v_zD = p.on("vector", lambda e: e.memset(xnD[:, :, 0:1], 0.0), inc=1)

    v_pr = v_zD
    g_st = 0
    mwD_last = [0, 0]
    ev2 = [0, 0]
    for ci in range(NT):
        c0 = ci * TC
        p.wait("sync", "vector", v_pr)
        p.wait("sync", "gpsimd", g_st)
        x_wm = p.dma("sync", xD[:, :, :], x0i_dram[:, :, c0:c0 + TC])
        for e2 in ("scalar", "tensor", "vector"):
            p.wait(e2, "sync", x_wm)
        emit_ln(p, C, xD, None, TC, psD[0], psD[1], x2D, rowsD,
                gcols=lncols[:, 4 * ND:5 * ND], bcols=lncols[:, 5 * ND:6 * ND],
                outs=lambda c: xnD[:, c, 1:TC + 1])
        for c in range(ND):
            p.on("vector", lambda e, c=c: e.tensor_sub(
                tmixD[:, c, :], xnD[:, c, 1:TC + 1], xnD[:, c, 0:TC]))
        v_mx = 0
        for mq, dst in ((cmixk, xk2), (cmixr, xr2)):
            for c in range(ND):
                v_mx = p.on("vector", lambda e, c=c, mq=mq, dst=dst:
                            e.scalar_tensor_tensor(
                                dst[:, c, :], tmixD[:, c, :], mq[:, c:c + 1],
                                xnD[:, c, 0:TC], ALU.mult, ALU.add), inc=1)
        v_pr = p.on("vector", lambda e: e.tensor_copy(
            xnD[:, :, 0:1], xnD[:, :, TC:TC + 1]), inc=1)
        p.wait("tensor", "vector", v_pr)
        for g in range(NG):
            b = g % 2
            p.wait("gpsimd", "tensor", mwD_last[b])
            g_w = p.dma("gpsimd", wslD[b][:, :, :],
                        fm_dram(io["a1_cmwk"], ND)[:, :, g * PC:(g + 1) * PC])
            p.wait("tensor", "gpsimd", g_w)
            if ev2[b]:
                p.wait("tensor", "scalar", ev2[b])
            pt = psD[2 + b]
            t_k2 = 0
            for c in range(ND):
                t_k2 = p.on("tensor", lambda e, pt=pt, c=c, b=b: e.matmul(
                    pt[:, :], wslD[b][:, c, :], xk2[:, c, :],
                    start=(c == 0), stop=(c == ND - 1)), inc=(1 if c == ND - 1 else 0))
            mwD_last[b] = t_k2
            p.wait("scalar", "tensor", t_k2)
            s_sl = p.on("scalar", lambda e, pt=pt, g=g: e.activation(
                silb[:, g, :], pt[:, :], AF.Silu), inc=1)
            ev2[b] = s_sl
        for oc in range(ND):
            t_r2 = 0
            b = oc % 2
            pt = psD[2 + b]
            if ev2[b]:
                p.wait("tensor", "scalar", ev2[b])
            for c in range(ND):
                t_r2 = p.on("tensor", lambda e, pt=pt, oc=oc, c=c: e.matmul(
                    pt[:, :], cmwrb[:, c, oc * PC:(oc + 1) * PC], xr2[:, c, :],
                    start=(c == 0), stop=(c == ND - 1)), inc=(1 if c == ND - 1 else 0))
            p.wait("scalar", "tensor", t_r2)
            s_g2 = p.on("scalar", lambda e, pt=pt, oc=oc: e.activation(
                sgD[:, oc, :], pt[:, :], AF.Sigmoid), inc=1)
            ev2[b] = s_g2
        p.wait("tensor", "scalar", p.wm("scalar"))
        for g in range(NG):
            b = g % 2
            p.wait("gpsimd", "tensor", mwD_last[b])
            g_wv = p.dma("gpsimd", wslD[b][:, :, :],
                         io["a1_cmwv"][g * PC:(g + 1) * PC, :].rearrange(
                             "pp (cc o) -> pp cc o", cc=ND))
            p.wait("tensor", "gpsimd", g_wv)
            t_v2 = 0
            for oc in range(ND):
                t_v2 = p.on("tensor", lambda e, g=g, oc=oc, b=b: e.matmul(
                    psD[oc][:, :], wslD[b][:, oc, :], silb[:, g, :],
                    start=(g == 0), stop=(g == NG - 1)), inc=(1 if oc == ND - 1 else 0))
            mwD_last[b] = t_v2
        p.wait("sync", "vector", p.wm("vector"))
        xin_wm = p.dma("sync", xinD[:, :, :], fm_dram(io["x_in"], ND)[:, :, c0:c0 + TC])
        p.wait("vector", "sync", xin_wm)
        p.wait("vector", "tensor", t_v2)
        v_pr = 0
        for oc in range(ND):
            p.on("vector", lambda e, oc=oc: e.tensor_add(
                xD[:, oc, :], xD[:, oc, :], sgD[:, oc, :]))
            p.on("vector", lambda e, oc=oc: e.tensor_add(
                xD[:, oc, :], xD[:, oc, :], psD[oc][:, :]))
            v_pr = p.on("vector", lambda e, oc=oc: e.tensor_add(
                xD[:, oc, :], xD[:, oc, :], xinD[:, oc, :]), inc=1)
        p.wait("gpsimd", "vector", v_pr)
        g_st = p.dma("gpsimd", fm_dram(io["x_out"], ND)[:, :, c0:c0 + TC], xD[:, :, :])
        p.wait("tensor", "vector", v_pr)
    p.drain_dma("gpsimd")
    p.barrier()
    esD.close()


def declare_l0attna(nc, io):
    io["a1_wk"] = nc.dram_tensor("a1_wk", [D, 4096], F32, kind="ExternalInput").ap()
    io["a1_wv"] = nc.dram_tensor("a1_wv", [D, 4096], F32, kind="ExternalInput").ap()
    io["a1_wr"] = nc.dram_tensor("a1_wr", [D, 4096], F32, kind="ExternalInput").ap()
    io["a1_wo"] = nc.dram_tensor("a1_wo", [4096, D], F32, kind="ExternalInput").ap()
    io["a1_cmwk"] = nc.dram_tensor("a1_cmwk", [D, 4096], F32, kind="ExternalInput").ap()
    io["a1_cmwv"] = nc.dram_tensor("a1_cmwv", [4096, D], F32, kind="ExternalInput").ap()
    io["a1_cmwr"] = nc.dram_tensor("a1_cmwr", [D, D], F32, kind="ExternalInput").ap()
    for nm in ("mixk", "mixv", "mixr", "cmixk", "cmixr", "decay", "first"):
        n = {"decay": 32, "first": 32}.get(nm, ND)
        io["a1_" + nm] = nc.dram_tensor("a1_" + nm, [PC, n], F32, kind="ExternalInput").ap()
    io["a1_lncols"] = nc.dram_tensor("a1_lncols", [PC, 6 * ND], F32, kind="ExternalInput").ap()


def prep_l0attna(inputs):
    d = {}
    d["a1_wk"] = np.ascontiguousarray(inputs["L0_attn_a_tm_Wk"])
    d["a1_wv"] = np.ascontiguousarray(inputs["L0_attn_a_tm_Wv"])
    d["a1_wr"] = np.ascontiguousarray(inputs["L0_attn_a_tm_Wr"])
    d["a1_wo"] = np.ascontiguousarray(inputs["L0_attn_a_tm_Wo"])
    d["a1_cmwk"] = np.ascontiguousarray(inputs["L0_attn_a_cm_Wk"])
    d["a1_cmwv"] = np.ascontiguousarray(inputs["L0_attn_a_cm_Wv"])
    d["a1_cmwr"] = np.ascontiguousarray(inputs["L0_attn_a_cm_Wr"])
    col = lambda a, n: np.ascontiguousarray(np.asarray(a, np.float32).reshape(n, PC).T)
    d["a1_mixk"] = col(inputs["L0_attn_a_tm_mix_k"], ND)
    d["a1_mixv"] = col(inputs["L0_attn_a_tm_mix_v"], ND)
    d["a1_mixr"] = col(inputs["L0_attn_a_tm_mix_r"], ND)
    d["a1_cmixk"] = col(inputs["L0_attn_a_cm_mix_k"], ND)
    d["a1_cmixr"] = col(inputs["L0_attn_a_cm_mix_r"], ND)
    d["a1_decay"] = col(inputs["L0_attn_a_tm_decay"], 32)
    d["a1_first"] = col(inputs["L0_attn_a_tm_first"], 32)
    d["a1_lncols"] = np.ascontiguousarray(np.concatenate(
        [col(inputs["L0_attn_a_" + nm], ND) for nm in
         ("ln0_g", "ln0_b", "pre_g", "pre_b", "post_g", "post_b")], axis=1))
    return d



def emit_l0attnb(p, es, io, C, tag):
    nc = p.nc
    TC = 512
    NT = T // TC
    CH = 128
    NC_ = T // CH
    NXC = 16
    z_dram = nc.dram_tensor(tag + "zd", [PC, NC_, NXC, CH], BF16, kind="Internal").ap()
    xs_dram = nc.dram_tensor(tag + "xsd", [PC, NC_, NXC, CH], BF16, kind="Internal").ap()
    rt_dram = nc.dram_tensor(tag + "rtd", [PC, 1], F32, kind="Internal").ap()
    dd_dram = nc.dram_tensor(tag + "ddd", [1, 64 * CH], F32, kind="Internal").ap()

    convw = sb(es, nc, tag + "sconvw", [PC, 18, 4], F32)
    convb = sb(es, nc, tag + "sconvb", [PC, 18], F32)
    acol = sb(es, nc, tag + "acol", [PC, 1], F32)
    dtb = sb(es, nc, tag + "sdtb", [PC, 1], F32)
    dcolD = sb(es, nc, tag + "sdcolD", [PC, 16], F32)
    nwcol = sb(es, nc, tag + "snwcol", [PC, 16], F32)
    mask64 = sb(es, nc, tag + "smask64", [64, CH], F32)
    p2c = sb(es, nc, tag + "sp2c", [64, 16, PC], BF16)
    dmask = sb(es, nc, tag + "sdmask", [PC, 64 * CH], BF16)
    ctoks = [p.dma("sync", convw[:, :, :], io["b3_convw"].rearrange("c p j -> p c j")),
             p.dma("sync", convb[:, :], io["b3_convb"]),
             p.dma("sync", acol[:, :], io["b3_alog"]),
             p.dma("sync", dtb[:, :], io["b3_dtb"]),
             p.dma("sync", dcolD[:, :], io["b3_dcol"]),
             p.dma("sync", nwcol[:, :], io["b3_nw"]),
             p.dma("sync", mask64[:, :], io["b3_mask64"]),
             p.dma("gpsimd", p2c[:, :, :], io["b3_p2"].rearrange("j h p -> h j p")),
             p.dma("gpsimd", dmask[:, :], io["b3_dmask"])]
    for tok in ctoks:
        for e2 in ("scalar", "vector", "tensor", "gpsimd"):
            p.wait(e2, "sync", tok)
    p.on("scalar", lambda e: e.activation(acol[:, :], acol[:, :], AF.Exp))
    s_a = p.on("scalar", lambda e: e.activation(
        acol[:, :], acol[:, :], AF.Identity, scale=-1.0), inc=1)
    p.wait("gpsimd", "scalar", s_a)
    g_a = p.dma("gpsimd", rt_dram, acol[:, :])
    p.wait("sync", "gpsimd", g_a)
    s_a2 = p.dma("sync", acol[:, :], rt_dram)
    for e2 in ("scalar", "vector", "tensor"):
        p.wait(e2, "sync", s_a2)

    Bf = sb(es, nc, tag + "Bf", [16, T], BF16)
    Cf = sb(es, nc, tag + "Cf", [16, T], BF16)
    dtf = sb(es, nc, tag + "dtf", [64, T], BF16)
    abf = sb(es, nc, tag + "abf", [64, T], F32)

    esW = ExitStack()
    x_sb = fm(esW, nc, tag + "xW", ND, TC, F32)
    x2 = fm(esW, nc, tag + "x2W", ND, TC, F32)
    rows = sb(esW, nc, tag + "rowsW", [1, 2 * TC], F32)
    xnb = fm(esW, nc, tag + "xnbW", ND, TC, BF16)
    wsl = [fm(esW, nc, tag + f"wslW{i}", ND, PC, BF16) for i in range(4)]
    xbc = esW.enter_context(nc.sbuf_tensor(tag + "xbc", [PC, 18, TC + 3], F32))
    zb = fm(esW, nc, tag + "zbW", NXC, TC, BF16)
    xsb = fm(esW, nc, tag + "xsbW", NXC, TC, BF16)
    dtr = sb(esW, nc, tag + "dtr", [64, TC], F32)
    dtt = sb(esW, nc, tag + "dtt", [64, TC], F32)
    psW = [ps(esW, nc, tag + f"psW{i}", [PC, 512]) for i in range(3)]
    v_zw = p.on("vector", lambda e: e.memset(xbc[:, :, 0:3], 0.0), inc=1)

    g_z = 0
    g_xs = 0
    mwin_last = [0, 0, 0, 0]
    evW_last = [0, 0, 0]
    for ci in range(NT):
        c0 = ci * TC
        p.wait("sync", "vector", p.wm("vector"))
        x_wm = p.dma("sync", x_sb[:, :, :], fm_dram(io["x_in"], ND)[:, :, c0:c0 + TC])
        for e2 in ("scalar", "tensor", "vector"):
            p.wait(e2, "sync", x_wm)
        emit_ln(p, C, x_sb, x2, TC, psW[0], psW[1], x2, rows, out_bf=xnb)
        p.wait("tensor", "vector", p.wm("vector"))
        for oc in range(35):
            b = oc % 4
            pt = psW[oc % 3]
            p.wait("gpsimd", "tensor", mwin_last[b])
            g_w = p.dma("gpsimd", wsl[b][:, :, :],
                        fm_dram(io["b3_winT"], ND)[:, :, oc * PC:(oc + 1) * PC])
            p.wait("tensor", "gpsimd", g_w)
            if evW_last[oc % 3]:
                p.wait("tensor", "vector", evW_last[oc % 3])
            t_p = 0
            for c in range(ND):
                t_p = p.on("tensor", lambda e, pt=pt, oc=oc, c=c, b=b: e.matmul(
                    pt[:, :], wsl[b][:, c, :], xnb[:, c, :],
                    start=(c == 0), stop=(c == ND - 1)), inc=(1 if c == ND - 1 else 0))
            mwin_last[b] = t_p
            p.wait("vector", "tensor", t_p)
            if oc < 16:
                if oc == 0:
                    p.wait("vector", "gpsimd", g_z)
                v_e = p.on("vector", lambda e, pt=pt, oc=oc: e.tensor_copy(
                    zb[:, oc, :], pt[:, :]), inc=1)
            elif oc < 34:
                v_e = p.on("vector", lambda e, pt=pt, oc=oc: e.tensor_copy(
                    xbc[:, oc - 16, 3:TC + 3], pt[:, :]), inc=1)
            else:
                v_e = p.on("vector", lambda e, pt=pt: e.tensor_copy(
                    dtr[:, :], pt[0:64, :]), inc=1)
            evW_last[oc % 3] = v_e
        v_wm = p.wm("vector")
        p.wait("gpsimd", "vector", v_wm)
        for cc4 in range(4):
            g_z = p.dma("gpsimd", z_dram[:, ci * 4 + cc4, :, :],
                        zb[:, :, cc4 * CH:(cc4 + 1) * CH], slot=tag + "zd")
        s_cv = 0
        for cc in range(18):
            p.on("vector", lambda e, cc=cc: e.tensor_scalar(
                x2[:, 0, 0:TC], xbc[:, cc, 0:TC], convw[:, cc, 0:1], None,
                ALU.mult))
            for j in range(1, 4):
                p.on("vector", lambda e, cc=cc, j=j: e.scalar_tensor_tensor(
                    x2[:, 0, 0:TC], xbc[:, cc, j:j + TC], convw[:, cc, j:j + 1],
                    x2[:, 0, 0:TC], ALU.mult, ALU.add))
            p.on("vector", lambda e, cc=cc: e.tensor_copy(
                xbc[:, cc, 0:3], xbc[:, cc, TC:TC + 3]))
            v_cv = p.on("vector", lambda e: e.tensor_copy(
                x2[:, 1, 0:TC], x2[:, 0, 0:TC]), inc=1)
            p.wait("scalar", "vector", v_cv)
            s_cv = p.on("scalar", lambda e, cc=cc: e.activation(
                x2[:, 2, 0:TC], x2[:, 1, 0:TC], AF.Silu,
                bias=convb[:, cc:cc + 1]), inc=1)
            p.wait("vector", "scalar", s_cv)
            if cc < 16:
                v_rt = p.on("vector", lambda e, cc=cc: e.tensor_copy(
                    xsb[:, cc, :], x2[:, 2, 0:TC]), inc=1)
            elif cc == 16:
                v_rt = p.on("vector", lambda e, c0=c0: e.tensor_copy(
                    Bf[:, c0:c0 + TC], x2[0:16, 2, 0:TC]), inc=1)
            else:
                v_rt = p.on("vector", lambda e, c0=c0: e.tensor_copy(
                    Cf[:, c0:c0 + TC], x2[0:16, 2, 0:TC]), inc=1)
            p.wait("scalar", "vector", v_rt)
        v_wm2 = p.wm("vector")
        p.wait("gpsimd", "vector", v_wm2)
        for cc4 in range(4):
            g_xs = p.dma("gpsimd", xs_dram[:, ci * 4 + cc4, :, :],
                         xsb[:, :, cc4 * CH:(cc4 + 1) * CH], slot=tag + "xsd")
        p.wait("vector", "gpsimd", g_xs)
        p.on("vector", lambda e: e.tensor_scalar(
            dtr[:, :], dtr[:, :], dtb[0:64, :], None, ALU.add))
        p.on("vector", lambda e: e.scalar_tensor_tensor(
            dtt[:, :], dtr[:, :], -1.0, dtr[:, :], ALU.mult, ALU.max))
        v_dt = p.on("vector", lambda e: e.tensor_scalar(
            dtr[:, :], dtr[:, :], 0.0, None, ALU.max), inc=1)
        p.wait("scalar", "vector", v_dt)
        p.on("scalar", lambda e: e.activation(dtt[:, :], dtt[:, :], AF.Exp, scale=-1.0))
        s_sp = p.on("scalar", lambda e: e.activation(
            dtt[:, :], dtt[:, :], AF.Ln, bias=1.0), inc=1)
        p.wait("vector", "scalar", s_sp)
        v_dt2 = p.on("vector", lambda e: e.tensor_add(
            dtr[:, :], dtr[:, :], dtt[:, :]), inc=1)
        v_dt2 = p.on("vector", lambda e, c0=c0: e.tensor_copy(
            dtf[:, c0:c0 + TC], dtr[:, :]), inc=1)
        v_dt3 = p.on("vector", lambda e, c0=c0: e.tensor_scalar(
            abf[:, c0:c0 + TC], dtr[:, :], acol[0:64, :], None,
            ALU.mult), inc=1)
    p.barrier()
    esW.close()

    esS = ExitStack()
    acum = sb(esS, nc, tag + "acum", [64, CH], F32)
    d0row = sb(esS, nc, tag + "d0row", [64, CH], F32)
    d0rm = sb(esS, nc, tag + "d0rm", [64, CH], F32)
    d0b = sb(esS, nc, tag + "d0b", [PC, 64 * CH], F32)
    Gb = sb(esS, nc, tag + "Gb", [PC, 64 * CH], BF16)
    CcE = esS.enter_context(nc.sbuf_tensor(tag + "CcE", [PC, 64 * CH], BF16))
    corow = sb(esS, nc, tag + "corow", [1, 64], F32)
    corow16 = sb(esS, nc, tag + "corow16", [16, 64], F32)
    dscol = sb(esS, nc, tag + "dscol", [PC, 64], F32)
    CBb = sb(esS, nc, tag + "CBb", [PC, CH], BF16)
    xsc = esS.enter_context(nc.sbuf_tensor(tag + "xsc", [PC, NXC, CH], BF16))
    xdt = esS.enter_context(nc.sbuf_tensor(tag + "xdt", [PC, NXC, CH], BF16))
    xdtT = sb(esS, nc, tag + "xdtT", [PC, 2048], BF16)
    xdtd = sb(esS, nc, tag + "xdtd", [PC, 2048], BF16)
    BcT = sb(esS, nc, tag + "BcT", [PC, 16], BF16)
    stateT = sb(esS, nc, tag + "stateT", [16, 2048], F32)
    prevT = sb(esS, nc, tag + "prevT", [PC, 2048], BF16)
    cd16 = sb(esS, nc, tag + "cd16", [16, 64], F32)
    Ysb = esS.enter_context(nc.sbuf_tensor(tag + "Ysb", [PC, NXC, CH], F32))
    ygr = sb(esS, nc, tag + "ygr", [1, CH], F32)
    xo_sb = fm(esS, nc, tag + "xoS", ND, CH, F32)
    woutb = fm(esS, nc, tag + "woutb", NXC, D, BF16)
    psS = [ps(esS, nc, tag + f"psS{i}", [PC, 512]) for i in range(4)]
    psQ = [ps(esS, nc, tag + f"psQ{i}", [PC, 512]) for i in range(4)]
    psT = psQ[3][:, 0:64].bitcast(BF16)
    g_wo = p.dma("gpsimd", woutb[:, :, :], fm_dram(io["b3_woutT"], NXC))
    p.wait("tensor", "gpsimd", g_wo)
    v_zs = p.on("vector", lambda e: e.memset(stateT[:, :], 0.0))
    v_zs = p.on("vector", lambda e: e.memset(prevT[:, :], 0.0))
    v_zs = p.on("vector", lambda e: e.memset(CcE[:, :], 0.0), inc=1)
    p.wait("vector", "vector", v_zs)

    g_st = 0
    for cn in range(NC_):
        l0 = cn * CH
        p.wait("vector", "tensor", p.wm("tensor"))
        v_ac = p.on("vector", lambda e, l0=l0: e.tensor_tensor_scan(
            acum[:, :], C["cval"][1.0][0:64, :].broadcast_to([64, CH]),
            abf[:, l0:l0 + CH], 0.0, ALU.mult, ALU.add), inc=1)
        p.wait("scalar", "vector", v_ac)
        s_d0 = p.on("scalar", lambda e, l0=l0: e.activation(
            d0row[:, :], abf[:, l0:l0 + CH], AF.Exp), inc=1)
        p.wait("vector", "scalar", s_d0)
        v_d0 = p.on("vector", lambda e: e.tensor_mul(
            d0rm[:, :], d0row[:, :], mask64[:, :]), inc=1)
        p.wait("tensor", "vector", v_d0)
        p.wait("gpsimd", "vector", v_d0)
        g_dd = p.dma("gpsimd", dd_dram[0:1, :].rearrange("x (h l) -> (x h) l", h=64),
                     d0rm[:, :])
        p.wait("gpsimd", "gpsimd", g_dd)
        s_dd = p.dma("gpsimd", CcE[0:1, :], dd_dram[0:1, :])
        p.wait("tensor", "gpsimd", s_dd)
        s_bc = 0
        bc_ev = [0, 0]
        for hh in range(0, 64, 4):
            pt = psS[(hh // 4) % 2]
            if bc_ev[(hh // 4) % 2]:
                p.wait("tensor", "scalar", bc_ev[(hh // 4) % 2])
            t_b = p.on("tensor", lambda e, pt=pt, hh=hh: e.matmul(
                pt[:, :], C["ones1b"], CcE[0:1, hh * CH:(hh + 4) * CH],
                start=True, stop=True), inc=1)
            p.wait("scalar", "tensor", t_b)
            s_bc = p.on("scalar", lambda e, pt=pt, hh=hh: e.activation(
                d0b[:, hh * CH:(hh + 4) * CH], pt[:, :], AF.Identity), inc=1)
            bc_ev[(hh // 4) % 2] = s_bc
        p.wait("vector", "scalar", s_bc)
        v_g = p.on("vector", lambda e: e.tensor_tensor_scan(
            Gb[:, :], d0b[:, :], dmask[:, :], 0.0, ALU.mult, ALU.add), inc=1)
        v_ds = p.on("vector", lambda e: e.tensor_copy(
            dscol[:, :].rearrange("s (h x) -> s h x", x=1),
            Gb[:, :].rearrange("s (h l) -> s h l", h=64)[:, :, CH - 1:CH]), inc=1)
        p.wait("gpsimd", "vector", v_d0)
        g_cr = p.dma("gpsimd", rt_dram[0:64, 0:1], d0row[:, 0:1])
        p.wait("sync", "gpsimd", g_cr)
        s_cr = p.dma("sync", corow16[:, :],
                     rt_dram[0:64, 0:1].rearrange("h x -> x h").broadcast_to([16, 64]))
        p.wait("vector", "sync", s_cr)
        p.wait("tensor", "vector", v_g)
        v_eab = 0
        for pc_ in range(16):
            pt = psS[3]
            t_e = p.on("tensor", lambda e, pc_=pc_: e.matmul(
                pt[0:16, :], C["ones1b"][0:1, 0:16],
                Gb[0:1, pc_ * 512:(pc_ + 1) * 512], start=True, stop=True), inc=1)
            p.wait("vector", "tensor", t_e)
            v_eab = p.on("vector", lambda e, pc_=pc_: e.tensor_copy(
                CcE[0:16, pc_ * 512:(pc_ + 1) * 512], pt[0:16, :]), inc=1)
            p.wait("tensor", "vector", v_eab)
        v_cce = p.on("vector", lambda e: e.tensor_mul(
            CcE[0:16, :].rearrange("n (h l) -> n h l", h=64),
            CcE[0:16, :].rearrange("n (h l) -> n h l", h=64),
            corow16[:, :].rearrange("n (h x) -> n h x", x=1)
            .broadcast_to([16, 64, CH])), inc=1)
        v_cce = p.on("vector", lambda e, l0=l0: e.tensor_mul(
            CcE[0:16, :].rearrange("n (h l) -> n h l", h=64),
            CcE[0:16, :].rearrange("n (h l) -> n h l", h=64),
            Cf[:, l0:l0 + CH].rearrange("n (x l) -> n x l", x=1)
            .broadcast_to([16, 64, CH])), inc=1)
        t_cb = p.on("tensor", lambda e, l0=l0: e.matmul(
            psS[2][:, 0:CH], Bf[:, l0:l0 + CH], Cf[:, l0:l0 + CH],
            start=True, stop=True), inc=1)
        p.wait("vector", "tensor", t_cb)
        v_cb = p.on("vector", lambda e: e.tensor_copy(CBb[:, :], psS[2][:, 0:CH]), inc=1)
        v_m = p.on("vector", lambda e: e.tensor_mul(
            Gb[:, :].rearrange("p (h l) -> p h l", h=64),
            Gb[:, :].rearrange("p (h l) -> p h l", h=64),
            CBb[:, :].rearrange("p (a l) -> p a l", a=1).broadcast_to([PC, 64, CH])),
            inc=1)
        p.wait("sync", "vector", p.wm("vector"))
        xs_wm = p.dma("sync", xsc[:, :, :], xs_dram[:, cn, :, :])
        for e2 in ("tensor", "vector", "scalar"):
            p.wait(e2, "sync", xs_wm)
        for jj in range(0, 16, 4):
            pt = psS[2]
            t_d = 0
            for j in range(4):
                t_d = p.on("tensor", lambda e, pt=pt, jj=jj, j=j, l0=l0: e.matmul(
                    pt[:, j * CH:(j + 1) * CH], p2c[:, jj + j, :],
                    dtf[:, l0:l0 + CH], start=True, stop=True),
                    inc=(1 if j == 3 else 0))
            p.wait("vector", "tensor", t_d)
            v_x = 0
            for j in range(4):
                v_x = p.on("vector", lambda e, pt=pt, jj=jj, j=j: e.tensor_mul(
                    xdt[:, jj + j, :], xsc[:, jj + j, :], pt[:, j * CH:(j + 1) * CH]),
                    inc=1)
            p.wait("tensor", "vector", v_x)
        for j in range(NXC):
            t_t = p.on("tensor", lambda e, j=j: e.transpose(
                psT[:, :], xdt[:, j, :], C["identbf"]), inc=1)
            p.wait("vector", "tensor", t_t)
            v_t = p.on("vector", lambda e, j=j: e.tensor_copy(
                xdtT[:, j * CH:(j + 1) * CH], psT[:, :]), inc=1)
            p.wait("tensor", "vector", v_t)
        t_bt = p.on("tensor", lambda e, l0=l0: e.transpose(
            psT[:, 0:16], Bf[:, l0:l0 + CH], C["identbf"][0:16, 0:16]), inc=1)
        p.wait("vector", "tensor", t_bt)
        v_bt = p.on("vector", lambda e: e.tensor_copy(BcT[:, :], psT[:, 0:16]), inc=1)
        v_xd = p.on("vector", lambda e: e.tensor_mul(
            xdtd[:, :].rearrange("s (h pp) -> s h pp", h=64),
            xdtT[:, :].rearrange("s (h pp) -> s h pp", h=64),
            dscol[:, :].rearrange("s (h x) -> s h x", x=1)
            .broadcast_to([PC, 64, 32])), inc=1)
        p.wait("tensor", "vector", max(v_bt, v_xd))
        p.wait("tensor", "scalar", s_bc)
        t_st = 0
        for ff in range(4):
            t_st = p.on("tensor", lambda e, ff=ff: e.matmul(
                psS[ff][0:16, :], BcT[:, :], xdtd[:, ff * 512:(ff + 1) * 512],
                start=True, stop=True), inc=1)
        p.wait("scalar", "vector", v_ac)
        s_cd = p.on("scalar", lambda e: e.activation(
            d0rm[:, 0:1], acum[:, CH - 1:CH], AF.Exp), inc=1)
        p.wait("gpsimd", "scalar", s_cd)
        g_cd = p.dma("gpsimd", rt_dram[0:64, 0:1], d0rm[:, 0:1])
        p.wait("sync", "gpsimd", g_cd)
        s_cd2 = p.dma("sync", corow[:, :], rt_dram[0:64, 0:1].rearrange("h 1 -> 1 h"))
        p.wait("tensor", "sync", s_cd2)
        t_cd = p.on("tensor", lambda e: e.matmul(
            psT[0:16, 0:64].bitcast(BF16) if False else psS[0][0:16, 0:64],
            C["ones1"][0:1, 0:16], corow[:, :], start=True, stop=True), inc=1) \
            if False else p.on("tensor", lambda e: e.matmul(
                psQ[0][0:16, 0:64], C["ones1"][0:1, 0:16], corow[:, :],
                start=True, stop=True), inc=1)
        p.wait("vector", "tensor", max(t_st, t_cd))
        v_cd = p.on("vector", lambda e: e.tensor_copy(cd16[:, :], psQ[0][0:16, 0:64]), inc=1)
        v_pv = p.on("vector", lambda e: e.tensor_copy(
            prevT[0:16, :], stateT[:, :]), inc=1)
        v_st = p.on("vector", lambda e: e.tensor_mul(
            stateT[:, :].rearrange("n (h pp) -> n h pp", h=64),
            stateT[:, :].rearrange("n (h pp) -> n h pp", h=64),
            cd16[:, :].rearrange("n (h x) -> n h x", x=1)
            .broadcast_to([16, 64, 32])))
        for ff in range(4):
            v_st = p.on("vector", lambda e, ff=ff: e.tensor_add(
                stateT[:, ff * 512:(ff + 1) * 512],
                stateT[:, ff * 512:(ff + 1) * 512], psS[ff][0:16, :]), inc=1)
        p.wait("tensor", "vector", max(v_m, v_cce, v_pv, v_st))
        def yslot(h):
            if h % 4 < 3:
                t3 = h // 4
                return psQ[t3 // 4], (h % 4) * 32, (t3 % 4) * 128
            sl = h // 4
            return (psS[0], psS[1])[sl // 8], ((sl % 8) % 3) * 32, \
                ((sl % 8) // 3) * 128
        t_y = 0
        for h in range(64):
            pq, prow, qoff = yslot(h)
            p.on("tensor", lambda e, h=h, pq=pq, qoff=qoff, prow=prow: e.matmul(
                pq[prow:prow + 32, qoff:qoff + CH],
                xdtT[:, h * 32:(h + 1) * 32], Gb[:, h * CH:(h + 1) * CH],
                start=True, stop=False))
            t_y = p.on("tensor", lambda e, h=h, pq=pq, qoff=qoff, prow=prow: e.matmul(
                pq[prow:prow + 32, qoff:qoff + CH],
                prevT[:, h * 32:(h + 1) * 32], CcE[:, h * CH:(h + 1) * CH],
                start=False, stop=True), inc=1)
        p.wait("vector", "tensor", t_y)
        v_y = 0
        for t3 in range(16):
            pq, qoff = psQ[t3 // 4], (t3 % 4) * 128
            v_y = p.on("vector",
                       lambda e, pq=pq, qoff=qoff, j=t3:
                       e.scalar_tensor_tensor(
                           Ysb[0:96, j, :], xsc[0:96, j, :],
                           dcolD[0:96, j:j + 1],
                           pq[0:96, qoff:qoff + CH],
                           ALU.mult, ALU.add), inc=1)
        for h in range(3, 64, 4):
            pq, prow, qoff = yslot(h)
            j = h // 4
            v_y = p.on("vector",
                       lambda e, pq=pq, prow=prow, qoff=qoff, j=j:
                       e.scalar_tensor_tensor(
                           Ysb[96:128, j, :], xsc[96:128, j, :],
                           dcolD[96:128, j:j + 1],
                           pq[prow:prow + 32, qoff:qoff + CH],
                           ALU.mult, ALU.add), inc=1)
        p.wait("tensor", "vector", v_y)
        p.wait("sync", "tensor", t_y)
        z_wm = p.dma("sync", xdtT[:, :].rearrange("p (a b) -> p a b", a=NXC),
                     z_dram[:, cn, :, :])
        p.wait("scalar", "sync", z_wm)
        p.wait("scalar", "vector", v_y)
        p.wait("scalar", "tensor", p.wm("tensor"))
        s_z = p.on("scalar", lambda e: e.activation(
            d0b[:, 0:NXC * CH].rearrange("p (a b) -> p a b", a=NXC),
            xdtT[:, :].rearrange("p (a b) -> p a b", a=NXC), AF.Silu), inc=1)
        p.wait("vector", "scalar", s_z)
        v_yg = 0
        for j in range(NXC):
            v_yg = p.on("vector", lambda e, j=j: e.tensor_mul(
                Ysb[:, j, :], Ysb[:, j, :],
                d0b[:, j * CH:(j + 1) * CH]), inc=1)
        p.wait("scalar", "vector", v_yg)
        s_q = p.on("scalar", lambda e: e.activation(
            d0b[:, 0:NXC * CH].rearrange("p (a b) -> p a b", a=NXC),
            Ysb[:, :, :], AF.Square), inc=1)
        p.wait("tensor", "scalar", s_q)
        t_ms = 0
        for j in range(NXC):
            t_ms = p.on("tensor", lambda e, j=j: e.matmul(
                psS[0][0:1, 0:CH], C["invn"],
                d0b[:, j * CH:(j + 1) * CH],
                start=(j == 0), stop=(j == NXC - 1)), inc=(1 if j == NXC - 1 else 0))
        p.wait("vector", "tensor", t_ms)
        v_ms = p.on("vector", lambda e: e.tensor_scalar(
            ygr[:, :], psS[0][0:1, 0:CH], 0.5, None, ALU.mult), inc=1)
        p.wait("scalar", "vector", v_ms)
        p.on("scalar", lambda e: e.activation(ygr[:, :], ygr[:, :], AF.Ln, bias=1e-5))
        s_rs = p.on("scalar", lambda e: e.activation(
            ygr[:, :], ygr[:, :], AF.Exp, scale=-0.5), inc=1)
        p.wait("tensor", "scalar", s_rs)
        t_bcr = p.on("tensor", lambda e: e.matmul(
            psS[1][:, 0:CH], C["ones1"], ygr[:, :], start=True, stop=True), inc=1)
        p.wait("vector", "tensor", t_bcr)
        v_yn = 0
        for j in range(NXC):
            p.on("vector", lambda e, j=j: e.tensor_mul(
                Ysb[:, j, :], Ysb[:, j, :], psS[1][:, 0:CH]))
            v_yn = p.on("vector", lambda e, j=j: e.tensor_scalar(
                xdt[:, j, :], Ysb[:, j, :], nwcol[:, j:j + 1], None, ALU.mult), inc=1)
        p.wait("tensor", "vector", v_yn)
        t_wo = 0
        for oc in range(ND):
            pw = psS[2 + oc % 2][:, (oc // 2 % 2) * CH:(oc // 2 % 2) * CH + CH] \
                if False else psS[2][:, (oc % 4) * CH:(oc % 4) * CH + CH]
            pw = (psS[2], psS[3])[oc // 4][:, (oc % 4) * CH:(oc % 4) * CH + CH]
            for j in range(NXC):
                t_wo = p.on("tensor", lambda e, pw=pw, oc=oc, j=j: e.matmul(
                    pw, woutb[:, j, oc * PC:(oc + 1) * PC], xdt[:, j, :],
                    start=(j == 0), stop=(j == NXC - 1)), inc=(1 if j == NXC - 1 else 0))
        p.wait("sync", "vector", p.wm("vector"))
        p.wait("sync", "gpsimd", g_st)
        xi_wm = p.dma("sync", xo_sb[:, :, :], fm_dram(io["x_in"], ND)[:, :, l0:l0 + CH])
        p.wait("vector", "sync", xi_wm)
        p.wait("vector", "tensor", t_wo)
        v_f = 0
        for oc in range(ND):
            pw = (psS[2], psS[3])[oc // 4][:, (oc % 4) * CH:(oc % 4) * CH + CH]
            v_f = p.on("vector", lambda e, pw=pw, oc=oc: e.tensor_add(
                xo_sb[:, oc, :], pw, xo_sb[:, oc, :]), inc=1)
        p.wait("gpsimd", "vector", v_f)
        g_st = p.dma("gpsimd", fm_dram(io["x_out"], ND)[:, :, l0:l0 + CH], xo_sb[:, :, :])
        p.wait("tensor", "vector", v_f)
    p.drain_dma("gpsimd")
    p.barrier()
    esS.close()


def declare_l0attnb(nc, io):
    io["b3_winT"] = nc.dram_tensor("b3_winT", [D, 4480], F32, kind="ExternalInput").ap()
    io["b3_woutT"] = nc.dram_tensor("b3_woutT", [2048, D], F32, kind="ExternalInput").ap()
    io["b3_convw"] = nc.dram_tensor("b3_convw", [18, PC, 4], F32, kind="ExternalInput").ap()
    io["b3_convb"] = nc.dram_tensor("b3_convb", [PC, 18], F32, kind="ExternalInput").ap()
    io["b3_alog"] = nc.dram_tensor("b3_alog", [PC, 1], F32, kind="ExternalInput").ap()
    io["b3_dtb"] = nc.dram_tensor("b3_dtb", [PC, 1], F32, kind="ExternalInput").ap()
    io["b3_dcol"] = nc.dram_tensor("b3_dcol", [PC, 16], F32, kind="ExternalInput").ap()
    io["b3_nw"] = nc.dram_tensor("b3_nw", [PC, 16], F32, kind="ExternalInput").ap()
    io["b3_mask64"] = nc.dram_tensor("b3_mask64", [64, 128], F32, kind="ExternalInput").ap()
    io["b3_p2"] = nc.dram_tensor("b3_p2", [16, 64, PC], F32, kind="ExternalInput").ap()
    io["b3_dmask"] = nc.dram_tensor("b3_dmask", [PC, 64 * 128], F32, kind="ExternalInput").ap()


def prep_l0attnb(inputs):
    d = {}
    W_in = inputs["L0_attn_b_W_in"]
    winp = np.zeros((4480, D), np.float32)
    winp[0:2048] = W_in[0:2048]
    winp[2048:4096] = W_in[2048:4096]
    winp[4096:4112] = W_in[4096:4112]
    winp[4224:4240] = W_in[4112:4128]
    winp[4352:4416] = W_in[4128:4192]
    d["b3_winT"] = np.ascontiguousarray(winp.T)
    d["b3_woutT"] = np.ascontiguousarray(inputs["L0_attn_b_W_out"].T)
    cw = inputs["L0_attn_b_conv_w"][:, 0, :]
    cwp = np.zeros((18 * PC, 4), np.float32)
    cwp[0:2048] = cw[0:2048]
    cwp[2048:2064] = cw[2048:2064]
    cwp[2176:2192] = cw[2064:2080]
    d["b3_convw"] = np.ascontiguousarray(cwp.reshape(18, PC, 4))
    cb = inputs["L0_attn_b_conv_b"]
    cbp = np.zeros((18 * PC,), np.float32)
    cbp[0:2048] = cb[0:2048]
    cbp[2048:2064] = cb[2048:2064]
    cbp[2176:2192] = cb[2064:2080]
    d["b3_convb"] = np.ascontiguousarray(cbp.reshape(18, PC).T)
    al = np.zeros((PC, 1), np.float32)
    al[0:64, 0] = inputs["L0_attn_b_A_log"]
    d["b3_alog"] = al
    db = np.zeros((PC, 1), np.float32)
    db[0:64, 0] = inputs["L0_attn_b_dt_bias"]
    d["b3_dtb"] = db
    d["b3_dcol"] = np.ascontiguousarray(
        np.repeat(inputs["L0_attn_b_D"], 32).reshape(16, PC).T)
    d["b3_nw"] = np.ascontiguousarray(inputs["L0_attn_b_norm_w"].reshape(16, PC).T)
    m64 = np.ones((64, 128), np.float32)
    m64[:, 0] = 0.0
    d["b3_mask64"] = m64
    p2 = np.zeros((16, 64, PC), np.float32)
    for j in range(16):
        for pp in range(PC):
            p2[j, (j * PC + pp) // 32, pp] = 1.0
    d["b3_p2"] = p2
    dm = np.zeros((PC, 64 * 128), np.float32)
    for h in range(64):
        for s in range(128):
            dm[s, h * 128 + s] = 1.0
    d["b3_dmask"] = dm
    return d



def emit_l1attna(p, es, io, C, tag):
    nc = p.nc
    TC = 512
    NT = T // TC
    NH = 16
    q_dram = nc.dram_tensor(tag + "qd", [PC, 24, T], BF16, kind="Internal").ap()
    ao_dram = nc.dram_tensor(tag + "aod", [PC, 8, T], BF16, kind="Internal").ap()

    esA = ExitStack()
    wqkv = fm(esA, nc, tag + "s_wqkv", ND, 3072, BF16)
    bq = sb(esA, nc, tag + "s_bq", [PC, 24], F32)
    cosq = sb(esA, nc, tag + "s_cosq", [PC, T], F32)
    sinq = sb(esA, nc, tag + "s_sinq", [PC, T], F32)
    cosk = sb(esA, nc, tag + "s_cosk", [PC, T], F32)
    sink = sb(esA, nc, tag + "s_sink", [PC, T], F32)
    g_w = p.dma("gpsimd", wqkv[:, :, :], fm_dram(io["a5_wqkvT"], ND))
    toks = [p.dma("sync", bq[:, :], io["a5_bqkv"]),
            p.dma("sync", cosq[:, :], io["a5_cosq"]),
            p.dma("sync", sinq[:, :], io["a5_sinq"]),
            p.dma("sync", cosk[:, :], io["a5_cosk"]),
            p.dma("sync", sink[:, :], io["a5_sink"])]
    p.wait("tensor", "gpsimd", g_w)
    for tok in toks:
        p.wait("tensor", "sync", tok)
        p.wait("vector", "sync", tok)

    x_sb = fm(esA, nc, tag + "x", ND, TC, F32)
    x2 = fm(esA, nc, tag + "x2", ND, TC, F32)
    rows = sb(esA, nc, tag + "rows", [1, 2 * TC], F32)
    xnb = fm(esA, nc, tag + "xnb", ND, TC, BF16)
    ev = sb(esA, nc, tag + "ev", [PC, TC], F32)
    qs = sb(esA, nc, tag + "qs", [PC, TC], F32)
    qb = sb(esA, nc, tag + "qb", [PC, TC], BF16)
    psA_ = ps(esA, nc, tag + "psA", [PC, 512])
    psB_ = ps(esA, nc, tag + "psB", [PC, 512])
    pool = [ps(esA, nc, tag + f"pp{i}", [PC, 512]) for i in range(3)]

    v_ev = 0
    for ci in range(NT):
        c0 = ci * TC
        p.wait("sync", "vector", v_ev)
        x_wm = p.dma("sync", x_sb[:, :, :], fm_dram(io["x_in"], ND)[:, :, c0:c0 + TC])
        for e2 in ("scalar", "tensor", "vector"):
            p.wait(e2, "sync", x_wm)
        emit_ln(p, C, x_sb, x2, TC, psA_, psB_, x2, rows, out_bf=xnb)
        p.wait("tensor", "vector", p.wm("vector"))
        for oc in range(24):
            pt = pool[oc % 3]
            t_p = 0
            for c in range(ND):
                t_p = p.on("tensor", lambda e, pt=pt, oc=oc, c=c: e.matmul(
                    pt[:, :], wqkv[:, c, oc * PC:(oc + 1) * PC], xnb[:, c, :],
                    start=(c == 0), stop=(c == ND - 1)), inc=(1 if c == ND - 1 else 0))
            p.wait("vector", "tensor", t_p)
            p.on("vector", lambda e, pt=pt, oc=oc: e.tensor_scalar(
                ev[:, :], pt[:, :], bq[:, oc:oc + 1], None, ALU.add))
            if oc < 16:
                cos_t, sin_t = (cosq, sinq) if oc < 8 else (cosk, sink)
                p.on("vector", lambda e: e.tensor_scalar(
                    qs[0:32, :], ev[32:64, :], -1.0, None, ALU.mult))
                p.on("vector", lambda e: e.tensor_copy(qs[32:64, :], ev[0:32, :]))
                p.on("vector", lambda e: e.tensor_scalar(
                    qs[64:96, :], ev[96:128, :], -1.0, None, ALU.mult))
                p.on("vector", lambda e: e.tensor_copy(qs[96:128, :], ev[64:96, :]))
                p.on("vector", lambda e, cos_t=cos_t, c0=c0: e.tensor_mul(
                    ev[:, :], ev[:, :], cos_t[:, c0:c0 + TC]))
                p.on("vector", lambda e, sin_t=sin_t, c0=c0: e.tensor_mul(
                    qs[:, :], qs[:, :], sin_t[:, c0:c0 + TC]))
                p.on("vector", lambda e: e.tensor_add(ev[:, :], ev[:, :], qs[:, :]))
            v_ev = p.on("vector", lambda e: e.tensor_copy(qb[:, :], ev[:, :]), inc=1)
            p.wait("tensor", "vector", v_ev)
            p.wait("gpsimd", "vector", v_ev)
            g_sp = p.dma("gpsimd", q_dram[:, oc, c0:c0 + TC], qb[:, :])
            p.wait("vector", "gpsimd", g_sp)
    p.barrier()
    esA.close()

    esB = ExitStack()
    kh = sb(esB, nc, tag + "kh", [PC, T], BF16)
    vh = sb(esB, nc, tag + "vh", [PC, T], BF16)
    vt = sb(esB, nc, tag + "vt", [PC, 32, 65], BF16)
    qt_b = sb(esB, nc, tag + "qtb", [PC, TC], BF16)
    P_sb = [sb(esB, nc, tag + f"Psb{i}", [PC, 1024], BF16) for i in range(2)]
    zrow = sb(esB, nc, tag + "zr", [1, TC], F32)
    zb = sb(esB, nc, tag + "zb", [64, TC], F32)
    ao = sb(esB, nc, tag + "ao", [64, TC], BF16)
    ps_s = [ps(esB, nc, tag + f"pss{i}", [PC, 1024]) for i in range(2)]
    ps_o = ps(esB, nc, tag + "pso", [PC, 512])
    ps_tr = esB.enter_context(nc.psum_tensor(tag + "pstr", [PC, 64], BF16))
    ps_z = ps(esB, nc, tag + "psz", [64, 512])
    v_z1 = p.on("vector", lambda e: e.memset(vt[:, :, 64:65], 1.0), inc=1)
    p.wait("vector", "vector", v_z1)

    for h in range(NH):
        hrow = (h % 2) * 64
        hc = h // 2
        p.wait("sync", "tensor", p.wm("tensor"))
        p.wait("sync", "vector", p.wm("vector"))
        p.dma("sync", kh[hrow:hrow + 64, :], q_dram[hrow:hrow + 64, 8 + hc, :])
        v_wm = p.dma("sync", vh[hrow:hrow + 64, :], q_dram[hrow:hrow + 64, 16 + hc, :])
        p.wait("tensor", "sync", v_wm)
        for kt in range(32):
            t_tr = p.on("tensor", lambda e, kt=kt, hrow=hrow: e.transpose(
                ps_tr[:, :], vh[hrow:hrow + 64, kt * 128:(kt + 1) * 128],
                C["identbf"][hrow:hrow + 64, hrow:hrow + 64]), inc=1)
            p.wait("vector", "tensor", t_tr)
            v_vt = p.on("vector", lambda e, kt=kt: e.tensor_copy(
                vt[:, kt, 0:64], ps_tr[:, :]), inc=1)
            p.wait("tensor", "vector", v_vt)
        for qt in range(NT):
            q0 = qt * TC
            p.wait("sync", "tensor", p.wm("tensor"))
            q_wm = p.dma("sync", qt_b[hrow:hrow + 64, :],
                         q_dram[hrow:hrow + 64, hc, q0:q0 + TC])
            p.wait("tensor", "sync", q_wm)
            av_wm = {0: 0, 1: 0}
            s_exps = {}

            def emit_qk(kg):
                pss = ps_s[kg % 2]
                t_s = 0
                for j in range(2):
                    kt = kg * 2 + j
                    t_s = p.on("tensor", lambda e, pss=pss, j=j, kt=kt, hrow=hrow:
                               e.matmul(pss[:, j * 512:(j + 1) * 512],
                                        kh[hrow:hrow + 64, kt * 128:(kt + 1) * 128],
                                        qt_b[hrow:hrow + 64, :],
                                        start=True, stop=True), inc=(1 if j == 1 else 0))
                p.wait("scalar", "tensor", t_s)
                p.wait("scalar", "tensor", av_wm[kg % 2])
                s_exps[kg] = p.on("scalar", lambda e, pss=pss, kg=kg: e.activation(
                    P_sb[kg % 2][:, :], pss[:, :], AF.Exp), inc=1)

            emit_qk(0)
            for kg in range(16):
                if kg + 1 < 16:
                    emit_qk(kg + 1)
                p.wait("tensor", "scalar", s_exps[kg])
                for j in range(2):
                    kt = kg * 2 + j
                    av_wm[kg % 2] = p.on("tensor", lambda e, j=j, kt=kt, kg=kg: e.matmul(
                        ps_o[0:65, :], vt[:, kt, :], P_sb[kg % 2][:, j * 512:(j + 1) * 512],
                        start=(kt == 0), stop=(kt == 31)), inc=1)
            t_av = p.wm("tensor")
            p.wait("scalar", "tensor", t_av)
            p.on("scalar", lambda e: e.activation(zrow[:, :], ps_o[64:65, :], AF.Ln))
            s_z = p.on("scalar", lambda e: e.activation(
                zrow[:, :], zrow[:, :], AF.Exp, scale=-1.0), inc=1)
            p.wait("tensor", "scalar", s_z)
            t_zb = p.on("tensor", lambda e: e.matmul(
                ps_z[:, :], C["ones1"][0:1, 0:64], zrow[:, :], start=True, stop=True), inc=1)
            p.wait("vector", "tensor", t_zb)
            p.on("vector", lambda e: e.tensor_copy(zb[:, :], ps_z[:, :]))
            v_ao = p.on("vector", lambda e: e.tensor_mul(
                ao[:, :], zb[:, :], ps_o[0:64, :]), inc=1)
            p.wait("gpsimd", "vector", v_ao)
            g_ao = p.dma("gpsimd", ao_dram[hrow:hrow + 64, hc, q0:q0 + TC], ao[:, :])
            p.wait("vector", "gpsimd", g_ao)
            p.wait("tensor", "vector", p.wm("vector"))
    p.barrier()
    esB.close()

    esC = ExitStack()
    wout = fm(esC, nc, tag + "s_wout", ND, D, BF16)
    bo = sb(esC, nc, tag + "s_bo", [PC, 8], F32)
    aob = fm(esC, nc, tag + "aob", ND, TC, BF16)
    x_sb2 = fm(esC, nc, tag + "x2b", ND, TC, F32)
    psO = [ps(esC, nc, tag + f"psO{i}", [PC, 512]) for i in range(4)]
    g_wo = p.dma("gpsimd", wout[:, :, :], fm_dram(io["a5_woutT"], ND))
    g_w2 = p.dma("sync", bo[:, :], io["a5_bout"])
    p.wait("tensor", "gpsimd", g_wo)
    p.wait("vector", "sync", g_w2)
    v_prev = 0
    g_st = 0
    for ci in range(NT):
        c0 = ci * TC
        p.wait("sync", "vector", v_prev)
        p.wait("sync", "gpsimd", g_st)
        x_wm = p.dma("sync", x_sb2[:, :, :], fm_dram(io["x_in"], ND)[:, :, c0:c0 + TC])
        a_wm = p.dma("sync", aob[:, :, :], ao_dram[:, :, c0:c0 + TC])
        for e2 in ("tensor", "vector"):
            p.wait(e2, "sync", a_wm)
        v_prev = 0
        for f in range(2):
            fs = f * 256
            t_p = 0
            for oc in range(ND):
                pt = psO[oc // 2][:, (oc % 2) * 256:(oc % 2) * 256 + 256]
                for c in range(ND):
                    t_p = p.on("tensor", lambda e, pt=pt, oc=oc, c=c, fs=fs: e.matmul(
                        pt, wout[:, c, oc * PC:(oc + 1) * PC], aob[:, c, fs:fs + 256],
                        start=(c == 0), stop=(c == ND - 1)),
                        inc=(1 if c == ND - 1 else 0))
            p.wait("vector", "tensor", t_p)
            for oc in range(ND):
                pt = psO[oc // 2][:, (oc % 2) * 256:(oc % 2) * 256 + 256]
                v_prev = p.on("vector", lambda e, pt=pt, oc=oc, fs=fs:
                              e.scalar_tensor_tensor(
                                  x_sb2[:, oc, fs:fs + 256], pt, bo[:, oc:oc + 1],
                                  x_sb2[:, oc, fs:fs + 256], ALU.add, ALU.add), inc=1)
            if f == 0:
                p.wait("tensor", "vector", v_prev)
        p.wait("gpsimd", "vector", v_prev)
        g_st = p.dma("gpsimd", fm_dram(io["x_out"], ND)[:, :, c0:c0 + TC], x_sb2[:, :, :])
    p.drain_dma("gpsimd")
    p.barrier()
    esC.close()


def declare_l1attna(nc, io):
    io["a5_wqkvT"] = nc.dram_tensor("a5_wqkvT", [D, 3072], F32, kind="ExternalInput").ap()
    io["a5_bqkv"] = nc.dram_tensor("a5_bqkv", [PC, 24], F32, kind="ExternalInput").ap()
    io["a5_woutT"] = nc.dram_tensor("a5_woutT", [D, D], F32, kind="ExternalInput").ap()
    io["a5_bout"] = nc.dram_tensor("a5_bout", [PC, 8], F32, kind="ExternalInput").ap()
    for nm in ("cosq", "sinq", "cosk", "sink"):
        io["a5_" + nm] = nc.dram_tensor("a5_" + nm, [PC, T], F32, kind="ExternalInput").ap()


def prep_l1attna(inputs):
    d = {}
    d["a5_wqkvT"] = np.ascontiguousarray(inputs["L1_attn_a_W_qkv"].T)
    d["a5_bqkv"] = np.ascontiguousarray(inputs["L1_attn_a_b_qkv"].reshape(24, PC).T)
    d["a5_woutT"] = np.ascontiguousarray(inputs["L1_attn_a_W_out"].T)
    d["a5_bout"] = np.ascontiguousarray(inputs["L1_attn_a_b_out"].reshape(8, PC).T)
    hd = 64
    inv_freq = 1.0 / 10000.0 ** (np.arange(0, hd, 2, dtype=np.float64) / hd)
    t = np.arange(T, dtype=np.float64)
    freqs = np.einsum("i,j->ij", t, inv_freq).astype(np.float32)
    emb = np.concatenate([freqs, freqs], axis=-1)
    cos = np.cos(emb).T.astype(np.float32)
    sin = np.sin(emb).T.astype(np.float32)
    cos2 = np.concatenate([cos, cos], 0)
    sin2 = np.concatenate([sin, sin], 0)
    d["a5_cosq"] = np.ascontiguousarray(cos2 * 0.125)
    d["a5_sinq"] = np.ascontiguousarray(sin2 * 0.125)
    d["a5_cosk"] = np.ascontiguousarray(cos2)
    d["a5_sink"] = np.ascontiguousarray(sin2)
    return d



def declare_common(nc, io):
    io["x_in"] = nc.dram_tensor("x_in", [D, T], F32, kind="ExternalInput").ap()
    io["x_out"] = nc.dram_tensor("x_out", [D, T], F32, kind="ExternalOutput").ap()
    io["c_invn"] = nc.dram_tensor("c_invn", [PC, 1], F32, kind="ExternalInput").ap()
    io["c_ones1"] = nc.dram_tensor("c_ones1", [1, PC], F32, kind="ExternalInput").ap()
    io["c_ones8col"] = nc.dram_tensor("c_ones8col", [8, 1], F32, kind="ExternalInput").ap()
    io["c_vals"] = nc.dram_tensor("c_vals", [PC, len(CONST_VALS)], F32,
                                  kind="ExternalInput").ap()
    io["c_ident"] = nc.dram_tensor("c_ident", [PC, PC], F32, kind="ExternalInput").ap()


def host_consts():
    return {
        "c_invn": np.full((PC, 1), 1.0 / D, np.float32),
        "c_ones1": np.ones((1, PC), np.float32),
        "c_ones8col": np.ones((8, 1), np.float32),
        "c_vals": np.tile(np.asarray(CONST_VALS, np.float32)[None, :], (PC, 1)),
        "c_ident": np.eye(PC, dtype=np.float32),
    }


BF16_SHADOW = ["ffn0_wcpT", "ffn0_bd2", "a1_wk", "a1_wv", "a1_wr", "a1_wo",
               "a1_cmwk", "a1_cmwv", "a1_cmwr", "b3_winT", "b3_woutT"]


def add_bf16_shadows(nc, p, io):
    toks = []

    def shadow(name, ap):
        sh = nc.dram_tensor(f"bfsh_{name}", list(ap.shape), BF16,
                            kind="Internal").ap()
        n = ap.shape[0]
        step = max(1, (8 << 20) // max(1, int(np.prod(ap.shape[1:])) * 4))
        for r0 in range(0, n, step):
            r1 = min(n, r0 + step)
            toks.append(p.dma("gpsimd", sh[r0:r1], ap[r0:r1],
                              slot=f"bfsh_{name}"))
        return sh

    for key in BF16_SHADOW:
        if key not in io:
            continue
        v = io[key]
        if isinstance(v, list):
            io[key] = [shadow(f"{key}_{j}", ap) for j, ap in enumerate(v)]
        else:
            io[key] = shadow(key, v)
    for t in toks:
        for e2 in ENGS:
            p.wait(e2, t[0], t)


def _build(declares, emits):
    nc = bass.Bass("TRN2", target_bir_lowering=False)
    io = {}
    declare_common(nc, io)
    for d in declares:
        d(nc, io)
    es = ExitStack()
    p = Prog(nc, es)
    add_bf16_shadows(nc, p, io)
    C = load_consts(p, es, nc, io)
    C["ones8col"] = sb(es, nc, "sb_o8", [8, 1], F32)[:, :]
    wm = p.dma("sync", C["ones8col"], io["c_ones8col"])
    p.wait("tensor", "sync", wm)
    for em in emits:
        em(p, es, io, C)
    block = es.enter_context(nc.Block())
    p.finalize(block)
    es.close()
    return nc


def _run(nc, x_full, weight_dicts):
    cc = host_consts()
    in_maps = []
    for core in range(8):
        b = core // 2
        m = {"x_in": np.ascontiguousarray(x_full[b].T)}
        for wd in weight_dicts:
            m.update(wd)
        m.update(cc)
        in_maps.append(m)
    res = run_bass_kernel_spmd(nc, in_maps, core_ids=list(range(8)))
    return np.stack([np.asarray(res.results[2 * b]["x_out"]).T for b in range(B)])


FUSED_EMITS = [
    ("a1_", lambda p, es, io, C, tag: emit_l0attna(p, es, io, C, tag)),
    ("f0_", lambda p, es, io, C, tag: emit_l0ffn(p, es, io, C, tag)),
    ("b3_", lambda p, es, io, C, tag: emit_l0attnb(p, es, io, C, tag)),
    ("f2_", lambda p, es, io, C, tag: emit_l0ffn(p, es, io, C, tag)),
    ("a5_", lambda p, es, io, C, tag: emit_l1attna(p, es, io, C, tag)),
    ("f1_", lambda p, es, io, C, tag: emit_l1ffn(p, es, io, C, tag)),
    ("b7_", lambda p, es, io, C, tag: emit_l1attnb(p, es, io, C, tag)),
    ("f3_", lambda p, es, io, C, tag: emit_l1ffn(p, es, io, C, tag)),
]


def build_fused():
    nc = bass.Bass("TRN2", target_bir_lowering=False)
    io = {}
    declare_common(nc, io)
    for d in (declare_l0attna, declare_l0ffn, declare_l0attnb,
              declare_l1attna, declare_l1ffn, declare_l1attnb):
        d(nc, io)
    xs = [io["x_in"]]
    for i in range(len(FUSED_EMITS) - 1):
        xs.append(nc.dram_tensor(f"xh{i}", [D, T], F32, kind="Internal").ap())
    xs.append(io["x_out"])
    es = ExitStack()
    p = Prog(nc, es)
    add_bf16_shadows(nc, p, io)
    C = load_consts(p, es, nc, io)
    C["ones8col"] = sb(es, nc, "sb_o8", [8, 1], F32)[:, :]
    wm = p.dma("sync", C["ones8col"], io["c_ones8col"])
    p.wait("tensor", "sync", wm)
    for k_, (tag, em) in enumerate(FUSED_EMITS):
        io_k = dict(io)
        io_k["x_in"] = xs[k_]
        io_k["x_out"] = xs[k_ + 1]
        with ExitStack() as es_k:
            em(p, es_k, io_k, C, tag)
        p.barrier()
        p.recycle_dma_sems()
    block = es.enter_context(nc.Block())
    p.finalize(block)
    es.close()
    return nc


def fused_weight_dict(inputs):
    wd = {}
    for prep in (prep_l0attna, prep_l0ffn, prep_l0attnb,
                 prep_l1attna, prep_l1ffn, prep_l1attnb):
        d = prep(inputs)
        clash = set(d) & set(wd)
        assert not clash, clash
        wd.update(d)
    return wd


def run_fused(x_full, inputs):
    nc = build_fused()
    return _run(nc, x_full, [fused_weight_dict(inputs)])


def run_l1ffn(x_full, inputs):
    nc = _build([declare_l1ffn], [lambda p, es, io, C: emit_l1ffn(p, es, io, C, "f1_")])
    return _run(nc, x_full, [prep_l1ffn(inputs)])


def run_l0attnb(x_full, inputs):
    nc = _build([declare_l0attnb], [lambda p, es, io, C: emit_l0attnb(p, es, io, C, "b3_")])
    return _run(nc, x_full, [prep_l0attnb(inputs)])


def run_l0attna(x_full, inputs, phases="ABCD"):
    nc = _build([declare_l0attna],
                [lambda p, es, io, C: emit_l0attna(p, es, io, C, "a1_", phases=phases)])
    return _run(nc, x_full, [prep_l0attna(inputs)])


def run_l1attna(x_full, inputs):
    nc = _build([declare_l1attna], [lambda p, es, io, C: emit_l1attna(p, es, io, C, "a5_")])
    return _run(nc, x_full, [prep_l1attna(inputs)])


def run_l1attnb(x_full, inputs, debug=False):
    def decl(nc, io):
        declare_l1attnb(nc, io)
        if debug:
            io["dbg"] = {"o1": nc.dram_tensor("dbg_o1", [PC, ND, T], BF16,
                                              kind="ExternalOutput").ap()}
            for nm in ("Soc", "woc", "uoc"):
                io["dbg"][nm] = nc.dram_tensor("dbg_" + nm, [PC, T], F32,
                                               kind="ExternalOutput").ap()
            io["dbg"]["koc"] = nc.dram_tensor("dbg_koc", [PC, T], BF16,
                                              kind="ExternalOutput").ap()
            io["dbg"]["ksum"] = nc.dram_tensor("dbg_ksum", [2, T], BF16,
                                               kind="ExternalOutput").ap()
    nc = _build([decl], [lambda p, es, io, C: emit_l1attnb(p, es, io, C, "b7_",
                                                          dbg=io.get("dbg"))])
    cc = host_consts()
    wd = prep_l1attnb(inputs)
    in_maps = []
    for core in range(8):
        bb = core // 2
        m = {"x_in": np.ascontiguousarray(x_full[bb].T)}
        m.update(wd)
        m.update(cc)
        in_maps.append(m)
    res = run_bass_kernel_spmd(nc, in_maps, core_ids=list(range(8)))
    out = np.stack([np.asarray(res.results[2 * bb]["x_out"]).T for bb in range(B)])
    if debug:
        return out, {k2: np.asarray(res.results[0]["dbg_" + k2])
                     for k2 in ("o1", "Soc", "woc", "uoc", "koc", "ksum")}
    return out


def run_l0ffn(x_full, inputs, debug=False):
    def decl_dbg(nc, io):
        declare_l0ffn(nc, io)
        if debug:
            io["dbg"] = {
                "yh": nc.dram_tensor("dbg_yh", [PC, ND, 512], BF16, kind="ExternalOutput").ap(),
                "hall": nc.dram_tensor("dbg_hall", [PC, 128, 512], BF16, kind="ExternalOutput").ap(),
                "xnb": nc.dram_tensor("dbg_xnb", [PC, ND, 512], BF16, kind="ExternalOutput").ap(),
            }
    nc = _build([decl_dbg], [lambda p, es, io, C: emit_l0ffn(p, es, io, C, "f0_",
                                                            dbg=io.get("dbg"))])
    cc = host_consts()
    wd = prep_l0ffn(inputs)
    in_maps = []
    for core in range(8):
        bb = core // 2
        m = {"x_in": np.ascontiguousarray(x_full[bb].T)}
        m.update(wd)
        m.update(cc)
        in_maps.append(m)
    res = run_bass_kernel_spmd(nc, in_maps, core_ids=list(range(8)))
    out = np.stack([np.asarray(res.results[2 * bb]["x_out"]).T for bb in range(B)])
    if debug:
        return out, {k2: np.asarray(res.results[0][f"dbg_{k2}"]) for k2 in ("yh", "hall", "xnb")}
    return out



def kernel(**inputs):
    x = np.ascontiguousarray(np.asarray(inputs["L0_attn_a_x"], np.float32))
    nc = build_fused()
    return _run(nc, x, [fused_weight_dict(inputs)])


# version 04
# speedup vs baseline: 1.4585x; 1.1627x over previous
import numpy as np
import ml_dtypes
from contextlib import ExitStack
import concourse.bass as bass
import concourse.mybir as mybir
from concourse.bass_utils import run_bass_kernel_spmd

F32 = mybir.dt.float32
BF16 = mybir.dt.bfloat16
AF = mybir.ActivationFunctionType
ALU = mybir.AluOpType

B, T, D = 4, 4096, 1024
PC = 128
ND = D // PC
ENGS = ("tensor", "vector", "scalar", "gpsimd", "sync")


class Prog:

    def __init__(self, nc, es):
        self.nc = nc
        self.es = es
        self.q = {e: [] for e in ENGS}
        self.sems = {e: es.enter_context(nc.semaphore(f"pm_{e}")) for e in ENGS}
        self.cnt = {e: 0 for e in ENGS}
        self.dsems = {}
        self.dcnt = {}
        self.free_dsems = []
        self.sem_serial = 0

    def nsem(self):
        self.sem_serial += 1
        return self.sem_serial

    def recycle_dma_sems(self):
        for slot, sem in self.dsems.items():
            self.free_dsems.append((sem, self.dcnt[slot]))
        self.dsems = {}
        self.dcnt = {}

    def on(self, eng, fn, inc=0):
        self.cnt[eng] += inc
        sem = self.sems[eng]

        def thunk(h, fn=fn, inc=inc, sem=sem):
            inst = fn(h)
            if inc:
                inst.then_inc(sem, inc)
        self.q[eng].append(thunk)
        return self.cnt[eng]

    def dma(self, eng, out, in_, slot=None):
        if slot is None:
            t = out.tensor if hasattr(out, "tensor") else out
            slot = getattr(t, "name", None) or str(id(out))
        if slot not in self.dsems:
            if self.free_dsems:
                sem, base = self.free_dsems.pop()
                self.dsems[slot] = sem
                self.dcnt[slot] = base
            else:
                self.dsems[slot] = self.es.enter_context(
                    self.nc.semaphore(f"dm_{self.nsem()}"))
                self.dcnt[slot] = 0
        self.dcnt[slot] += 16
        sem = self.dsems[slot]

        def thunk(h, o=out, i=in_, sem=sem):
            h.dma_start(out=o, in_=i).then_inc(sem, 16)
        self.q[eng].append(thunk)
        return ("D", slot, self.dcnt[slot])

    def wait(self, eng, other, count):
        if isinstance(count, tuple) and count and count[0] == "D":
            _, slot, c = count
            sem = self.dsems[slot]
            self.q[eng].append(lambda h, sem=sem, c=c: h.wait_ge(sem, c))
            return
        if not isinstance(count, int) or count <= 0:
            return
        sem = self.sems[other]
        self.q[eng].append(lambda h, sem=sem, c=count: h.wait_ge(sem, c))

    def wm(self, eng):
        return self.cnt[eng]

    def barrier(self, engs=ENGS):
        marks = {e: self.cnt[e] for e in engs}
        dmarks = dict(self.dcnt)
        for e in engs:
            for o in engs:
                if o != e:
                    self.wait(e, o, marks[o])
            for slot, c in dmarks.items():
                if c > 0:
                    sem = self.dsems[slot]
                    self.q[e].append(lambda h, sem=sem, c=c: h.wait_ge(sem, c))

    def dma_last(self, slot):
        return ("D", slot, self.dcnt.get(slot, 0)) if self.dcnt.get(slot, 0) else 0

    def drain_dma(self, eng):
        for slot, c in self.dcnt.items():
            if c > 0:
                sem = self.dsems[slot]
                self.q[eng].append(lambda h, sem=sem, c=c: h.wait_ge(sem, c))

    def finalize(self, block):
        for e, thunks in self.q.items():
            def body(h, thunks=thunks):
                for t in thunks:
                    t(h)
            getattr(block, e)(body)


def sb(es, nc, name, shape, dt):
    return es.enter_context(nc.sbuf_tensor(name, shape, dt))


def ps(es, nc, name, shape):
    return es.enter_context(nc.psum_tensor(name, shape, F32))


def fm(es, nc, name, nchunks, n, dt):
    return es.enter_context(nc.sbuf_tensor(name, [PC, nchunks, n], dt))


def fm_dram(dram_ap, nchunks):
    return dram_ap.rearrange("(c p) n -> p c n", p=PC)


CONST_VALS = [1e-5, -0.5, 0.5, 1e-8, 1.1920929e-7, 60.0, -1.0, 1.0, 0.0,
              0.99, 0.125, 2.0, -2.0]


def load_consts(p, es, nc, io):
    C = {}
    C["invn"] = sb(es, nc, "sb_invn", [PC, 1], F32)[:, :]
    C["ones1"] = sb(es, nc, "sb_ones1", [1, PC], F32)[:, :]
    C["identbf"] = sb(es, nc, "sb_identbf", [PC, PC], BF16)[:, :]
    C["ones1b"] = sb(es, nc, "sb_ones1b", [1, PC], BF16)[:, :]
    p.dma("gpsimd", C["identbf"], io["c_ident"])
    gm = p.dma("gpsimd", C["ones1b"], io["c_ones1"])
    cv = sb(es, nc, "sb_cvals", [PC, len(CONST_VALS)], F32)
    for i, v in enumerate(CONST_VALS):
        nc.const_aps.register(F32, float(v), cv[:, i:i + 1])
    C["cval"] = {float(v): cv[:, i:i + 1] for i, v in enumerate(CONST_VALS)}
    p.dma("sync", C["invn"], io["c_invn"])
    p.dma("sync", cv[:, :], io["c_vals"])
    wm = p.dma("sync", C["ones1"], io["c_ones1"])
    for e2 in ("tensor", "vector", "scalar", "gpsimd"):
        p.wait(e2, "sync", wm)
        p.wait(e2, "sync", p.dma_last("sb_invn"))
        p.wait(e2, "sync", p.dma_last("sb_cvals"))
        p.wait(e2, "gpsimd", p.dma_last("sb_identbf"))
        p.wait(e2, "gpsimd", gm)
    return C



def emit_ln(p, C, x_sb, xn_sb, tc, ps_m, ps_r, x2_sb, rows, gcols=None,
            bcols=None, eps=1e-5, out_bf=None, outs=None):
    p.wait("scalar", "vector", p.wm("vector"))
    s_sq = p.on("scalar", lambda e: e.activation(
        x2_sb[:, :, 0:tc], x_sb[:, :, 0:tc], AF.Square), inc=1)
    p.wait("tensor", "scalar", s_sq)
    p.wait("tensor", "vector", p.wm("vector"))
    mrow = rows[0:1, 0:tc]
    vrow = rows[0:1, tc:2 * tc]
    t_mm = 0
    for src, dst in ((x_sb, ps_m), (x2_sb, ps_r)):
        for c in range(ND):
            t_mm = p.on("tensor", lambda e, src=src, dst=dst, c=c: e.matmul(
                dst[0:1, 0:tc], C["invn"], src[:, c, 0:tc],
                start=(c == 0), stop=(c == ND - 1)), inc=(1 if c == ND - 1 else 0))
    p.wait("vector", "tensor", t_mm)
    p.on("vector", lambda e: e.tensor_copy(mrow, ps_m[0:1, 0:tc]))
    p.on("vector", lambda e: e.tensor_mul(vrow, mrow, mrow))
    v_var = p.on("vector", lambda e: e.tensor_sub(vrow, ps_r[0:1, 0:tc], vrow), inc=1)
    p.wait("scalar", "vector", v_var)
    p.on("scalar", lambda e: e.activation(vrow, vrow, AF.Ln, bias=eps))
    s_rs = p.on("scalar", lambda e: e.activation(vrow, vrow, AF.Exp, scale=-0.5), inc=1)
    p.wait("tensor", "scalar", s_rs)
    p.on("tensor", lambda e: e.matmul(ps_m[:, 0:tc], C["ones1"], mrow,
                                      start=True, stop=True))
    t_bc = p.on("tensor", lambda e: e.matmul(ps_r[:, 0:tc], C["ones1"], vrow,
                                             start=True, stop=True), inc=1)
    p.wait("vector", "tensor", t_bc)
    if outs is None:
        outs = lambda c: xn_sb[:, c, 0:tc]
    v_last = 0
    for c in range(ND):
        p.on("vector", lambda e, c=c: e.tensor_sub(
            outs(c), x_sb[:, c, 0:tc], ps_m[:, 0:tc]))
        if gcols is None:
            v_last = p.on("vector", lambda e, c=c: e.tensor_mul(
                outs(c), outs(c), ps_r[:, 0:tc]), inc=1)
        else:
            p.on("vector", lambda e, c=c: e.tensor_mul(
                outs(c), outs(c), ps_r[:, 0:tc]))
            v_last = p.on("vector", lambda e, c=c: e.tensor_scalar(
                outs(c), outs(c), gcols[:, c:c + 1],
                None if bcols is None else bcols[:, c:c + 1],
                ALU.mult, ALU.mult if bcols is None else ALU.add), inc=1)
        if out_bf is not None:
            v_last = p.on("vector", lambda e, c=c: e.tensor_copy(
                out_bf[:, c, 0:tc], outs(c)), inc=1)
    return v_last



def emit_l1ffn(p, es, io, C, tag):
    nc = p.nc
    TC = 512
    wb = fm(es, nc, tag + "wb", ND, D, BF16)
    wA = fm(es, nc, tag + "wA", ND, 8, BF16)
    wR = fm(es, nc, tag + "wR", ND, 8, BF16)
    wB = sb(es, nc, tag + "wB", [128, D], BF16)
    eG = sb(es, nc, tag + "eG", [8, 64], BF16)
    eM = sb(es, nc, tag + "eM", [8, 64], BF16)
    bbc = sb(es, nc, tag + "bbc", [PC, ND], F32)
    rb = sb(es, nc, tag + "rb", [8, 1], F32)
    wtoks = [p.dma("gpsimd", wb[:, :, :], fm_dram(io["ffn_base_w"], ND)),
             p.dma("gpsimd", wA[:, :, :], fm_dram(io["ffn_lora_A"], ND)),
             p.dma("gpsimd", wR[:, :, :], fm_dram(io["ffn_router_w"], ND)),
             p.dma("gpsimd", wB[0:64, :], io["ffn_lora_B"]),
             p.dma("gpsimd", eG[:, :], io["ffn_eG"]),
             p.dma("gpsimd", eM[:, :], io["ffn_eM"]),
             p.dma("sync", bbc[:, :], io["ffn_base_b"]),
             p.dma("sync", rb[:, :], io["ffn_router_b"])]
    v_z = p.on("vector", lambda e: e.memset(wB[64:128, :], 0.0), inc=1)
    for tok in wtoks:
        p.wait("tensor", "gpsimd", tok)
        p.wait("vector", "gpsimd", tok)
    p.wait("tensor", "vector", v_z)

    x_sb = fm(es, nc, tag + "x", ND, TC, F32)
    xn = fm(es, nc, tag + "xn", ND, TC, F32)
    xnb = fm(es, nc, tag + "xnb", ND, TC, BF16)
    x2 = fm(es, nc, tag + "x2", ND, TC, F32)
    rows = sb(es, nc, tag + "rows", [1, 2 * TC], F32)
    prows = sb(es, nc, tag + "prows", [8, TC], F32)
    gat = sb(es, nc, tag + "gat", [8, TC], BF16)
    midb = sb(es, nc, tag + "midb", [8, TC], BF16)
    mge = sb(es, nc, tag + "mge", [64, TC], F32)
    mg = sb(es, nc, tag + "mg", [128, TC], BF16)
    zrow = sb(es, nc, tag + "zrow", [1, TC], F32)
    xo = fm(es, nc, tag + "xo", ND, TC, F32)
    psA = ps(es, nc, tag + "psA", [PC, 512])
    psB = ps(es, nc, tag + "psB", [PC, 512])
    psC = ps(es, nc, tag + "psC", [PC, 512])
    psO = [ps(es, nc, tag + f"psO{i}", [PC, 512]) for i in range(4)]
    v_z2 = p.on("vector", lambda e: e.memset(mg[64:128, :], 0.0), inc=1)

    v_out_prev = v_z2
    g_st_prev = 0
    for ci in range(T // TC):
        c0 = ci * TC
        p.wait("sync", "vector", v_out_prev)
        x_wm = p.dma("sync", x_sb[:, :, :], fm_dram(io["x_in"], ND)[:, :, c0:c0 + TC])
        for e2 in ("scalar", "tensor", "vector"):
            p.wait(e2, "sync", x_wm)
        emit_ln(p, C, x_sb, xn, TC, psA, psB, x2, rows, out_bf=xnb)
        p.wait("tensor", "vector", p.wm("vector"))
        t_g = 0
        for c in range(ND):
            t_g = p.on("tensor", lambda e, c=c: e.matmul(
                psC[0:8, :], wR[:, c, :], xnb[:, c, :], start=(c == 0), stop=(c == ND - 1)),
                inc=(1 if c == ND - 1 else 0))
        p.wait("vector", "tensor", t_g)
        v_lg = p.on("vector", lambda e: e.tensor_scalar(
            prows[:, :], psC[0:8, :], rb[:, :], None, ALU.add), inc=1)
        p.wait("scalar", "vector", v_lg)
        s_pe = p.on("scalar", lambda e: e.activation(prows[:, :], prows[:, :], AF.Exp), inc=1)
        p.wait("tensor", "scalar", s_pe)
        t_z = p.on("tensor", lambda e: e.matmul(
            psC[0:1, :], C["ones8col"], prows[:, :], start=True, stop=True), inc=1)
        p.wait("vector", "tensor", t_z)
        v_zc = p.on("vector", lambda e: e.tensor_copy(zrow[:, :], psC[0:1, :]), inc=1)
        p.wait("scalar", "vector", v_zc)
        p.on("scalar", lambda e: e.activation(zrow[:, :], zrow[:, :], AF.Ln))
        s_rz = p.on("scalar", lambda e: e.activation(
            zrow[:, :], zrow[:, :], AF.Exp, scale=-1.0), inc=1)
        p.wait("tensor", "scalar", s_rz)
        t_rz = p.on("tensor", lambda e: e.matmul(
            psC[0:8, :], C["ones1"][0:1, 0:8], zrow[:, :], start=True, stop=True), inc=1)
        p.wait("vector", "tensor", t_rz)
        v_gat = p.on("vector", lambda e: e.tensor_mul(gat[:, :], prows[:, :], psC[0:8, :]), inc=1)
        t_mid = 0
        for c in range(ND):
            t_mid = p.on("tensor", lambda e, c=c: e.matmul(
                psB[0:8, :], wA[:, c, :], xnb[:, c, :], start=(c == 0), stop=(c == ND - 1)),
                inc=(1 if c == ND - 1 else 0))
        p.wait("vector", "tensor", t_mid)
        v_mid = p.on("vector", lambda e: e.tensor_copy(midb[:, :], psB[0:8, :]), inc=1)
        p.wait("tensor", "vector", max(v_gat, v_mid))
        t_ge = p.on("tensor", lambda e: e.matmul(
            psA[0:64, :], eG[:, :], gat[:, :], start=True, stop=True), inc=1)
        t_me = p.on("tensor", lambda e: e.matmul(
            psB[0:64, :], eM[:, :], midb[:, :], start=True, stop=True), inc=1)
        p.wait("vector", "tensor", t_me)
        p.on("vector", lambda e: e.tensor_copy(mge[:, :], psB[0:64, :]))
        v_mg = p.on("vector", lambda e: e.tensor_mul(mg[0:64, :], mge[:, :], psA[0:64, :]), inc=1)
        p.wait("tensor", "vector", v_mg)
        v_out_prev = 0
        for f in range(2):
            fs = f * 256
            t_o = 0
            for oc in range(ND):
                pt = psO[oc // 2][:, (oc % 2) * 256:(oc % 2) * 256 + 256]
                for c in range(ND):
                    p.on("tensor", lambda e, pt=pt, oc=oc, c=c, fs=fs: e.matmul(
                        pt, wb[:, c, oc * PC:(oc + 1) * PC],
                        xnb[:, c, fs:fs + 256],
                        start=(c == 0), stop=False))
                t_o = p.on("tensor", lambda e, pt=pt, oc=oc, fs=fs: e.matmul(
                    pt, wB[:, oc * PC:(oc + 1) * PC], mg[:, fs:fs + 256],
                    start=False, stop=True), inc=1)
            p.wait("vector", "tensor", t_o)
            if f == 0:
                p.wait("vector", "gpsimd", g_st_prev)
            for oc in range(ND):
                pslice = psO[oc // 2][:, (oc % 2) * 256:(oc % 2) * 256 + 256]
                p.on("vector", lambda e, pslice=pslice, fs=fs, oc=oc: e.tensor_scalar(
                    xo[:, oc, fs:fs + 256], pslice, bbc[:, oc:oc + 1], None, ALU.add))
                v_out_prev = p.on("vector", lambda e, fs=fs, oc=oc: e.tensor_add(
                    xo[:, oc, fs:fs + 256], xo[:, oc, fs:fs + 256],
                    x_sb[:, oc, fs:fs + 256]), inc=1)
            if f == 0:
                p.wait("tensor", "vector", v_out_prev)
        p.wait("gpsimd", "vector", v_out_prev)
        g_st_prev = p.dma("gpsimd", fm_dram(io["x_out"], ND)[:, :, c0:c0 + TC], xo[:, :, :])
    p.drain_dma("gpsimd")


def declare_l1ffn(nc, io):
    io["ffn_base_w"] = nc.dram_tensor("ffn_base_w", [D, D], F32, kind="ExternalInput").ap()
    io["ffn_base_b"] = nc.dram_tensor("ffn_base_b", [PC, ND], F32, kind="ExternalInput").ap()
    io["ffn_lora_A"] = nc.dram_tensor("ffn_lora_A", [D, 8], F32, kind="ExternalInput").ap()
    io["ffn_lora_B"] = nc.dram_tensor("ffn_lora_B", [64, D], F32, kind="ExternalInput").ap()
    io["ffn_router_w"] = nc.dram_tensor("ffn_router_w", [D, 8], F32, kind="ExternalInput").ap()
    io["ffn_router_b"] = nc.dram_tensor("ffn_router_b", [8, 1], F32, kind="ExternalInput").ap()
    io["ffn_eG"] = nc.dram_tensor("ffn_eG", [8, 64], F32, kind="ExternalInput").ap()
    io["ffn_eM"] = nc.dram_tensor("ffn_eM", [8, 64], F32, kind="ExternalInput").ap()


def prep_l1ffn(inputs):
    eG = np.zeros((8, 64), np.float32)
    eM = np.zeros((8, 64), np.float32)
    for e in range(8):
        for r in range(8):
            eG[e, e * 8 + r] = 2.0
            eM[r, e * 8 + r] = 1.0
    return {
        "ffn_base_w": np.ascontiguousarray(inputs["L1_ffn_base_w"].T),
        "ffn_base_b": np.ascontiguousarray(inputs["L1_ffn_base_b"].reshape(ND, PC).T),
        "ffn_lora_A": np.ascontiguousarray(inputs["L1_ffn_lora_A"]),
        "ffn_lora_B": np.ascontiguousarray(inputs["L1_ffn_lora_B"].reshape(64, D)),
        "ffn_router_w": np.ascontiguousarray(inputs["L1_ffn_router_w"].T),
        "ffn_router_b": np.ascontiguousarray(inputs["L1_ffn_router_b"].reshape(8, 1)),
        "ffn_eG": eG,
        "ffn_eM": eM,
    }



def emit_l0ffn(p, es, io, C, tag, dbg=None):
    nc = p.nc
    TC = 512
    NH, NZC = 4, 128
    bd1 = [fm(es, nc, tag + f"bd1_{h}", ND, PC, BF16) for h in range(NH)]
    for h in range(NH):
        g_w = p.dma("gpsimd", bd1[h][:, :, :], io["ffn0_bd1"][h].rearrange("c p k -> p c k"))
        p.wait("tensor", "gpsimd", g_w)
    bd2buf = [fm(es, nc, tag + f"bd2_{i}", ND, 512, BF16) for i in range(2)]

    x_sb = fm(es, nc, tag + "x", ND, TC, F32)
    xnb = fm(es, nc, tag + "xnb", ND, TC, BF16)
    x2 = fm(es, nc, tag + "x2", ND, TC, F32)
    y_h = fm(es, nc, tag + "yh", ND, TC, BF16)
    rows = y_h[0:1, 0:4, :].bitcast(F32).rearrange("p a b -> p (a b)")
    h_all = fm(es, nc, tag + "hall", NZC, TC, BF16)
    wcp = [sb(es, nc, tag + f"wcp{i}", [PC, D], BF16) for i in range(2)]
    wcp.append(x2[:, 0, :].bitcast(BF16))
    wcp.append(x2[:, 1, :].bitcast(BF16))
    pool = [ps(es, nc, tag + f"pp{i}", [PC, 512]) for i in range(8)]

    v_prev = 0
    g_st_prev = 0
    s_gelu = 0
    vy_last = [0, 0]
    mb2_last = [0, 0]
    for ci in range(T // TC):
        c0 = ci * TC
        p.wait("sync", "vector", v_prev)
        p.wait("sync", "gpsimd", g_st_prev)
        x_wm = p.dma("sync", x_sb[:, :, :], fm_dram(io["x_in"], ND)[:, :, c0:c0 + TC])
        for e2 in ("scalar", "tensor", "vector"):
            p.wait(e2, "sync", x_wm)
        emit_ln(p, C, x_sb, x2, TC, pool[3], pool[4], x2, rows, out_bf=xnb)
        p.wait("tensor", "vector", p.wm("vector"))
        for h in range(NH):
            b2 = h % 2
            p.wait("gpsimd", "tensor", mb2_last[b2])
            g_b2 = p.dma("gpsimd", bd2buf[b2][:, :, :],
                         io["ffn0_bd2"][h].rearrange("c p k -> p c k"))
            v_y = 0
            for c in range(ND):
                py = pool[(3, 4)[c % 2]]
                if vy_last[c % 2]:
                    p.wait("tensor", "vector", vy_last[c % 2])
                t_y = p.on("tensor", lambda e, h=h, c=c, py=py: e.matmul(
                    py[:, :], bd1[h][:, c, :], xnb[:, c, :],
                    start=True, stop=True), inc=1)
                p.wait("vector", "tensor", t_y)
                v_y = p.on("vector", lambda e, c=c, py=py: e.tensor_copy(
                    y_h[:, c, :], py[:, :]), inc=1)
                vy_last[c % 2] = v_y
            p.wait("tensor", "gpsimd", g_b2)
            mb2_last[b2] = 0
            if dbg is not None and ci == 0 and h == 0:
                vv = p.wm("vector")
                p.wait("gpsimd", "vector", vv)
                p.dma("gpsimd", dbg["yh"], y_h[:, :, :])
            p.wait("tensor", "vector", v_y)
            for c in range(ND):
                for s4 in range(4):
                    zc = h * 32 + c * 4 + s4
                    pz = pool[1 + (zc % 2)]
                    p.wait("tensor", "scalar", max(0, s_gelu - 1))
                    t_zz = p.on("tensor", lambda e, c=c, s4=s4, pz=pz, b2=b2:
                                e.matmul(
                        pz[:, :], bd2buf[b2][:, c, s4 * PC:(s4 + 1) * PC],
                        y_h[:, c, :], start=True, stop=True), inc=1)
                    p.wait("scalar", "tensor", t_zz)
                    s_gelu = p.on("scalar", lambda e, zc=zc, pz=pz: e.activation(
                        h_all[:, zc, :], pz[:, :], AF.Gelu), inc=1)
                    mb2_last[b2] = t_zz
        p.wait("tensor", "scalar", s_gelu)
        if dbg is not None and ci == 0:
            p.wait("gpsimd", "scalar", s_gelu)
            p.dma("gpsimd", dbg["hall"], h_all[:, :, :])
            vv2 = p.wm("vector")
            p.wait("gpsimd", "vector", vv2)
            p.dma("gpsimd", dbg["xnb"], xnb[:, :, :])
        p.wait("gpsimd", "tensor", p.wm("tensor"))
        p.wait("gpsimd", "vector", p.wm("vector"))
        t_o = 0
        g_wm = {}
        for j in range(4):
            g_wm[j] = p.dma("gpsimd", wcp[j][:, :],
                            io["ffn0_wcpT"][j * PC:(j + 1) * PC, :])
        for m in range(NZC):
            p.wait("tensor", "gpsimd", g_wm[m % 4])
            for oc in range(ND):
                t_o = p.on("tensor", lambda e, m=m, oc=oc: e.matmul(
                    pool[oc][:, :], wcp[m % 4][:, oc * PC:(oc + 1) * PC],
                    h_all[:, m, :], start=(m == 0), stop=(m == NZC - 1)),
                    inc=(1 if oc == ND - 1 else 0))
            if m + 4 < NZC:
                p.wait("gpsimd", "tensor", t_o)
                g_wm[m % 4] = p.dma("gpsimd", wcp[m % 4][:, :],
                                    io["ffn0_wcpT"][(m + 4) * PC:(m + 5) * PC, :])
        p.wait("vector", "tensor", t_o)
        p.wait("vector", "gpsimd", g_st_prev)
        v_prev = 0
        for oc in range(ND):
            v_prev = p.on("vector", lambda e, oc=oc: e.tensor_add(
                x_sb[:, oc, :], pool[oc][:, :], x_sb[:, oc, :]), inc=1)
        p.wait("gpsimd", "vector", v_prev)
        g_st_prev = p.dma("gpsimd", fm_dram(io["x_out"], ND)[:, :, c0:c0 + TC], x_sb[:, :, :])
    p.drain_dma("gpsimd")


def declare_l0ffn(nc, io):
    io["ffn0_bd1"] = [nc.dram_tensor(f"ffn0_bd1_{h}", [ND, PC, PC], F32,
                                     kind="ExternalInput").ap() for h in range(4)]
    io["ffn0_bd2"] = [nc.dram_tensor(f"ffn0_bd2_{h}", [ND, PC, 512], F32,
                                     kind="ExternalInput").ap() for h in range(4)]
    io["ffn0_wcpT"] = nc.dram_tensor("ffn0_wcpT", [16384, D], F32,
                                     kind="ExternalInput").ap()


def prep_l0ffn(inputs):
    maker = inputs["L0_ffn_rail_maker"]
    up = inputs["L0_ffn_rail_up_proj"]
    d = {}
    bd1 = np.zeros((4, 8, PC, PC), np.float32)
    bd2 = np.zeros((4, 8, PC, 512), np.float32)
    for h in range(4):
        for c in range(8):
            for rl in range(8):
                r = 8 * c + rl
                bd1[h, c, 16 * rl:16 * rl + 16, 16 * rl:16 * rl + 16] = maker[h, r]
                bd2[h, c, 16 * rl:16 * rl + 16, 64 * rl:64 * rl + 64] = up[h, r].T
    for h in range(4):
        d[f"ffn0_bd1_{h}"] = np.ascontiguousarray(bd1[h])
        d[f"ffn0_bd2_{h}"] = np.ascontiguousarray(bd2[h])
    d["ffn0_wcpT"] = np.ascontiguousarray(inputs["L0_ffn_w_cproj"].T)
    return d



def emit_l1attnb(p, es, io, C, tag, dbg=None):
    nc = p.nc
    TC = 512
    NT = T // TC
    wsl = [fm(es, nc, tag + f"wsl{i}", ND, PC, BF16) for i in range(2)]
    ehs = sb(es, nc, tag + "sehs", [PC, 8, 16], BF16)
    xhs = sb(es, nc, tag + "sxhs", [16, 8, PC], BF16)
    g_e = p.dma("gpsimd", ehs[:, :, :], io["b7_ehs"].rearrange("c p j -> p c j"))
    g_w = p.dma("gpsimd", xhs[:, :, :], io["b7_xhs"].rearrange("c j p -> j c p"))
    p.wait("tensor", "gpsimd", g_e)
    p.wait("tensor", "gpsimd", g_w)

    x_sb = fm(es, nc, tag + "x", ND, TC, F32)
    x2 = fm(es, nc, tag + "x2", ND, TC, F32)
    rows = sb(es, nc, tag + "rows", [1, 2 * TC], F32)
    xnb = fm(es, nc, tag + "xnb", ND, T, BF16)
    o1c = fm(es, nc, tag + "o1c", ND, TC, BF16)
    w_oc = sb(es, nc, tag + "woc", [PC, T], F32)
    u_oc = sb(es, nc, tag + "uoc", [PC, T], F32)
    S_oc = sb(es, nc, tag + "soc", [PC, T], F32)
    k_oc = sb(es, nc, tag + "koc", [PC, T], BF16)
    ksum = sb(es, nc, tag + "ksum", [2, T], BF16)
    o1_dram = nc.dram_tensor(tag + "o1d", [PC, ND, T], BF16, kind="Internal").ap()
    psA = ps(es, nc, tag + "psA", [PC, 512])
    psB = ps(es, nc, tag + "psB", [PC, 512])
    psK = ps(es, nc, tag + "psK", [16, 512])
    psO = [ps(es, nc, tag + f"psO{i}", [PC, 512]) for i in range(2)]

    g_st_prev = 0
    v_ln = 0
    for ci in range(NT):
        c0 = ci * TC
        p.wait("sync", "vector", v_ln)
        x_wm = p.dma("sync", x_sb[:, :, :], fm_dram(io["x_in"], ND)[:, :, c0:c0 + TC])
        for e2 in ("scalar", "tensor", "vector"):
            p.wait(e2, "sync", x_wm)
        v_ln = emit_ln(p, C, x_sb, x2, TC, psA, psB, x2, rows,
                       out_bf=None)
        for c in range(ND):
            v_ln = p.on("vector", lambda e, c=c, c0=c0: e.tensor_copy(
                xnb[:, c, c0:c0 + TC], x2[:, c, 0:TC]), inc=1)

    p.wait("tensor", "vector", p.wm("vector"))
    p.wait("scalar", "vector", p.wm("vector"))
    p.wait("gpsimd", "vector", p.wm("vector"))

    def proj_oc(wname, oc, consumer):
        buf = proj_oc.flip
        proj_oc.flip ^= 1
        p.wait("gpsimd", "tensor", p.wm("tensor"))
        g_s = p.dma("gpsimd", wsl[buf][:, :, :],
                    fm_dram(io["b7_" + wname], ND)[:, :, oc * PC:(oc + 1) * PC])
        p.wait("tensor", "gpsimd", g_s)
        vres = 0
        for f in range(NT):
            fs = f * TC
            pt = psO[f % 2]
            t_p = 0
            for c in range(ND):
                t_p = p.on("tensor", lambda e, pt=pt, c=c, fs=fs, buf=buf:
                           e.matmul(pt[:, :], wsl[buf][:, c, :],
                                    xnb[:, c, fs:fs + TC],
                                    start=(c == 0), stop=(c == ND - 1)),
                           inc=(1 if c == ND - 1 else 0))
            p.wait("vector", "tensor", t_p)
            p.wait("scalar", "tensor", t_p)
            vres = consumer(fs, pt)
            p.wait("tensor", "vector", p.wm("vector"))
            p.wait("tensor", "scalar", p.wm("scalar"))
        return vres
    proj_oc.flip = 0

    for oc in range(ND):
        proj_oc("wk", oc, lambda fs, pt: p.on(
            "vector", lambda e, fs=fs, pt=pt: e.tensor_copy(
                k_oc[:, fs:fs + TC], pt[:, :]), inc=1))
        v_kb = p.wm("vector")
        p.wait("tensor", "vector", v_kb)
        for f in range(NT):
            fs = f * TC
            t_ks = p.on("tensor", lambda e, fs=fs: e.matmul(
                psK[0:2, :], ehs[:, 0, 0:2], k_oc[:, fs:fs + TC],
                start=True, stop=True), inc=1)
            p.wait("vector", "tensor", t_ks)
            v_ks = p.on("vector", lambda e, fs=fs: e.tensor_copy(
                ksum[:, fs:fs + TC], psK[0:2, :]), inc=1)
            p.wait("tensor", "vector", v_ks)
        if dbg is not None and oc == 0:
            vks_ = p.wm("vector")
            p.wait("gpsimd", "vector", vks_)
            p.dma("gpsimd", dbg["koc"], k_oc[:, :])
            g_dk = p.dma("gpsimd", dbg["ksum"], ksum[:, :])
            p.wait("vector", "gpsimd", g_dk)
            p.wait("tensor", "gpsimd", g_dk)
        def cons_v(fs, pt):
            t_e = p.on("tensor", lambda e, fs=fs: e.matmul(
                psB[:, :], xhs[0:2, 0, :], ksum[:, fs:fs + TC],
                start=True, stop=True), inc=1)
            p.on("vector", lambda e, fs=fs, pt=pt: e.tensor_copy(
                u_oc[:, fs:fs + TC], pt[:, :]))
            p.wait("vector", "tensor", t_e)
            return p.on("vector", lambda e, fs=fs: e.tensor_mul(
                u_oc[:, fs:fs + TC], u_oc[:, fs:fs + TC], psB[:, :]), inc=1)
        proj_oc("wv", oc, cons_v)
        def cons_w(fs, pt):
            s_s = p.on("scalar", lambda e, fs=fs, pt=pt: e.activation(
                w_oc[:, fs:fs + TC], pt[:, :], AF.Sigmoid), inc=1)
            p.wait("vector", "scalar", s_s)
            return p.on("vector", lambda e, fs=fs: e.tensor_scalar(
                w_oc[:, fs:fs + TC], w_oc[:, fs:fs + TC], 0.99, None, ALU.mult), inc=1)
        proj_oc("ww", oc, cons_w)
        v_sc = p.on("vector", lambda e: e.tensor_tensor_scan(
            S_oc[:, :], w_oc[:, :], u_oc[:, :], 0.0, ALU.mult, ALU.add), inc=1)
        def cons_r(fs, pt):
            s_s = p.on("scalar", lambda e, fs=fs, pt=pt: e.activation(
                u_oc[:, fs:fs + TC], pt[:, :], AF.Sigmoid), inc=1)
            p.wait("vector", "scalar", s_s)
            return p.on("vector", lambda e, fs=fs: e.tensor_mul(
                k_oc[:, fs:fs + TC], u_oc[:, fs:fs + TC], S_oc[:, fs:fs + TC]), inc=1)
        p.wait("scalar", "vector", v_sc)
        if dbg is not None and oc == 0:
            p.wait("gpsimd", "vector", v_sc)
            p.dma("gpsimd", dbg["Soc"], S_oc[:, :])
            p.dma("gpsimd", dbg["woc"], w_oc[:, :])
            g_dd = p.dma("gpsimd", dbg["uoc"], u_oc[:, :])
            p.wait("vector", "gpsimd", g_dd)
            p.wait("scalar", "gpsimd", g_dd)
        v_o1 = proj_oc("wr", oc, cons_r)
        p.wait("gpsimd", "vector", v_o1)
        g_o1 = p.dma("gpsimd", o1_dram[:, oc, :], k_oc[:, :])
        p.wait("vector", "gpsimd", g_o1)

    if dbg is not None:
        p.wait("gpsimd", "vector", p.wm("vector"))
        p.dma("gpsimd", dbg["o1"], o1_dram)
    p.wait("sync", "vector", p.wm("vector"))
    p.drain_dma("sync")
    wo = fm(es, nc, tag + "swo", ND, D, BF16)
    g_wo = p.dma("gpsimd", wo[:, :, :], fm_dram(io["b7_wo"], ND))
    p.wait("tensor", "gpsimd", g_wo)
    rsb = fm(es, nc, tag + "rsb", ND, TC, BF16)
    v_prev = 0
    for ci in range(NT):
        c0 = ci * TC
        p.wait("sync", "vector", v_prev)
        p.wait("sync", "gpsimd", g_st_prev)
        x_wm = p.dma("sync", x_sb[:, :, :], fm_dram(io["x_in"], ND)[:, :, c0:c0 + TC])
        for e2 in ("scalar", "tensor", "vector"):
            p.wait(e2, "sync", x_wm)
        p.wait("sync", "scalar", p.wm("scalar"))
        o_wm = p.dma("sync", o1c[:, :, :], o1_dram[:, :, c0:c0 + TC])
        for e2 in ("scalar", "vector", "tensor"):
            p.wait(e2, "sync", o_wm)
        s_q = p.on("scalar", lambda e: e.activation(
            x2[:, :, 0:TC], o1c[:, :, 0:TC], AF.Square), inc=1)
        p.wait("tensor", "scalar", s_q)
        t_ms = 0
        for c in range(ND):
            t_ms = p.on("tensor", lambda e, c=c: e.matmul(
                psA[0:1, 0:TC], C["invn"], x2[:, c, 0:TC],
                start=(c == 0), stop=(c == ND - 1)), inc=(1 if c == ND - 1 else 0))
        p.wait("vector", "tensor", t_ms)
        v_ms = p.on("vector", lambda e: e.tensor_copy(rows[0:1, 0:TC], psA[0:1, 0:TC]), inc=1)
        p.wait("scalar", "vector", v_ms)
        p.on("scalar", lambda e: e.activation(rows[0:1, 0:TC], rows[0:1, 0:TC],
                                              AF.Ln, bias=1.1920929e-7))
        s_rr = p.on("scalar", lambda e: e.activation(rows[0:1, 0:TC], rows[0:1, 0:TC],
                                                     AF.Exp, scale=-0.5), inc=1)
        p.wait("tensor", "scalar", s_rr)
        t_bc = p.on("tensor", lambda e: e.matmul(
            psA[:, 0:TC], C["ones1"], rows[0:1, 0:TC], start=True, stop=True), inc=1)
        p.wait("vector", "tensor", t_bc)
        v_rs = 0
        for c in range(ND):
            v_rs = p.on("vector", lambda e, c=c: e.tensor_mul(
                rsb[:, c, :], o1c[:, c, 0:TC], psA[:, 0:TC]), inc=1)
        p.wait("tensor", "vector", v_rs)
        v_prev = 0
        for f in range(2):
            fs = f * 256
            t_p = 0
            for oc2 in range(ND):
                pb = (psO[0], psO[1], psA, psB)[oc2 // 2]
                pt = pb[:, (oc2 % 2) * 256:(oc2 % 2) * 256 + 256]
                for c in range(ND):
                    t_p = p.on("tensor", lambda e, pt=pt, oc2=oc2, c=c, fs=fs: e.matmul(
                        pt, wo[:, c, oc2 * PC:(oc2 + 1) * PC],
                        rsb[:, c, fs:fs + 256],
                        start=(c == 0), stop=(c == ND - 1)),
                        inc=(1 if c == ND - 1 else 0))
            p.wait("vector", "tensor", t_p)
            if f == 0:
                p.wait("vector", "gpsimd", g_st_prev)
            for oc2 in range(ND):
                pb = (psO[0], psO[1], psA, psB)[oc2 // 2]
                pt = pb[:, (oc2 % 2) * 256:(oc2 % 2) * 256 + 256]
                v_prev = p.on("vector", lambda e, oc2=oc2, fs=fs, pt=pt: e.tensor_add(
                    x_sb[:, oc2, fs:fs + 256], pt, x_sb[:, oc2, fs:fs + 256]), inc=1)
            if f == 0:
                p.wait("tensor", "vector", v_prev)
        p.wait("gpsimd", "vector", v_prev)
        g_st_prev = p.dma("gpsimd", fm_dram(io["x_out"], ND)[:, :, c0:c0 + TC], x_sb[:, :, :])
    p.drain_dma("gpsimd")


def declare_l1attnb(nc, io):
    for nm in ("wr", "wk", "wv", "ww", "wo"):
        io["b7_" + nm] = nc.dram_tensor("b7_" + nm, [D, D], F32, kind="ExternalInput").ap()
    io["b7_ehs"] = nc.dram_tensor("b7_ehs", [8, PC, 16], F32, kind="ExternalInput").ap()
    io["b7_xhs"] = nc.dram_tensor("b7_xhs", [8, 16, PC], F32, kind="ExternalInput").ap()


def prep_l1attnb(inputs):
    d = {}
    for nm, key in (("wr", "L1_attn_b_Wr"), ("wk", "L1_attn_b_Wk"), ("wv", "L1_attn_b_Wv"),
                    ("ww", "L1_attn_b_Ww"), ("wo", "L1_attn_b_Wo")):
        d["b7_" + nm] = np.ascontiguousarray(inputs[key].T)
    ehs = np.zeros((8, PC, 16), np.float32)
    xhs = np.zeros((8, 16, PC), np.float32)
    for c in range(8):
        for pp in range(PC):
            hl = pp // 64
            ehs[c, pp, hl] = 1.0
            xhs[c, hl, pp] = 1.0
    d["b7_ehs"] = ehs
    d["b7_xhs"] = xhs
    return d



def emit_l0attna(p, es, io, C, tag, phases="ABCD"):
    nc = p.nc
    TC = 512
    NT = T // TC
    NG = 32
    x0i_dram = nc.dram_tensor(tag + "x0i", [PC, ND, T], F32, kind="Internal").ap()
    xr_dram = nc.dram_tensor(tag + "xr", [PC, ND, T], BF16, kind="Internal").ap()
    rw_dram = nc.dram_tensor(tag + "rw", [PC, NG, T], BF16, kind="Internal").ap()
    dc_dram = nc.dram_tensor(tag + "dcd", [PC, 2 * NG], F32, kind="Internal").ap()

    dcol = sb(es, nc, tag + "s_dcol", [PC, NG], F32)
    fcol = sb(es, nc, tag + "s_fcol", [PC, NG], F32)
    mixk = sb(es, nc, tag + "s_mixk", [PC, ND], F32)
    mixv = sb(es, nc, tag + "s_mixv", [PC, ND], F32)
    mixr = sb(es, nc, tag + "s_mixr", [PC, ND], F32)
    cmixk = sb(es, nc, tag + "s_cmixk", [PC, ND], F32)
    cmixr = sb(es, nc, tag + "s_cmixr", [PC, ND], F32)
    lncols = sb(es, nc, tag + "s_lnc", [PC, 6 * ND], F32)
    ctoks = [p.dma("sync", mixk[:, :], io["a1_mixk"]),
             p.dma("sync", mixv[:, :], io["a1_mixv"]),
             p.dma("sync", mixr[:, :], io["a1_mixr"]),
             p.dma("sync", cmixk[:, :], io["a1_cmixk"]),
             p.dma("sync", cmixr[:, :], io["a1_cmixr"]),
             p.dma("sync", lncols[:, :], io["a1_lncols"]),
             p.dma("sync", dcol[:, :], io["a1_decay"]),
             p.dma("sync", fcol[:, :], io["a1_first"])]
    for tok in ctoks[-2:]:
        p.wait("scalar", "sync", tok)
    p.on("scalar", lambda e: e.activation(dcol[:, :], dcol[:, :], AF.Exp))
    p.on("scalar", lambda e: e.activation(dcol[:, :], dcol[:, :], AF.Exp, scale=-1.0))
    s_d = p.on("scalar", lambda e: e.activation(fcol[:, :], fcol[:, :], AF.Exp), inc=1)
    p.wait("gpsimd", "scalar", s_d)
    p.dma("gpsimd", dc_dram[:, 0:NG], dcol[:, :])
    g_rt = p.dma("gpsimd", dc_dram[:, NG:2 * NG], fcol[:, :])
    p.wait("sync", "gpsimd", g_rt)
    rt1 = p.dma("sync", dcol[:, :], dc_dram[:, 0:NG])
    rt2 = p.dma("sync", fcol[:, :], dc_dram[:, NG:2 * NG])
    for e2 in ("vector", "scalar", "tensor", "gpsimd"):
        for tok in ctoks[:6] + [rt1, rt2]:
            p.wait(e2, "sync", tok)

    esAB = ExitStack()
    xk = fm(esAB, nc, tag + "s_xk", ND, T, BF16)
    xv = fm(esAB, nc, tag + "s_xv", ND, T, BF16)

    esP = ExitStack()
    x_sb = fm(esP, nc, tag + "xA", ND, TC, F32)
    x2 = fm(esP, nc, tag + "x2A", ND, TC, F32)
    x0c = fm(esP, nc, tag + "x0c", ND, TC, F32)
    rows = sb(esP, nc, tag + "rowsA", [1, 2 * TC], F32)
    xn_buf = esP.enter_context(nc.sbuf_tensor(tag + "xnbuf", [PC, ND, TC + 1], F32))
    xrc = fm(esP, nc, tag + "xrc", ND, TC, BF16)
    tmix = x2
    psA_ = ps(esP, nc, tag + "psA", [PC, 512])
    psB_ = ps(esP, nc, tag + "psB", [PC, 512])
    v_pr = p.on("vector", lambda e: e.memset(xn_buf[:, :, 0:1], 0.0), inc=1)

    g_pr = 0
    for ci in range(NT):
        c0 = ci * TC
        p.wait("sync", "vector", v_pr)
        p.wait("sync", "gpsimd", g_pr)
        x_wm = p.dma("sync", x_sb[:, :, :], fm_dram(io["x_in"], ND)[:, :, c0:c0 + TC])
        for e2 in ("scalar", "tensor", "vector"):
            p.wait(e2, "sync", x_wm)
        emit_ln(p, C, x_sb, x2, TC, psA_, psB_, x2, rows)
        emit_ln(p, C, x2, x0c, TC, psA_, psB_, x_sb, rows,
                gcols=lncols[:, 0:ND], bcols=lncols[:, ND:2 * ND])
        emit_ln(p, C, x0c, None, TC, psA_, psB_, x_sb, rows,
                gcols=lncols[:, 2 * ND:3 * ND], bcols=lncols[:, 3 * ND:4 * ND],
                outs=lambda c: xn_buf[:, c, 1:TC + 1])
        v_sp = p.wm("vector")
        p.wait("gpsimd", "vector", v_sp)
        g_pr = p.dma("gpsimd", x0i_dram[:, :, c0:c0 + TC], x0c[:, :, :])
        for c in range(ND):
            p.on("vector", lambda e, c=c: e.tensor_sub(
                tmix[:, c, :], xn_buf[:, c, 1:TC + 1], xn_buf[:, c, 0:TC]))
        for mq, kind in ((mixk, "k"), (mixv, "v"), (mixr, "r")):
            for c in range(ND):
                if kind == "k":
                    dv = xk[:, c, c0:c0 + TC]
                elif kind == "v":
                    dv = xv[:, c, c0:c0 + TC]
                else:
                    dv = xrc[:, c, :]
                v_pr = p.on("vector", lambda e, c=c, mq=mq, dv=dv: e.scalar_tensor_tensor(
                    dv, tmix[:, c, :], mq[:, c:c + 1], xn_buf[:, c, 0:TC],
                    ALU.mult, ALU.add), inc=1)
        p.wait("gpsimd", "vector", v_pr)
        g_pr = p.dma("gpsimd", xr_dram[:, :, c0:c0 + TC], xrc[:, :, :])
        v_pr = p.on("vector", lambda e: e.tensor_copy(
            xn_buf[:, :, 0:1], xn_buf[:, :, TC:TC + 1]), inc=1)
    p.wait("vector", "gpsimd", g_pr)
    p.barrier()
    if "B" not in phases:
        for ci2 in range(NT):
            p.dma("gpsimd", fm_dram(io["x_out"], ND)[:, :, ci2 * TC:(ci2 + 1) * TC],
                  x_sb[:, :, :] if False else None) if False else None
        g_f = 0
        xf = x_sb
        for ci2 in range(NT):
            p.wait("sync", "gpsimd", g_f)
            w1 = p.dma("sync", xf[:, :, :], x0i_dram[:, :, ci2 * TC:(ci2 + 1) * TC])
            p.wait("gpsimd", "sync", w1)
            g_f = p.dma("gpsimd", fm_dram(io["x_out"], ND)[:, :, ci2 * TC:(ci2 + 1) * TC],
                        xf[:, :, :])
        p.drain_dma("gpsimd")
        esP.close()
        esAB.close()
        return
    esP.close()

    esB = ExitStack()
    wsl = [fm(esB, nc, tag + f"wslB{i}", ND, PC, BF16) for i in range(2)]
    k_buf = esB.enter_context(nc.sbuf_tensor(tag + "kbuf", [PC, T + 1], F32))
    kv_buf = esB.enter_context(nc.sbuf_tensor(tag + "kvbuf", [PC, T + 1], F32))
    sk_b = sb(esB, nc, tag + "skb", [PC, T], F32)
    skv_b = sb(esB, nc, tag + "skvb", [PC, T], F32)
    rwb = sb(esB, nc, tag + "rwbB", [PC, T], BF16)
    psP = [ps(esB, nc, tag + f"psP{i}", [PC, 512]) for i in range(2)]
    v_zb = p.on("vector", lambda e: e.memset(k_buf[:, 0:1], 0.0))
    v_zb = p.on("vector", lambda e: e.memset(kv_buf[:, 0:1], 0.0), inc=1)
    p.wait("vector", "vector", v_zb)
    g_rwsp = 0

    mk_last = 0
    mv_last = 0
    ev_last = [("scalar", 0), ("scalar", 0)]
    v_scans = 0
    for g in range(NG):
        p.wait("gpsimd", "tensor", mk_last)
        g_k = p.dma("gpsimd", wsl[0][:, :, :],
                    fm_dram(io["a1_wk"], ND)[:, :, g * PC:(g + 1) * PC])
        p.wait("tensor", "gpsimd", g_k)
        s_k = 0
        for f in range(NT):
            fs = f * TC
            pt = psP[f % 2]
            eng, cc2 = ev_last[f % 2]
            if cc2:
                p.wait("tensor", eng, cc2)
            t_k = 0
            for c in range(ND):
                t_k = p.on("tensor", lambda e, pt=pt, c=c, fs=fs: e.matmul(
                    pt[:, :], wsl[0][:, c, :], xk[:, c, fs:fs + TC],
                    start=(c == 0), stop=(c == ND - 1)), inc=(1 if c == ND - 1 else 0))
            if f == NT - 1:
                mk_last = t_k
            p.wait("scalar", "tensor", t_k)
            if f == 0:
                p.wait("scalar", "vector", v_scans)
            s_k = p.on("scalar", lambda e, pt=pt, fs=fs: e.activation(
                k_buf[:, 1 + fs:1 + fs + TC], pt[:, :], AF.Exp), inc=1)
            ev_last[f % 2] = ("scalar", s_k)
        p.wait("vector", "scalar", s_k)
        v_k = p.on("vector", lambda e: e.tensor_scalar(
            k_buf[:, 1:T + 1], k_buf[:, 1:T + 1],
            1.1420073898156842e26, None, ALU.min), inc=1)
        p.wait("gpsimd", "tensor", mv_last)
        g_v = p.dma("gpsimd", wsl[1][:, :, :],
                    fm_dram(io["a1_wv"], ND)[:, :, g * PC:(g + 1) * PC])
        p.wait("tensor", "gpsimd", g_v)
        v_kv = 0
        for f in range(NT):
            fs = f * TC
            pt = psP[f % 2]
            eng, cc2 = ev_last[f % 2]
            if cc2:
                p.wait("tensor", eng, cc2)
            t_v = 0
            for c in range(ND):
                t_v = p.on("tensor", lambda e, pt=pt, c=c, fs=fs: e.matmul(
                    pt[:, :], wsl[1][:, c, :], xv[:, c, fs:fs + TC],
                    start=(c == 0), stop=(c == ND - 1)), inc=(1 if c == ND - 1 else 0))
            if f == NT - 1:
                mv_last = t_v
            p.wait("vector", "tensor", t_v)
            if f == 0:
                p.wait("vector", "vector", v_k)
            v_kv = p.on("vector", lambda e, pt=pt, fs=fs: e.tensor_mul(
                kv_buf[:, 1 + fs:1 + fs + TC], k_buf[:, 1 + fs:1 + fs + TC],
                pt[:, :]), inc=1)
            ev_last[f % 2] = ("vector", v_kv)
        dbc = dcol[:, g:g + 1].broadcast_to([PC, T])
        p.on("vector", lambda e, dbc=dbc: e.tensor_tensor_scan(
            sk_b[:, :], dbc, k_buf[:, 0:T], 0.0, ALU.mult, ALU.add))
        p.on("vector", lambda e, dbc=dbc: e.tensor_tensor_scan(
            skv_b[:, :], dbc, kv_buf[:, 0:T], 0.0, ALU.mult, ALU.add))
        p.on("vector", lambda e, g=g: e.scalar_tensor_tensor(
            sk_b[:, :], k_buf[:, 1:T + 1], fcol[:, g:g + 1], sk_b[:, :],
            ALU.mult, ALU.add))
        v_wk = p.on("vector", lambda e: e.tensor_scalar(
            sk_b[:, :], sk_b[:, :], 1e-8, None, ALU.add), inc=1)
        v_scans = p.on("vector", lambda e, g=g: e.scalar_tensor_tensor(
            skv_b[:, :], kv_buf[:, 1:T + 1], fcol[:, g:g + 1], skv_b[:, :],
            ALU.mult, ALU.add), inc=1)
        p.wait("scalar", "vector", v_wk)
        p.on("scalar", lambda e: e.activation(sk_b[:, :], sk_b[:, :], AF.Ln))
        s_inv = p.on("scalar", lambda e: e.activation(
            sk_b[:, :], sk_b[:, :], AF.Exp, scale=-1.0), inc=1)
        p.wait("vector", "scalar", s_inv)
        p.wait("vector", "gpsimd", g_rwsp)
        v_rw = p.on("vector", lambda e: e.tensor_mul(
            rwb[:, :], skv_b[:, :], sk_b[:, :]), inc=1)
        p.wait("gpsimd", "vector", v_rw)
        g_rwsp = p.dma("gpsimd", rw_dram[:, g, :], rwb[:, :])
    p.barrier()
    esB.close()
    esAB.close()
    if "C" not in phases:
        esF = ExitStack()
        xf = fm(esF, nc, tag + "xf", ND, TC, F32)
        g_f = 0
        for ci2 in range(NT):
            p.wait("sync", "gpsimd", g_f)
            w1 = p.dma("sync", xf[:, :, :], x0i_dram[:, :, ci2 * TC:(ci2 + 1) * TC])
            p.wait("gpsimd", "sync", w1)
            g_f = p.dma("gpsimd", fm_dram(io["x_out"], ND)[:, :, ci2 * TC:(ci2 + 1) * TC],
                        xf[:, :, :])
        p.drain_dma("gpsimd")
        esF.close()
        return

    esC = ExitStack()
    wslr = [fm(esC, nc, tag + f"wslr{i}", ND, PC, BF16) for i in range(2)]
    wslo = [sb(esC, nc, tag + f"wslo{i}", [PC, D], BF16) for i in range(2)]
    xrb = fm(esC, nc, tag + "xrb", ND, TC, BF16)
    rwall = esC.enter_context(nc.sbuf_tensor(tag + "rwall", [PC, NG, TC], BF16))
    rwg = [sb(esC, nc, tag + f"rwgC{i}", [PC, TC], BF16) for i in range(2)]
    sgt = [sb(esC, nc, tag + f"sgtC{i}", [PC, TC], F32) for i in range(2)]
    x0b = fm(esC, nc, tag + "x0b", ND, TC, F32)
    psO = [ps(esC, nc, tag + f"psOC{i}", [PC, 512]) for i in range(8)]
    g_st = 0
    mw_last = [0, 0]
    mo_last = [0, 0]
    ev_last = [0, 0]
    vg_last = [0, 0]
    for f in range(NT):
        fs = f * TC
        p.wait("sync", "vector", p.wm("vector"))
        xr_wm = p.dma("sync", xrb[:, :, :], xr_dram[:, :, fs:fs + TC])
        p.wait("tensor", "sync", xr_wm)
        v_rg = 0
        for g in range(NG):
            b = g % 2
            p.wait("gpsimd", "tensor", mw_last[b])
            g_r = p.dma("gpsimd", wslr[b][:, :, :],
                        fm_dram(io["a1_wr"], ND)[:, :, g * PC:(g + 1) * PC])
            p.wait("sync", "vector", vg_last[b])
            rg_wm = p.dma("sync", rwg[b][:, :], rw_dram[:, g, fs:fs + TC])
            p.wait("tensor", "gpsimd", g_r)
            if ev_last[b]:
                p.wait("tensor", "scalar", ev_last[b])
            pt = psO[b]
            t_r = 0
            for c in range(ND):
                t_r = p.on("tensor", lambda e, pt=pt, c=c, b=b: e.matmul(
                    pt[:, :], wslr[b][:, c, :], xrb[:, c, :],
                    start=(c == 0), stop=(c == ND - 1)), inc=(1 if c == ND - 1 else 0))
            mw_last[b] = t_r
            p.wait("scalar", "tensor", t_r)
            if vg_last[b]:
                p.wait("scalar", "vector", vg_last[b])
            s_s = p.on("scalar", lambda e, pt=pt, b=b: e.activation(
                sgt[b][:, :], pt[:, :], AF.Sigmoid), inc=1)
            ev_last[b] = s_s
            p.wait("vector", "scalar", s_s)
            p.wait("vector", "sync", rg_wm)
            v_rg = p.on("vector", lambda e, g=g, b=b: e.tensor_mul(
                rwall[:, g, :], rwg[b][:, :], sgt[b][:, :]), inc=1)
            vg_last[b] = v_rg
        p.wait("tensor", "vector", v_rg)
        t_o = 0
        for g in range(NG):
            b = g % 2
            p.wait("gpsimd", "tensor", mo_last[b])
            g_o = p.dma("gpsimd", wslo[b][:, :], io["a1_wo"][g * PC:(g + 1) * PC, :])
            p.wait("tensor", "gpsimd", g_o)
            for oc in range(ND):
                t_o = p.on("tensor", lambda e, oc=oc, g=g, b=b: e.matmul(
                    psO[oc][:, :], wslo[b][:, oc * PC:(oc + 1) * PC], rwall[:, g, :],
                    start=(g == 0), stop=(g == NG - 1)), inc=(1 if oc == ND - 1 else 0))
            mo_last[b] = t_o
        p.wait("sync", "vector", p.wm("vector"))
        p.wait("sync", "gpsimd", g_st)
        x0_wm = p.dma("sync", x0b[:, :, :], x0i_dram[:, :, fs:fs + TC])
        p.wait("vector", "sync", x0_wm)
        p.wait("vector", "tensor", t_o)
        v_ev = 0
        for oc in range(ND):
            v_ev = p.on("vector", lambda e, oc=oc: e.tensor_add(
                x0b[:, oc, :], psO[oc][:, :], x0b[:, oc, :]), inc=1)
        p.wait("gpsimd", "vector", v_ev)
        g_st = p.dma("gpsimd", x0i_dram[:, :, fs:fs + TC], x0b[:, :, :])
        p.wait("tensor", "vector", v_ev)
    p.wait("vector", "gpsimd", g_st)
    p.barrier()
    esC.close()
    if "D" not in phases:
        esF = ExitStack()
        xf = fm(esF, nc, tag + "xf2", ND, TC, F32)
        g_f = 0
        for ci2 in range(NT):
            p.wait("sync", "gpsimd", g_f)
            w1 = p.dma("sync", xf[:, :, :], x0i_dram[:, :, ci2 * TC:(ci2 + 1) * TC])
            p.wait("gpsimd", "sync", w1)
            g_f = p.dma("gpsimd", fm_dram(io["x_out"], ND)[:, :, ci2 * TC:(ci2 + 1) * TC],
                        xf[:, :, :])
        p.drain_dma("gpsimd")
        esF.close()
        return

    esD = ExitStack()
    xD = fm(esD, nc, tag + "xD", ND, TC, F32)
    x2D = fm(esD, nc, tag + "x2D", ND, TC, F32)
    rowsD = sb(esD, nc, tag + "rowsD", [1, 2 * TC], F32)
    xnD = esD.enter_context(nc.sbuf_tensor(tag + "xnD", [PC, ND, TC + 1], F32))
    tmixD = fm(esD, nc, tag + "tmixD", ND, TC, F32)
    xk2 = fm(esD, nc, tag + "xk2", ND, TC, BF16)
    xr2 = fm(esD, nc, tag + "xr2", ND, TC, BF16)
    silb = esD.enter_context(nc.sbuf_tensor(tag + "silb", [PC, NG, TC], BF16))
    sgD = fm(esD, nc, tag + "sgD", ND, TC, F32)
    xinD = fm(esD, nc, tag + "xinD", ND, TC, F32)
    wslD = [fm(esD, nc, tag + f"wslD{i}", ND, PC, BF16) for i in range(2)]
    cmwrb = fm(esD, nc, tag + "cmwrb", ND, D, BF16)
    g_wr0 = p.dma("gpsimd", cmwrb[:, :, :], fm_dram(io["a1_cmwr"], ND))
    p.wait("tensor", "gpsimd", g_wr0)
    psD = [ps(esD, nc, tag + f"psD{i}", [PC, 512]) for i in range(8)]
    v_zD = p.on("vector", lambda e: e.memset(xnD[:, :, 0:1], 0.0), inc=1)

    v_pr = v_zD
    g_st = 0
    mwD_last = [0, 0]
    ev2 = [0, 0]
    for ci in range(NT):
        c0 = ci * TC
        p.wait("sync", "vector", v_pr)
        p.wait("sync", "gpsimd", g_st)
        x_wm = p.dma("sync", xD[:, :, :], x0i_dram[:, :, c0:c0 + TC])
        for e2 in ("scalar", "tensor", "vector"):
            p.wait(e2, "sync", x_wm)
        emit_ln(p, C, xD, None, TC, psD[0], psD[1], x2D, rowsD,
                gcols=lncols[:, 4 * ND:5 * ND], bcols=lncols[:, 5 * ND:6 * ND],
                outs=lambda c: xnD[:, c, 1:TC + 1])
        for c in range(ND):
            p.on("vector", lambda e, c=c: e.tensor_sub(
                tmixD[:, c, :], xnD[:, c, 1:TC + 1], xnD[:, c, 0:TC]))
        v_mx = 0
        for mq, dst in ((cmixk, xk2), (cmixr, xr2)):
            for c in range(ND):
                v_mx = p.on("vector", lambda e, c=c, mq=mq, dst=dst:
                            e.scalar_tensor_tensor(
                                dst[:, c, :], tmixD[:, c, :], mq[:, c:c + 1],
                                xnD[:, c, 0:TC], ALU.mult, ALU.add), inc=1)
        v_pr = p.on("vector", lambda e: e.tensor_copy(
            xnD[:, :, 0:1], xnD[:, :, TC:TC + 1]), inc=1)
        p.wait("tensor", "vector", v_pr)
        for g in range(NG):
            b = g % 2
            p.wait("gpsimd", "tensor", mwD_last[b])
            g_w = p.dma("gpsimd", wslD[b][:, :, :],
                        fm_dram(io["a1_cmwk"], ND)[:, :, g * PC:(g + 1) * PC])
            p.wait("tensor", "gpsimd", g_w)
            if ev2[b]:
                p.wait("tensor", "scalar", ev2[b])
            pt = psD[2 + b]
            t_k2 = 0
            for c in range(ND):
                t_k2 = p.on("tensor", lambda e, pt=pt, c=c, b=b: e.matmul(
                    pt[:, :], wslD[b][:, c, :], xk2[:, c, :],
                    start=(c == 0), stop=(c == ND - 1)), inc=(1 if c == ND - 1 else 0))
            mwD_last[b] = t_k2
            p.wait("scalar", "tensor", t_k2)
            s_sl = p.on("scalar", lambda e, pt=pt, g=g: e.activation(
                silb[:, g, :], pt[:, :], AF.Silu), inc=1)
            ev2[b] = s_sl
        for oc in range(ND):
            t_r2 = 0
            b = oc % 2
            pt = psD[2 + b]
            if ev2[b]:
                p.wait("tensor", "scalar", ev2[b])
            for c in range(ND):
                t_r2 = p.on("tensor", lambda e, pt=pt, oc=oc, c=c: e.matmul(
                    pt[:, :], cmwrb[:, c, oc * PC:(oc + 1) * PC], xr2[:, c, :],
                    start=(c == 0), stop=(c == ND - 1)), inc=(1 if c == ND - 1 else 0))
            p.wait("scalar", "tensor", t_r2)
            s_g2 = p.on("scalar", lambda e, pt=pt, oc=oc: e.activation(
                sgD[:, oc, :], pt[:, :], AF.Sigmoid), inc=1)
            ev2[b] = s_g2
        p.wait("tensor", "scalar", p.wm("scalar"))
        for g in range(NG):
            b = g % 2
            p.wait("gpsimd", "tensor", mwD_last[b])
            g_wv = p.dma("gpsimd", wslD[b][:, :, :],
                         io["a1_cmwv"][g * PC:(g + 1) * PC, :].rearrange(
                             "pp (cc o) -> pp cc o", cc=ND))
            p.wait("tensor", "gpsimd", g_wv)
            t_v2 = 0
            for oc in range(ND):
                t_v2 = p.on("tensor", lambda e, g=g, oc=oc, b=b: e.matmul(
                    psD[oc][:, :], wslD[b][:, oc, :], silb[:, g, :],
                    start=(g == 0), stop=(g == NG - 1)), inc=(1 if oc == ND - 1 else 0))
            mwD_last[b] = t_v2
        p.wait("sync", "vector", p.wm("vector"))
        xin_wm = p.dma("sync", xinD[:, :, :], fm_dram(io["x_in"], ND)[:, :, c0:c0 + TC])
        p.wait("vector", "sync", xin_wm)
        p.wait("vector", "tensor", t_v2)
        v_pr = 0
        for oc in range(ND):
            p.on("vector", lambda e, oc=oc: e.tensor_add(
                xD[:, oc, :], xD[:, oc, :], sgD[:, oc, :]))
            p.on("vector", lambda e, oc=oc: e.tensor_add(
                xD[:, oc, :], xD[:, oc, :], psD[oc][:, :]))
            v_pr = p.on("vector", lambda e, oc=oc: e.tensor_add(
                xD[:, oc, :], xD[:, oc, :], xinD[:, oc, :]), inc=1)
        p.wait("gpsimd", "vector", v_pr)
        g_st = p.dma("gpsimd", fm_dram(io["x_out"], ND)[:, :, c0:c0 + TC], xD[:, :, :])
        p.wait("tensor", "vector", v_pr)
    p.drain_dma("gpsimd")
    p.barrier()
    esD.close()


def declare_l0attna(nc, io):
    io["a1_wk"] = nc.dram_tensor("a1_wk", [D, 4096], F32, kind="ExternalInput").ap()
    io["a1_wv"] = nc.dram_tensor("a1_wv", [D, 4096], F32, kind="ExternalInput").ap()
    io["a1_wr"] = nc.dram_tensor("a1_wr", [D, 4096], F32, kind="ExternalInput").ap()
    io["a1_wo"] = nc.dram_tensor("a1_wo", [4096, D], F32, kind="ExternalInput").ap()
    io["a1_cmwk"] = nc.dram_tensor("a1_cmwk", [D, 4096], F32, kind="ExternalInput").ap()
    io["a1_cmwv"] = nc.dram_tensor("a1_cmwv", [4096, D], F32, kind="ExternalInput").ap()
    io["a1_cmwr"] = nc.dram_tensor("a1_cmwr", [D, D], F32, kind="ExternalInput").ap()
    for nm in ("mixk", "mixv", "mixr", "cmixk", "cmixr", "decay", "first"):
        n = {"decay": 32, "first": 32}.get(nm, ND)
        io["a1_" + nm] = nc.dram_tensor("a1_" + nm, [PC, n], F32, kind="ExternalInput").ap()
    io["a1_lncols"] = nc.dram_tensor("a1_lncols", [PC, 6 * ND], F32, kind="ExternalInput").ap()


def prep_l0attna(inputs):
    d = {}
    d["a1_wk"] = np.ascontiguousarray(inputs["L0_attn_a_tm_Wk"])
    d["a1_wv"] = np.ascontiguousarray(inputs["L0_attn_a_tm_Wv"])
    d["a1_wr"] = np.ascontiguousarray(inputs["L0_attn_a_tm_Wr"])
    d["a1_wo"] = np.ascontiguousarray(inputs["L0_attn_a_tm_Wo"])
    d["a1_cmwk"] = np.ascontiguousarray(inputs["L0_attn_a_cm_Wk"])
    d["a1_cmwv"] = np.ascontiguousarray(inputs["L0_attn_a_cm_Wv"])
    d["a1_cmwr"] = np.ascontiguousarray(inputs["L0_attn_a_cm_Wr"])
    col = lambda a, n: np.ascontiguousarray(np.asarray(a, np.float32).reshape(n, PC).T)
    d["a1_mixk"] = col(inputs["L0_attn_a_tm_mix_k"], ND)
    d["a1_mixv"] = col(inputs["L0_attn_a_tm_mix_v"], ND)
    d["a1_mixr"] = col(inputs["L0_attn_a_tm_mix_r"], ND)
    d["a1_cmixk"] = col(inputs["L0_attn_a_cm_mix_k"], ND)
    d["a1_cmixr"] = col(inputs["L0_attn_a_cm_mix_r"], ND)
    d["a1_decay"] = col(inputs["L0_attn_a_tm_decay"], 32)
    d["a1_first"] = col(inputs["L0_attn_a_tm_first"], 32)
    d["a1_lncols"] = np.ascontiguousarray(np.concatenate(
        [col(inputs["L0_attn_a_" + nm], ND) for nm in
         ("ln0_g", "ln0_b", "pre_g", "pre_b", "post_g", "post_b")], axis=1))
    return d



def emit_l0attnb(p, es, io, C, tag):
    nc = p.nc
    TC = 512
    NT = T // TC
    CH = 128
    NC_ = T // CH
    NXC = 16
    z_dram = nc.dram_tensor(tag + "zd", [PC, NC_, NXC, CH], BF16, kind="Internal").ap()
    xs_dram = nc.dram_tensor(tag + "xsd", [PC, NC_, NXC, CH], BF16, kind="Internal").ap()
    rt_dram = nc.dram_tensor(tag + "rtd", [PC, 1], F32, kind="Internal").ap()
    dd_dram = nc.dram_tensor(tag + "ddd", [1, 64 * CH], F32, kind="Internal").ap()

    convw = sb(es, nc, tag + "sconvw", [PC, 18, 4], F32)
    convb = sb(es, nc, tag + "sconvb", [PC, 18], F32)
    acol = sb(es, nc, tag + "acol", [PC, 1], F32)
    dtb = sb(es, nc, tag + "sdtb", [PC, 1], F32)
    dcolD = sb(es, nc, tag + "sdcolD", [PC, 16], F32)
    nwcol = sb(es, nc, tag + "snwcol", [PC, 16], F32)
    mask64 = sb(es, nc, tag + "smask64", [64, CH], F32)
    p2c = sb(es, nc, tag + "sp2c", [64, 16, PC], BF16)
    dmask = sb(es, nc, tag + "sdmask", [PC, 64 * CH], BF16)
    ctoks = [p.dma("sync", convw[:, :, :], io["b3_convw"].rearrange("c p j -> p c j")),
             p.dma("sync", convb[:, :], io["b3_convb"]),
             p.dma("sync", acol[:, :], io["b3_alog"]),
             p.dma("sync", dtb[:, :], io["b3_dtb"]),
             p.dma("sync", dcolD[:, :], io["b3_dcol"]),
             p.dma("sync", nwcol[:, :], io["b3_nw"]),
             p.dma("sync", mask64[:, :], io["b3_mask64"]),
             p.dma("gpsimd", p2c[:, :, :], io["b3_p2"].rearrange("j h p -> h j p")),
             p.dma("gpsimd", dmask[:, :], io["b3_dmask"])]
    for tok in ctoks:
        for e2 in ("scalar", "vector", "tensor", "gpsimd"):
            p.wait(e2, "sync", tok)
    p.on("scalar", lambda e: e.activation(acol[:, :], acol[:, :], AF.Exp))
    s_a = p.on("scalar", lambda e: e.activation(
        acol[:, :], acol[:, :], AF.Identity, scale=-1.0), inc=1)
    p.wait("gpsimd", "scalar", s_a)
    g_a = p.dma("gpsimd", rt_dram, acol[:, :])
    p.wait("sync", "gpsimd", g_a)
    s_a2 = p.dma("sync", acol[:, :], rt_dram)
    for e2 in ("scalar", "vector", "tensor"):
        p.wait(e2, "sync", s_a2)

    Bf = sb(es, nc, tag + "Bf", [16, T], BF16)
    Cf = sb(es, nc, tag + "Cf", [16, T], BF16)
    dtf = sb(es, nc, tag + "dtf", [64, T], BF16)
    abf = sb(es, nc, tag + "abf", [64, T], F32)

    esW = ExitStack()
    x_sb = fm(esW, nc, tag + "xW", ND, TC, F32)
    x2 = fm(esW, nc, tag + "x2W", ND, TC, F32)
    rows = sb(esW, nc, tag + "rowsW", [1, 2 * TC], F32)
    xnb = fm(esW, nc, tag + "xnbW", ND, TC, BF16)
    wsl = [fm(esW, nc, tag + f"wslW{i}", ND, PC, BF16) for i in range(4)]
    xbc = esW.enter_context(nc.sbuf_tensor(tag + "xbc", [PC, 18, TC + 3], F32))
    zb = fm(esW, nc, tag + "zbW", NXC, TC, BF16)
    xsb = fm(esW, nc, tag + "xsbW", NXC, TC, BF16)
    dtr = sb(esW, nc, tag + "dtr", [64, TC], F32)
    dtt = sb(esW, nc, tag + "dtt", [64, TC], F32)
    psW = [ps(esW, nc, tag + f"psW{i}", [PC, 512]) for i in range(3)]
    v_zw = p.on("vector", lambda e: e.memset(xbc[:, :, 0:3], 0.0), inc=1)

    g_z = 0
    g_xs = 0
    mwin_last = [0, 0, 0, 0]
    evW_last = [0, 0, 0]
    for ci in range(NT):
        c0 = ci * TC
        p.wait("sync", "vector", p.wm("vector"))
        x_wm = p.dma("sync", x_sb[:, :, :], fm_dram(io["x_in"], ND)[:, :, c0:c0 + TC])
        for e2 in ("scalar", "tensor", "vector"):
            p.wait(e2, "sync", x_wm)
        emit_ln(p, C, x_sb, x2, TC, psW[0], psW[1], x2, rows, out_bf=xnb)
        p.wait("tensor", "vector", p.wm("vector"))
        for oc in range(35):
            b = oc % 4
            pt = psW[oc % 3]
            p.wait("gpsimd", "tensor", mwin_last[b])
            g_w = p.dma("gpsimd", wsl[b][:, :, :],
                        fm_dram(io["b3_winT"], ND)[:, :, oc * PC:(oc + 1) * PC])
            p.wait("tensor", "gpsimd", g_w)
            if evW_last[oc % 3]:
                p.wait("tensor", "vector", evW_last[oc % 3])
            t_p = 0
            for c in range(ND):
                t_p = p.on("tensor", lambda e, pt=pt, oc=oc, c=c, b=b: e.matmul(
                    pt[:, :], wsl[b][:, c, :], xnb[:, c, :],
                    start=(c == 0), stop=(c == ND - 1)), inc=(1 if c == ND - 1 else 0))
            mwin_last[b] = t_p
            p.wait("vector", "tensor", t_p)
            if oc < 16:
                if oc == 0:
                    p.wait("vector", "gpsimd", g_z)
                v_e = p.on("vector", lambda e, pt=pt, oc=oc: e.tensor_copy(
                    zb[:, oc, :], pt[:, :]), inc=1)
            elif oc < 34:
                v_e = p.on("vector", lambda e, pt=pt, oc=oc: e.tensor_copy(
                    xbc[:, oc - 16, 3:TC + 3], pt[:, :]), inc=1)
            else:
                v_e = p.on("vector", lambda e, pt=pt: e.tensor_copy(
                    dtr[:, :], pt[0:64, :]), inc=1)
            evW_last[oc % 3] = v_e
        v_wm = p.wm("vector")
        p.wait("gpsimd", "vector", v_wm)
        for cc4 in range(4):
            g_z = p.dma("gpsimd", z_dram[:, ci * 4 + cc4, :, :],
                        zb[:, :, cc4 * CH:(cc4 + 1) * CH], slot=tag + "zd")
        s_cv = 0
        for cc in range(18):
            p.on("vector", lambda e, cc=cc: e.tensor_scalar(
                x2[:, 0, 0:TC], xbc[:, cc, 0:TC], convw[:, cc, 0:1], None,
                ALU.mult))
            for j in range(1, 4):
                p.on("vector", lambda e, cc=cc, j=j: e.scalar_tensor_tensor(
                    x2[:, 0, 0:TC], xbc[:, cc, j:j + TC], convw[:, cc, j:j + 1],
                    x2[:, 0, 0:TC], ALU.mult, ALU.add))
            p.on("vector", lambda e, cc=cc: e.tensor_copy(
                xbc[:, cc, 0:3], xbc[:, cc, TC:TC + 3]))
            v_cv = p.on("vector", lambda e: e.tensor_copy(
                x2[:, 1, 0:TC], x2[:, 0, 0:TC]), inc=1)
            p.wait("scalar", "vector", v_cv)
            s_cv = p.on("scalar", lambda e, cc=cc: e.activation(
                x2[:, 2, 0:TC], x2[:, 1, 0:TC], AF.Silu,
                bias=convb[:, cc:cc + 1]), inc=1)
            p.wait("vector", "scalar", s_cv)
            if cc < 16:
                v_rt = p.on("vector", lambda e, cc=cc: e.tensor_copy(
                    xsb[:, cc, :], x2[:, 2, 0:TC]), inc=1)
            elif cc == 16:
                v_rt = p.on("vector", lambda e, c0=c0: e.tensor_copy(
                    Bf[:, c0:c0 + TC], x2[0:16, 2, 0:TC]), inc=1)
            else:
                v_rt = p.on("vector", lambda e, c0=c0: e.tensor_copy(
                    Cf[:, c0:c0 + TC], x2[0:16, 2, 0:TC]), inc=1)
            p.wait("scalar", "vector", v_rt)
        v_wm2 = p.wm("vector")
        p.wait("gpsimd", "vector", v_wm2)
        for cc4 in range(4):
            g_xs = p.dma("gpsimd", xs_dram[:, ci * 4 + cc4, :, :],
                         xsb[:, :, cc4 * CH:(cc4 + 1) * CH], slot=tag + "xsd")
        p.wait("vector", "gpsimd", g_xs)
        p.on("vector", lambda e: e.tensor_scalar(
            dtr[:, :], dtr[:, :], dtb[0:64, :], None, ALU.add))
        p.on("vector", lambda e: e.scalar_tensor_tensor(
            dtt[:, :], dtr[:, :], -1.0, dtr[:, :], ALU.mult, ALU.max))
        v_dt = p.on("vector", lambda e: e.tensor_scalar(
            dtr[:, :], dtr[:, :], 0.0, None, ALU.max), inc=1)
        p.wait("scalar", "vector", v_dt)
        p.on("scalar", lambda e: e.activation(dtt[:, :], dtt[:, :], AF.Exp, scale=-1.0))
        s_sp = p.on("scalar", lambda e: e.activation(
            dtt[:, :], dtt[:, :], AF.Ln, bias=1.0), inc=1)
        p.wait("vector", "scalar", s_sp)
        v_dt2 = p.on("vector", lambda e: e.tensor_add(
            dtr[:, :], dtr[:, :], dtt[:, :]), inc=1)
        v_dt2 = p.on("vector", lambda e, c0=c0: e.tensor_copy(
            dtf[:, c0:c0 + TC], dtr[:, :]), inc=1)
        v_dt3 = p.on("vector", lambda e, c0=c0: e.tensor_scalar(
            abf[:, c0:c0 + TC], dtr[:, :], acol[0:64, :], None,
            ALU.mult), inc=1)
    p.barrier()
    esW.close()

    esS = ExitStack()
    acum = sb(esS, nc, tag + "acum", [64, CH], F32)
    d0row = sb(esS, nc, tag + "d0row", [64, CH], F32)
    d0rm = sb(esS, nc, tag + "d0rm", [64, CH], F32)
    d0b = sb(esS, nc, tag + "d0b", [PC, 64 * CH], F32)
    Gb = sb(esS, nc, tag + "Gb", [PC, 64 * CH], BF16)
    CcE = esS.enter_context(nc.sbuf_tensor(tag + "CcE", [PC, 64 * CH], BF16))
    corow = sb(esS, nc, tag + "corow", [1, 64], F32)
    corow16 = sb(esS, nc, tag + "corow16", [16, 64], F32)
    dscol = sb(esS, nc, tag + "dscol", [PC, 64], F32)
    CBb = sb(esS, nc, tag + "CBb", [PC, CH], BF16)
    xsc = esS.enter_context(nc.sbuf_tensor(tag + "xsc", [PC, NXC, CH], BF16))
    xdt = esS.enter_context(nc.sbuf_tensor(tag + "xdt", [PC, NXC, CH], BF16))
    xdtT = sb(esS, nc, tag + "xdtT", [PC, 2048], BF16)
    xdtd = sb(esS, nc, tag + "xdtd", [PC, 2048], BF16)
    BcT = sb(esS, nc, tag + "BcT", [PC, 16], BF16)
    stateT = sb(esS, nc, tag + "stateT", [16, 2048], F32)
    prevT = sb(esS, nc, tag + "prevT", [PC, 2048], BF16)
    cd16 = sb(esS, nc, tag + "cd16", [16, 64], F32)
    Ysb = esS.enter_context(nc.sbuf_tensor(tag + "Ysb", [PC, NXC, CH], F32))
    ygr = sb(esS, nc, tag + "ygr", [1, CH], F32)
    xo_sb = fm(esS, nc, tag + "xoS", ND, CH, F32)
    woutb = fm(esS, nc, tag + "woutb", NXC, D, BF16)
    psS = [ps(esS, nc, tag + f"psS{i}", [PC, 512]) for i in range(4)]
    psQ = [ps(esS, nc, tag + f"psQ{i}", [PC, 512]) for i in range(4)]
    psT = psQ[3][:, 0:64].bitcast(BF16)
    psT2 = psQ[2][:, 0:64].bitcast(BF16)
    g_wo = p.dma("gpsimd", woutb[:, :, :], fm_dram(io["b3_woutT"], NXC))
    p.wait("tensor", "gpsimd", g_wo)
    v_zs = p.on("vector", lambda e: e.memset(stateT[:, :], 0.0))
    v_zs = p.on("vector", lambda e: e.memset(prevT[:, :], 0.0))
    v_zs = p.on("vector", lambda e: e.memset(CcE[:, :], 0.0), inc=1)
    p.wait("vector", "vector", v_zs)

    g_st = 0
    for cn in range(NC_):
        l0 = cn * CH
        p.wait("vector", "tensor", p.wm("tensor"))
        v_ac = p.on("vector", lambda e, l0=l0: e.tensor_tensor_scan(
            acum[:, :], C["cval"][1.0][0:64, :].broadcast_to([64, CH]),
            abf[:, l0:l0 + CH], 0.0, ALU.mult, ALU.add), inc=1)
        p.wait("scalar", "vector", v_ac)
        s_d0 = p.on("scalar", lambda e, l0=l0: e.activation(
            d0row[:, :], abf[:, l0:l0 + CH], AF.Exp), inc=1)
        p.wait("vector", "scalar", s_d0)
        v_d0 = p.on("vector", lambda e: e.tensor_mul(
            d0rm[:, :], d0row[:, :], mask64[:, :]), inc=1)
        p.wait("tensor", "vector", v_d0)
        p.wait("gpsimd", "vector", v_d0)
        g_dd = p.dma("gpsimd", dd_dram[0:1, :].rearrange("x (h l) -> (x h) l", h=64),
                     d0rm[:, :])
        p.wait("gpsimd", "gpsimd", g_dd)
        s_dd = p.dma("gpsimd", CcE[0:1, :], dd_dram[0:1, :])
        p.wait("tensor", "gpsimd", s_dd)
        s_bc = 0
        bc_ev = [0, 0]
        for hh in range(0, 64, 4):
            pt = psS[(hh // 4) % 2]
            if bc_ev[(hh // 4) % 2]:
                p.wait("tensor", "scalar", bc_ev[(hh // 4) % 2])
            t_b = p.on("tensor", lambda e, pt=pt, hh=hh: e.matmul(
                pt[:, :], C["ones1b"], CcE[0:1, hh * CH:(hh + 4) * CH],
                start=True, stop=True), inc=1)
            p.wait("scalar", "tensor", t_b)
            s_bc = p.on("scalar", lambda e, pt=pt, hh=hh: e.activation(
                d0b[:, hh * CH:(hh + 4) * CH], pt[:, :], AF.Identity), inc=1)
            bc_ev[(hh // 4) % 2] = s_bc
        p.wait("vector", "scalar", s_bc)
        v_g = p.on("vector", lambda e: e.tensor_tensor_scan(
            Gb[:, :], d0b[:, :], dmask[:, :], 0.0, ALU.mult, ALU.add), inc=1)
        v_ds = p.on("vector", lambda e: e.tensor_copy(
            dscol[:, :].rearrange("s (h x) -> s h x", x=1),
            Gb[:, :].rearrange("s (h l) -> s h l", h=64)[:, :, CH - 1:CH]), inc=1)
        p.wait("gpsimd", "vector", v_d0)
        g_cr = p.dma("gpsimd", rt_dram[0:64, 0:1], d0row[:, 0:1])
        p.wait("sync", "gpsimd", g_cr)
        s_cr = p.dma("sync", corow16[:, :],
                     rt_dram[0:64, 0:1].rearrange("h x -> x h").broadcast_to([16, 64]))
        p.wait("vector", "sync", s_cr)
        p.wait("tensor", "vector", v_g)
        v_eab = 0
        for pc_ in range(16):
            pt = psS[3]
            t_e = p.on("tensor", lambda e, pc_=pc_: e.matmul(
                pt[0:16, :], C["ones1b"][0:1, 0:16],
                Gb[0:1, pc_ * 512:(pc_ + 1) * 512], start=True, stop=True), inc=1)
            p.wait("vector", "tensor", t_e)
            v_eab = p.on("vector", lambda e, pc_=pc_: e.tensor_copy(
                CcE[0:16, pc_ * 512:(pc_ + 1) * 512], pt[0:16, :]), inc=1)
            p.wait("tensor", "vector", v_eab)
        v_cce = p.on("vector", lambda e: e.tensor_mul(
            CcE[0:16, :].rearrange("n (h l) -> n h l", h=64),
            CcE[0:16, :].rearrange("n (h l) -> n h l", h=64),
            corow16[:, :].rearrange("n (h x) -> n h x", x=1)
            .broadcast_to([16, 64, CH])), inc=1)
        v_cce = p.on("vector", lambda e, l0=l0: e.tensor_mul(
            CcE[0:16, :].rearrange("n (h l) -> n h l", h=64),
            CcE[0:16, :].rearrange("n (h l) -> n h l", h=64),
            Cf[:, l0:l0 + CH].rearrange("n (x l) -> n x l", x=1)
            .broadcast_to([16, 64, CH])), inc=1)
        t_cb = p.on("tensor", lambda e, l0=l0: e.matmul(
            psS[2][:, 0:CH], Bf[:, l0:l0 + CH], Cf[:, l0:l0 + CH],
            start=True, stop=True), inc=1)
        p.wait("vector", "tensor", t_cb)
        v_cb = p.on("vector", lambda e: e.tensor_copy(CBb[:, :], psS[2][:, 0:CH]), inc=1)
        v_m = p.on("vector", lambda e: e.tensor_mul(
            Gb[:, :].rearrange("p (h l) -> p h l", h=64),
            Gb[:, :].rearrange("p (h l) -> p h l", h=64),
            CBb[:, :].rearrange("p (a l) -> p a l", a=1).broadcast_to([PC, 64, CH])),
            inc=1)
        p.wait("sync", "vector", p.wm("vector"))
        xs_wm = p.dma("sync", xsc[:, :, :], xs_dram[:, cn, :, :])
        for e2 in ("tensor", "vector", "scalar"):
            p.wait(e2, "sync", xs_wm)
        for jj in range(0, 16, 4):
            pt = psS[2]
            t_d = 0
            for j in range(4):
                t_d = p.on("tensor", lambda e, pt=pt, jj=jj, j=j, l0=l0: e.matmul(
                    pt[:, j * CH:(j + 1) * CH], p2c[:, jj + j, :],
                    dtf[:, l0:l0 + CH], start=True, stop=True),
                    inc=(1 if j == 3 else 0))
            p.wait("vector", "tensor", t_d)
            v_x = 0
            for j in range(4):
                v_x = p.on("vector", lambda e, pt=pt, jj=jj, j=j: e.tensor_mul(
                    xdt[:, jj + j, :], xsc[:, jj + j, :], pt[:, j * CH:(j + 1) * CH]),
                    inc=1)
            p.wait("tensor", "vector", v_x)
        tr_ev = [0, 0]
        for j in range(NXC):
            pT = (psT, psT2)[j % 2]
            if tr_ev[j % 2]:
                p.wait("tensor", "vector", tr_ev[j % 2])
            t_t = p.on("tensor", lambda e, j=j, pT=pT: e.transpose(
                pT[:, :], xdt[:, j, :], C["identbf"]), inc=1)
            p.wait("vector", "tensor", t_t)
            v_t = p.on("vector", lambda e, j=j, pT=pT: e.tensor_copy(
                xdtT[:, j * CH:(j + 1) * CH], pT[:, :]), inc=1)
            tr_ev[j % 2] = v_t
        p.wait("tensor", "vector", v_t)
        t_bt = p.on("tensor", lambda e, l0=l0: e.transpose(
            psT[:, 0:16], Bf[:, l0:l0 + CH], C["identbf"][0:16, 0:16]), inc=1)
        p.wait("vector", "tensor", t_bt)
        v_bt = p.on("vector", lambda e: e.tensor_copy(BcT[:, :], psT[:, 0:16]), inc=1)
        v_xd = p.on("vector", lambda e: e.tensor_mul(
            xdtd[:, :].rearrange("s (h pp) -> s h pp", h=64),
            xdtT[:, :].rearrange("s (h pp) -> s h pp", h=64),
            dscol[:, :].rearrange("s (h x) -> s h x", x=1)
            .broadcast_to([PC, 64, 32])), inc=1)
        p.wait("tensor", "vector", max(v_bt, v_xd))
        p.wait("tensor", "scalar", s_bc)
        t_st = 0
        for ff in range(4):
            t_st = p.on("tensor", lambda e, ff=ff: e.matmul(
                psS[ff][0:16, :], BcT[:, :], xdtd[:, ff * 512:(ff + 1) * 512],
                start=True, stop=True), inc=1)
        p.wait("scalar", "vector", v_ac)
        s_cd = p.on("scalar", lambda e: e.activation(
            d0rm[:, 0:1], acum[:, CH - 1:CH], AF.Exp), inc=1)
        p.wait("gpsimd", "scalar", s_cd)
        g_cd = p.dma("gpsimd", rt_dram[0:64, 0:1], d0rm[:, 0:1])
        p.wait("sync", "gpsimd", g_cd)
        s_cd2 = p.dma("sync", corow[:, :], rt_dram[0:64, 0:1].rearrange("h 1 -> 1 h"))
        p.wait("tensor", "sync", s_cd2)
        t_cd = p.on("tensor", lambda e: e.matmul(
            psT[0:16, 0:64].bitcast(BF16) if False else psS[0][0:16, 0:64],
            C["ones1"][0:1, 0:16], corow[:, :], start=True, stop=True), inc=1) \
            if False else p.on("tensor", lambda e: e.matmul(
                psQ[0][0:16, 0:64], C["ones1"][0:1, 0:16], corow[:, :],
                start=True, stop=True), inc=1)
        p.wait("vector", "tensor", max(t_st, t_cd))
        v_cd = p.on("vector", lambda e: e.tensor_copy(cd16[:, :], psQ[0][0:16, 0:64]), inc=1)
        v_pv = p.on("vector", lambda e: e.tensor_copy(
            prevT[0:16, :], stateT[:, :]), inc=1)
        v_st = p.on("vector", lambda e: e.tensor_mul(
            stateT[:, :].rearrange("n (h pp) -> n h pp", h=64),
            stateT[:, :].rearrange("n (h pp) -> n h pp", h=64),
            cd16[:, :].rearrange("n (h x) -> n h x", x=1)
            .broadcast_to([16, 64, 32])))
        for ff in range(4):
            v_st = p.on("vector", lambda e, ff=ff: e.tensor_add(
                stateT[:, ff * 512:(ff + 1) * 512],
                stateT[:, ff * 512:(ff + 1) * 512], psS[ff][0:16, :]), inc=1)
        p.wait("tensor", "vector", max(v_m, v_cce, v_pv, v_st))
        def yslot(h):
            if h % 4 < 3:
                t3 = h // 4
                return psQ[t3 // 4], (h % 4) * 32, (t3 % 4) * 128
            sl = h // 4
            return (psS[0], psS[1])[sl // 8], ((sl % 8) % 3) * 32, \
                ((sl % 8) // 3) * 128
        t_y = 0
        for h in range(64):
            pq, prow, qoff = yslot(h)
            p.on("tensor", lambda e, h=h, pq=pq, qoff=qoff, prow=prow: e.matmul(
                pq[prow:prow + 32, qoff:qoff + CH],
                xdtT[:, h * 32:(h + 1) * 32], Gb[:, h * CH:(h + 1) * CH],
                start=True, stop=False))
            t_y = p.on("tensor", lambda e, h=h, pq=pq, qoff=qoff, prow=prow: e.matmul(
                pq[prow:prow + 32, qoff:qoff + CH],
                prevT[:, h * 32:(h + 1) * 32], CcE[:, h * CH:(h + 1) * CH],
                start=False, stop=True), inc=1)
        p.wait("vector", "tensor", t_y)
        v_y = 0
        for t3 in range(16):
            pq, qoff = psQ[t3 // 4], (t3 % 4) * 128
            v_y = p.on("vector",
                       lambda e, pq=pq, qoff=qoff, j=t3:
                       e.scalar_tensor_tensor(
                           Ysb[0:96, j, :], xsc[0:96, j, :],
                           dcolD[0:96, j:j + 1],
                           pq[0:96, qoff:qoff + CH],
                           ALU.mult, ALU.add), inc=1)
        for h in range(3, 64, 4):
            pq, prow, qoff = yslot(h)
            j = h // 4
            v_y = p.on("vector",
                       lambda e, pq=pq, prow=prow, qoff=qoff, j=j:
                       e.scalar_tensor_tensor(
                           Ysb[96:128, j, :], xsc[96:128, j, :],
                           dcolD[96:128, j:j + 1],
                           pq[prow:prow + 32, qoff:qoff + CH],
                           ALU.mult, ALU.add), inc=1)
        p.wait("tensor", "vector", v_y)
        p.wait("sync", "tensor", t_y)
        z_wm = p.dma("sync", xdtT[:, :].rearrange("p (a b) -> p a b", a=NXC),
                     z_dram[:, cn, :, :])
        p.wait("scalar", "sync", z_wm)
        p.wait("scalar", "vector", v_y)
        p.wait("scalar", "tensor", p.wm("tensor"))
        s_z = p.on("scalar", lambda e: e.activation(
            d0b[:, 0:NXC * CH].rearrange("p (a b) -> p a b", a=NXC),
            xdtT[:, :].rearrange("p (a b) -> p a b", a=NXC), AF.Silu), inc=1)
        p.wait("vector", "scalar", s_z)
        v_yg = 0
        for j in range(NXC):
            v_yg = p.on("vector", lambda e, j=j: e.tensor_mul(
                Ysb[:, j, :], Ysb[:, j, :],
                d0b[:, j * CH:(j + 1) * CH]), inc=1)
        p.wait("scalar", "vector", v_yg)
        s_q = p.on("scalar", lambda e: e.activation(
            d0b[:, 0:NXC * CH].rearrange("p (a b) -> p a b", a=NXC),
            Ysb[:, :, :], AF.Square), inc=1)
        p.wait("tensor", "scalar", s_q)
        t_ms = 0
        for j in range(NXC):
            t_ms = p.on("tensor", lambda e, j=j: e.matmul(
                psS[0][0:1, 0:CH], C["invn"],
                d0b[:, j * CH:(j + 1) * CH],
                start=(j == 0), stop=(j == NXC - 1)), inc=(1 if j == NXC - 1 else 0))
        p.wait("vector", "tensor", t_ms)
        v_ms = p.on("vector", lambda e: e.tensor_scalar(
            ygr[:, :], psS[0][0:1, 0:CH], 0.5, None, ALU.mult), inc=1)
        p.wait("scalar", "vector", v_ms)
        p.on("scalar", lambda e: e.activation(ygr[:, :], ygr[:, :], AF.Ln, bias=1e-5))
        s_rs = p.on("scalar", lambda e: e.activation(
            ygr[:, :], ygr[:, :], AF.Exp, scale=-0.5), inc=1)
        p.wait("tensor", "scalar", s_rs)
        t_bcr = p.on("tensor", lambda e: e.matmul(
            psS[1][:, 0:CH], C["ones1"], ygr[:, :], start=True, stop=True), inc=1)
        p.wait("vector", "tensor", t_bcr)
        v_yn = 0
        for j in range(NXC):
            p.on("vector", lambda e, j=j: e.tensor_mul(
                Ysb[:, j, :], Ysb[:, j, :], psS[1][:, 0:CH]))
            v_yn = p.on("vector", lambda e, j=j: e.tensor_scalar(
                xdt[:, j, :], Ysb[:, j, :], nwcol[:, j:j + 1], None, ALU.mult), inc=1)
        p.wait("tensor", "vector", v_yn)
        t_wo = 0
        for oc in range(ND):
            pw = psS[2 + oc % 2][:, (oc // 2 % 2) * CH:(oc // 2 % 2) * CH + CH] \
                if False else psS[2][:, (oc % 4) * CH:(oc % 4) * CH + CH]
            pw = (psS[2], psS[3])[oc // 4][:, (oc % 4) * CH:(oc % 4) * CH + CH]
            for j in range(NXC):
                t_wo = p.on("tensor", lambda e, pw=pw, oc=oc, j=j: e.matmul(
                    pw, woutb[:, j, oc * PC:(oc + 1) * PC], xdt[:, j, :],
                    start=(j == 0), stop=(j == NXC - 1)), inc=(1 if j == NXC - 1 else 0))
        p.wait("sync", "vector", p.wm("vector"))
        p.wait("sync", "gpsimd", g_st)
        xi_wm = p.dma("sync", xo_sb[:, :, :], fm_dram(io["x_in"], ND)[:, :, l0:l0 + CH])
        p.wait("vector", "sync", xi_wm)
        p.wait("vector", "tensor", t_wo)
        v_f = 0
        for oc in range(ND):
            pw = (psS[2], psS[3])[oc // 4][:, (oc % 4) * CH:(oc % 4) * CH + CH]
            v_f = p.on("vector", lambda e, pw=pw, oc=oc: e.tensor_add(
                xo_sb[:, oc, :], pw, xo_sb[:, oc, :]), inc=1)
        p.wait("gpsimd", "vector", v_f)
        g_st = p.dma("gpsimd", fm_dram(io["x_out"], ND)[:, :, l0:l0 + CH], xo_sb[:, :, :])
        p.wait("tensor", "vector", v_f)
    p.drain_dma("gpsimd")
    p.barrier()
    esS.close()


def declare_l0attnb(nc, io):
    io["b3_winT"] = nc.dram_tensor("b3_winT", [D, 4480], F32, kind="ExternalInput").ap()
    io["b3_woutT"] = nc.dram_tensor("b3_woutT", [2048, D], F32, kind="ExternalInput").ap()
    io["b3_convw"] = nc.dram_tensor("b3_convw", [18, PC, 4], F32, kind="ExternalInput").ap()
    io["b3_convb"] = nc.dram_tensor("b3_convb", [PC, 18], F32, kind="ExternalInput").ap()
    io["b3_alog"] = nc.dram_tensor("b3_alog", [PC, 1], F32, kind="ExternalInput").ap()
    io["b3_dtb"] = nc.dram_tensor("b3_dtb", [PC, 1], F32, kind="ExternalInput").ap()
    io["b3_dcol"] = nc.dram_tensor("b3_dcol", [PC, 16], F32, kind="ExternalInput").ap()
    io["b3_nw"] = nc.dram_tensor("b3_nw", [PC, 16], F32, kind="ExternalInput").ap()
    io["b3_mask64"] = nc.dram_tensor("b3_mask64", [64, 128], F32, kind="ExternalInput").ap()
    io["b3_p2"] = nc.dram_tensor("b3_p2", [16, 64, PC], F32, kind="ExternalInput").ap()
    io["b3_dmask"] = nc.dram_tensor("b3_dmask", [PC, 64 * 128], F32, kind="ExternalInput").ap()


def prep_l0attnb(inputs):
    d = {}
    W_in = inputs["L0_attn_b_W_in"]
    winp = np.zeros((4480, D), np.float32)
    winp[0:2048] = W_in[0:2048]
    winp[2048:4096] = W_in[2048:4096]
    winp[4096:4112] = W_in[4096:4112]
    winp[4224:4240] = W_in[4112:4128]
    winp[4352:4416] = W_in[4128:4192]
    d["b3_winT"] = np.ascontiguousarray(winp.T)
    d["b3_woutT"] = np.ascontiguousarray(inputs["L0_attn_b_W_out"].T)
    cw = inputs["L0_attn_b_conv_w"][:, 0, :]
    cwp = np.zeros((18 * PC, 4), np.float32)
    cwp[0:2048] = cw[0:2048]
    cwp[2048:2064] = cw[2048:2064]
    cwp[2176:2192] = cw[2064:2080]
    d["b3_convw"] = np.ascontiguousarray(cwp.reshape(18, PC, 4))
    cb = inputs["L0_attn_b_conv_b"]
    cbp = np.zeros((18 * PC,), np.float32)
    cbp[0:2048] = cb[0:2048]
    cbp[2048:2064] = cb[2048:2064]
    cbp[2176:2192] = cb[2064:2080]
    d["b3_convb"] = np.ascontiguousarray(cbp.reshape(18, PC).T)
    al = np.zeros((PC, 1), np.float32)
    al[0:64, 0] = inputs["L0_attn_b_A_log"]
    d["b3_alog"] = al
    db = np.zeros((PC, 1), np.float32)
    db[0:64, 0] = inputs["L0_attn_b_dt_bias"]
    d["b3_dtb"] = db
    d["b3_dcol"] = np.ascontiguousarray(
        np.repeat(inputs["L0_attn_b_D"], 32).reshape(16, PC).T)
    d["b3_nw"] = np.ascontiguousarray(inputs["L0_attn_b_norm_w"].reshape(16, PC).T)
    m64 = np.ones((64, 128), np.float32)
    m64[:, 0] = 0.0
    d["b3_mask64"] = m64
    p2 = np.zeros((16, 64, PC), np.float32)
    for j in range(16):
        for pp in range(PC):
            p2[j, (j * PC + pp) // 32, pp] = 1.0
    d["b3_p2"] = p2
    dm = np.zeros((PC, 64 * 128), np.float32)
    for h in range(64):
        for s in range(128):
            dm[s, h * 128 + s] = 1.0
    d["b3_dmask"] = dm
    return d



def emit_l1attna(p, es, io, C, tag):
    nc = p.nc
    TC = 512
    NT = T // TC
    NH = 16
    q_dram = nc.dram_tensor(tag + "qd", [PC, 24, T], BF16, kind="Internal").ap()
    ao_dram = nc.dram_tensor(tag + "aod", [PC, 8, T], BF16, kind="Internal").ap()

    esA = ExitStack()
    wqkv = fm(esA, nc, tag + "s_wqkv", ND, 3072, BF16)
    bq = sb(esA, nc, tag + "s_bq", [PC, 24], F32)
    cosq = sb(esA, nc, tag + "s_cosq", [PC, T], F32)
    sinq = sb(esA, nc, tag + "s_sinq", [PC, T], F32)
    cosk = sb(esA, nc, tag + "s_cosk", [PC, T], F32)
    sink = sb(esA, nc, tag + "s_sink", [PC, T], F32)
    g_w = p.dma("gpsimd", wqkv[:, :, :], fm_dram(io["a5_wqkvT"], ND))
    toks = [p.dma("sync", bq[:, :], io["a5_bqkv"]),
            p.dma("sync", cosq[:, :], io["a5_cosq"]),
            p.dma("sync", sinq[:, :], io["a5_sinq"]),
            p.dma("sync", cosk[:, :], io["a5_cosk"]),
            p.dma("sync", sink[:, :], io["a5_sink"])]
    p.wait("tensor", "gpsimd", g_w)
    for tok in toks:
        p.wait("tensor", "sync", tok)
        p.wait("vector", "sync", tok)

    x_sb = fm(esA, nc, tag + "x", ND, TC, F32)
    x2 = fm(esA, nc, tag + "x2", ND, TC, F32)
    rows = sb(esA, nc, tag + "rows", [1, 2 * TC], F32)
    xnb = fm(esA, nc, tag + "xnb", ND, TC, BF16)
    ev = sb(esA, nc, tag + "ev", [PC, TC], F32)
    qs = sb(esA, nc, tag + "qs", [PC, TC], F32)
    qb = sb(esA, nc, tag + "qb", [PC, TC], BF16)
    psA_ = ps(esA, nc, tag + "psA", [PC, 512])
    psB_ = ps(esA, nc, tag + "psB", [PC, 512])
    pool = [ps(esA, nc, tag + f"pp{i}", [PC, 512]) for i in range(3)]

    v_ev = 0
    for ci in range(NT):
        c0 = ci * TC
        p.wait("sync", "vector", v_ev)
        x_wm = p.dma("sync", x_sb[:, :, :], fm_dram(io["x_in"], ND)[:, :, c0:c0 + TC])
        for e2 in ("scalar", "tensor", "vector"):
            p.wait(e2, "sync", x_wm)
        emit_ln(p, C, x_sb, x2, TC, psA_, psB_, x2, rows, out_bf=xnb)
        p.wait("tensor", "vector", p.wm("vector"))
        for oc in range(24):
            pt = pool[oc % 3]
            t_p = 0
            for c in range(ND):
                t_p = p.on("tensor", lambda e, pt=pt, oc=oc, c=c: e.matmul(
                    pt[:, :], wqkv[:, c, oc * PC:(oc + 1) * PC], xnb[:, c, :],
                    start=(c == 0), stop=(c == ND - 1)), inc=(1 if c == ND - 1 else 0))
            p.wait("vector", "tensor", t_p)
            p.on("vector", lambda e, pt=pt, oc=oc: e.tensor_scalar(
                ev[:, :], pt[:, :], bq[:, oc:oc + 1], None, ALU.add))
            if oc < 16:
                cos_t, sin_t = (cosq, sinq) if oc < 8 else (cosk, sink)
                p.on("vector", lambda e: e.tensor_scalar(
                    qs[0:32, :], ev[32:64, :], -1.0, None, ALU.mult))
                p.on("vector", lambda e: e.tensor_copy(qs[32:64, :], ev[0:32, :]))
                p.on("vector", lambda e: e.tensor_scalar(
                    qs[64:96, :], ev[96:128, :], -1.0, None, ALU.mult))
                p.on("vector", lambda e: e.tensor_copy(qs[96:128, :], ev[64:96, :]))
                p.on("vector", lambda e, cos_t=cos_t, c0=c0: e.tensor_mul(
                    ev[:, :], ev[:, :], cos_t[:, c0:c0 + TC]))
                p.on("vector", lambda e, sin_t=sin_t, c0=c0: e.tensor_mul(
                    qs[:, :], qs[:, :], sin_t[:, c0:c0 + TC]))
                p.on("vector", lambda e: e.tensor_add(ev[:, :], ev[:, :], qs[:, :]))
            v_ev = p.on("vector", lambda e: e.tensor_copy(qb[:, :], ev[:, :]), inc=1)
            p.wait("tensor", "vector", v_ev)
            p.wait("gpsimd", "vector", v_ev)
            g_sp = p.dma("gpsimd", q_dram[:, oc, c0:c0 + TC], qb[:, :])
            p.wait("vector", "gpsimd", g_sp)
    p.barrier()
    esA.close()

    esB = ExitStack()
    kh = sb(esB, nc, tag + "kh", [PC, T], BF16)
    vh = sb(esB, nc, tag + "vh", [PC, T], BF16)
    vt = sb(esB, nc, tag + "vt", [PC, 32, 65], BF16)
    qt_b = sb(esB, nc, tag + "qtb", [PC, TC], BF16)
    P_sb = [sb(esB, nc, tag + f"Psb{i}", [PC, 1024], BF16) for i in range(2)]
    zrow = sb(esB, nc, tag + "zr", [1, TC], F32)
    zb = sb(esB, nc, tag + "zb", [64, TC], F32)
    ao = sb(esB, nc, tag + "ao", [64, TC], BF16)
    ps_s = [ps(esB, nc, tag + f"pss{i}", [PC, 1024]) for i in range(2)]
    ps_o = ps(esB, nc, tag + "pso", [PC, 512])
    ps_tr = esB.enter_context(nc.psum_tensor(tag + "pstr", [PC, 64], BF16))
    ps_z = ps(esB, nc, tag + "psz", [64, 512])
    v_z1 = p.on("vector", lambda e: e.memset(vt[:, :, 64:65], 1.0), inc=1)
    p.wait("vector", "vector", v_z1)

    for h in range(NH):
        hrow = (h % 2) * 64
        hc = h // 2
        p.wait("sync", "tensor", p.wm("tensor"))
        p.wait("sync", "vector", p.wm("vector"))
        p.dma("sync", kh[hrow:hrow + 64, :], q_dram[hrow:hrow + 64, 8 + hc, :])
        v_wm = p.dma("sync", vh[hrow:hrow + 64, :], q_dram[hrow:hrow + 64, 16 + hc, :])
        p.wait("tensor", "sync", v_wm)
        for kt in range(32):
            t_tr = p.on("tensor", lambda e, kt=kt, hrow=hrow: e.transpose(
                ps_tr[:, :], vh[hrow:hrow + 64, kt * 128:(kt + 1) * 128],
                C["identbf"][hrow:hrow + 64, hrow:hrow + 64]), inc=1)
            p.wait("vector", "tensor", t_tr)
            v_vt = p.on("vector", lambda e, kt=kt: e.tensor_copy(
                vt[:, kt, 0:64], ps_tr[:, :]), inc=1)
            p.wait("tensor", "vector", v_vt)
        for qt in range(NT):
            q0 = qt * TC
            p.wait("sync", "tensor", p.wm("tensor"))
            q_wm = p.dma("sync", qt_b[hrow:hrow + 64, :],
                         q_dram[hrow:hrow + 64, hc, q0:q0 + TC])
            p.wait("tensor", "sync", q_wm)
            av_wm = {0: 0, 1: 0}
            s_exps = {}

            def emit_qk(kg):
                pss = ps_s[kg % 2]
                t_s = 0
                for j in range(2):
                    kt = kg * 2 + j
                    t_s = p.on("tensor", lambda e, pss=pss, j=j, kt=kt, hrow=hrow:
                               e.matmul(pss[:, j * 512:(j + 1) * 512],
                                        kh[hrow:hrow + 64, kt * 128:(kt + 1) * 128],
                                        qt_b[hrow:hrow + 64, :],
                                        start=True, stop=True), inc=(1 if j == 1 else 0))
                p.wait("scalar", "tensor", t_s)
                p.wait("scalar", "tensor", av_wm[kg % 2])
                s_exps[kg] = p.on("scalar", lambda e, pss=pss, kg=kg: e.activation(
                    P_sb[kg % 2][:, :], pss[:, :], AF.Exp), inc=1)

            emit_qk(0)
            for kg in range(16):
                if kg + 1 < 16:
                    emit_qk(kg + 1)
                p.wait("tensor", "scalar", s_exps[kg])
                for j in range(2):
                    kt = kg * 2 + j
                    av_wm[kg % 2] = p.on("tensor", lambda e, j=j, kt=kt, kg=kg: e.matmul(
                        ps_o[0:65, :], vt[:, kt, :], P_sb[kg % 2][:, j * 512:(j + 1) * 512],
                        start=(kt == 0), stop=(kt == 31)), inc=1)
            t_av = p.wm("tensor")
            p.wait("scalar", "tensor", t_av)
            p.on("scalar", lambda e: e.activation(zrow[:, :], ps_o[64:65, :], AF.Ln))
            s_z = p.on("scalar", lambda e: e.activation(
                zrow[:, :], zrow[:, :], AF.Exp, scale=-1.0), inc=1)
            p.wait("tensor", "scalar", s_z)
            t_zb = p.on("tensor", lambda e: e.matmul(
                ps_z[:, :], C["ones1"][0:1, 0:64], zrow[:, :], start=True, stop=True), inc=1)
            p.wait("vector", "tensor", t_zb)
            p.on("vector", lambda e: e.tensor_copy(zb[:, :], ps_z[:, :]))
            v_ao = p.on("vector", lambda e: e.tensor_mul(
                ao[:, :], zb[:, :], ps_o[0:64, :]), inc=1)
            p.wait("gpsimd", "vector", v_ao)
            g_ao = p.dma("gpsimd", ao_dram[hrow:hrow + 64, hc, q0:q0 + TC], ao[:, :])
            p.wait("vector", "gpsimd", g_ao)
            p.wait("tensor", "vector", p.wm("vector"))
    p.barrier()
    esB.close()

    esC = ExitStack()
    wout = fm(esC, nc, tag + "s_wout", ND, D, BF16)
    bo = sb(esC, nc, tag + "s_bo", [PC, 8], F32)
    aob = fm(esC, nc, tag + "aob", ND, TC, BF16)
    x_sb2 = fm(esC, nc, tag + "x2b", ND, TC, F32)
    psO = [ps(esC, nc, tag + f"psO{i}", [PC, 512]) for i in range(4)]
    g_wo = p.dma("gpsimd", wout[:, :, :], fm_dram(io["a5_woutT"], ND))
    g_w2 = p.dma("sync", bo[:, :], io["a5_bout"])
    p.wait("tensor", "gpsimd", g_wo)
    p.wait("vector", "sync", g_w2)
    v_prev = 0
    g_st = 0
    for ci in range(NT):
        c0 = ci * TC
        p.wait("sync", "vector", v_prev)
        p.wait("sync", "gpsimd", g_st)
        x_wm = p.dma("sync", x_sb2[:, :, :], fm_dram(io["x_in"], ND)[:, :, c0:c0 + TC])
        a_wm = p.dma("sync", aob[:, :, :], ao_dram[:, :, c0:c0 + TC])
        for e2 in ("tensor", "vector"):
            p.wait(e2, "sync", a_wm)
        v_prev = 0
        for f in range(2):
            fs = f * 256
            t_p = 0
            for oc in range(ND):
                pt = psO[oc // 2][:, (oc % 2) * 256:(oc % 2) * 256 + 256]
                for c in range(ND):
                    t_p = p.on("tensor", lambda e, pt=pt, oc=oc, c=c, fs=fs: e.matmul(
                        pt, wout[:, c, oc * PC:(oc + 1) * PC], aob[:, c, fs:fs + 256],
                        start=(c == 0), stop=(c == ND - 1)),
                        inc=(1 if c == ND - 1 else 0))
            p.wait("vector", "tensor", t_p)
            for oc in range(ND):
                pt = psO[oc // 2][:, (oc % 2) * 256:(oc % 2) * 256 + 256]
                v_prev = p.on("vector", lambda e, pt=pt, oc=oc, fs=fs:
                              e.scalar_tensor_tensor(
                                  x_sb2[:, oc, fs:fs + 256], pt, bo[:, oc:oc + 1],
                                  x_sb2[:, oc, fs:fs + 256], ALU.add, ALU.add), inc=1)
            if f == 0:
                p.wait("tensor", "vector", v_prev)
        p.wait("gpsimd", "vector", v_prev)
        g_st = p.dma("gpsimd", fm_dram(io["x_out"], ND)[:, :, c0:c0 + TC], x_sb2[:, :, :])
    p.drain_dma("gpsimd")
    p.barrier()
    esC.close()


def declare_l1attna(nc, io):
    io["a5_wqkvT"] = nc.dram_tensor("a5_wqkvT", [D, 3072], F32, kind="ExternalInput").ap()
    io["a5_bqkv"] = nc.dram_tensor("a5_bqkv", [PC, 24], F32, kind="ExternalInput").ap()
    io["a5_woutT"] = nc.dram_tensor("a5_woutT", [D, D], F32, kind="ExternalInput").ap()
    io["a5_bout"] = nc.dram_tensor("a5_bout", [PC, 8], F32, kind="ExternalInput").ap()
    for nm in ("cosq", "sinq", "cosk", "sink"):
        io["a5_" + nm] = nc.dram_tensor("a5_" + nm, [PC, T], F32, kind="ExternalInput").ap()


def prep_l1attna(inputs):
    d = {}
    d["a5_wqkvT"] = np.ascontiguousarray(inputs["L1_attn_a_W_qkv"].T)
    d["a5_bqkv"] = np.ascontiguousarray(inputs["L1_attn_a_b_qkv"].reshape(24, PC).T)
    d["a5_woutT"] = np.ascontiguousarray(inputs["L1_attn_a_W_out"].T)
    d["a5_bout"] = np.ascontiguousarray(inputs["L1_attn_a_b_out"].reshape(8, PC).T)
    hd = 64
    inv_freq = 1.0 / 10000.0 ** (np.arange(0, hd, 2, dtype=np.float64) / hd)
    t = np.arange(T, dtype=np.float64)
    freqs = np.einsum("i,j->ij", t, inv_freq).astype(np.float32)
    emb = np.concatenate([freqs, freqs], axis=-1)
    cos = np.cos(emb).T.astype(np.float32)
    sin = np.sin(emb).T.astype(np.float32)
    cos2 = np.concatenate([cos, cos], 0)
    sin2 = np.concatenate([sin, sin], 0)
    d["a5_cosq"] = np.ascontiguousarray(cos2 * 0.125)
    d["a5_sinq"] = np.ascontiguousarray(sin2 * 0.125)
    d["a5_cosk"] = np.ascontiguousarray(cos2)
    d["a5_sink"] = np.ascontiguousarray(sin2)
    return d



def declare_common(nc, io):
    io["x_in"] = nc.dram_tensor("x_in", [D, T], F32, kind="ExternalInput").ap()
    io["x_out"] = nc.dram_tensor("x_out", [D, T], F32, kind="ExternalOutput").ap()
    io["c_invn"] = nc.dram_tensor("c_invn", [PC, 1], F32, kind="ExternalInput").ap()
    io["c_ones1"] = nc.dram_tensor("c_ones1", [1, PC], F32, kind="ExternalInput").ap()
    io["c_ones8col"] = nc.dram_tensor("c_ones8col", [8, 1], F32, kind="ExternalInput").ap()
    io["c_vals"] = nc.dram_tensor("c_vals", [PC, len(CONST_VALS)], F32,
                                  kind="ExternalInput").ap()
    io["c_ident"] = nc.dram_tensor("c_ident", [PC, PC], F32, kind="ExternalInput").ap()


def host_consts():
    return {
        "c_invn": np.full((PC, 1), 1.0 / D, np.float32),
        "c_ones1": np.ones((1, PC), np.float32),
        "c_ones8col": np.ones((8, 1), np.float32),
        "c_vals": np.tile(np.asarray(CONST_VALS, np.float32)[None, :], (PC, 1)),
        "c_ident": np.eye(PC, dtype=np.float32),
    }


BF16_SHADOW = ["ffn0_wcpT", "ffn0_bd2", "a1_wk", "a1_wv", "a1_wr", "a1_wo",
               "a1_cmwk", "a1_cmwv", "a1_cmwr", "b3_winT", "b3_woutT"]


def add_bf16_shadows(nc, p, io):
    toks = []

    def shadow(name, ap):
        sh = nc.dram_tensor(f"bfsh_{name}", list(ap.shape), BF16,
                            kind="Internal").ap()
        n = ap.shape[0]
        step = max(1, (8 << 20) // max(1, int(np.prod(ap.shape[1:])) * 4))
        for r0 in range(0, n, step):
            r1 = min(n, r0 + step)
            toks.append(p.dma("gpsimd", sh[r0:r1], ap[r0:r1],
                              slot=f"bfsh_{name}"))
        return sh

    for key in BF16_SHADOW:
        if key not in io:
            continue
        v = io[key]
        if isinstance(v, list):
            io[key] = [shadow(f"{key}_{j}", ap) for j, ap in enumerate(v)]
        else:
            io[key] = shadow(key, v)
    for t in toks:
        for e2 in ENGS:
            p.wait(e2, t[0], t)


def _build(declares, emits):
    nc = bass.Bass("TRN2", target_bir_lowering=False)
    io = {}
    declare_common(nc, io)
    for d in declares:
        d(nc, io)
    es = ExitStack()
    p = Prog(nc, es)
    add_bf16_shadows(nc, p, io)
    C = load_consts(p, es, nc, io)
    C["ones8col"] = sb(es, nc, "sb_o8", [8, 1], F32)[:, :]
    wm = p.dma("sync", C["ones8col"], io["c_ones8col"])
    p.wait("tensor", "sync", wm)
    for em in emits:
        em(p, es, io, C)
    block = es.enter_context(nc.Block())
    p.finalize(block)
    es.close()
    return nc


def _run(nc, x_full, weight_dicts):
    cc = host_consts()
    in_maps = []
    for core in range(8):
        b = core // 2
        m = {"x_in": np.ascontiguousarray(x_full[b].T)}
        for wd in weight_dicts:
            m.update(wd)
        m.update(cc)
        in_maps.append(m)
    res = run_bass_kernel_spmd(nc, in_maps, core_ids=list(range(8)))
    return np.stack([np.asarray(res.results[2 * b]["x_out"]).T for b in range(B)])


FUSED_EMITS = [
    ("a1_", lambda p, es, io, C, tag: emit_l0attna(p, es, io, C, tag)),
    ("f0_", lambda p, es, io, C, tag: emit_l0ffn(p, es, io, C, tag)),
    ("b3_", lambda p, es, io, C, tag: emit_l0attnb(p, es, io, C, tag)),
    ("f2_", lambda p, es, io, C, tag: emit_l0ffn(p, es, io, C, tag)),
    ("a5_", lambda p, es, io, C, tag: emit_l1attna(p, es, io, C, tag)),
    ("f1_", lambda p, es, io, C, tag: emit_l1ffn(p, es, io, C, tag)),
    ("b7_", lambda p, es, io, C, tag: emit_l1attnb(p, es, io, C, tag)),
    ("f3_", lambda p, es, io, C, tag: emit_l1ffn(p, es, io, C, tag)),
]


def build_fused():
    nc = bass.Bass("TRN2", target_bir_lowering=False)
    io = {}
    declare_common(nc, io)
    for d in (declare_l0attna, declare_l0ffn, declare_l0attnb,
              declare_l1attna, declare_l1ffn, declare_l1attnb):
        d(nc, io)
    xs = [io["x_in"]]
    for i in range(len(FUSED_EMITS) - 1):
        xs.append(nc.dram_tensor(f"xh{i}", [D, T], F32, kind="Internal").ap())
    xs.append(io["x_out"])
    es = ExitStack()
    p = Prog(nc, es)
    add_bf16_shadows(nc, p, io)
    C = load_consts(p, es, nc, io)
    C["ones8col"] = sb(es, nc, "sb_o8", [8, 1], F32)[:, :]
    wm = p.dma("sync", C["ones8col"], io["c_ones8col"])
    p.wait("tensor", "sync", wm)
    for k_, (tag, em) in enumerate(FUSED_EMITS):
        io_k = dict(io)
        io_k["x_in"] = xs[k_]
        io_k["x_out"] = xs[k_ + 1]
        with ExitStack() as es_k:
            em(p, es_k, io_k, C, tag)
        p.barrier()
        p.recycle_dma_sems()
    block = es.enter_context(nc.Block())
    p.finalize(block)
    es.close()
    return nc


def fused_weight_dict(inputs):
    wd = {}
    for prep in (prep_l0attna, prep_l0ffn, prep_l0attnb,
                 prep_l1attna, prep_l1ffn, prep_l1attnb):
        d = prep(inputs)
        clash = set(d) & set(wd)
        assert not clash, clash
        wd.update(d)
    return wd


def run_fused(x_full, inputs):
    nc = build_fused()
    return _run(nc, x_full, [fused_weight_dict(inputs)])


def run_l1ffn(x_full, inputs):
    nc = _build([declare_l1ffn], [lambda p, es, io, C: emit_l1ffn(p, es, io, C, "f1_")])
    return _run(nc, x_full, [prep_l1ffn(inputs)])


def run_l0attnb(x_full, inputs):
    nc = _build([declare_l0attnb], [lambda p, es, io, C: emit_l0attnb(p, es, io, C, "b3_")])
    return _run(nc, x_full, [prep_l0attnb(inputs)])


def run_l0attna(x_full, inputs, phases="ABCD"):
    nc = _build([declare_l0attna],
                [lambda p, es, io, C: emit_l0attna(p, es, io, C, "a1_", phases=phases)])
    return _run(nc, x_full, [prep_l0attna(inputs)])


def run_l1attna(x_full, inputs):
    nc = _build([declare_l1attna], [lambda p, es, io, C: emit_l1attna(p, es, io, C, "a5_")])
    return _run(nc, x_full, [prep_l1attna(inputs)])


def run_l1attnb(x_full, inputs, debug=False):
    def decl(nc, io):
        declare_l1attnb(nc, io)
        if debug:
            io["dbg"] = {"o1": nc.dram_tensor("dbg_o1", [PC, ND, T], BF16,
                                              kind="ExternalOutput").ap()}
            for nm in ("Soc", "woc", "uoc"):
                io["dbg"][nm] = nc.dram_tensor("dbg_" + nm, [PC, T], F32,
                                               kind="ExternalOutput").ap()
            io["dbg"]["koc"] = nc.dram_tensor("dbg_koc", [PC, T], BF16,
                                              kind="ExternalOutput").ap()
            io["dbg"]["ksum"] = nc.dram_tensor("dbg_ksum", [2, T], BF16,
                                               kind="ExternalOutput").ap()
    nc = _build([decl], [lambda p, es, io, C: emit_l1attnb(p, es, io, C, "b7_",
                                                          dbg=io.get("dbg"))])
    cc = host_consts()
    wd = prep_l1attnb(inputs)
    in_maps = []
    for core in range(8):
        bb = core // 2
        m = {"x_in": np.ascontiguousarray(x_full[bb].T)}
        m.update(wd)
        m.update(cc)
        in_maps.append(m)
    res = run_bass_kernel_spmd(nc, in_maps, core_ids=list(range(8)))
    out = np.stack([np.asarray(res.results[2 * bb]["x_out"]).T for bb in range(B)])
    if debug:
        return out, {k2: np.asarray(res.results[0]["dbg_" + k2])
                     for k2 in ("o1", "Soc", "woc", "uoc", "koc", "ksum")}
    return out


def run_l0ffn(x_full, inputs, debug=False):
    def decl_dbg(nc, io):
        declare_l0ffn(nc, io)
        if debug:
            io["dbg"] = {
                "yh": nc.dram_tensor("dbg_yh", [PC, ND, 512], BF16, kind="ExternalOutput").ap(),
                "hall": nc.dram_tensor("dbg_hall", [PC, 128, 512], BF16, kind="ExternalOutput").ap(),
                "xnb": nc.dram_tensor("dbg_xnb", [PC, ND, 512], BF16, kind="ExternalOutput").ap(),
            }
    nc = _build([decl_dbg], [lambda p, es, io, C: emit_l0ffn(p, es, io, C, "f0_",
                                                            dbg=io.get("dbg"))])
    cc = host_consts()
    wd = prep_l0ffn(inputs)
    in_maps = []
    for core in range(8):
        bb = core // 2
        m = {"x_in": np.ascontiguousarray(x_full[bb].T)}
        m.update(wd)
        m.update(cc)
        in_maps.append(m)
    res = run_bass_kernel_spmd(nc, in_maps, core_ids=list(range(8)))
    out = np.stack([np.asarray(res.results[2 * bb]["x_out"]).T for bb in range(B)])
    if debug:
        return out, {k2: np.asarray(res.results[0][f"dbg_{k2}"]) for k2 in ("yh", "hall", "xnb")}
    return out



def kernel(**inputs):
    x = np.ascontiguousarray(np.asarray(inputs["L0_attn_a_x"], np.float32))
    nc = build_fused()
    return _run(nc, x, [fused_weight_dict(inputs)])
